# version 03
# baseline (speedup 1.0000x reference)
import contextlib
import numpy as np
import ml_dtypes
import concourse.bass as bass
import concourse.mybir as mybir
from concourse.bass_utils import run_bass_kernel_spmd

F32 = mybir.dt.float32
BF16 = mybir.dt.bfloat16
I32 = mybir.dt.int32
AF = mybir.ActivationFunctionType
ALU = mybir.AluOpType
AX = mybir.AxisListType
BF = ml_dtypes.bfloat16

H, HD, D, S, NQ = 16, 64, 1024, 4096, 2048
NKC, NQT, NE, ED = 32, 16, 18, 512
EPS = 1e-5
MSLACK = 24.0

def region_table(kc):
    regions = []
    if kc == 0:
        regions.append((256, 2048, 1))
    elif 1 <= kc <= 15:
        qa = 128 * (kc - 1)
        if qa > 0:
            regions.append((0, qa, 0))
        if qa + 384 < 2048:
            regions.append((qa + 384, 2048, 1))
    elif kc == 16:
        regions.append((0, 1920, 2))
    elif 17 <= kc <= 30:
        regions.append((0, 2048, 2))
    else:
        regions.append((128, 2048, 2))
    return regions

def band_pieces_of_kc(kc):
    out = []
    for p in (0, 1, 2):
        qt = (kc if kc != 31 else -1) + 1 - p
        if 0 <= qt < NQT:
            out.append((qt, p, qt * 128))
    return out

ENGS = ("sync", "gpsimd", "tensor", "vector", "scalar")

DMA_ISSUER = {"sIn": "sync", "sKV": "sync", "sWr": "sync", "sGa": "sync",
              "sTp": "sync", "sSt": "sync", "sFw": "gpsimd"}
RELAY_SEM = {"sync": "sSy", "gpsimd": "sGy"}

class Plan:
    def __init__(self, nc):
        self.nc = nc
        self.stream = {e: [] for e in ENGS}
        self.sems = {}
        self.relay_cache = {}

    def sem(self, es, name):
        h = es.enter_context(self.nc.semaphore(name))
        self.sems[name] = [h, 0]

    def inc(self, name, k):
        s = self.sems[name]
        s[1] += k
        return s[1]

    def val(self, name):
        return self.sems[name][1]

    def emit(self, eng, fn):
        self.stream[eng].append(fn)

    def _rawwait(self, eng, name, v):
        h = self.sems[name][0]
        fn = lambda E, h=h, v=v: E.wait_ge(h, v)
        fn._is_wait = True
        self.emit(eng, fn)

    def wait(self, eng, name, v):
        if v <= 0:
            return
        assert v <= self.sems[name][1], f"wait on future {name}: {v} > {self.sems[name][1]}"
        if name in DMA_ISSUER:
            issuer = DMA_ISSUER[name]
            total = self.sems[name][1]
            if eng == issuer:
                self._rawwait(eng, name, total)
                self.relay_cache.pop(name, None) if False else None
                return
            rs = RELAY_SEM[issuer]
            cached = self.relay_cache.get(name)
            if cached is None or cached[0] != total:
                self._rawwait(issuer, name, total)
                h = self.sems[rs][0]
                rv = self.inc(rs, 1)
                self.emit(issuer, lambda E, h=h: E.nop().then_inc(h, 1))
                self.relay_cache[name] = (total, rv)
            rv = self.relay_cache[name][1]
            self._rawwait(eng, rs, rv)
            return
        self._rawwait(eng, name, v)

    def mark(self, eng, sem):
        assert self.stream[eng], "mark on empty stream"
        last = self.stream[eng][-1]
        assert not getattr(last, "_has_upd", False), f"double mark on {eng}"
        assert not getattr(last, "_is_wait", False), f"mark on wait on {eng}"
        v = self.inc(sem, 1)
        h = self.sems[sem][0]
        self.stream[eng].pop()
        wrapped = lambda E, last=last, h=h: last(E).then_inc(h, 1)
        wrapped._has_upd = True
        self.stream[eng].append(wrapped)
        return v

    def op(self, eng, fn, sem=None, k=1):
        if sem is None:
            self.emit(eng, lambda E, fn=fn: fn(E))
            return None
        v = self.inc(sem, k)
        h = self.sems[sem][0]
        wrapped = lambda E, fn=fn, h=h, k=k: fn(E).then_inc(h, k)
        wrapped._has_upd = True
        self.emit(eng, wrapped)
        return v

    def run(self, block):
        for eng in ENGS:
            ops = self.stream[eng]
            getattr(block, eng)(lambda E, ops=ops: [f(E) for f in ops])


def build_program(n_heads=H, do_p3=True, dbg=False):
    nc = bass.Bass("TRN2", target_bir_lowering=False)
    es = contextlib.ExitStack()

    din = lambda n, s, dt=F32: nc.dram_tensor(n, s, dt, kind="ExternalInput")
    xT = din("xT", [D, S])
    wqT = din("wqT", [D, D]); wkT = din("wkT", [D, D]); wvT = din("wvT", [D, D])
    bq_i = din("bq", [128, 8]); bk_i = din("bk", [128, 8])
    bv_row = din("bv_row", [1, D])
    wrpadT = din("wrpadT", [64, 512]); brpad_i = din("brpad", [1, 512])
    wredgeT = din("wredgeT", [64, 2]); bredge_i = din("bredge", [2, 1])
    wupT = din("wupT", [NE, D, ED]); wdnT = din("wdnT", [NE, ED, D])
    wrtT = din("wrtT", [D, 16])
    iden_i = din("iden", [128, 128], BF16)
    iota_i = din("iota", [128, 16])
    auxsel_i = din("auxsel_in", [6, 128], BF16)
    ovr_i = din("ovr", [2, 128, 512], BF16)
    ovrinv_i = din("ovrinv", [2, 128, 512], BF16)

    out_d = nc.dram_tensor("out", [D, NQ], F32, kind="ExternalOutput")

    k_hbm = nc.dram_tensor("k_hbm", [8, 128, S], BF16)
    q_hbm = nc.dram_tensor("q_hbm", [8, 128, NQ], BF16)
    v_hbm = nc.dram_tensor("v_hbm", [NKC, 128, H * 65], BF16)
    rpad_hbm = nc.dram_tensor("rpad_hbm", [2, NQ, 512], F32)
    m_hbm = nc.dram_tensor("m_hbm", [2, NQ], F32)
    mb_hbm = nc.dram_tensor("mb_hbm", [2, NQ], BF16)
    aux_hbm = nc.dram_tensor("aux_hbm", [2, 2, NQ], BF16)
    gate_hbm = nc.dram_tensor("gate_hbm", [16, 1024], BF16)
    ao_hbm = nc.dram_tensor("ao_hbm", [64, NQ], F32)

    sbt = lambda n, s, dt: es.enter_context(nc.sbuf_tensor(n, s, dt))
    x_own = sbt("x_own", [128, 16384], F32)
    big1 = sbt("big1", [128, 16384], BF16)
    big2 = sbt("big2", [128, 8192], BF16)
    big3 = sbt("big3", [128, 8192], BF16)
    f32b = sbt("f32b", [128, 8192], F32)
    f32c = sbt("f32c", [128, 4096], F32)
    bfc = sbt("bfc", [128, 4096], BF16)
    rw = sbt("rw", [128, 80], F32)
    epi = sbt("epi", [128, 2048], F32)
    ones_bf = sbt("ones_bf", [128, 128], BF16)
    iden_bf = sbt("iden_bf", [128, 128], BF16)
    iota_sb = sbt("iota_sb", [128, 16], F32)
    auxsel = sbt("auxsel", [128, 384], BF16)
    auxv = sbt("auxv", [128, 2048], BF16)
    ovrb = sbt("ovrb", [128, 2048], BF16)
    bq_sb = sbt("bq_sb", [128, 8], F32)
    bk8_sb = sbt("bk8_sb", [128, 8], F32)
    bVbc = sbt("bVbc", [128, 1024], BF16)
    brbc = sbt("brbc", [128, 512], BF16)
    scal = sbt("scal", [128, 32], F32)
    scb = sbt("scb", [128, 32], BF16)
    mqcol = sbt("mqcol", [128, 16], F32)
    mhcol = sbt("mhcol", [128, 16], F32)
    mhbf = sbt("mhbf", [128, 16], BF16)
    c1bc = sbt("c1bc", [128, 1], F32)
    c2bc = sbt("c2bc", [128, 1], F32)
    bredge_sb = sbt("bredge_sb", [128, 1], F32)
    wrpad2 = sbt("wrpad2", [128, 512], BF16)
    wredge2 = sbt("wredge2", [128, 2], BF16)

    T1 = lambda r=slice(0, 128): f32c[r, 0:1024]
    T2 = lambda r=slice(0, 128): f32c[r, 1024:2048]
    BC1 = f32c[:, 2048:3072]
    BC2 = f32c[:, 3072:4096]
    vstage = bfc[:, 1536:2576]
    sqP1 = bfc[:, 3072:3584]
    sqP2 = big1[:, 10240:14336]
    sqP3 = bfc[:, 2048:2560]
    wrt_sb2 = big1[:, 14336:14464]

    psA = [es.enter_context(nc.psum_tensor("psA0", [128, 512], F32)),
           es.enter_context(nc.psum_tensor("psA1", [128, 512], F32))]
    psPV = es.enter_context(nc.psum_tensor("psPV", [128, 2048], F32))
    psB = [es.enter_context(nc.psum_tensor("psB0", [128, 512], F32)),
           es.enter_context(nc.psum_tensor("psB1", [128, 512], F32))]

    P = Plan(nc)
    for s in ("sIn", "sKV", "sWr", "sGa", "sTp", "sSt", "sFw", "sPe", "sVe", "sSc",
              "sSy", "sGy"):
        P.sem(es, s)

    def dma(eng, dst, src, sem="sIn", slow=False):
        return P.op(eng, lambda E, d=dst, s=src, sl=slow:
                    E.dma_start(out=d, in_=s, allow_slow_non_contiguous=sl), sem, 16)

    def mm(dst, lhsT, rhs, start, stop, sem=None):
        return P.op("tensor", lambda E, d=dst, l=lhsT, r=rhs, st=start, sp=stop:
                    E.matmul(d, l, r, start=st, stop=sp), sem, 1)

    def vop(fn, sem=None):
        return P.op("vector", fn, sem, 1)

    def sop(fn, sem=None):
        return P.op("scalar", fn, sem, 1)

    def vdrain():
        P.emit("vector", lambda E: E.drain())

    def sdrain():
        P.emit("scalar", lambda E: E.drain())

    def fisr(dst, src, n, rows=slice(0, 128), it=3):
        ws = f32c[rows, 1024:1024 + n]
        ii = ws.bitcast(I32)
        tt = ws
        vop(lambda E: E.tensor_copy(ii, src.bitcast(I32))); vdrain()
        vop(lambda E: E.tensor_scalar(ii, ii, 1, None, ALU.arith_shift_right)); vdrain()
        vop(lambda E: E.tensor_scalar(ii, ii, -1, 0x5f3759df, ALU.mult, ALU.add)); vdrain()
        vop(lambda E: E.tensor_copy(dst, ii.bitcast(F32))); vdrain()
        for _ in range(it):
            vop(lambda E: E.tensor_mul(tt, dst, dst)); vdrain()
            vop(lambda E: E.tensor_mul(tt, tt, src)); vdrain()
            vop(lambda E: E.tensor_scalar(tt, tt, -0.5, 1.5, ALU.mult, ALU.add)); vdrain()
            vop(lambda E: E.tensor_mul(dst, dst, tt)); vdrain()

    dma("sync", iden_bf[:, :], iden_i.ap())
    dma("sync", iota_sb[:, :], iota_i.ap())
    for r0 in (0, 64):
        for v in range(3):
            dma("sync", auxsel[r0:r0 + 2, v * 128:(v + 1) * 128],
                auxsel_i.ap()[2 * v:2 * v + 2, :])
    dma("sync", ovrb[:, 0:512], ovr_i.ap()[0])
    dma("sync", ovrb[:, 512:1024], ovr_i.ap()[1])
    dma("sync", ovrb[:, 1024:1536], ovrinv_i.ap()[0])
    dma("sync", ovrb[:, 1536:2048], ovrinv_i.ap()[1])
    dma("sync", bq_sb[:, :], bq_i.ap())
    dma("sync", bk8_sb[:, :], bk_i.ap())
    dma("sync", f32c[0:1, 2048:3072], bv_row.ap())
    dma("sync", bredge_sb[0:2, 0:1], bredge_i.ap())
    dma("sync", f32c[0:1, 3584:4096], brpad_i.ap())
    for r0 in (0, 64):
        dma("gpsimd", wrpad2[r0:r0 + 64, :], wrpadT.ap(), "sFw")
        dma("gpsimd", wredge2[r0:r0 + 64, :], wredgeT.ap(), "sFw")
    for c in range(8):
        dma("gpsimd", big1[:, c * 1024:(c + 1) * 1024], wqT.ap()[c * 128:(c + 1) * 128, :],
            "sFw")
        dma("gpsimd", big1[:, 8192 + c * 1024:8192 + (c + 1) * 1024],
            wkT.ap()[c * 128:(c + 1) * 128, :], "sFw")
        dma("gpsimd", big2[:, c * 1024:(c + 1) * 1024], wvT.ap()[c * 128:(c + 1) * 128, :],
            "sFw")
    v_in = P.val("sIn")
    v_gw = P.val("sFw")

    P.wait("vector", "sIn", v_in)
    vop(lambda E: E.memset(ones_bf[:, :], 1.0))
    vop(lambda E: E.tensor_scalar(bk8_sb[:, :], bk8_sb[:, :], 0.125, None, ALU.mult))
    vop(lambda E: E.tensor_copy(bfc[0:1, 3072:4096], f32c[0:1, 2048:3072]))
    vop(lambda E: E.tensor_copy(bfc[0:1, 1024:1536], f32c[0:1, 3584:4096]))
    vdrain()
    v_rows0 = P.mark("vector", "sVe")
    P.wait("tensor", "sVe", v_rows0)
    for jp in range(2):
        mm(psPV[:, jp * 512:(jp + 1) * 512], ones_bf[0:1, :],
           bfc[0:1, 3072 + jp * 512:3072 + (jp + 1) * 512], True, True)
    mm(psB[0][:, :], ones_bf[0:1, :], bfc[0:1, 1024:1536], True, True)
    t_bc0 = P.mark("tensor", "sPe")
    P.wait("vector", "sPe", t_bc0)
    vop(lambda E: E.tensor_copy(bVbc[:, :], psPV[:, 0:1024]))
    vop(lambda E: E.tensor_copy(brbc[:, :], psB[0][:, :]))
    vdrain()
    v_bc0 = P.mark("vector", "sVe")

    P.wait("scalar", "sIn", v_in)
    P.wait("scalar", "sFw", v_gw)
    P.wait("scalar", "sPe", t_bc0)
    s_wr2 = sop(lambda E: E.square(sqP1[0:64, :], wrpad2[0:64, :]), "sSc")
    P.wait("tensor", "sSc", s_wr2)
    P.wait("tensor", "sVe", v_rows0)
    mm(psB[1][0:1, :], ones_bf[0:64, 0:1], sqP1[0:64, :], True, True)
    t_wr = P.mark("tensor", "sPe")
    P.wait("vector", "sPe", t_wr)
    vop(lambda E: E.reduce_max(scal[0:1, 0:1], psB[1][0:1, 128:385], axis=AX.X)); vdrain()
    fisr(scal[0:1, 1:2], scal[0:1, 0:1], 1, rows=slice(0, 1))
    vop(lambda E: E.tensor_mul(scal[0:1, 2:3], scal[0:1, 0:1], scal[0:1, 1:2])); vdrain()
    vop(lambda E: E.tensor_scalar(f32c[0:1, 1024:1536], f32c[0:1, 3584:4096],
                                  -1.0, None, ALU.mult)); vdrain()
    vop(lambda E: E.tensor_max(f32c[0:1, 1024:1536], f32c[0:1, 1024:1536],
                               f32c[0:1, 3584:4096])); vdrain()
    vop(lambda E: E.reduce_max(scal[0:1, 3:4], f32c[0:1, 1152:1409],
                               axis=AX.X)); vdrain()
    vop(lambda E: E.tensor_scalar(scal[0:1, 5:6], scal[0:1, 3:4], 1.0, -MSLACK,
                                  ALU.mult, ALU.add)); vdrain()
    vop(lambda E: E.tensor_copy(scb[0:1, 0:1], scal[0:1, 5:6])); vdrain()
    v_c2 = P.mark("vector", "sVe")
    P.wait("tensor", "sVe", v_c2)
    mm(psB[1][:, 0:1], ones_bf[0:1, :], scb[0:1, 0:1], True, True)
    t_c2 = P.mark("tensor", "sPe")
    P.wait("vector", "sPe", t_c2)
    vop(lambda E: E.tensor_copy(c2bc[:, :], psB[1][:, 0:1]))
    vdrain()
    v_c2b = P.mark("vector", "sVe")

    P.wait("tensor", "sFw", v_gw)
    xslab = [f32b[:, 0:4096], f32b[:, 4096:8192]]
    xlnsl = [big3[:, 0:4096], big3[:, 4096:8192]]
    kstage = [bfc[:, 0:512], bfc[:, 512:1024]]

    slab_free_v = [0, 0]
    slab_xln_free = [0, 0]
    psA_fr = [0, 0]
    psV_fr = v_bc0
    kst_fr = [0, 0]
    vst_fr = 0
    x_own_ld = 0

    for s_i in range(8):
        b = s_i % 2
        P.wait("sync", "sVe", slab_free_v[b])
        for c in range(8):
            dma("sync", xslab[b][:, c * 512:(c + 1) * 512],
                xT.ap()[c * 128:(c + 1) * 128, s_i * 512:(s_i + 1) * 512])
        if s_i < 4:
            for c in range(8):
                dma("sync", x_own[:, c * 2048 + s_i * 512:c * 2048 + (s_i + 1) * 512],
                    xT.ap()[c * 128:(c + 1) * 128, s_i * 512:(s_i + 1) * 512])
            x_own_ld = P.val("sIn")
        ld_done = P.val("sIn")

        P.wait("vector", "sIn", ld_done)
        P.wait("vector", "sPe", slab_xln_free[b])
        for c in range(8):
            vop(lambda E, b=b, c=c: E.tensor_copy(
                xlnsl[b][:, c * 512:(c + 1) * 512], xslab[b][:, c * 512:(c + 1) * 512]))
        v_cast = P.mark("vector", "sVe")
        P.wait("scalar", "sVe", v_cast)
        P.wait("tensor", "sVe", v_cast)
        for c in range(8):
            mm(psB[0][0:1, :], ones_bf[:, 0:1], xlnsl[b][:, c * 512:(c + 1) * 512],
               c == 0, c == 7)
        for c in range(8):
            s_sq = sop(lambda E, b=b, c=c: E.square(
                sqP1[:, :], xlnsl[b][:, c * 512:(c + 1) * 512]), "sSc")
            P.wait("tensor", "sSc", s_sq)
            mm(psB[1][0:1, :], ones_bf[:, 0:1], sqP1[:, :], c == 0, c == 7)
            t_sqc = P.mark("tensor", "sPe")
            P.wait("scalar", "sPe", t_sqc)
        t_st = P.val("sPe")
        P.wait("vector", "sPe", t_st)
        MUr = f32c[0:1, 2048:2560]
        VARr = f32c[0:1, 2560:3072]
        MSQ = f32c[0:1, 3072:3584]
        INVr = f32c[0:1, 3584:4096]
        vop(lambda E: E.tensor_scalar(MUr, psB[0][0:1, :], 1.0 / 1024, None, ALU.mult))
        vop(lambda E: E.tensor_scalar(VARr, psB[1][0:1, :], 1.0 / 1024, None, ALU.mult))
        vdrain()
        vop(lambda E: E.tensor_mul(MSQ, MUr, MUr)); vdrain()
        vop(lambda E: E.tensor_sub(VARr, VARr, MSQ)); vdrain()
        vop(lambda E: E.tensor_scalar(VARr, VARr, 1.0, EPS, ALU.mult, ALU.add)); vdrain()
        fisr(INVr, VARr, 512, rows=slice(0, 1))
        vop(lambda E: E.tensor_copy(bfc[0:1, 3584:4096], MUr))
        vop(lambda E: E.tensor_copy(bfc[0:1, 1024:1536], INVr))
        vdrain()
        v_r = P.mark("vector", "sVe")
        P.wait("tensor", "sVe", v_r)
        mm(psB[0][:, :], ones_bf[0:1, :], bfc[0:1, 3584:4096], True, True)
        mm(psB[1][:, :], ones_bf[0:1, :], bfc[0:1, 1024:1536], True, True)
        t_bc = P.mark("tensor", "sPe")
        P.wait("vector", "sPe", t_bc)
        vop(lambda E: E.tensor_copy(BC1[:, 0:512], psB[0][:, :]))
        vop(lambda E: E.tensor_copy(BC2[:, 0:512], psB[1][:, :]))
        vdrain()
        for c in range(8):
            vop(lambda E, b=b, c=c: E.tensor_sub(
                f32c[:, 0:512], xslab[b][:, c * 512:(c + 1) * 512], f32c[:, 2048:2560]))
            vdrain()
            vop(lambda E, b=b, c=c: E.tensor_mul(
                xlnsl[b][:, c * 512:(c + 1) * 512], f32c[:, 0:512], f32c[:, 3072:3584]))
            vdrain()
        v_xln = P.mark("vector", "sVe")
        slab_free_v[b] = v_xln

        P.wait("tensor", "sVe", v_xln)
        kinds = ("k", "q") if s_i < 4 else ("k",)
        for kind in kinds:
            woff = 8192 if kind == "k" else 0
            bias = bk8_sb if kind == "k" else bq_sb
            scale = 0.125 if kind == "k" else 1.0
            hbm = k_hbm if kind == "k" else q_hbm
            for fc in range(8):
                pA = fc % 2
                P.wait("tensor", "sSc", psA_fr[pA])
                for dc in range(8):
                    mm(psA[pA][:, :],
                       big1[:, woff + dc * 1024 + fc * 128:woff + dc * 1024 + (fc + 1) * 128],
                       xlnsl[b][:, dc * 512:(dc + 1) * 512], dc == 0, dc == 7)
                t_p = P.mark("tensor", "sPe")
                P.wait("scalar", "sPe", t_p)
                P.wait("scalar", "sWr", kst_fr[pA])
                sv = sop(lambda E, pA=pA, bias=bias, fc=fc, scale=scale: E.activation(
                    kstage[pA][:, :], psA[pA][:, :], AF.Identity,
                    bias=bias[:, fc:fc + 1], scale=scale), "sSc")
                psA_fr[pA] = sv
                P.wait("sync", "sSc", sv)
                kst_fr[pA] = dma("sync", hbm.ap()[fc, :, s_i * 512:(s_i + 1) * 512],
                                 kstage[pA][:, :], "sWr")
        for tc in range(4):
            kc_g = s_i * 4 + tc
            P.wait("tensor", "sVe", psV_fr)
            for jp in range(2):
                for dc in range(8):
                    mm(psPV[:, jp * 512:(jp + 1) * 512],
                       xlnsl[b][:, dc * 512 + tc * 128:dc * 512 + (tc + 1) * 128],
                       big2[:, dc * 1024 + jp * 512:dc * 1024 + (jp + 1) * 512],
                       dc == 0, dc == 7)
            t_v = P.mark("tensor", "sPe")
            P.wait("vector", "sPe", t_v)
            P.wait("vector", "sWr", vst_fr)
            if s_i == 0 and tc == 0:
                vop(lambda E: E.memset(vstage[:, :], 1.0))
                vdrain()
            for hh in range(16):
                vop(lambda E, hh=hh: E.tensor_add(
                    vstage[:, hh * 65:hh * 65 + 64], psPV[:, hh * 64:(hh + 1) * 64],
                    bVbc[:, hh * 64:(hh + 1) * 64]))
            psV_fr = P.mark("vector", "sVe")
            P.wait("sync", "sVe", psV_fr)
            vst_fr = dma("sync", v_hbm.ap()[kc_g], vstage[:, :], "sWr")
        slab_xln_free[b] = P.val("sPe")

    p1_pe = P.val("sPe")
    p1_ve = P.val("sVe")
    p1_sc = P.val("sSc")
    p1_wr = P.val("sWr")

    qbuf = [big1[:, 0:2048], big1[:, 2048:4096]]
    bandT = big1[:, 4096:4096 + 48 * 128]
    kbuf = [big2[:, 0:4096], big2[:, 4096:8192]]
    vbuf = [big3[:, 0:2080], big3[:, 2080:4160]]
    pchunk = [big3[:, 4160:4672], big3[:, 4672:5184]]
    bandP = [big3[:, 5184:5568], big3[:, 5568:5952]]
    possb = [f32b[:, 0:384], f32b[:, 384:768]]
    rpstg = [f32b[:, 768:1280], f32b[:, 1280:1792]]
    mqrow = f32b[0:1, 1792:3840]
    mkrow = f32b[0:1, 3840:7936]

    kv_rdy = [0] * H
    head_free = [p1_pe, p1_pe]
    epi_done = p1_ve
    prev_band_sc = p1_sc
    x_ready = x_own_ld

    def emit_head_loads(h):
        hb = h % 2
        pb = (h % 2) * 64
        ch = h // 2
        P.wait("sync", "sPe", head_free[hb])
        if h < 2:
            P.wait("sync", "sWr", p1_wr)
        dma("sync", kbuf[hb][pb:pb + 64, :], k_hbm.ap()[ch, pb:pb + 64, :], "sKV")
        dma("sync", qbuf[hb][pb:pb + 64, :], q_hbm.ap()[ch, pb:pb + 64, :], "sKV")
        vdst = bass.AP(big3, hb * 2080, [[8192, 128], [65, NKC], [1, 65]])
        vsrc = bass.AP(v_hbm, h * 65, [[H * 65, 128], [128 * H * 65, NKC], [1, 65]])
        dma("sync", vdst, vsrc, "sKV")
        kv_rdy[h] = P.val("sKV")

    if n_heads > 0:
        emit_head_loads(0)
    if n_heads > 1:
        emit_head_loads(1)

    for h in range(n_heads):
        hb = h % 2
        pb = (h % 2) * 64
        auxb = 64 - pb
        xc = (h // 2) * 2048

        P.wait("tensor", "sKV", kv_rdy[h])
        P.wait("tensor", "sSc", prev_band_sc)
        P.wait("tensor", "sVe", epi_done)
        rp_ev = [0, 0]
        rp_st = [0, 0]
        rp_wr = [0] * NQT
        for qt in range(NQT):
            pB = qt % 2
            P.wait("tensor", "sVe", rp_ev[pB])
            mm(psB[pB][:, :], qbuf[hb][pb:pb + 64, qt * 128:(qt + 1) * 128],
               wrpad2[pb:pb + 64, :], True, True)
            t_rp = P.mark("tensor", "sPe")
            P.wait("vector", "sPe", t_rp)
            P.wait("vector", "sWr", rp_st[pB])
            vop(lambda E, pB=pB: E.tensor_add(rpstg[pB][:, :], psB[pB][:, :], brbc[:, :]))
            if qt in (0, 15):
                mi = 0 if qt == 0 else 1
                ecol = 384 if qt == 0 else 128
                vdrain()
                vop(lambda E, pB=pB, ecol=ecol: E.tensor_copy(
                    scal[:, 8:9], rpstg[pB][:, ecol:ecol + 1]))
                vdrain()
                vop(lambda E, pB=pB, mi=mi: E.tensor_mul(
                    rpstg[pB][:, :], rpstg[pB][:, :], ovrb[:, 1024 + mi * 512:1536 + mi * 512]))
                vop(lambda E, mi=mi: E.tensor_scalar(
                    f32c[:, 0:512], ovrb[:, mi * 512:(mi + 1) * 512],
                    scal[:, 8:9], None, ALU.mult))
                vdrain()
                vop(lambda E, pB=pB: E.tensor_add(rpstg[pB][:, :], rpstg[pB][:, :],
                                                  f32c[:, 0:512]))
            rp_ev[pB] = P.mark("vector", "sVe")
            P.wait("sync", "sVe", rp_ev[pB])
            rp_st[pB] = dma("sync",
                            bass.AP(rpad_hbm, hb * NQ * 512 + qt * 128 * 512,
                                    [[512, 128], [1, 512]]),
                            rpstg[pB][:, :], "sWr")
            rp_wr[qt] = rp_st[pB]

        P.wait("tensor", "sVe", P.val("sVe"))
        P.wait("scalar", "sKV", kv_rdy[h])
        s_q2 = sop(lambda E, hb=hb, pb=pb: E.square(
            sqP2[pb:pb + 64, 0:2048], qbuf[hb][pb:pb + 64, :]), "sSc")
        P.wait("tensor", "sSc", s_q2)
        t_mq_last = 0
        v_prev = 0
        for jp in range(4):
            P.wait("tensor", "sVe", v_prev)
            mm(psB[0][0:1, :], ones_bf[pb:pb + 64, 0:1],
               sqP2[pb:pb + 64, jp * 512:(jp + 1) * 512], True, True)
            t_mq = P.mark("tensor", "sPe")
            t_mq_last = t_mq
            P.wait("vector", "sPe", t_mq)
            v_prev = vop(lambda E, jp=jp: E.tensor_copy(
                mqrow[:, jp * 512:(jp + 1) * 512], psB[0][0:1, :]), "sVe")
        v_mq = v_prev
        P.wait("sync", "sVe", v_mq)
        w_mq = dma("sync", m_hbm.ap()[hb:hb + 1, :], mqrow[:, :], "sWr")
        P.wait("sync", "sWr", w_mq)
        g_mq = dma("sync", mqcol[:, :],
                   bass.AP(m_hbm, hb * NQ, [[1, 128], [128, 16]]), "sGa", slow=True)
        P.wait("scalar", "sPe", t_mq_last)
        s_k2 = sop(lambda E, hb=hb, pb=pb: E.square(
            sqP2[pb:pb + 64, 0:4096], kbuf[hb][pb:pb + 64, :]), "sSc")
        P.wait("tensor", "sSc", s_k2)
        v_prev = 0
        for jp in range(8):
            P.wait("tensor", "sVe", v_prev)
            mm(psB[1][0:1, :], ones_bf[pb:pb + 64, 0:1],
               sqP2[pb:pb + 64, jp * 512:(jp + 1) * 512], True, True)
            t_mk = P.mark("tensor", "sPe")
            P.wait("vector", "sPe", t_mk)
            v_prev = vop(lambda E, jp=jp: E.tensor_copy(
                mkrow[:, jp * 512:(jp + 1) * 512], psB[1][0:1, :]), "sVe")
        vdrain()
        vop(lambda E: E.reduce_max(scal[0:1, 16:17], mkrow[:, :], axis=AX.X)); vdrain()
        fisr(scal[0:1, 17:18], scal[0:1, 16:17], 1, rows=slice(0, 1))
        vop(lambda E: E.tensor_mul(scal[0:1, 18:19], scal[0:1, 16:17], scal[0:1, 17:18]))
        vdrain()
        vop(lambda E: E.tensor_add(scal[0:1, 19:20], scal[0:1, 18:19], scal[0:1, 2:3]))
        vdrain()
        vop(lambda E: E.tensor_copy(scb[0:1, 1:2], scal[0:1, 19:20])); vdrain()
        v_c1 = P.mark("vector", "sVe")
        P.wait("tensor", "sVe", v_c1)
        mm(psB[1][:, 0:1], ones_bf[0:1, :], scb[0:1, 1:2], True, True)
        t_c1 = P.mark("tensor", "sPe")
        P.wait("vector", "sPe", t_c1)
        vop(lambda E: E.tensor_copy(c1bc[:, :], psB[1][:, 0:1]))
        P.wait("vector", "sGa", g_mq)
        vdrain()
        fisr(f32c[:, 0:16], mqcol[:, :], 16)
        vop(lambda E: E.tensor_mul(f32c[:, 16:32], mqcol[:, :], f32c[:, 0:16])); vdrain()
        P.wait("vector", "sVe", v_c2b)
        vop(lambda E: E.tensor_scalar(mhcol[:, :], f32c[:, 16:32], c1bc[:, 0:1],
                                      c2bc[:, 0:1], ALU.mult, ALU.add)); vdrain()
        vop(lambda E: E.tensor_copy(mhbf[:, :], mhcol[:, :])); vdrain()
        vop(lambda E: E.tensor_scalar(mhcol[:, :], mhbf[:, :], -1.0, None, ALU.mult))
        vdrain()
        v_mh = P.mark("vector", "sVe")
        P.wait("sync", "sVe", v_mh)
        w_mh = dma("sync", bass.AP(mb_hbm, hb * NQ, [[1, 128], [128, 16]]),
                   mhbf[:, :], "sWr", slow=True)
        P.wait("sync", "sWr", w_mh)
        g_m0 = dma("sync", bfc[0:1, 1024:3072], mb_hbm.ap()[hb:hb + 1, :], "sGa")
        g_m1 = dma("sync", bfc[1:2, 1024:3072], mb_hbm.ap()[hb:hb + 1, :], "sGa")

        P.wait("tensor", "sVe", P.val("sVe"))
        P.wait("vector", "sGa", g_m1)
        v_prev = 0
        for jp in range(4):
            P.wait("tensor", "sVe", v_prev)
            mm(psB[1][0:2, :], wredge2[pb:pb + 64, :],
               qbuf[hb][pb:pb + 64, jp * 512:(jp + 1) * 512], True, True)
            t_e = P.mark("tensor", "sPe")
            P.wait("vector", "sPe", t_e)
            vop(lambda E: E.tensor_scalar(f32c[0:2, 0:512], psB[1][0:2, :],
                                          bredge_sb[0:2, 0:1], None, ALU.add))
            vdrain()
            vop(lambda E, jp=jp: E.tensor_copy(
                f32c[0:2, 512:1024], bfc[0:2, 1024 + jp * 512:1024 + (jp + 1) * 512]))
            vdrain()
            vop(lambda E: E.tensor_sub(f32c[0:2, 0:512], f32c[0:2, 0:512],
                                       f32c[0:2, 512:1024]))
            vdrain()
            v_prev = vop(lambda E, jp=jp: E.tensor_copy(
                auxv[0:2, jp * 512:(jp + 1) * 512], f32c[0:2, 0:512]), "sVe")
        v_ax = v_prev
        P.wait("sync", "sVe", v_ax)
        w_ax = dma("sync", aux_hbm.ap()[hb], auxv[0:2, :], "sWr")
        P.wait("sync", "sWr", w_ax)
        g_ax = dma("sync", auxv[64:66, :], aux_hbm.ap()[hb], "sGa")

        P.wait("tensor", "sVe", P.val("sVe"))
        bp_sc = [prev_band_sc, prev_band_sc]
        bp_tp = [0, 0]
        for qt in range(NQT):
            pB = qt % 2
            P.wait("sync", "sWr", rp_wr[qt])
            P.wait("sync", "sSc", bp_sc[pB])
            g_pos = dma("sync", possb[pB][:, :],
                        bass.AP(rpad_hbm, hb * NQ * 512 + qt * 128 * 512 + 128,
                                [[511, 128], [1, 384]]), "sGa")
            P.wait("tensor", "sSc", bp_sc[pB])
            if qt == 0:
                mm(psB[pB][:, 0:128], qbuf[hb][pb:pb + 64, 0:128],
                   kbuf[hb][pb:pb + 64, 3968:4096], True, True)
                mm(psB[pB][:, 128:384], qbuf[hb][pb:pb + 64, 0:128],
                   kbuf[hb][pb:pb + 64, 0:256], True, True)
            else:
                j0 = qt * 128 - 128
                mm(psB[pB][:, 0:384], qbuf[hb][pb:pb + 64, qt * 128:(qt + 1) * 128],
                   kbuf[hb][pb:pb + 64, j0:j0 + 384], True, True)
            t_b = P.mark("tensor", "sPe")
            P.wait("vector", "sPe", t_b)
            P.wait("vector", "sGa", g_pos)
            v_b = vop(lambda E, pB=pB: E.tensor_add(
                psB[pB][:, 0:384], psB[pB][:, 0:384], possb[pB][:, :]), "sVe")
            P.wait("scalar", "sVe", v_b)
            P.wait("scalar", "sVe", v_mh)
            P.wait("scalar", "sTp", bp_tp[pB])
            s_b = sop(lambda E, pB=pB, qt=qt: E.activation(
                bandP[pB][:, :], psB[pB][:, 0:384], AF.Exp,
                bias=mhcol[:, qt:qt + 1]), "sSc")
            bp_sc[pB] = s_b
            P.wait("sync", "sSc", s_b)
            last_tp = 0
            for p in range(3):
                slot = qt * 3 + p
                last_tp = P.op("sync", lambda E, pB=pB, p=p, slot=slot:
                               E.dma_start_transpose(
                                   bandT[:, slot * 128:(slot + 1) * 128],
                                   bandP[pB][:, p * 128:(p + 1) * 128]), "sTp", 16)
            bp_tp[pB] = last_tp
        band_tp = P.val("sTp")
        prev_band_sc = P.val("sSc")

        P.wait("tensor", "sGa", g_ax)
        P.wait("tensor", "sVe", epi_done)
        pc_cp = [0, 0]
        pv_dn = [0, 0]
        psA_f2 = [p1_sc, p1_sc] if h == 0 else [0, 0]
        piece = 0
        for qhalf in range(2):
            for kc in range(NKC):
                regions = region_table(kc)
                bandp = band_pieces_of_kc(kc)
                for jp in range(2):
                    q0 = qhalf * 1024 + jp * 512
                    pA = piece % 2
                    pc = piece % 2
                    P.wait("tensor", "sSc", psA_f2[pA])
                    mm(psA[pA][:, :], kbuf[hb][pb:pb + 64, kc * 128:(kc + 1) * 128],
                       qbuf[hb][pb:pb + 64, q0:q0 + 512], True, False)
                    subr = [(max(lo, q0), min(hi, q0 + 512), v)
                            for (lo, hi, v) in regions
                            if max(lo, q0) < min(hi, q0 + 512)]
                    for (lo, hi, v) in subr:
                        mm(psA[pA][:, lo - q0:hi - q0],
                           auxsel[auxb:auxb + 2, v * 128:v * 128 + 128],
                           auxv[auxb:auxb + 2, lo:hi], False, True)
                    t_s = P.mark("tensor", "sPe")
                    P.wait("scalar", "sPe", t_s)
                    P.wait("scalar", "sPe", pv_dn[pc])
                    for (lo, hi, v) in subr:
                        sop(lambda E, pA=pA, pc=pc, lo=lo, hi=hi, q0=q0: E.activation(
                            pchunk[pc][:, lo - q0:hi - q0],
                            psA[pA][:, lo - q0:hi - q0], AF.Exp))
                    s_e = P.mark("scalar", "sSc")
                    psA_f2[pA] = s_e
                    P.wait("vector", "sTp", band_tp)
                    P.wait("vector", "sPe", pv_dn[pc])
                    ncopy = 0
                    for (qt, p, bq0) in bandp:
                        if q0 <= bq0 < q0 + 512:
                            slot = qt * 3 + p
                            vop(lambda E, pc=pc, slot=slot, off=bq0 - q0: E.tensor_copy(
                                pchunk[pc][:, off:off + 128],
                                bandT[:, slot * 128:(slot + 1) * 128]))
                            ncopy += 1
                    v_cp = P.mark("vector", "sVe") if ncopy else P.val("sVe")
                    pc_cp[pc] = v_cp
                    P.wait("tensor", "sSc", s_e)
                    P.wait("tensor", "sVe", v_cp)
                    pv_dn[pc] = mm(psPV[0:65, q0:q0 + 512],
                                   vbuf[hb][:, kc * 65:(kc + 1) * 65],
                                   pchunk[pc][:, :], kc == 0, kc == NKC - 1, "sPe")
                    piece += 1
        main_pe = P.val("sPe")
        head_free[hb] = main_pe
        if h + 2 < n_heads:
            emit_head_loads(h + 2)

        P.wait("vector", "sPe", main_pe)
        if h == 0:
            P.wait("vector", "sIn", x_ready)
        vop(lambda E: E.reciprocal(epi[64:65, :], psPV[64:65, :])); vdrain()
        vop(lambda E: E.tensor_copy(bfc[64:65, 0:2048], epi[64:65, :])); vdrain()
        vop(lambda E: E.tensor_copy(f32c[64:65, 0:2048], bfc[64:65, 0:2048])); vdrain()
        vop(lambda E: E.tensor_sub(f32c[64:65, 0:2048], epi[64:65, :],
                                   f32c[64:65, 0:2048])); vdrain()
        vop(lambda E: E.tensor_copy(bfc[64:65, 2048:4096], f32c[64:65, 0:2048])); vdrain()
        v_rc = P.mark("vector", "sVe")
        P.wait("tensor", "sVe", v_rc)
        P.wait("tensor", "sSc", P.val("sSc"))
        ao_wr = 0
        v_prev = 0
        for pp in range(4):
            P.wait("tensor", "sVe", v_prev)
            mm(psB[0][0:64, :], ones_bf[64:65, 0:64],
               bfc[64:65, pp * 512:(pp + 1) * 512], True, False)
            mm(psB[0][0:64, :], ones_bf[64:65, 0:64],
               bfc[64:65, 2048 + pp * 512:2048 + (pp + 1) * 512], False, True)
            t_r = P.mark("tensor", "sPe")
            P.wait("vector", "sPe", t_r)
            v_prev = vop(lambda E: E.tensor_copy(epi[0:64, 0:512], psB[0][0:64, :]),
                         "sVe")
            vdrain()
            if pb == 64:
                P.wait("vector", "sWr", ao_wr)
            vop(lambda E, pp=pp: E.tensor_mul(
                epi[0:64, 512:1024], psPV[0:64, pp * 512:(pp + 1) * 512],
                epi[0:64, 0:512]))
            vdrain()
            if pb == 0:
                vop(lambda E, pp=pp, xc=xc: E.tensor_add(
                    x_own[0:64, xc + pp * 512:xc + (pp + 1) * 512],
                    x_own[0:64, xc + pp * 512:xc + (pp + 1) * 512],
                    epi[0:64, 512:1024]))
                vdrain()
            else:
                v_pp = P.mark("vector", "sVe")
                P.wait("sync", "sVe", v_pp)
                ao_wr = dma("sync", ao_hbm.ap()[:, pp * 512:(pp + 1) * 512],
                            epi[0:64, 512:1024], "sWr")
        if pb == 64:
            P.wait("sync", "sWr", ao_wr)
            ao_rd = dma("sync", epi[64:128, 0:2048], ao_hbm.ap(), "sGa")
            P.wait("vector", "sGa", ao_rd)
            for pp in range(4):
                vop(lambda E, pp=pp, xc=xc: E.tensor_add(
                    x_own[64:128, xc + pp * 512:xc + (pp + 1) * 512],
                    x_own[64:128, xc + pp * 512:xc + (pp + 1) * 512],
                    epi[64:128, pp * 512:(pp + 1) * 512]))
                vdrain()
        epi_done = P.mark("vector", "sVe")

    p2_ve = P.val("sVe")
    p2_sc = P.val("sSc")
    p2_pe = P.val("sPe")

    xln2 = big1[:, 0:8192]
    act = big1[:, 8192:12288]
    wup = [big2[:, 0:4096], big2[:, 4096:8192]]
    wdn = [big3[:, 0:4096], big3[:, 4096:8192]]
    acc = f32b
    grow = bfc[0:1, 0:1024]
    gateT = bfc[0:16, 1024:2048]
    wrt_sb = wrt_sb2

    P.wait("gpsimd", "sPe", p1_pe)
    for c in range(8):
        dma("gpsimd", wrt_sb[:, c * 16:(c + 1) * 16],
            wrtT.ap()[c * 128:(c + 1) * 128, :], "sFw")
    wrt_rdy = P.val("sFw")

    if not do_p3:
        P.wait("vector", "sPe", p2_pe)
        P.emit("vector", lambda E: E.drain())
        st = 0
        for th in range(2):
            toff = th * 1024
            for c in range(8):
                v_o = vop(lambda E, c=c, toff=toff: E.tensor_copy(
                    T2(), x_own[:, c * 2048 + toff:c * 2048 + toff + 1024]), "sVe")
                P.wait("sync", "sVe", v_o)
                st = dma("sync", out_d.ap()[c * 128:(c + 1) * 128, toff:toff + 1024],
                         T2(), "sSt")
                P.wait("vector", "sSt", st)
        P.wait("sync", "sSt", P.val("sSt"))
        P.wait("sync", "sWr", P.val("sWr"))
        with nc.Block() as block:
            P.run(block)
        es.close()
        return nc

    wexp_free = [p2_pe, p2_pe]
    up_fr = [0, 0]
    dn_fr = [0, 0]
    gbc_fr = 0
    glt_fr = [0, 0]
    out_st = 0
    gstore = 0

    for th in range(2):
        toff = th * 1024
        P.wait("vector", "sPe", p2_pe)
        for c in range(8):
            vop(lambda E, c=c, toff=toff: E.tensor_copy(
                xln2[:, c * 1024:(c + 1) * 1024],
                x_own[:, c * 2048 + toff:c * 2048 + toff + 1024]))
        v_cast2 = P.mark("vector", "sVe")
        P.wait("tensor", "sVe", v_cast2)
        v_prev = 0
        for h2 in range(2):
            P.wait("scalar", "sPe", P.val("sPe"))
            P.wait("tensor", "sVe", v_prev)
            for c in range(8):
                mm(psB[0][0:1, :], ones_bf[:, 0:1],
                   xln2[:, c * 1024 + h2 * 512:c * 1024 + h2 * 512 + 512], c == 0, c == 7)
            t_a = P.mark("tensor", "sPe")
            P.wait("vector", "sPe", t_a)
            v_prev = vop(lambda E, h2=h2: E.tensor_scalar(
                f32c[0:1, 2048 + h2 * 512:2048 + h2 * 512 + 512], psB[0][0:1, :],
                1.0 / 1024, None, ALU.mult), "sVe")
            s_last = 0
            for c in range(8):
                P.wait("scalar", "sVe", v_cast2)
                s_last = sop(lambda E, c=c, h2=h2: E.square(
                    sqP3[:, :],
                    xln2[:, c * 1024 + h2 * 512:c * 1024 + h2 * 512 + 512]), "sSc")
                P.wait("tensor", "sSc", s_last)
                mm(psB[1][0:1, :], ones_bf[:, 0:1], sqP3[:, :], c == 0, c == 7)
                t_s3 = P.mark("tensor", "sPe")
                P.wait("scalar", "sPe", t_s3)
            t_b2 = P.val("sPe")
            P.wait("vector", "sPe", t_b2)
            v_prev = vop(lambda E, h2=h2: E.tensor_scalar(
                f32c[0:1, 3072 + h2 * 512:3072 + h2 * 512 + 512], psB[1][0:1, :],
                1.0 / 1024, None, ALU.mult), "sVe")
        vdrain()
        MU2 = f32c[0:1, 2048:3072]
        EX2 = f32c[0:1, 3072:4096]
        vop(lambda E: E.tensor_mul(f32c[0:1, 1024:2048], MU2, MU2)); vdrain()
        vop(lambda E: E.tensor_sub(EX2, EX2, f32c[0:1, 1024:2048])); vdrain()
        vop(lambda E: E.tensor_scalar(EX2, EX2, 1.0, EPS, ALU.mult, ALU.add)); vdrain()
        fisr(f32c[0:1, 0:1024], EX2, 1024, rows=slice(0, 1))
        vop(lambda E: E.tensor_copy(bfc[0:1, 2048:3072], MU2))
        vop(lambda E: E.tensor_copy(bfc[0:1, 3072:4096], f32c[0:1, 0:1024]))
        vdrain()
        v_r2 = P.mark("vector", "sVe")
        P.wait("tensor", "sVe", v_r2)
        v_prev = 0
        for h2 in range(2):
            P.wait("tensor", "sVe", v_prev)
            mm(psB[0][:, :], ones_bf[0:1, :], bfc[0:1, 2048 + h2 * 512:2048 + h2 * 512 + 512],
               True, True)
            mm(psB[1][:, :], ones_bf[0:1, :], bfc[0:1, 3072 + h2 * 512:3072 + h2 * 512 + 512],
               True, True)
            t_bc2 = P.mark("tensor", "sPe")
            P.wait("vector", "sPe", t_bc2)
            vop(lambda E, h2=h2: E.tensor_copy(epi[:, h2 * 512:h2 * 512 + 512],
                                               psB[0][:, :]))
            v_prev = vop(lambda E, h2=h2: E.tensor_copy(
                epi[:, 1024 + h2 * 512:1024 + h2 * 512 + 512], psB[1][:, :]), "sVe")
        vdrain()
        for c in range(8):
            vop(lambda E, c=c, toff=toff: E.tensor_sub(
                T1(), x_own[:, c * 2048 + toff:c * 2048 + toff + 1024], epi[:, 0:1024]))
            vdrain()
            vop(lambda E, c=c: E.tensor_mul(
                xln2[:, c * 1024:(c + 1) * 1024], T1(), epi[:, 1024:2048]))
            vdrain()
        v_x2 = P.mark("vector", "sVe")

        P.wait("tensor", "sFw", wrt_rdy)
        P.wait("tensor", "sVe", v_x2)
        v_lgfree = 0
        v_gtfree = 0
        for tc in range(8):
            P.wait("tensor", "sVe", v_lgfree)
            for dc in range(8):
                mm(psPV[0:128, 0:16],
                   xln2[:, dc * 1024 + tc * 128:dc * 1024 + (tc + 1) * 128],
                   wrt_sb[:, dc * 16:(dc + 1) * 16], dc == 0, dc == 7)
            t_lg = P.mark("tensor", "sPe")
            P.wait("vector", "sPe", t_lg)
            W0 = rw[:, 0:16]
            W1 = rw[:, 16:32]
            W2 = rw[:, 32:48]
            KEEP = rw[:, 48:64]
            MX = rw[:, 64:65]
            SM = rw[:, 65:66]
            vop(lambda E: E.reduce_max(MX, psPV[0:128, 0:16], axis=AX.X)); vdrain()
            v_lgfree = vop(lambda E: E.tensor_scalar(
                W0, psPV[0:128, 0:16], MX, None, ALU.subtract), "sVe")
            vdrain()
            v_sf = v_lgfree
            P.wait("scalar", "sVe", v_sf)
            s_ex = sop(lambda E: E.activation(W1, W0, AF.Exp), "sSc")
            P.wait("vector", "sSc", s_ex)
            vop(lambda E: E.reduce_sum(SM, W1, axis=AX.X)); vdrain()
            vop(lambda E: E.reciprocal(MX, SM)); vdrain()
            vop(lambda E: E.tensor_scalar(W1, W1, MX, None, ALU.mult)); vdrain()
            vop(lambda E: E.tensor_copy(W0, W1))
            vop(lambda E: E.memset(KEEP, 0.0)); vdrain()
            for kk in range(4):
                vop(lambda E: E.reduce_max(MX, W0, axis=AX.X)); vdrain()
                vop(lambda E: E.tensor_scalar(W2, W0, MX, None, ALU.is_ge)); vdrain()
                vop(lambda E: E.tensor_scalar(W2, W2, 1e30, -1e30, ALU.mult, ALU.add))
                vdrain()
                vop(lambda E: E.tensor_sub(W2, W2, iota_sb[:, :])); vdrain()
                vop(lambda E: E.reduce_max(MX, W2, axis=AX.X)); vdrain()
                vop(lambda E: E.tensor_scalar(W2, W2, MX, None, ALU.is_ge)); vdrain()
                vop(lambda E: E.tensor_add(KEEP, KEEP, W2)); vdrain()
                vop(lambda E: E.scalar_tensor_tensor(W0, W2, -1e30, W0, ALU.mult, ALU.add))
                vdrain()
            vop(lambda E: E.tensor_mul(W2, W1, KEEP)); vdrain()
            vop(lambda E: E.tensor_copy(scb[:, 0:16], W2)); vdrain()
            v_g = P.mark("vector", "sVe")
            P.wait("tensor", "sVe", v_g)
            P.wait("tensor", "sVe", v_gtfree)
            mm(psB[0][0:16, 0:128], scb[:, 0:16], iden_bf[:, :], True, True)
            t_g = P.mark("tensor", "sPe")
            P.wait("vector", "sPe", t_g)
            v_gtfree = vop(lambda E, tc=tc: E.tensor_copy(
                gateT[:, tc * 128:(tc + 1) * 128], psB[0][0:16, 0:128]), "sVe")
        v_gT = v_gtfree
        P.wait("sync", "sVe", v_gT)
        P.wait("sync", "sGa", gstore)
        w_g = dma("sync", gate_hbm.ap(), gateT[:, :], "sWr")

        g_row = 0
        t_gbc_prev = 0
        for e in range(NE):
            we = e % 2
            P.wait("gpsimd", "sPe", wexp_free[we])
            for c in range(8):
                dma("gpsimd", wup[we][:, c * 512:(c + 1) * 512],
                    wupT.ap()[e, c * 128:(c + 1) * 128, :], "sFw")
            for c in range(4):
                dma("gpsimd", wdn[we][:, c * 1024:(c + 1) * 1024],
                    wdnT.ap()[e, c * 128:(c + 1) * 128, :], "sFw")
            w_rdy = P.val("sFw")
            routed = e >= 2
            if routed:
                P.wait("sync", "sWr", w_g)
                P.wait("sync", "sPe", t_gbc_prev)
                g_row = dma("sync", grow[:, :], gate_hbm.ap()[e - 2:e - 1, :], "sGa")
                P.wait("tensor", "sGa", g_row)
                P.wait("tensor", "sVe", max(gbc_fr, v_gT))
                for jp in range(2):
                    mm(psB[jp][:, :], ones_bf[0:1, :], grow[0:1, jp * 512:(jp + 1) * 512],
                       True, True)
            t_gbc = P.mark("tensor", "sPe") if routed else P.val("sPe")
            t_gbc_prev = t_gbc
            gstore = g_row
            P.wait("tensor", "sFw", w_rdy)
            for ec in range(4):
                for jp in range(2):
                    pA = (ec * 2 + jp) % 2
                    P.wait("tensor", "sSc", up_fr[pA])
                    P.wait("tensor", "sVe", dn_fr[pA])
                    for dc in range(8):
                        mm(psA[pA][:, :],
                           wup[we][:, dc * 512 + ec * 128:dc * 512 + (ec + 1) * 128],
                           xln2[:, dc * 1024 + jp * 512:dc * 1024 + (jp + 1) * 512],
                           dc == 0, dc == 7)
                    t_u = P.mark("tensor", "sPe")
                    P.wait("scalar", "sPe", t_u)
                    if routed:
                        P.wait("scalar", "sVe", glt_fr[pA])
                        s_gl = sop(lambda E, pA=pA: E.activation(
                            f32c[:, pA * 512:(pA + 1) * 512], psA[pA][:, :], AF.Gelu),
                            "sSc")
                        up_fr[pA] = s_gl
                        P.wait("vector", "sSc", s_gl)
                        P.wait("vector", "sPe", t_gbc)
                        v_ac = vop(lambda E, ec=ec, jp=jp, pA=pA: E.tensor_mul(
                            act[:, ec * 1024 + jp * 512:ec * 1024 + (jp + 1) * 512],
                            f32c[:, pA * 512:(pA + 1) * 512], psB[jp][:, :]), "sVe")
                        gbc_fr = v_ac
                        glt_fr[pA] = v_ac
                    else:
                        s_gl = sop(lambda E, pA=pA, ec=ec, jp=jp: E.activation(
                            act[:, ec * 1024 + jp * 512:ec * 1024 + (jp + 1) * 512],
                            psA[pA][:, :], AF.Gelu), "sSc")
                        up_fr[pA] = s_gl
            vdrain()
            v_actdone = vop(lambda E: E.nop(), "sVe")
            sdrain()
            s_actdone = sop(lambda E: E.nop(), "sSc")
            P.wait("tensor", "sVe", v_actdone)
            P.wait("tensor", "sSc", s_actdone)
            for hc in range(8):
                for jp in range(2):
                    pA = (hc * 2 + jp) % 2
                    P.wait("tensor", "sSc", up_fr[pA])
                    P.wait("tensor", "sVe", dn_fr[pA])
                    for ec in range(4):
                        mm(psA[pA][:, :],
                           wdn[we][:, ec * 1024 + hc * 128:ec * 1024 + (hc + 1) * 128],
                           act[:, ec * 1024 + jp * 512:ec * 1024 + (jp + 1) * 512],
                           ec == 0, ec == 3)
                    t_d = P.mark("tensor", "sPe")
                    P.wait("vector", "sPe", t_d)
                    if e == 0:
                        v_aa = vop(lambda E, pA=pA, hc=hc, jp=jp: E.tensor_copy(
                            acc[:, hc * 1024 + jp * 512:hc * 1024 + (jp + 1) * 512],
                            psA[pA][:, :]), "sVe")
                    else:
                        v_aa = vop(lambda E, pA=pA, hc=hc, jp=jp: E.tensor_add(
                            acc[:, hc * 1024 + jp * 512:hc * 1024 + (jp + 1) * 512],
                            acc[:, hc * 1024 + jp * 512:hc * 1024 + (jp + 1) * 512],
                            psA[pA][:, :]), "sVe")
                    dn_fr[pA] = v_aa
            wexp_free[we] = P.val("sPe")

        P.emit("vector", lambda E: E.drain())
        P.wait("vector", "sSt", out_st)
        for c in range(8):
            vop(lambda E, c=c, toff=toff: E.tensor_add(
                T2(), acc[:, c * 1024:(c + 1) * 1024],
                x_own[:, c * 2048 + toff:c * 2048 + toff + 1024]))
            vdrain()
            v_o = P.mark("vector", "sVe")
            P.wait("sync", "sVe", v_o)
            out_st = dma("sync", out_d.ap()[c * 128:(c + 1) * 128, toff:toff + 1024],
                         T2(), "sSt")
            P.wait("vector", "sSt", out_st)

    P.wait("sync", "sSt", P.val("sSt"))
    P.wait("sync", "sWr", P.val("sWr"))

    with nc.Block() as block:
        P.run(block)
    es.close()
    return nc


_CACHE = {}

def _prep_core_inputs(inputs, c):
    x = inputs["L0_attn_x"]
    b, qh = c // 2, c % 2
    key = "shared"
    if key not in _CACHE:
        Wqkv = np.asarray(inputs["L0_attn_Wqkv"]).reshape(16, 3, 64, 1024)
        WQ = Wqkv[:, 0].reshape(1024, 1024)
        WK = Wqkv[:, 1].reshape(1024, 1024)
        WV = Wqkv[:, 2].reshape(1024, 1024)
        bqkv = np.asarray(inputs["L0_attn_bqkv"]).reshape(16, 3, 64)
        bQ, bK, bV = [np.ascontiguousarray(bqkv[:, i].reshape(1024)) for i in range(3)]
        Wr = np.asarray(inputs["L0_attn_Wr"])
        br = np.asarray(inputs["L0_attn_br"])
        WrT = np.ascontiguousarray(Wr.T)
        wrpadT = np.concatenate([np.repeat(WrT[:, 0:1], 128, 1), WrT,
                                 np.repeat(WrT[:, 256:257], 127, 1)], axis=1)
        brpad = np.concatenate([np.repeat(br[0:1], 128), br,
                                np.repeat(br[256:257], 127)])[None, :]
        wredgeT = np.ascontiguousarray(WrT[:, [256, 0]])
        bredge = np.ascontiguousarray(br[[256, 0]][:, None])
        su = np.asarray(inputs["L0_ffn_shared_up"])
        sd = np.asarray(inputs["L0_ffn_shared_down"])
        ru = np.asarray(inputs["L0_ffn_routed_up"])
        rd = np.asarray(inputs["L0_ffn_routed_down"])
        wupT = np.ascontiguousarray(
            np.concatenate([su, ru], axis=0).transpose(0, 2, 1))
        wdnT = np.ascontiguousarray(
            np.concatenate([sd, rd], axis=0).transpose(0, 2, 1))
        wrtT = np.ascontiguousarray(np.asarray(inputs["L0_ffn_router_w"]).T)
        iden = np.eye(128, dtype=BF)
        iota = np.broadcast_to(np.arange(16, dtype=np.float32), (128, 16)).copy()
        _CACHE[key] = dict(
            wqT=np.ascontiguousarray(WQ.T), wkT=np.ascontiguousarray(WK.T),
            wvT=np.ascontiguousarray(WV.T),
            bq=np.ascontiguousarray(bQ.reshape(8, 128).T),
            bk=np.ascontiguousarray(bK.reshape(8, 128).T),
            bv_row=np.ascontiguousarray(bV[None, :]),
            wrpadT=np.ascontiguousarray(wrpadT), brpad=np.ascontiguousarray(brpad),
            wredgeT=wredgeT, bredge=bredge,
            wupT=wupT, wdnT=wdnT, wrtT=wrtT, iden=iden, iota=iota)
    shared = _CACHE[key]

    ovr = np.zeros((2, 128, 512), np.float32)
    if qh == 0:
        for di in range(128):
            ovr[0, di, 128 - di:256 - di] = 1.0
    else:
        for di in range(128):
            ovr[1, di, 384 - di:512] = 1.0
    ovrinv = (1.0 - ovr).astype(BF)
    ovr = ovr.astype(BF)
    Rsel = np.array([[1.0] * 128, [0.0] * 128], dtype=BF)
    Lsel = np.array([[0.0] * 128, [1.0] * 128], dtype=BF)
    Fsel = Rsel if qh == 0 else Lsel
    auxsel = np.concatenate([Rsel, Lsel, Fsel], axis=0)

    xr = np.roll(np.asarray(x[b]), -2048 * qh, axis=0)
    xTc = np.ascontiguousarray(xr.T)

    d = dict(shared)
    d.update(xT=xTc, auxsel_in=auxsel, ovr=ovr, ovrinv=ovrinv)
    return d


_NC = None

def kernel(**inputs):
    global _NC
    in_maps = [_prep_core_inputs(inputs, c) for c in range(8)]
    if _NC is None:
        _NC = build_program()
    res = run_bass_kernel_spmd(_NC, in_maps, core_ids=list(range(8)))
    x = np.asarray(inputs["L0_attn_x"])
    out = np.empty_like(x)
    for c in range(8):
        b, qh = c // 2, c % 2
        out[b, 2048 * qh:2048 * (qh + 1), :] = res.results[c]["out"].T
    _CACHE.clear()
    return out


# version 07
# speedup vs baseline: 2.0481x; 2.0481x over previous
import contextlib
import numpy as np
import ml_dtypes
import concourse.bass as bass
import concourse.mybir as mybir
from concourse.bass_utils import run_bass_kernel_spmd

F32 = mybir.dt.float32
BF16 = mybir.dt.bfloat16
I32 = mybir.dt.int32
AF = mybir.ActivationFunctionType
ALU = mybir.AluOpType
AX = mybir.AxisListType
BF = ml_dtypes.bfloat16

H, HD, D, S, NQ = 16, 64, 1024, 4096, 2048
NKC, NQT, NE, ED = 32, 16, 18, 512
EPS = 1e-5
MSLACK = 24.0

def region_table(kc):
    if kc == 0:
        return [(0, 2048, 1)]
    if 1 <= kc <= 15:
        qa = 128 * (kc - 1)
        cut = min(qa + 384, 2048)
        regs = [(0, cut, 0)]
        if cut < 2048:
            regs.append((cut, 2048, 1))
        return regs
    return [(0, 2048, 2)]

def band_pieces_of_kc(kc):
    out = []
    for p in (0, 1, 2):
        qt = (kc if kc != 31 else -1) + 1 - p
        if 0 <= qt < NQT:
            out.append((qt, p, qt * 128))
    return out

ENGS = ("sync", "gpsimd", "tensor", "vector", "scalar")

DMA_ISSUER = {"sIn": "sync", "sWr": "sync", "sTp": "sync", "sSt": "sync",
              "sGa2": "sync",
              "sKV": "gpsimd", "sFw": "gpsimd", "sRp": "gpsimd", "sGa": "gpsimd"}
RELAY_SEM = {"sync": "sSy", "gpsimd": "sGy"}

class Plan:
    def __init__(self, nc):
        self.nc = nc
        self.stream = {e: [] for e in ENGS}
        self.sems = {}
        self.relay_cache = {}

    def sem(self, es, name):
        h = es.enter_context(self.nc.semaphore(name))
        self.sems[name] = [h, 0]

    def inc(self, name, k):
        s = self.sems[name]
        s[1] += k
        return s[1]

    def val(self, name):
        return self.sems[name][1]

    def emit(self, eng, fn):
        self.stream[eng].append(fn)

    def _rawwait(self, eng, name, v):
        h = self.sems[name][0]
        fn = lambda E, h=h, v=v: E.wait_ge(h, v)
        fn._is_wait = True
        self.emit(eng, fn)

    def wait(self, eng, name, v):
        if v <= 0:
            return
        assert v <= self.sems[name][1], f"wait on future {name}: {v} > {self.sems[name][1]}"
        if name in DMA_ISSUER:
            issuer = DMA_ISSUER[name]
            total = self.sems[name][1]
            if eng == issuer:
                self._rawwait(eng, name, total)
                return
            rs = RELAY_SEM[issuer]
            cached = self.relay_cache.get(name)
            if cached is None or cached[0] != total:
                self._rawwait(issuer, name, total)
                h = self.sems[rs][0]
                rv = self.inc(rs, 1)
                self.emit(issuer, lambda E, h=h: E.nop().then_inc(h, 1))
                self.relay_cache[name] = (total, rv)
            rv = self.relay_cache[name][1]
            self._rawwait(eng, rs, rv)
            return
        self._rawwait(eng, name, v)

    def mark(self, eng, sem):
        assert self.stream[eng], "mark on empty stream"
        last = self.stream[eng][-1]
        assert not getattr(last, "_has_upd", False), f"double mark on {eng}"
        assert not getattr(last, "_is_wait", False), f"mark on wait on {eng}"
        v = self.inc(sem, 1)
        h = self.sems[sem][0]
        self.stream[eng].pop()
        wrapped = lambda E, last=last, h=h: last(E).then_inc(h, 1)
        wrapped._has_upd = True
        self.stream[eng].append(wrapped)
        return v

    def op(self, eng, fn, sem=None, k=1):
        if sem is None:
            self.emit(eng, lambda E, fn=fn: fn(E))
            return None
        v = self.inc(sem, k)
        h = self.sems[sem][0]
        wrapped = lambda E, fn=fn, h=h, k=k: fn(E).then_inc(h, k)
        wrapped._has_upd = True
        self.emit(eng, wrapped)
        return v

    def run(self, block):
        for eng in ENGS:
            ops = self.stream[eng]
            getattr(block, eng)(lambda E, ops=ops: [f(E) for f in ops])


def build_program(n_pairs=8, dbg=False):
    nc = bass.Bass("TRN2", target_bir_lowering=False)
    es = contextlib.ExitStack()

    din = lambda n, s, dt=F32: nc.dram_tensor(n, s, dt, kind="ExternalInput")
    xT = din("xT", [D, S])
    wqT = din("wqT", [D, D]); wkT = din("wkT", [D, D]); wvT = din("wvT", [D, D])
    bq_i = din("bq", [128, 8]); bk_i = din("bk", [128, 8])
    bv_row = din("bv_row", [1, D])
    wrpadT = din("wrpadT", [64, 512]); brpad_i = din("brpad", [1, 512])
    wredge_i = din("wredge4", [64, 4]); bredge_i = din("bredge", [4, 1])
    wupT = din("wupT", [NE, D, ED]); wdnT = din("wdnT", [NE, ED, D])
    wrtT = din("wrtT", [D, 16])
    iden_i = din("iden", [128, 128], BF16)
    iota_i = din("iota", [128, 16])
    auxsel_i = din("auxsel_in", [4, 384], BF16)
    ovr_i = din("ovr", [2, 128, 512], BF16)

    out_d = nc.dram_tensor("out", [D, NQ], F32, kind="ExternalOutput")

    k_hbm = nc.dram_tensor("k_hbm", [8, 128, S], BF16)
    q_hbm = nc.dram_tensor("q_hbm", [8, 128, NQ], BF16)
    v_hbm = nc.dram_tensor("v_hbm", [NKC, 128, D], BF16)
    rp_hbm = nc.dram_tensor("rp_hbm", [2, 2, NQT, 128, 512], BF16)
    m_hbm = nc.dram_tensor("m_hbm", [2, 2, NQ], BF16)
    gate_hbm = nc.dram_tensor("gate_hbm", [16, 1024], BF16)

    sbt = lambda n, s, dt: es.enter_context(nc.sbuf_tensor(n, s, dt))
    x_own = sbt("x_own", [128, 16384], F32)
    big1 = sbt("big1", [128, 16384], BF16)
    big2 = sbt("big2", [128, 8192], BF16)
    big3 = sbt("big3", [128, 8192], BF16)
    f32s = sbt("f32s", [128, 8192], F32)
    f32c = sbt("f32c", [128, 4096], F32)
    bfc = sbt("bfc", [128, 4096], BF16)
    rw = sbt("rw", [128, 80], F32)
    ones_bf = sbt("ones_bf", [128, 128], BF16)
    iden_bf = sbt("iden_bf", [128, 128], BF16)
    iota_sb = sbt("iota_sb", [128, 16], F32)
    auxsel = sbt("auxsel", [128, 384], BF16)
    ovrb = sbt("ovrb", [128, 1024], BF16)
    bq_sb = sbt("bq_sb", [128, 8], F32)
    bk8_sb = sbt("bk8_sb", [128, 8], F32)
    bVbc = sbt("bVbc", [128, 1024], BF16)
    brbc = sbt("brbc", [128, 512], BF16)
    wrpad2 = sbt("wrpad2", [128, 512], BF16)
    wredge_sb = sbt("wredge_sb", [128, 36], BF16)
    bredge_sb = sbt("bredge_sb", [128, 1], F32)
    scal = sbt("scal", [128, 32], F32)
    scb = sbt("scb", [128, 32], BF16)
    nsq = sbt("nsq", [128, 32], F32)
    mwork = sbt("mwork", [128, 32], F32)
    mhbf = sbt("mhbf", [128, 32], BF16)
    mcolneg = sbt("mcolneg", [128, 32], F32)
    c1bc = sbt("c1bc", [128, 2], F32)
    c2bc = sbt("c2bc", [128, 1], F32)

    T1 = f32c[:, 0:1024]
    T2 = f32c[:, 1024:2048]
    BC1 = f32c[:, 2048:3072]
    BC2 = f32c[:, 3072:4096]

    xslab = [f32s[:, 0:4096], f32s[:, 4096:8192]]
    xlnsl = [big3[:, 0:4096], big3[:, 4096:8192]]
    kstage = [bfc[:, 0:512], bfc[:, 512:1024]]
    vstage = bfc[:, 1536:2560]
    sqP1 = bfc[:, 3072:3584]
    QP = bass.AP(f32s, 1024, [[8192, 128], [1, 2048]]).bitcast(BF16)
    AXT = bass.AP(f32s, 3072, [[8192, 128], [1, 2048]]).bitcast(BF16)
    M2V = bass.AP(f32s, 5120, [[8192, 128], [1, 1024]]).bitcast(BF16)
    SQB = bass.AP(f32s, 6144, [[8192, 128], [1, 1024]]).bitcast(BF16)
    rpstg2 = [f32s[:, 0:512], f32s[:, 512:1024]]
    qpair = [QP[:, 0:2048], QP[:, 2048:4096]]
    auxt = [AXT[:, 0:2048], AXT[:, 2048:4096]]
    kpair = [big2[:, 0:4096], big2[:, 4096:8192]]
    vpair = [big3[:, 0:4096], big3[:, 4096:8192]]
    bandT = big1[:, 0:12288]
    bandP = [[big1[:, 12288 + (2 * pS + hh) * 384:12288 + (2 * pS + hh + 1) * 384]
              for hh in range(2)] for pS in range(2)]
    msb = big1[:, 13824:13952]
    rpbf = [big1[:, 13952 + hh * 512:13952 + (hh + 1) * 512] for hh in range(2)]
    rcbf = big1[:, 14976:15488]
    rclo = big1[:, 15488:16000]
    possb = [[bfc[:, (2 * pS + hh) * 384:(2 * pS + hh + 1) * 384] for hh in range(2)]
             for pS in range(2)]
    RCV = bass.AP(f32s, 7168, [[8192, 128], [1, 1024]]).bitcast(BF16)
    rcbf = RCV[:, 0:512]
    rclo = RCV[:, 512:1024]
    pchunk = [bfc[:, 1536 + i * 1024:1536 + (i + 1) * 1024] for i in range(2)] + \
             [big1[:, 14976:16000]]
    xln2 = big1[:, 0:8192]
    act = big1[:, 8192:12288]
    wrt_sb = big1[:, 12288:12416]
    wup = [big2[:, 0:4096], big2[:, 4096:8192]]
    wdn = [big3[:, 0:4096], big3[:, 4096:8192]]
    acc = f32s
    grow = bfc[0:1, 0:1024]
    gateT = bfc[0:16, 1024:2048]
    sqP3 = bfc[:, 2048:2560]

    PA = es.enter_context(nc.psum_tensor("PA", [128, 1024], F32))
    PB = es.enter_context(nc.psum_tensor("PB", [128, 1024], F32))
    PV1 = es.enter_context(nc.psum_tensor("PV1", [128, 512], F32))
    PD = es.enter_context(nc.psum_tensor("PD", [128, 512], F32))
    PP = [es.enter_context(nc.psum_tensor("PP0", [128, 512], F32)),
          es.enter_context(nc.psum_tensor("PP1", [128, 512], F32))]

    P = Plan(nc)
    for s in ("sIn", "sWr", "sTp", "sSt", "sGa2", "sKV", "sFw", "sRp", "sGa",
              "sPe", "sVe", "sSc", "sSy", "sGy"):
        P.sem(es, s)

    def dma(eng, dst, src, sem="sIn", slow=False):
        return P.op(eng, lambda E, d=dst, s=src, sl=slow:
                    E.dma_start(out=d, in_=s, allow_slow_non_contiguous=sl), sem, 16)

    def mm(dst, lhsT, rhs, start, stop, sem=None, tp=None):
        return P.op("tensor", lambda E, d=dst, l=lhsT, r=rhs, st=start, sp=stop, tp=tp:
                    E.matmul(d, l, r, start=st, stop=sp, tile_position=tp,
                             skip_group_check=True), sem, 1)

    def vop(fn, sem=None):
        return P.op("vector", fn, sem, 1)

    def sop(fn, sem=None):
        return P.op("scalar", fn, sem, 1)

    def vdrain():
        P.emit("vector", lambda E: E.drain())

    def sdrain():
        P.emit("scalar", lambda E: E.drain())

    def recip_nr(dst, src, ws):
        vop(lambda E: E.reciprocal(dst, src)); vdrain()

    dma("sync", iden_bf[:, :], iden_i.ap())
    dma("sync", iota_sb[:, :], iota_i.ap())
    dma("sync", auxsel[0:2, :], auxsel_i.ap()[0:2, :])
    dma("sync", auxsel[32:34, :], auxsel_i.ap()[2:4, :])
    dma("sync", ovrb[:, 0:512], ovr_i.ap()[0])
    dma("sync", ovrb[:, 512:1024], ovr_i.ap()[1])
    dma("sync", bq_sb[:, :], bq_i.ap())
    dma("sync", bk8_sb[:, :], bk_i.ap())
    dma("sync", f32c[0:1, 2048:3072], bv_row.ap())
    dma("sync", bredge_sb[0:2, 0:1], bredge_i.ap()[0:2])
    dma("sync", bredge_sb[32:34, 0:1], bredge_i.ap()[2:4])
    dma("sync", f32c[0:1, 3072:3584], brpad_i.ap())
    dma("gpsimd", wredge_sb[0:64, 0:2], wredge_i.ap()[:, 0:2], "sFw")
    dma("gpsimd", wredge_sb[64:128, 32:34], wredge_i.ap()[:, 2:4], "sFw")
    for r0 in (0, 64):
        dma("gpsimd", wrpad2[r0:r0 + 64, :], wrpadT.ap(), "sFw")
    for c in range(8):
        dma("gpsimd", big1[:, c * 1024:(c + 1) * 1024], wqT.ap()[c * 128:(c + 1) * 128, :],
            "sFw")
        dma("gpsimd", big1[:, 8192 + c * 1024:8192 + (c + 1) * 1024],
            wkT.ap()[c * 128:(c + 1) * 128, :], "sFw")
        dma("gpsimd", big2[:, c * 1024:(c + 1) * 1024], wvT.ap()[c * 128:(c + 1) * 128, :],
            "sFw")
    v_in = P.val("sIn")
    v_gw = P.val("sFw")

    P.wait("vector", "sIn", v_in)
    vop(lambda E: E.memset(ones_bf[:, :], 1.0))
    vop(lambda E: E.memset(wredge_sb[64:128, 0:32], 0.0))
    vop(lambda E: E.tensor_scalar(bk8_sb[:, :], bk8_sb[:, :], 0.125, None, ALU.mult))
    vop(lambda E: E.tensor_copy(bfc[0:1, 3072:4096], f32c[0:1, 2048:3072]))
    vop(lambda E: E.tensor_copy(bfc[0:1, 1024:1536], f32c[0:1, 3072:3584]))
    vdrain()
    v_rows0 = P.mark("vector", "sVe")
    P.wait("tensor", "sVe", v_rows0)
    for jp in range(2):
        mm(PA[:, jp * 512:(jp + 1) * 512], ones_bf[0:1, :],
           bfc[0:1, 3072 + jp * 512:3072 + (jp + 1) * 512], True, True)
    mm(PP[0][:, :], ones_bf[0:1, :], bfc[0:1, 1024:1536], True, True)
    t_bc0 = P.mark("tensor", "sPe")
    P.wait("vector", "sPe", t_bc0)
    vop(lambda E: E.tensor_copy(bVbc[:, :], PA[:, 0:1024]))
    vop(lambda E: E.tensor_copy(brbc[:, :], PP[0][:, :]))
    vdrain()
    v_bc0 = P.mark("vector", "sVe")

    P.wait("scalar", "sIn", v_in)
    P.wait("scalar", "sFw", v_gw)
    P.wait("scalar", "sPe", t_bc0)
    s_wr2 = sop(lambda E: E.square(sqP1[0:64, :], wrpad2[0:64, :]), "sSc")
    P.wait("tensor", "sSc", s_wr2)
    P.wait("tensor", "sVe", v_rows0)
    mm(PP[1][0:1, :], ones_bf[0:64, 0:1], sqP1[0:64, :], True, True)
    t_wr = P.mark("tensor", "sPe")
    P.wait("vector", "sPe", t_wr)
    vop(lambda E: E.reduce_max(scal[0:1, 0:1], PP[1][0:1, 128:385], axis=AX.X)); vdrain()
    v_mx = P.mark("vector", "sVe")
    P.wait("scalar", "sVe", v_mx)
    s_sq = sop(lambda E: E.sqrt(scal[0:1, 2:3], scal[0:1, 0:1]), "sSc")
    P.wait("vector", "sSc", s_sq)
    vop(lambda E: E.tensor_scalar(f32c[0:1, 1024:1536], f32c[0:1, 3072:3584],
                                  -1.0, None, ALU.mult)); vdrain()
    vop(lambda E: E.tensor_max(f32c[0:1, 1024:1536], f32c[0:1, 1024:1536],
                               f32c[0:1, 3072:3584])); vdrain()
    vop(lambda E: E.reduce_max(scal[0:1, 3:4], f32c[0:1, 1152:1409],
                               axis=AX.X)); vdrain()
    vop(lambda E: E.tensor_scalar(scal[0:1, 5:6], scal[0:1, 3:4], 1.0, -MSLACK,
                                  ALU.mult, ALU.add)); vdrain()
    vop(lambda E: E.tensor_copy(scb[0:1, 0:1], scal[0:1, 5:6])); vdrain()
    v_c2 = P.mark("vector", "sVe")
    P.wait("tensor", "sVe", v_c2)
    mm(PP[1][:, 0:1], ones_bf[0:1, :], scb[0:1, 0:1], True, True)
    t_c2 = P.mark("tensor", "sPe")
    P.wait("vector", "sPe", t_c2)
    vop(lambda E: E.tensor_copy(c2bc[:, :], PP[1][:, 0:1]))
    vdrain()
    v_c2b = P.mark("vector", "sVe")

    P.wait("tensor", "sFw", v_gw)

    slab_free_v = [0, 0]
    slab_xln_free = [0, 0]
    psP_fr = [0, 0]
    psV_fr = v_bc0
    kst_fr = [0, 0]
    vst_fr = 0
    x_own_ld = 0
    prev_bcc = [v_c2b]

    for s_i in range(8):
        b = s_i % 2
        P.wait("sync", "sVe", slab_free_v[b])
        for c in range(8):
            dma("sync", xslab[b][:, c * 512:(c + 1) * 512],
                xT.ap()[c * 128:(c + 1) * 128, s_i * 512:(s_i + 1) * 512])
        if s_i < 4:
            for c in range(8):
                dma("sync", x_own[:, c * 2048 + s_i * 512:c * 2048 + (s_i + 1) * 512],
                    xT.ap()[c * 128:(c + 1) * 128, s_i * 512:(s_i + 1) * 512])
            x_own_ld = P.val("sIn")
        ld_done = P.val("sIn")

        P.wait("scalar", "sIn", ld_done)
        P.wait("scalar", "sPe", slab_xln_free[b])
        for c in range(8):
            sop(lambda E, b=b, c=c: E.copy(
                xlnsl[b][:, c * 512:(c + 1) * 512], xslab[b][:, c * 512:(c + 1) * 512]))
        s_cast = P.mark("scalar", "sSc")
        sdrain()
        P.wait("tensor", "sSc", s_cast)
        P.wait("tensor", "sVe", prev_bcc[0])
        for c in range(8):
            mm(PV1[0:1, :], ones_bf[:, 0:1], xlnsl[b][:, c * 512:(c + 1) * 512],
               c == 0, c == 7)
        for c in range(8):
            s_sq = sop(lambda E, b=b, c=c: E.square(
                sqP1[:, :], xlnsl[b][:, c * 512:(c + 1) * 512]), "sSc")
            P.wait("tensor", "sSc", s_sq)
            mm(PD[0:1, :], ones_bf[:, 0:1], sqP1[:, :], c == 0, c == 7)
            t_sqc = P.mark("tensor", "sPe")
            P.wait("scalar", "sPe", t_sqc)
        t_st = P.val("sPe")
        P.wait("vector", "sPe", t_st)
        MUr = f32c[0:1, 2048:2560]
        VARr = f32c[0:1, 2560:3072]
        MSQ = f32c[0:1, 3072:3584]
        INVr = f32c[0:1, 3584:4096]
        vop(lambda E: E.tensor_scalar(MUr, PV1[0:1, :], 1.0 / 1024, None, ALU.mult))
        vop(lambda E: E.tensor_scalar(VARr, PD[0:1, :], 1.0 / 1024, None, ALU.mult))
        vdrain()
        v_rowrd = P.mark("vector", "sVe")
        vop(lambda E: E.tensor_mul(MSQ, MUr, MUr)); vdrain()
        vop(lambda E: E.tensor_sub(VARr, VARr, MSQ)); vdrain()
        vop(lambda E: E.tensor_scalar(VARr, VARr, 1.0, EPS, ALU.mult, ALU.add)); vdrain()
        recip_nr(f32c[0:1, 3072:3584], VARr, f32c[0:1, 1024:1536])
        v_var = P.mark("vector", "sVe")
        P.wait("scalar", "sVe", v_var)
        s_rs = sop(lambda E: E.sqrt(INVr, f32c[0:1, 3072:3584]), "sSc")
        P.wait("vector", "sSc", s_rs)
        vop(lambda E: E.tensor_copy(bfc[0:1, 3072:3584], MUr))
        vop(lambda E: E.tensor_copy(bfc[0:1, 1024:1536], INVr))
        vdrain()
        v_r = P.mark("vector", "sVe")
        P.wait("tensor", "sVe", v_r)
        mm(PV1[:, :], ones_bf[0:1, :], bfc[0:1, 3072:3584], True, True)
        mm(PD[:, :], ones_bf[0:1, :], bfc[0:1, 1024:1536], True, True)
        t_bc = P.mark("tensor", "sPe")
        P.wait("vector", "sPe", t_bc)
        vop(lambda E: E.tensor_copy(BC1[:, 0:512], PV1[:, :]))
        vop(lambda E: E.tensor_copy(BC2[:, 0:512], PD[:, :]))
        vdrain()
        v_bcc = P.mark("vector", "sVe")
        prev_bcc[0] = v_bcc
        for c in range(8):
            vop(lambda E, b=b, c=c: E.tensor_sub(
                T1[:, 0:512], xslab[b][:, c * 512:(c + 1) * 512], f32c[:, 2048:2560]))
            vdrain()
            vop(lambda E, b=b, c=c: E.tensor_mul(
                xlnsl[b][:, c * 512:(c + 1) * 512], T1[:, 0:512], f32c[:, 3072:3584]))
            vdrain()
        v_xln = P.mark("vector", "sVe")
        slab_free_v[b] = v_xln

        P.wait("tensor", "sVe", v_xln)
        kinds = ("k", "q") if s_i < 4 else ("k",)
        for kind in kinds:
            woff = 8192 if kind == "k" else 0
            bias = bk8_sb if kind == "k" else bq_sb
            scale = 0.125 if kind == "k" else 1.0
            hbm = k_hbm if kind == "k" else q_hbm
            for fc in range(8):
                pA = fc % 2
                P.wait("tensor", "sSc", psP_fr[pA])
                for dc in range(8):
                    mm(PP[pA][:, :],
                       big1[:, woff + dc * 1024 + fc * 128:woff + dc * 1024 + (fc + 1) * 128],
                       xlnsl[b][:, dc * 512:(dc + 1) * 512], dc == 0, dc == 7)
                t_p = P.mark("tensor", "sPe")
                P.wait("scalar", "sPe", t_p)
                P.wait("scalar", "sWr", kst_fr[pA])
                sv = sop(lambda E, pA=pA, bias=bias, fc=fc, scale=scale: E.activation(
                    kstage[pA][:, :], PP[pA][:, :], AF.Identity,
                    bias=bias[:, fc:fc + 1], scale=scale), "sSc")
                psP_fr[pA] = sv
                P.wait("sync", "sSc", sv)
                kst_fr[pA] = dma("sync", hbm.ap()[fc, :, s_i * 512:(s_i + 1) * 512],
                                 kstage[pA][:, :], "sWr")
        for tc in range(4):
            kc_g = s_i * 4 + tc
            P.wait("tensor", "sVe", psV_fr)
            for jp in range(2):
                for dc in range(8):
                    mm(PA[:, jp * 512:(jp + 1) * 512],
                       xlnsl[b][:, dc * 512 + tc * 128:dc * 512 + (tc + 1) * 128],
                       big2[:, dc * 1024 + jp * 512:dc * 1024 + (jp + 1) * 512],
                       dc == 0, dc == 7)
            t_v = P.mark("tensor", "sPe")
            P.wait("vector", "sPe", t_v)
            P.wait("vector", "sWr", vst_fr)
            vop(lambda E: E.tensor_add(vstage[:, :], PA[:, 0:1024], bVbc[:, :]))
            vdrain()
            psV_fr = P.mark("vector", "sVe")
            P.wait("sync", "sVe", psV_fr)
            vst_fr = dma("sync", v_hbm.ap()[kc_g], vstage[:, :], "sWr")
        slab_xln_free[b] = P.val("sPe")

    p1_pe = P.val("sPe")
    p1_ve = P.val("sVe")
    p1_sc = P.val("sSc")
    p1_wr = P.val("sWr")

    ppfree = [("sSc", p1_sc), ("sSc", p1_sc)]
    pp_i = [0]

    def pp_next():
        b = pp_i[0] % 2
        pp_i[0] += 1
        return b

    kv_rdy = [0] * 8
    pair_free = [p1_pe, p1_pe]
    epil_sems = {}
    band_region_rdy = {}
    G = {"sqb_rd": 0}

    def emit_pair_loads(ch):
        pr = ch % 2
        P.wait("gpsimd", "sPe", pair_free[pr])
        if ch < 2:
            P.wait("gpsimd", "sWr", p1_wr)
        dma("gpsimd", kpair[pr][:, :], k_hbm.ap()[ch], "sKV")
        dma("gpsimd", qpair[pr][:, :], q_hbm.ap()[ch], "sKV")
        vsrc = bass.AP(v_hbm, ch * 128, [[D, 128], [128 * D, NKC], [1, 128]])
        vdst = bass.AP(big3, pr * 4096, [[8192, 128], [128, NKC], [1, 128]])
        dma("gpsimd", vdst, vsrc, "sKV")
        kv_rdy[ch] = P.val("sKV")

    def prep_steps(ch):
        pr = ch % 2
        st = {"aux_rdy": 0, "mcol_rdy": 0}
        loc = {"rp_wr": [0] * NQT, "rp_st_fr": [0, 0], "m_wr": 0, "m_ga": 0,
               "pS_fr": [(0, 0), (0, 0)], "norm_done": 0}

        def sqq_step():
            P.wait("scalar", "sKV", kv_rdy[ch])
            P.wait("scalar", "sPe", G["sqb_rd"])
            sop(lambda E: E.square(SQB[:, 0:2048], qpair[pr][:, :]))
            loc["s_sqq"] = P.mark("scalar", "sSc")

        def normq_step(g):
            P.wait("tensor", "sSc", loc["s_sqq"])
            if g == 0:
                b = pp_next()
                P.wait("tensor", *ppfree[b])
                b2 = pp_next()
                P.wait("tensor", *ppfree[b2])
                loc["nq_banks"] = (b, b2)
            b, b2 = loc["nq_banks"]
            for qt in range(g * 4, g * 4 + 4):
                mm(PP[b][:, qt:qt + 1], SQB[0:64, qt * 128:(qt + 1) * 128],
                   ones_bf[0:64, 0:1], True, True)
                mm(PP[b2][:, qt:qt + 1], SQB[64:128, qt * 128:(qt + 1) * 128],
                   ones_bf[64:128, 0:1], True, True)
            t_nq = P.mark("tensor", "sPe")
            G["sqb_rd"] = t_nq
            if g == 3:
                P.wait("vector", "sPe", t_nq)
                vop(lambda E, b=b: E.tensor_copy(nsq[:, 0:16], PP[b][:, 0:16]))
                vop(lambda E, b2=b2: E.tensor_copy(nsq[:, 16:32], PP[b2][:, 0:16]))
                vdrain()
                v_nq = P.mark("vector", "sVe")
                ppfree[b] = ("sVe", v_nq)
                ppfree[b2] = ("sVe", v_nq)
                loc["v_nsq"] = v_nq

        def knorm_step(g):
            cb, sub = g // 2, g % 2
            if sub == 0:
                P.wait("scalar", "sPe", G["sqb_rd"])
                sop(lambda E, cb=cb: E.square(
                    SQB[:, 0:2048], kpair[pr][:, cb * 2048:(cb + 1) * 2048]))
                loc["s_sqk"] = P.mark("scalar", "sSc")
            P.wait("tensor", "sSc", loc["s_sqk"])
            for ss in range(2):
                b = pp_next()
                P.wait("tensor", *ppfree[b])
                b2 = pp_next()
                P.wait("tensor", *ppfree[b2])
                off = (sub * 2 + ss) * 512
                mm(PP[b][0:1, :], ones_bf[0:64, 0:1], SQB[0:64, off:off + 512],
                   True, True)
                mm(PP[b2][0:1, :], ones_bf[64:128, 0:1], SQB[64:128, off:off + 512],
                   True, True)
                t_kn = P.mark("tensor", "sPe")
                G["sqb_rd"] = t_kn
                P.wait("vector", "sPe", t_kn)
                first = (g == 0 and ss == 0)
                vop(lambda E, b=b: E.reduce_max(scal[0:1, 16:17], PP[b][0:1, :],
                                                axis=AX.X))
                vop(lambda E, b2=b2: E.reduce_max(scal[0:1, 18:19], PP[b2][0:1, :],
                                                  axis=AX.X))
                vdrain()
                if first:
                    vop(lambda E: E.tensor_copy(scal[0:1, 17:18], scal[0:1, 16:17]))
                    vop(lambda E: E.tensor_copy(scal[0:1, 19:20], scal[0:1, 18:19]))
                else:
                    vop(lambda E: E.tensor_max(scal[0:1, 17:18], scal[0:1, 17:18],
                                               scal[0:1, 16:17]))
                    vop(lambda E: E.tensor_max(scal[0:1, 19:20], scal[0:1, 19:20],
                                               scal[0:1, 18:19]))
                vdrain()
                v_kn = P.mark("vector", "sVe")
                ppfree[b] = ("sVe", v_kn)
                ppfree[b2] = ("sVe", v_kn)
            loc["v_knorm"] = P.val("sVe")

        def c1_step():
            P.wait("scalar", "sVe", loc["v_knorm"])
            sop(lambda E: E.sqrt(scal[0:1, 20:21], scal[0:1, 17:18]))
            sop(lambda E: E.sqrt(scal[0:1, 21:22], scal[0:1, 19:20]))
            sdrain()
            s_c1 = P.mark("scalar", "sSc")
            P.wait("vector", "sSc", s_c1)
            vop(lambda E: E.tensor_add(scal[0:1, 22:23], scal[0:1, 20:21],
                                       scal[0:1, 2:3]))
            vop(lambda E: E.tensor_add(scal[0:1, 23:24], scal[0:1, 21:22],
                                       scal[0:1, 2:3]))
            vdrain()
            vop(lambda E: E.tensor_copy(scb[0:1, 2:4], scal[0:1, 22:24])); vdrain()
            v_c1 = P.mark("vector", "sVe")
            P.wait("tensor", "sVe", v_c1)
            b = pp_next()
            P.wait("tensor", *ppfree[b])
            mm(PP[b][:, 0:2], ones_bf[0:1, :], scb[0:1, 2:4], True, True)
            t_c1 = P.mark("tensor", "sPe")
            P.wait("vector", "sPe", t_c1)
            vop(lambda E, b=b: E.tensor_copy(c1bc[:, :], PP[b][:, 0:2]))
            vdrain()
            v_c1b = P.mark("vector", "sVe")
            ppfree[b] = ("sVe", v_c1b)
            loc["v_c1b"] = v_c1b

        def mchain_step():
            P.wait("scalar", "sVe", loc["v_c1b"])
            sop(lambda E: E.activation(mwork[:, 0:32], nsq[:, 0:32], AF.Sqrt))
            sdrain()
            s_m = P.mark("scalar", "sSc")
            P.wait("vector", "sSc", s_m)
            P.wait("vector", "sVe", v_c2b)
            vop(lambda E: E.tensor_scalar(mwork[:, 0:16], mwork[:, 0:16],
                                          c1bc[:, 0:1], c2bc[:, 0:1],
                                          ALU.mult, ALU.add))
            vop(lambda E: E.tensor_scalar(mwork[:, 16:32], mwork[:, 16:32],
                                          c1bc[:, 1:2], c2bc[:, 0:1],
                                          ALU.mult, ALU.add))
            vdrain()
            vop(lambda E: E.tensor_copy(mhbf[:, :], mwork[:, :])); vdrain()
            vop(lambda E: E.tensor_scalar(mcolneg[:, :], mhbf[:, :], -1.0, None,
                                          ALU.mult))
            vdrain()
            st["mcol_rdy"] = P.mark("vector", "sVe")

        def mio_step():
            P.wait("tensor", "sVe", st["mcol_rdy"])
            b = pp_next()
            P.wait("tensor", *ppfree[b])
            P.op("tensor", lambda E, b=b: E.transpose(
                PP[b][0:32, 0:64].bitcast(BF16), mhbf[:, :], iden_bf[:, :]))
            t_tr = P.mark("tensor", "sPe")
            P.wait("vector", "sPe", t_tr)
            vop(lambda E, b=b: E.tensor_copy(msb[0:32, :],
                                             PP[b][0:32, 0:64].bitcast(BF16)))
            vdrain()
            v_mt = P.mark("vector", "sVe")
            ppfree[b] = ("sVe", v_mt)
            P.wait("gpsimd", "sVe", v_mt)
            for hh in range(2):
                dma("gpsimd", bass.AP(m_hbm, (pr * 2 + hh) * NQ, [[128, 16], [1, 128]]),
                    msb[hh * 16:(hh + 1) * 16, :], "sRp")
            loc["m_wr"] = P.val("sRp")
            P.wait("gpsimd", "sRp", loc["m_wr"])
            for hh in range(2):
                for r in range(2):
                    dma("gpsimd", M2V[hh * 32 + r:hh * 32 + r + 1, 0:2048],
                        bass.AP(m_hbm, (pr * 2 + hh) * NQ, [[NQ, 1], [1, NQ]]), "sGa")
            loc["m_ga"] = P.val("sGa")

        def edge_step(jp):
            P.wait("tensor", "sKV", kv_rdy[ch])
            b = pp_next()
            P.wait("tensor", *ppfree[b])
            b2 = pp_next()
            P.wait("tensor", *ppfree[b2])
            mm(PP[b][0:2, :], wredge_sb[0:64, 0:2],
               qpair[pr][0:64, jp * 512:(jp + 1) * 512], True, True)
            mm(PP[b2][0:34, :], wredge_sb[64:128, 0:34],
               qpair[pr][64:128, jp * 512:(jp + 1) * 512], True, True)
            t_e = P.mark("tensor", "sPe")
            P.wait("vector", "sPe", t_e)
            P.wait("vector", "sGa", loc["m_ga"])
            if jp == 0:
                P.wait("vector", "sPe", pair_free[pr])
            vop(lambda E, b=b, jp=jp: E.scalar_tensor_tensor(
                auxt[pr][0:2, jp * 512:(jp + 1) * 512], PP[b][0:2, :],
                bredge_sb[0:2, 0:1], M2V[0:2, jp * 512:(jp + 1) * 512],
                ALU.add, ALU.subtract))
            vop(lambda E, b2=b2, jp=jp: E.scalar_tensor_tensor(
                auxt[pr][32:34, jp * 512:(jp + 1) * 512], PP[b2][32:34, :],
                bredge_sb[32:34, 0:1], M2V[32:34, jp * 512:(jp + 1) * 512],
                ALU.add, ALU.subtract))
            vdrain()
            v_ed = P.mark("vector", "sVe")
            ppfree[b] = ("sVe", v_ed)
            ppfree[b2] = ("sVe", v_ed)
            if jp == 3:
                st["aux_rdy"] = v_ed

        def rp_step(qt):
            P.wait("tensor", "sKV", kv_rdy[ch])
            b = pp_next()
            P.wait("tensor", *ppfree[b])
            b2 = pp_next()
            P.wait("tensor", *ppfree[b2])
            mm(PP[b][:, :], qpair[pr][0:64, qt * 128:(qt + 1) * 128],
               wrpad2[0:64, :], True, True)
            mm(PP[b2][:, :], qpair[pr][64:128, qt * 128:(qt + 1) * 128],
               wrpad2[64:128, :], True, True)
            t_rp = P.mark("tensor", "sPe")
            P.wait("vector", "sPe", t_rp)
            P.wait("vector", "sRp", loc["rp_st_fr"][0])
            bb = (b, b2)
            if qt in (0, 15):
                mi = 0 if qt == 0 else 1
                ecol = 384 if qt == 0 else 128
                for hh in range(2):
                    vop(lambda E, hh=hh, bb=bb: E.tensor_add(
                        rpstg2[hh][:, :], PP[bb[hh]][:, :], brbc[:, :]))
                vdrain()
                for hh in range(2):
                    vop(lambda E, hh=hh, ecol=ecol: E.tensor_copy(
                        scal[:, 24 + hh:25 + hh], rpstg2[hh][:, ecol:ecol + 1]))
                vdrain()
                for hh in range(2):
                    vop(lambda E, hh=hh: E.tensor_scalar(
                        T1[:, 0:512], rpstg2[hh][:, :], scal[:, 24 + hh:25 + hh],
                        None, ALU.subtract))
                    vdrain()
                    vop(lambda E, mi=mi: E.tensor_mul(
                        T1[:, 0:512], T1[:, 0:512], ovrb[:, mi * 512:(mi + 1) * 512]))
                    vdrain()
                    vop(lambda E, hh=hh: E.tensor_sub(rpstg2[hh][:, :],
                                                      rpstg2[hh][:, :], T1[:, 0:512]))
                    vdrain()
                for hh in range(2):
                    vop(lambda E, hh=hh: E.tensor_copy(rpbf[hh][:, :], rpstg2[hh][:, :]))
                vdrain()
            else:
                for hh in range(2):
                    vop(lambda E, hh=hh, bb=bb: E.tensor_add(
                        rpbf[hh][:, :], PP[bb[hh]][:, :], brbc[:, :]))
                vdrain()
            v_rp = P.mark("vector", "sVe")
            ppfree[b] = ("sVe", v_rp)
            ppfree[b2] = ("sVe", v_rp)
            P.wait("gpsimd", "sVe", v_rp)
            for hh in range(2):
                dma("gpsimd",
                    bass.AP(rp_hbm, ((pr * 2 + hh) * NQT + qt) * 128 * 512,
                            [[512, 128], [1, 512]]),
                    rpbf[hh][:, :], "sRp")
            loc["rp_st_fr"][0] = P.val("sRp")
            loc["rp_wr"][qt] = P.val("sRp")

        def band_step(qt):
            pS = qt % 2
            qb = qt // 4
            P.wait("gpsimd", "sRp", loc["rp_wr"][qt])
            P.wait("gpsimd", "sVe", loc.get(("pos_fr", pS), 0))
            for hh in range(2):
                dma("gpsimd", possb[pS][hh][:, :],
                    bass.AP(rp_hbm, ((pr * 2 + hh) * NQT + qt) * 128 * 512 + 128,
                            [[511, 128], [1, 384]]), "sGa")
            g_pos = P.val("sGa")
            b = pp_next()
            P.wait("tensor", *ppfree[b])
            b2 = pp_next()
            P.wait("tensor", *ppfree[b2])
            if qt == 0:
                mm(PP[b][:, 0:128], qpair[pr][0:64, 0:128],
                   kpair[pr][0:64, 3968:4096], True, True)
                mm(PP[b2][:, 0:128], qpair[pr][64:128, 0:128],
                   kpair[pr][64:128, 3968:4096], True, True)
                mm(PP[b][:, 128:384], qpair[pr][0:64, 0:128],
                   kpair[pr][0:64, 0:256], True, True)
                mm(PP[b2][:, 128:384], qpair[pr][64:128, 0:128],
                   kpair[pr][64:128, 0:256], True, True)
            else:
                j0 = qt * 128 - 128
                mm(PP[b][:, 0:384], qpair[pr][0:64, qt * 128:(qt + 1) * 128],
                   kpair[pr][0:64, j0:j0 + 384], True, True)
                mm(PP[b2][:, 0:384], qpair[pr][64:128, qt * 128:(qt + 1) * 128],
                   kpair[pr][64:128, j0:j0 + 384], True, True)
            t_b = P.mark("tensor", "sPe")
            P.wait("vector", "sPe", t_b)
            P.wait("vector", "sGa", g_pos)
            bb = (b, b2)
            for hh in range(2):
                vop(lambda E, hh=hh, bb=bb, pS=pS: E.tensor_add(
                    PP[bb[hh]][:, 0:384], PP[bb[hh]][:, 0:384], possb[pS][hh][:, :]))
            vdrain()
            v_b = P.mark("vector", "sVe")
            loc[("pos_fr", pS)] = v_b
            P.wait("scalar", "sVe", v_b)
            P.wait("scalar", "sVe", st["mcol_rdy"])
            P.wait("scalar", "sTp", loc["pS_fr"][pS][1])
            for hh in range(2):
                sop(lambda E, hh=hh, bb=bb, pS=pS, qt=qt: E.activation(
                    bandP[pS][hh][:, :], PP[bb[hh]][:, 0:384], AF.Exp,
                    bias=mcolneg[:, hh * 16 + qt:hh * 16 + qt + 1]))
            sdrain()
            s_b = P.mark("scalar", "sSc")
            ppfree[b] = ("sSc", s_b)
            ppfree[b2] = ("sSc", s_b)
            P.wait("sync", "sSc", s_b)
            if ch > 0:
                P.wait("sync", "sVe", epil_sems.get((ch - 1, qb), 0))
            for hh in range(2):
                for p in range(3):
                    col = qb * 3072 + (qt % 4) * 768 + p * 256 + hh * 128
                    P.op("sync", lambda E, pS=pS, hh=hh, p=p, col=col:
                         E.dma_start_transpose(
                             bandT[:, col:col + 128],
                             bandP[pS][hh][:, p * 128:(p + 1) * 128]), "sTp", 16)
            loc["pS_fr"][pS] = (0, P.val("sTp"))
            if qt % 4 == 3:
                band_region_rdy[(ch, qb)] = P.val("sTp")

        steps = [sqq_step]
        steps += [lambda g=g: normq_step(g) for g in range(4)]
        steps += [lambda g=g: knorm_step(g) for g in range(4)]
        steps += [c1_step, mchain_step, mio_step]
        steps += [lambda jp=jp: edge_step(jp) for jp in range(4)]
        steps += [lambda qt=qt: rp_step(qt) for qt in range(NQT)]
        steps += [lambda qt=qt: band_step(qt) for qt in range(NQT)]
        return steps, st

    def emit_main(ch, st, steps, tail):
        pr = ch % 2
        exp_done = {-2: tail["exp"][0], -1: tail["exp"][1]}
        pvden_done = {-3: tail["pvden"][0], -2: tail["pvden"][1],
                      -1: tail["pvden"][2]}
        lag = None
        nsteps = len(steps)
        next_step_at = [0]
        pieces_total = 128

        def consume_steps(i):
            want = (i + 1) * nsteps // pieces_total
            while next_step_at[0] < want and steps:
                steps.pop(0)()
                next_step_at[0] += 1

        P.wait("tensor", "sKV", kv_rdy[ch])
        P.wait("tensor", "sVe", st["aux_rdy"])
        pv_start_guard = tail["pv_free"]
        for qblk in range(4):
            q0 = qblk * 512
            P.wait("vector", "sTp", band_region_rdy[(ch, qblk)])
            for kc in range(NKC):
                i = qblk * NKC + kc
                pa = PA if i % 2 == 0 else PB
                pc = i % 3
                P.wait("tensor", "sSc", exp_done.get(i - 2, 0))
                mm(pa[:, 0:512], kpair[pr][0:64, kc * 128:(kc + 1) * 128],
                   qpair[pr][0:64, q0:q0 + 512], True, False)
                mm(pa[:, 512:1024], kpair[pr][64:128, kc * 128:(kc + 1) * 128],
                   qpair[pr][64:128, q0:q0 + 512], True, False)
                subr = [(max(lo, q0), min(hi, q0 + 512), v)
                        for (lo, hi, v) in region_table(kc)
                        if max(lo, q0) < min(hi, q0 + 512)]
                for si, (lo, hi, v) in enumerate(subr):
                    last = si == len(subr) - 1
                    mm(pa[:, lo - q0:hi - q0], auxsel[0:2, v * 128:v * 128 + 128],
                       auxt[pr][0:2, lo:hi], False, last)
                    mm(pa[:, 512 + lo - q0:512 + hi - q0],
                       auxsel[32:34, v * 128:v * 128 + 128],
                       auxt[pr][32:34, lo:hi], False, last)
                t_qk = P.mark("tensor", "sPe")
                if lag is not None:
                    lag()
                    lag = None
                P.wait("scalar", "sPe", t_qk)
                P.wait("scalar", "sPe", pvden_done.get(i - 3, 0))
                sop(lambda E, pa=pa, pc=pc: E.activation(
                    pchunk[pc][:, :], pa[:, 0:1024], AF.Exp))
                s_e = P.mark("scalar", "sSc")
                exp_done[i] = s_e
                ncopy = 0
                P.wait("vector", "sSc", s_e)
                P.wait("vector", "sPe", pvden_done.get(i - 3, 0))
                for (qt, p, bq0) in band_pieces_of_kc(kc):
                    if q0 <= bq0 < q0 + 512:
                        col = qblk * 3072 + (qt % 4) * 768 + p * 256
                        off = bq0 - q0
                        vop(lambda E, pc=pc, col=col, off=off: E.tensor_copy(
                            pchunk[pc][:, off:off + 128], bandT[:, col:col + 128]))
                        vop(lambda E, pc=pc, col=col, off=off: E.tensor_copy(
                            pchunk[pc][:, 512 + off:512 + off + 128],
                            bandT[:, col + 128:col + 256]))
                        ncopy += 1
                v_cp = P.mark("vector", "sVe") if ncopy else P.val("sVe")

                def den_pv(i=i, kc=kc, pc=pc, s_e=s_e, v_cp=v_cp, q0=q0,
                           guard=pv_start_guard):
                    P.wait("tensor", "sSc", s_e)
                    P.wait("tensor", "sVe", v_cp)
                    if kc == 0:
                        P.wait("tensor", "sVe", guard)
                    mm(PD[0:1, :], ones_bf[:, 0:1], pchunk[pc][:, 0:512],
                       kc == 0, kc == NKC - 1)
                    mm(PD[64:65, :], ones_bf[:, 0:1], pchunk[pc][:, 512:1024],
                       kc == 0, kc == NKC - 1, tp=(0, 64))
                    mm(PV1[0:64, :], vpair[pr][:, kc * 128:kc * 128 + 64],
                       pchunk[pc][:, 0:512], kc == 0, kc == NKC - 1)
                    mm(PV1[64:128, :], vpair[pr][:, kc * 128 + 64:kc * 128 + 128],
                       pchunk[pc][:, 512:1024], kc == 0, kc == NKC - 1, tp=(0, 64))
                    pvden_done[i] = P.mark("tensor", "sPe")
                lag = den_pv
                consume_steps(i)
            if lag is not None:
                lag()
                lag = None
            t_last = pvden_done[qblk * NKC + NKC - 1]
            P.wait("vector", "sPe", t_last)
            vop(lambda E: E.tensor_copy(T2[0:65, 0:512], PD[0:65, :]))
            vdrain()
            vop(lambda E: E.tensor_copy(T2[:, 512:1024], PV1[:, :]))
            vdrain()
            v_pvc = P.mark("vector", "sVe")
            recip_nr(T1[0:65, 0:512], T2[0:65, 0:512], T1[0:65, 512:1024])
            vop(lambda E: E.tensor_copy(rcbf[0:65, :], T1[0:65, 0:512]))
            vdrain()
            vop(lambda E: E.tensor_sub(rclo[0:65, :], T1[0:65, 0:512],
                                       rcbf[0:65, :]))
            vdrain()
            v_rc = P.mark("vector", "sVe")
            b = pp_next()
            P.wait("tensor", *ppfree[b])
            P.wait("tensor", "sVe", v_rc)
            mm(PP[b][0:64, :], ones_bf[0:1, 0:64], rcbf[0:1, :], True, False)
            mm(PP[b][64:128, :], ones_bf[64:65, 0:64], rcbf[64:65, :], True, False,
               tp=(64, 64))
            mm(PP[b][0:64, :], ones_bf[0:1, 0:64], rclo[0:1, :], False, True)
            mm(PP[b][64:128, :], ones_bf[64:65, 0:64], rclo[64:65, :], False, True,
               tp=(64, 64))
            t_bc = P.mark("tensor", "sPe")
            P.wait("vector", "sPe", t_bc)
            vop(lambda E, b=b: E.tensor_mul(T1[:, 512:1024], T2[:, 512:1024],
                                            PP[b][:, :]))
            vdrain()
            v_pvf = P.mark("vector", "sVe")
            ppfree[b] = ("sVe", v_pvf)
            if ch == 0 and qblk == 0:
                P.wait("vector", "sIn", x_own_ld)
            vop(lambda E, ch=ch, q0=q0: E.tensor_add(
                x_own[:, ch * 2048 + q0:ch * 2048 + q0 + 512],
                x_own[:, ch * 2048 + q0:ch * 2048 + q0 + 512], T1[:, 512:1024]))
            vdrain()
            v_ep = P.mark("vector", "sVe")
            epil_sems[(ch, qblk)] = v_ep
            pv_start_guard = v_pvc
        while steps:
            steps.pop(0)()
        n = 4 * NKC
        tail_out = {"exp": [exp_done[n - 2], exp_done[n - 1]],
                    "pvden": [pvden_done[n - 3], pvden_done[n - 2],
                              pvden_done[n - 1]],
                    "pv_free": pv_start_guard}
        return P.val("sPe"), P.val("sVe"), tail_out

    if n_pairs > 0:
        emit_pair_loads(0)
        if n_pairs > 1:
            emit_pair_loads(1)
        steps0, st0 = prep_steps(0)
        for f in steps0:
            f()
        states = {0: st0}
        tail = {"exp": [0, 0], "pvden": [0, 0, 0], "pv_free": 0}
        for ch in range(n_pairs):
            st = states[ch]
            if ch + 1 < n_pairs:
                steps_n, st_n = prep_steps(ch + 1)
                states[ch + 1] = st_n
            else:
                steps_n = []
            main_pe, main_ve, tail = emit_main(ch, st, steps_n, tail)
            pair_free[ch % 2] = main_pe
            if ch + 2 < n_pairs:
                emit_pair_loads(ch + 2)

    p2_ve = P.val("sVe")
    p2_sc = P.val("sSc")
    p2_pe = P.val("sPe")

    PPX = PP

    P.wait("gpsimd", "sPe", p2_pe)
    for c in range(8):
        dma("gpsimd", wrt_sb[:, c * 16:(c + 1) * 16],
            wrtT.ap()[c * 128:(c + 1) * 128, :], "sFw")
    wrt_rdy = P.val("sFw")

    wexp_free = [p2_pe, p2_pe]
    up_fr = [0, 0]
    dn_fr = [0, 0]
    gbc_fr = 0
    glt_fr = [0, 0]
    out_st = 0
    gstore = 0

    MUB = f32c[:, 2048:3072]
    IVB = f32c[:, 3072:4096]

    for th in range(2):
        toff = th * 1024
        P.wait("scalar", "sPe", P.val("sPe"))
        P.wait("scalar", "sVe", p2_ve if th == 0 else P.val("sVe"))
        for c in range(8):
            sop(lambda E, c=c, toff=toff: E.copy(
                xln2[:, c * 1024:(c + 1) * 1024],
                x_own[:, c * 2048 + toff:c * 2048 + toff + 1024]))
        s_cast2 = P.mark("scalar", "sSc")
        sdrain()
        P.wait("tensor", "sSc", s_cast2)
        v_prev = 0
        for h2 in range(2):
            P.wait("scalar", "sPe", P.val("sPe"))
            P.wait("tensor", "sVe", v_prev)
            for c in range(8):
                mm(PV1[0:1, :], ones_bf[:, 0:1],
                   xln2[:, c * 1024 + h2 * 512:c * 1024 + h2 * 512 + 512], c == 0, c == 7)
            t_a = P.mark("tensor", "sPe")
            P.wait("vector", "sPe", t_a)
            v_prev = vop(lambda E, h2=h2: E.tensor_scalar(
                f32c[0:1, 2048 + h2 * 512:2048 + h2 * 512 + 512], PV1[0:1, :],
                1.0 / 1024, None, ALU.mult), "sVe")
            for c in range(8):
                s_last = sop(lambda E, c=c, h2=h2: E.square(
                    sqP3[:, :],
                    xln2[:, c * 1024 + h2 * 512:c * 1024 + h2 * 512 + 512]), "sSc")
                P.wait("tensor", "sSc", s_last)
                mm(PD[0:1, :], ones_bf[:, 0:1], sqP3[:, :], c == 0, c == 7)
                t_s3 = P.mark("tensor", "sPe")
                P.wait("scalar", "sPe", t_s3)
            t_b2 = P.val("sPe")
            P.wait("vector", "sPe", t_b2)
            v_prev = vop(lambda E, h2=h2: E.tensor_scalar(
                f32c[0:1, 3072 + h2 * 512:3072 + h2 * 512 + 512], PD[0:1, :],
                1.0 / 1024, None, ALU.mult), "sVe")
        vdrain()
        MU2 = f32c[0:1, 2048:3072]
        EX2 = f32c[0:1, 3072:4096]
        vop(lambda E: E.tensor_mul(f32c[0:1, 1024:2048], MU2, MU2)); vdrain()
        vop(lambda E: E.tensor_sub(EX2, EX2, f32c[0:1, 1024:2048])); vdrain()
        vop(lambda E: E.tensor_scalar(EX2, EX2, 1.0, EPS, ALU.mult, ALU.add)); vdrain()
        recip_nr(f32c[0:1, 1024:2048], EX2, f32c[0:1, 0:1024])
        v_v2 = P.mark("vector", "sVe")
        P.wait("scalar", "sVe", v_v2)
        s_r2 = sop(lambda E: E.sqrt(f32c[0:1, 0:1024], f32c[0:1, 1024:2048]), "sSc")
        P.wait("vector", "sSc", s_r2)
        vop(lambda E: E.tensor_copy(bfc[0:1, 2048:3072], MU2))
        vop(lambda E: E.tensor_copy(bfc[0:1, 3072:4096], f32c[0:1, 0:1024]))
        vdrain()
        v_r2 = P.mark("vector", "sVe")
        P.wait("tensor", "sVe", v_r2)
        v_prev = 0
        for h2 in range(2):
            P.wait("tensor", "sVe", v_prev)
            mm(PV1[:, :], ones_bf[0:1, :], bfc[0:1, 2048 + h2 * 512:2048 + h2 * 512 + 512],
               True, True)
            mm(PD[:, :], ones_bf[0:1, :], bfc[0:1, 3072 + h2 * 512:3072 + h2 * 512 + 512],
               True, True)
            t_bc2 = P.mark("tensor", "sPe")
            P.wait("vector", "sPe", t_bc2)
            vop(lambda E, h2=h2: E.tensor_copy(MUB[:, h2 * 512:h2 * 512 + 512],
                                               PV1[:, :]))
            v_prev = vop(lambda E, h2=h2: E.tensor_copy(
                IVB[:, h2 * 512:h2 * 512 + 512], PD[:, :]), "sVe")
        vdrain()
        for c in range(8):
            vop(lambda E, c=c, toff=toff: E.tensor_sub(
                T1, x_own[:, c * 2048 + toff:c * 2048 + toff + 1024], MUB))
            vdrain()
            vop(lambda E, c=c: E.tensor_mul(
                xln2[:, c * 1024:(c + 1) * 1024], T1, IVB))
            vdrain()
        v_x2 = P.mark("vector", "sVe")

        P.wait("tensor", "sFw", wrt_rdy)
        P.wait("tensor", "sVe", v_x2)
        v_lgfree = 0
        v_gtfree = 0
        for tc in range(8):
            P.wait("tensor", "sVe", v_lgfree)
            for dc in range(8):
                mm(PA[0:128, 0:16],
                   xln2[:, dc * 1024 + tc * 128:dc * 1024 + (tc + 1) * 128],
                   wrt_sb[:, dc * 16:(dc + 1) * 16], dc == 0, dc == 7)
            t_lg = P.mark("tensor", "sPe")
            P.wait("vector", "sPe", t_lg)
            W0 = rw[:, 0:16]
            W1 = rw[:, 16:32]
            W2 = rw[:, 32:48]
            KEEP = rw[:, 48:64]
            MX = rw[:, 64:65]
            SM = rw[:, 65:66]
            vop(lambda E: E.reduce_max(MX, PA[0:128, 0:16], axis=AX.X)); vdrain()
            v_lgfree = vop(lambda E: E.tensor_scalar(
                W0, PA[0:128, 0:16], MX, None, ALU.subtract), "sVe")
            vdrain()
            v_sf = v_lgfree
            P.wait("scalar", "sVe", v_sf)
            s_ex = sop(lambda E: E.activation(W1, W0, AF.Exp), "sSc")
            P.wait("vector", "sSc", s_ex)
            vop(lambda E: E.reduce_sum(SM, W1, axis=AX.X)); vdrain()
            vop(lambda E: E.reciprocal(MX, SM)); vdrain()
            vop(lambda E: E.tensor_scalar(W1, W1, MX, None, ALU.mult)); vdrain()
            vop(lambda E: E.tensor_copy(W0, W1))
            vop(lambda E: E.memset(KEEP, 0.0)); vdrain()
            for kk in range(4):
                vop(lambda E: E.reduce_max(MX, W0, axis=AX.X)); vdrain()
                vop(lambda E: E.tensor_scalar(W2, W0, MX, None, ALU.is_ge)); vdrain()
                vop(lambda E: E.tensor_scalar(W2, W2, 1e30, -1e30, ALU.mult, ALU.add))
                vdrain()
                vop(lambda E: E.tensor_sub(W2, W2, iota_sb[:, :])); vdrain()
                vop(lambda E: E.reduce_max(MX, W2, axis=AX.X)); vdrain()
                vop(lambda E: E.tensor_scalar(W2, W2, MX, None, ALU.is_ge)); vdrain()
                vop(lambda E: E.tensor_add(KEEP, KEEP, W2)); vdrain()
                vop(lambda E: E.scalar_tensor_tensor(W0, W2, -1e30, W0, ALU.mult, ALU.add))
                vdrain()
            vop(lambda E: E.tensor_mul(W2, W1, KEEP)); vdrain()
            vop(lambda E: E.tensor_copy(scb[:, 0:16], W2)); vdrain()
            v_g = P.mark("vector", "sVe")
            P.wait("tensor", "sVe", v_g)
            P.wait("tensor", "sVe", v_gtfree)
            mm(PV1[0:16, 0:128], scb[:, 0:16], iden_bf[:, :], True, True)
            t_g = P.mark("tensor", "sPe")
            P.wait("vector", "sPe", t_g)
            v_gtfree = vop(lambda E, tc=tc: E.tensor_copy(
                gateT[:, tc * 128:(tc + 1) * 128], PV1[0:16, 0:128]), "sVe")
        v_gT = v_gtfree
        P.wait("sync", "sVe", v_gT)
        P.wait("sync", "sGa", gstore)
        w_g = dma("sync", gate_hbm.ap(), gateT[:, :], "sWr")

        g_row = 0
        t_gbc_prev = 0
        for e in range(NE):
            we = e % 2
            P.wait("gpsimd", "sPe", wexp_free[we])
            for c in range(8):
                dma("gpsimd", wup[we][:, c * 512:(c + 1) * 512],
                    wupT.ap()[e, c * 128:(c + 1) * 128, :], "sFw")
            for c in range(4):
                dma("gpsimd", wdn[we][:, c * 1024:(c + 1) * 1024],
                    wdnT.ap()[e, c * 128:(c + 1) * 128, :], "sFw")
            w_rdy = P.val("sFw")
            routed = e >= 2
            if routed:
                P.wait("sync", "sWr", w_g)
                P.wait("sync", "sPe", t_gbc_prev)
                g_row = dma("sync", grow[:, :], gate_hbm.ap()[e - 2:e - 1, :], "sGa2")
                P.wait("tensor", "sGa2", g_row)
                P.wait("tensor", "sVe", max(gbc_fr, v_gT))
                for jp in range(2):
                    mm([PV1, PD][jp][:, :], ones_bf[0:1, :],
                       grow[0:1, jp * 512:(jp + 1) * 512], True, True)
            t_gbc = P.mark("tensor", "sPe") if routed else P.val("sPe")
            t_gbc_prev = t_gbc
            gstore = g_row
            P.wait("tensor", "sFw", w_rdy)
            for ec in range(4):
                for jp in range(2):
                    pA = (ec * 2 + jp) % 2
                    P.wait("tensor", "sSc", up_fr[pA])
                    P.wait("tensor", "sVe", dn_fr[pA])
                    for dc in range(8):
                        mm(PPX[pA][:, :],
                           wup[we][:, dc * 512 + ec * 128:dc * 512 + (ec + 1) * 128],
                           xln2[:, dc * 1024 + jp * 512:dc * 1024 + (jp + 1) * 512],
                           dc == 0, dc == 7)
                    t_u = P.mark("tensor", "sPe")
                    P.wait("scalar", "sPe", t_u)
                    if routed:
                        P.wait("scalar", "sVe", glt_fr[pA])
                        s_gl = sop(lambda E, pA=pA: E.activation(
                            f32c[:, pA * 512:(pA + 1) * 512], PPX[pA][:, :], AF.Gelu),
                            "sSc")
                        up_fr[pA] = s_gl
                        P.wait("vector", "sSc", s_gl)
                        P.wait("vector", "sPe", t_gbc)
                        v_ac = vop(lambda E, ec=ec, jp=jp, pA=pA: E.tensor_mul(
                            act[:, ec * 1024 + jp * 512:ec * 1024 + (jp + 1) * 512],
                            f32c[:, pA * 512:(pA + 1) * 512], [PV1, PD][jp][:, :]), "sVe")
                        gbc_fr = v_ac
                        glt_fr[pA] = v_ac
                    else:
                        s_gl = sop(lambda E, pA=pA, ec=ec, jp=jp: E.activation(
                            act[:, ec * 1024 + jp * 512:ec * 1024 + (jp + 1) * 512],
                            PPX[pA][:, :], AF.Gelu), "sSc")
                        up_fr[pA] = s_gl
            vdrain()
            v_actdone = vop(lambda E: E.nop(), "sVe")
            sdrain()
            s_actdone = sop(lambda E: E.nop(), "sSc")
            P.wait("tensor", "sVe", v_actdone)
            P.wait("tensor", "sSc", s_actdone)
            for hc in range(8):
                for jp in range(2):
                    pA = (hc * 2 + jp) % 2
                    P.wait("tensor", "sSc", up_fr[pA])
                    P.wait("tensor", "sVe", dn_fr[pA])
                    for ec in range(4):
                        mm(PPX[pA][:, :],
                           wdn[we][:, ec * 1024 + hc * 128:ec * 1024 + (hc + 1) * 128],
                           act[:, ec * 1024 + jp * 512:ec * 1024 + (jp + 1) * 512],
                           ec == 0, ec == 3)
                    t_d = P.mark("tensor", "sPe")
                    P.wait("vector", "sPe", t_d)
                    if e == 0:
                        v_aa = vop(lambda E, pA=pA, hc=hc, jp=jp: E.tensor_copy(
                            acc[:, hc * 1024 + jp * 512:hc * 1024 + (jp + 1) * 512],
                            PPX[pA][:, :]), "sVe")
                    else:
                        v_aa = vop(lambda E, pA=pA, hc=hc, jp=jp: E.tensor_add(
                            acc[:, hc * 1024 + jp * 512:hc * 1024 + (jp + 1) * 512],
                            acc[:, hc * 1024 + jp * 512:hc * 1024 + (jp + 1) * 512],
                            PPX[pA][:, :]), "sVe")
                    dn_fr[pA] = v_aa
            wexp_free[we] = P.val("sPe")

        P.emit("vector", lambda E: E.drain())
        P.wait("vector", "sSt", out_st)
        for c in range(8):
            vop(lambda E, c=c, toff=toff: E.tensor_add(
                T2, acc[:, c * 1024:(c + 1) * 1024],
                x_own[:, c * 2048 + toff:c * 2048 + toff + 1024]))
            vdrain()
            v_o = P.mark("vector", "sVe")
            P.wait("sync", "sVe", v_o)
            out_st = dma("sync", out_d.ap()[c * 128:(c + 1) * 128, toff:toff + 1024],
                         T2, "sSt")
            P.wait("vector", "sSt", out_st)

    P.wait("sync", "sSt", P.val("sSt"))
    P.wait("sync", "sWr", P.val("sWr"))
    P.wait("gpsimd", "sRp", P.val("sRp"))
    P.wait("gpsimd", "sGa", P.val("sGa"))

    with nc.Block() as block:
        P.run(block)
    es.close()
    return nc


_CACHE = {}

def _prep_core_inputs(inputs, c):
    x = inputs["L0_attn_x"]
    b, qh = c // 2, c % 2
    key = "shared"
    if key not in _CACHE:
        Wqkv = np.asarray(inputs["L0_attn_Wqkv"]).reshape(16, 3, 64, 1024)
        WQ = Wqkv[:, 0].reshape(1024, 1024)
        WK = Wqkv[:, 1].reshape(1024, 1024)
        WV = Wqkv[:, 2].reshape(1024, 1024)
        bqkv = np.asarray(inputs["L0_attn_bqkv"]).reshape(16, 3, 64)
        bQ, bK, bV = [np.ascontiguousarray(bqkv[:, i].reshape(1024)) for i in range(3)]
        Wr = np.asarray(inputs["L0_attn_Wr"])
        br = np.asarray(inputs["L0_attn_br"])
        WrT = np.ascontiguousarray(Wr.T)
        wrpadT = np.concatenate([np.repeat(WrT[:, 0:1], 128, 1), WrT,
                                 np.repeat(WrT[:, 256:257], 127, 1)], axis=1)
        brpad = np.concatenate([np.repeat(br[0:1], 128), br,
                                np.repeat(br[256:257], 127)])[None, :]
        wredge4 = np.ascontiguousarray(WrT[:, [256, 0, 256, 0]])
        bredge = np.ascontiguousarray(br[[256, 0, 256, 0]][:, None])
        su = np.asarray(inputs["L0_ffn_shared_up"])
        sd = np.asarray(inputs["L0_ffn_shared_down"])
        ru = np.asarray(inputs["L0_ffn_routed_up"])
        rd = np.asarray(inputs["L0_ffn_routed_down"])
        wupT = np.ascontiguousarray(
            np.concatenate([su, ru], axis=0).transpose(0, 2, 1))
        wdnT = np.ascontiguousarray(
            np.concatenate([sd, rd], axis=0).transpose(0, 2, 1))
        wrtT = np.ascontiguousarray(np.asarray(inputs["L0_ffn_router_w"]).T)
        iden = np.eye(128, dtype=BF)
        iota = np.broadcast_to(np.arange(16, dtype=np.float32), (128, 16)).copy()
        _CACHE[key] = dict(
            wqT=np.ascontiguousarray(WQ.T), wkT=np.ascontiguousarray(WK.T),
            wvT=np.ascontiguousarray(WV.T),
            bq=np.ascontiguousarray(bQ.reshape(8, 128).T),
            bk=np.ascontiguousarray(bK.reshape(8, 128).T),
            bv_row=np.ascontiguousarray(bV[None, :]),
            wrpadT=np.ascontiguousarray(wrpadT), brpad=np.ascontiguousarray(brpad),
            wredge4=wredge4.astype(np.float32), bredge=bredge.astype(np.float32),
            wupT=wupT, wdnT=wdnT, wrtT=wrtT, iden=iden, iota=iota)
    shared = _CACHE[key]

    ovr = np.zeros((2, 128, 512), np.float32)
    if qh == 0:
        for di in range(128):
            ovr[0, di, 128 - di:256 - di] = 1.0
    else:
        for di in range(128):
            ovr[1, di, 384 - di:512] = 1.0
    ovr = ovr.astype(BF)
    Rsel = np.array([[1.0] * 128, [0.0] * 128], dtype=BF)
    Lsel = np.array([[0.0] * 128, [1.0] * 128], dtype=BF)
    Fsel = Rsel if qh == 0 else Lsel
    auxsel = np.concatenate([Rsel, Lsel, Fsel], axis=1)
    auxsel = np.concatenate([auxsel, auxsel], axis=0)

    xr = np.roll(np.asarray(x[b]), -2048 * qh, axis=0)
    xTc = np.ascontiguousarray(xr.T)

    d = dict(shared)
    d.update(xT=xTc, auxsel_in=auxsel, ovr=ovr)
    return d


_NC = None

def kernel(**inputs):
    global _NC
    in_maps = [_prep_core_inputs(inputs, c) for c in range(8)]
    if _NC is None:
        _NC = build_program()
    res = run_bass_kernel_spmd(_NC, in_maps, core_ids=list(range(8)))
    x = np.asarray(inputs["L0_attn_x"])
    out = np.empty_like(x)
    for c in range(8):
        b, qh = c // 2, c % 2
        out[b, 2048 * qh:2048 * (qh + 1), :] = res.results[c]["out"].T
    _CACHE.clear()
    return out


# version 08
# speedup vs baseline: 2.1708x; 1.0599x over previous
import contextlib
import numpy as np
import ml_dtypes
import concourse.bass as bass
import concourse.mybir as mybir
from concourse.bass_utils import run_bass_kernel_spmd

F32 = mybir.dt.float32
BF16 = mybir.dt.bfloat16
I32 = mybir.dt.int32
AF = mybir.ActivationFunctionType
ALU = mybir.AluOpType
AX = mybir.AxisListType
BF = ml_dtypes.bfloat16

H, HD, D, S, NQ = 16, 64, 1024, 4096, 2048
NKC, NQT, NE, ED = 32, 16, 18, 512
EPS = 1e-5
MSLACK = 24.0

def region_table(kc):
    if kc == 0:
        return [(0, 2048, 1)]
    if 1 <= kc <= 15:
        qa = 128 * (kc - 1)
        cut = min(qa + 384, 2048)
        regs = [(0, cut, 0)]
        if cut < 2048:
            regs.append((cut, 2048, 1))
        return regs
    return [(0, 2048, 2)]

def band_pieces_of_kc(kc):
    out = []
    for p in (0, 1, 2):
        qt = (kc if kc != 31 else -1) + 1 - p
        if 0 <= qt < NQT:
            out.append((qt, p, qt * 128))
    return out

ENGS = ("sync", "gpsimd", "tensor", "vector", "scalar")

DMA_ISSUER = {"sIn": "sync", "sWr": "sync", "sTp": "sync", "sSt": "sync",
              "sGa2": "sync", "sXl": "gpsimd",
              "sKV": "gpsimd", "sFw": "gpsimd", "sRp": "gpsimd", "sGa": "gpsimd"}
RELAY_SEM = {"sync": "sSy", "gpsimd": "sGy"}

class Plan:
    def __init__(self, nc):
        self.nc = nc
        self.stream = {e: [] for e in ENGS}
        self.sems = {}
        self.relay_cache = {}

    def sem(self, es, name):
        h = es.enter_context(self.nc.semaphore(name))
        self.sems[name] = [h, 0]

    def inc(self, name, k):
        s = self.sems[name]
        s[1] += k
        return s[1]

    def val(self, name):
        return self.sems[name][1]

    def emit(self, eng, fn):
        self.stream[eng].append(fn)

    def _rawwait(self, eng, name, v):
        h = self.sems[name][0]
        fn = lambda E, h=h, v=v: E.wait_ge(h, v)
        fn._is_wait = True
        self.emit(eng, fn)

    def wait(self, eng, name, v):
        if v <= 0:
            return
        assert v <= self.sems[name][1], f"wait on future {name}: {v} > {self.sems[name][1]}"
        if name in DMA_ISSUER:
            issuer = DMA_ISSUER[name]
            total = self.sems[name][1]
            if eng == issuer:
                self._rawwait(eng, name, total)
                return
            rs = RELAY_SEM[issuer]
            cached = self.relay_cache.get(name)
            if cached is None or cached[0] != total:
                self._rawwait(issuer, name, total)
                h = self.sems[rs][0]
                rv = self.inc(rs, 1)
                self.emit(issuer, lambda E, h=h: E.nop().then_inc(h, 1))
                self.relay_cache[name] = (total, rv)
            rv = self.relay_cache[name][1]
            self._rawwait(eng, rs, rv)
            return
        self._rawwait(eng, name, v)

    def mark(self, eng, sem):
        assert self.stream[eng], "mark on empty stream"
        last = self.stream[eng][-1]
        assert not getattr(last, "_has_upd", False), f"double mark on {eng}"
        assert not getattr(last, "_is_wait", False), f"mark on wait on {eng}"
        v = self.inc(sem, 1)
        h = self.sems[sem][0]
        self.stream[eng].pop()
        wrapped = lambda E, last=last, h=h: last(E).then_inc(h, 1)
        wrapped._has_upd = True
        self.stream[eng].append(wrapped)
        return v

    def op(self, eng, fn, sem=None, k=1):
        if sem is None:
            self.emit(eng, lambda E, fn=fn: fn(E))
            return None
        v = self.inc(sem, k)
        h = self.sems[sem][0]
        wrapped = lambda E, fn=fn, h=h, k=k: fn(E).then_inc(h, k)
        wrapped._has_upd = True
        self.emit(eng, wrapped)
        return v

    def run(self, block):
        for eng in ENGS:
            ops = self.stream[eng]
            getattr(block, eng)(lambda E, ops=ops: [f(E) for f in ops])


def build_program(n_pairs=8, dbg=False):
    nc = bass.Bass("TRN2", target_bir_lowering=False)
    es = contextlib.ExitStack()

    din = lambda n, s, dt=F32: nc.dram_tensor(n, s, dt, kind="ExternalInput")
    xT = din("xT", [D, S])
    wqT = din("wqT", [D, D]); wkT = din("wkT", [D, D]); wvT = din("wvT", [D, D])
    bq_i = din("bq", [128, 8]); bk_i = din("bk", [128, 8])
    bv_row = din("bv_row", [1, D])
    wrpadT = din("wrpadT", [64, 512]); brpad_i = din("brpad", [1, 512])
    wredge_i = din("wredge4", [64, 4]); bredge_i = din("bredge", [4, 1])
    wupT = din("wupT", [NE, D, ED]); wdnT = din("wdnT", [NE, ED, D])
    wrtT = din("wrtT", [D, 16])
    iden_i = din("iden", [128, 128], BF16)
    iota_i = din("iota", [128, 16])
    auxsel_i = din("auxsel_in", [4, 384], BF16)
    ovr_i = din("ovr", [2, 128, 512], BF16)

    out_d = nc.dram_tensor("out", [D, NQ], F32, kind="ExternalOutput")

    k_hbm = nc.dram_tensor("k_hbm", [8, 128, S], BF16)
    q_hbm = nc.dram_tensor("q_hbm", [8, 128, NQ], BF16)
    v_hbm = nc.dram_tensor("v_hbm", [NKC, 128, D], BF16)
    rp_hbm = nc.dram_tensor("rp_hbm", [2, 2, NQT, 128, 512], BF16)
    m_hbm = nc.dram_tensor("m_hbm", [2, 2, NQ], BF16)
    gate_hbm = nc.dram_tensor("gate_hbm", [16, 1024], BF16)

    sbt = lambda n, s, dt: es.enter_context(nc.sbuf_tensor(n, s, dt))
    x_own = sbt("x_own", [128, 16384], F32)
    big1 = sbt("big1", [128, 16384], BF16)
    big2 = sbt("big2", [128, 8192], BF16)
    big3 = sbt("big3", [128, 8192], BF16)
    f32s = sbt("f32s", [128, 8192], F32)
    f32c = sbt("f32c", [128, 4096], F32)
    bfc = sbt("bfc", [128, 4096], BF16)
    rw = sbt("rw", [128, 80], F32)
    ones_bf = sbt("ones_bf", [128, 128], BF16)
    iden_bf = sbt("iden_bf", [128, 128], BF16)
    iota_sb = sbt("iota_sb", [128, 16], F32)
    auxsel = sbt("auxsel", [128, 384], BF16)
    ovrb = sbt("ovrb", [128, 1024], BF16)
    bq_sb = sbt("bq_sb", [128, 8], F32)
    bk8_sb = sbt("bk8_sb", [128, 8], F32)
    bVbc = sbt("bVbc", [128, 1024], BF16)
    brbc = sbt("brbc", [128, 512], BF16)
    wrpad2 = sbt("wrpad2", [128, 512], BF16)
    wredge_sb = sbt("wredge_sb", [128, 36], BF16)
    bredge_sb = sbt("bredge_sb", [128, 1], F32)
    scal = sbt("scal", [128, 32], F32)
    scb = sbt("scb", [128, 32], BF16)
    nsq = sbt("nsq", [128, 32], F32)
    mwork = sbt("mwork", [128, 32], F32)
    mhbf = sbt("mhbf", [128, 32], BF16)
    mcolneg = sbt("mcolneg", [128, 32], F32)
    c1bc = sbt("c1bc", [128, 2], F32)
    c2bc = sbt("c2bc", [128, 1], F32)

    T1 = f32c[:, 0:1024]
    T2 = f32c[:, 1024:2048]
    BC1 = f32c[:, 2048:3072]
    BC2 = f32c[:, 3072:4096]

    xslab = [f32s[:, 0:4096], f32s[:, 4096:8192]]
    xlnsl = [big3[:, 0:4096], big3[:, 4096:8192]]
    kstage = [bfc[:, 0:512], bfc[:, 512:1024]]
    vstage = bfc[:, 1536:2560]
    sqP1 = bfc[:, 3072:3584]
    QP = bass.AP(f32s, 1024, [[8192, 128], [1, 2048]]).bitcast(BF16)
    AXT = bass.AP(f32s, 3072, [[8192, 128], [1, 2048]]).bitcast(BF16)
    M2V = bass.AP(f32s, 5120, [[8192, 128], [1, 1024]]).bitcast(BF16)
    SQB = bass.AP(f32s, 6144, [[8192, 128], [1, 1024]]).bitcast(BF16)
    rpstg2 = [f32s[:, 0:512], f32s[:, 512:1024]]
    qpair = [QP[:, 0:2048], QP[:, 2048:4096]]
    auxt = [AXT[:, 0:2048], AXT[:, 2048:4096]]
    kpair = [big2[:, 0:4096], big2[:, 4096:8192]]
    vpair = [big3[:, 0:4096], big3[:, 4096:8192]]
    bandT = big1[:, 0:12288]
    bandP = [[big1[:, 12288 + (2 * pS + hh) * 384:12288 + (2 * pS + hh + 1) * 384]
              for hh in range(2)] for pS in range(2)]
    msb = big1[:, 13824:13952]
    rpbf = [big1[:, 13952 + hh * 512:13952 + (hh + 1) * 512] for hh in range(2)]
    rcbf = big1[:, 14976:15488]
    rclo = big1[:, 15488:16000]
    possb = [[bfc[:, (2 * pS + hh) * 384:(2 * pS + hh + 1) * 384] for hh in range(2)]
             for pS in range(2)]
    RCV = bass.AP(f32s, 7168, [[8192, 128], [1, 1024]]).bitcast(BF16)
    rcbf = RCV[:, 0:512]
    rclo = RCV[:, 512:1024]
    pchunk = [bfc[:, 1536 + i * 1024:1536 + (i + 1) * 1024] for i in range(2)] + \
             [big1[:, 14976:16000]]
    xln2 = big1[:, 0:8192]
    act = big1[:, 8192:12288]
    wrt_sb = big1[:, 12288:12416]
    wup = [big2[:, 0:4096], big2[:, 4096:8192]]
    wdn = [big3[:, 0:4096], big3[:, 4096:8192]]
    acc = f32s
    grow = bfc[0:1, 0:1024]
    gateT = bfc[0:16, 1024:2048]
    sqP3 = bfc[:, 2048:2560]

    PA = es.enter_context(nc.psum_tensor("PA", [128, 1024], F32))
    PB = es.enter_context(nc.psum_tensor("PB", [128, 1024], F32))
    PV1 = es.enter_context(nc.psum_tensor("PV1", [128, 512], F32))
    PD = es.enter_context(nc.psum_tensor("PD", [128, 512], F32))
    PP = [es.enter_context(nc.psum_tensor("PP0", [128, 512], F32)),
          es.enter_context(nc.psum_tensor("PP1", [128, 512], F32))]

    P = Plan(nc)
    for s in ("sIn", "sWr", "sTp", "sSt", "sGa2", "sXl", "sKV", "sFw", "sRp", "sGa",
              "sPe", "sVe", "sSc", "sSy", "sGy"):
        P.sem(es, s)

    def dma(eng, dst, src, sem="sIn", slow=False):
        return P.op(eng, lambda E, d=dst, s=src, sl=slow:
                    E.dma_start(out=d, in_=s, allow_slow_non_contiguous=sl), sem, 16)

    def mm(dst, lhsT, rhs, start, stop, sem=None, tp=None):
        return P.op("tensor", lambda E, d=dst, l=lhsT, r=rhs, st=start, sp=stop, tp=tp:
                    E.matmul(d, l, r, start=st, stop=sp, tile_position=tp,
                             skip_group_check=True), sem, 1)

    def vop(fn, sem=None):
        return P.op("vector", fn, sem, 1)

    def sop(fn, sem=None):
        return P.op("scalar", fn, sem, 1)

    def vdrain():
        P.emit("vector", lambda E: E.drain())

    def sdrain():
        P.emit("scalar", lambda E: E.drain())

    def recip_nr(dst, src, ws):
        vop(lambda E: E.reciprocal(dst, src)); vdrain()

    dma("sync", iden_bf[:, :], iden_i.ap())
    dma("sync", iota_sb[:, :], iota_i.ap())
    dma("sync", auxsel[0:2, :], auxsel_i.ap()[0:2, :])
    dma("sync", auxsel[32:34, :], auxsel_i.ap()[2:4, :])
    dma("sync", ovrb[:, 0:512], ovr_i.ap()[0])
    dma("sync", ovrb[:, 512:1024], ovr_i.ap()[1])
    dma("sync", bq_sb[:, :], bq_i.ap())
    dma("sync", bk8_sb[:, :], bk_i.ap())
    dma("sync", f32c[0:1, 2048:3072], bv_row.ap())
    dma("sync", bredge_sb[0:2, 0:1], bredge_i.ap()[0:2])
    dma("sync", bredge_sb[32:34, 0:1], bredge_i.ap()[2:4])
    dma("sync", f32c[0:1, 3072:3584], brpad_i.ap())
    dma("gpsimd", wredge_sb[0:64, 0:2], wredge_i.ap()[:, 0:2], "sFw")
    dma("gpsimd", wredge_sb[64:128, 32:34], wredge_i.ap()[:, 2:4], "sFw")
    for r0 in (0, 64):
        dma("gpsimd", wrpad2[r0:r0 + 64, :], wrpadT.ap(), "sFw")
    for c in range(8):
        dma("gpsimd", big1[:, c * 1024:(c + 1) * 1024], wqT.ap()[c * 128:(c + 1) * 128, :],
            "sFw")
        dma("gpsimd", big1[:, 8192 + c * 1024:8192 + (c + 1) * 1024],
            wkT.ap()[c * 128:(c + 1) * 128, :], "sFw")
        dma("gpsimd", big2[:, c * 1024:(c + 1) * 1024], wvT.ap()[c * 128:(c + 1) * 128, :],
            "sFw")
    v_in = P.val("sIn")
    v_gw = P.val("sFw")

    P.wait("vector", "sIn", v_in)
    vop(lambda E: E.memset(ones_bf[:, :], 1.0))
    vop(lambda E: E.memset(wredge_sb[64:128, 0:32], 0.0))
    vop(lambda E: E.tensor_scalar(bk8_sb[:, :], bk8_sb[:, :], 0.125, None, ALU.mult))
    vop(lambda E: E.tensor_copy(bfc[0:1, 3072:4096], f32c[0:1, 2048:3072]))
    vop(lambda E: E.tensor_copy(bfc[0:1, 1024:1536], f32c[0:1, 3072:3584]))
    vdrain()
    v_rows0 = P.mark("vector", "sVe")
    P.wait("tensor", "sVe", v_rows0)
    for jp in range(2):
        mm(PA[:, jp * 512:(jp + 1) * 512], ones_bf[0:1, :],
           bfc[0:1, 3072 + jp * 512:3072 + (jp + 1) * 512], True, True)
    mm(PP[0][:, :], ones_bf[0:1, :], bfc[0:1, 1024:1536], True, True)
    t_bc0 = P.mark("tensor", "sPe")
    P.wait("vector", "sPe", t_bc0)
    vop(lambda E: E.tensor_copy(bVbc[:, :], PA[:, 0:1024]))
    vop(lambda E: E.tensor_copy(brbc[:, :], PP[0][:, :]))
    vdrain()
    v_bc0 = P.mark("vector", "sVe")

    P.wait("scalar", "sIn", v_in)
    P.wait("scalar", "sFw", v_gw)
    P.wait("scalar", "sPe", t_bc0)
    s_wr2 = sop(lambda E: E.square(sqP1[0:64, :], wrpad2[0:64, :]), "sSc")
    P.wait("tensor", "sSc", s_wr2)
    P.wait("tensor", "sVe", v_rows0)
    mm(PP[1][0:1, :], ones_bf[0:64, 0:1], sqP1[0:64, :], True, True)
    t_wr = P.mark("tensor", "sPe")
    P.wait("vector", "sPe", t_wr)
    vop(lambda E: E.reduce_max(scal[0:1, 0:1], PP[1][0:1, 128:385], axis=AX.X)); vdrain()
    v_mx = P.mark("vector", "sVe")
    P.wait("scalar", "sVe", v_mx)
    s_sq = sop(lambda E: E.sqrt(scal[0:1, 2:3], scal[0:1, 0:1]), "sSc")
    P.wait("vector", "sSc", s_sq)
    vop(lambda E: E.tensor_scalar(f32c[0:1, 1024:1536], f32c[0:1, 3072:3584],
                                  -1.0, None, ALU.mult)); vdrain()
    vop(lambda E: E.tensor_max(f32c[0:1, 1024:1536], f32c[0:1, 1024:1536],
                               f32c[0:1, 3072:3584])); vdrain()
    vop(lambda E: E.reduce_max(scal[0:1, 3:4], f32c[0:1, 1152:1409],
                               axis=AX.X)); vdrain()
    vop(lambda E: E.tensor_scalar(scal[0:1, 5:6], scal[0:1, 3:4], 1.0, -MSLACK,
                                  ALU.mult, ALU.add)); vdrain()
    vop(lambda E: E.tensor_copy(scb[0:1, 0:1], scal[0:1, 5:6])); vdrain()
    v_c2 = P.mark("vector", "sVe")
    P.wait("tensor", "sVe", v_c2)
    mm(PP[1][:, 0:1], ones_bf[0:1, :], scb[0:1, 0:1], True, True)
    t_c2 = P.mark("tensor", "sPe")
    P.wait("vector", "sPe", t_c2)
    vop(lambda E: E.tensor_copy(c2bc[:, :], PP[1][:, 0:1]))
    vdrain()
    v_c2b = P.mark("vector", "sVe")

    P.wait("tensor", "sFw", v_gw)

    slab_free_v = [0, 0]
    slab_xln_free = [0, 0]
    psP_fr = [0, 0]
    psV_fr = v_bc0
    kst_fr = [0, 0]
    vst_fr = 0
    x_own_ld = 0
    prev_bcc = [v_c2b]

    for s_i in range(8):
        b = s_i % 2
        P.wait("gpsimd", "sVe", slab_free_v[b])
        for c in range(8):
            dma("gpsimd", xslab[b][:, c * 512:(c + 1) * 512],
                xT.ap()[c * 128:(c + 1) * 128, s_i * 512:(s_i + 1) * 512], "sXl")
        if s_i < 4:
            for c in range(8):
                dma("gpsimd", x_own[:, c * 2048 + s_i * 512:c * 2048 + (s_i + 1) * 512],
                    xT.ap()[c * 128:(c + 1) * 128, s_i * 512:(s_i + 1) * 512], "sXl")
            x_own_ld = P.val("sXl")
        ld_done = P.val("sXl")

        P.wait("scalar", "sXl", ld_done)
        P.wait("scalar", "sPe", slab_xln_free[b])
        for c in range(8):
            sop(lambda E, b=b, c=c: E.copy(
                xlnsl[b][:, c * 512:(c + 1) * 512], xslab[b][:, c * 512:(c + 1) * 512]))
        s_cast = P.mark("scalar", "sSc")
        sdrain()
        P.wait("tensor", "sSc", s_cast)
        P.wait("tensor", "sVe", prev_bcc[0])
        for c in range(8):
            mm(PV1[0:1, :], ones_bf[:, 0:1], xlnsl[b][:, c * 512:(c + 1) * 512],
               c == 0, c == 7)
        for c in range(8):
            s_sq = sop(lambda E, b=b, c=c: E.square(
                sqP1[:, :], xlnsl[b][:, c * 512:(c + 1) * 512]), "sSc")
            P.wait("tensor", "sSc", s_sq)
            mm(PD[0:1, :], ones_bf[:, 0:1], sqP1[:, :], c == 0, c == 7)
            t_sqc = P.mark("tensor", "sPe")
            P.wait("scalar", "sPe", t_sqc)
        t_st = P.val("sPe")
        P.wait("vector", "sPe", t_st)
        MUr = f32c[0:1, 2048:2560]
        VARr = f32c[0:1, 2560:3072]
        MSQ = f32c[0:1, 3072:3584]
        INVr = f32c[0:1, 3584:4096]
        vop(lambda E: E.tensor_scalar(MUr, PV1[0:1, :], 1.0 / 1024, None, ALU.mult))
        vop(lambda E: E.tensor_scalar(VARr, PD[0:1, :], 1.0 / 1024, None, ALU.mult))
        vdrain()
        v_rowrd = P.mark("vector", "sVe")
        vop(lambda E: E.tensor_mul(MSQ, MUr, MUr)); vdrain()
        vop(lambda E: E.tensor_sub(VARr, VARr, MSQ)); vdrain()
        vop(lambda E: E.tensor_scalar(VARr, VARr, 1.0, EPS, ALU.mult, ALU.add)); vdrain()
        recip_nr(f32c[0:1, 3072:3584], VARr, f32c[0:1, 1024:1536])
        v_var = P.mark("vector", "sVe")
        P.wait("scalar", "sVe", v_var)
        s_rs = sop(lambda E: E.sqrt(INVr, f32c[0:1, 3072:3584]), "sSc")
        P.wait("vector", "sSc", s_rs)
        vop(lambda E: E.tensor_copy(bfc[0:1, 3072:3584], MUr))
        vop(lambda E: E.tensor_copy(bfc[0:1, 1024:1536], INVr))
        vdrain()
        v_r = P.mark("vector", "sVe")
        P.wait("tensor", "sVe", v_r)
        mm(PV1[:, :], ones_bf[0:1, :], bfc[0:1, 3072:3584], True, True)
        mm(PD[:, :], ones_bf[0:1, :], bfc[0:1, 1024:1536], True, True)
        t_bc = P.mark("tensor", "sPe")
        P.wait("vector", "sPe", t_bc)
        vop(lambda E: E.tensor_copy(BC1[:, 0:512], PV1[:, :]))
        vop(lambda E: E.tensor_copy(BC2[:, 0:512], PD[:, :]))
        vdrain()
        v_bcc = P.mark("vector", "sVe")
        prev_bcc[0] = v_bcc
        for c in range(8):
            vop(lambda E, b=b, c=c: E.tensor_sub(
                T1[:, 0:512], xslab[b][:, c * 512:(c + 1) * 512], f32c[:, 2048:2560]))
            vdrain()
            vop(lambda E, b=b, c=c: E.tensor_mul(
                xlnsl[b][:, c * 512:(c + 1) * 512], T1[:, 0:512], f32c[:, 3072:3584]))
            vdrain()
        v_xln = P.mark("vector", "sVe")
        slab_free_v[b] = v_xln

        P.wait("tensor", "sVe", v_xln)
        kinds = ("k", "q") if s_i < 4 else ("k",)
        for kind in kinds:
            woff = 8192 if kind == "k" else 0
            bias = bk8_sb if kind == "k" else bq_sb
            scale = 0.125 if kind == "k" else 1.0
            hbm = k_hbm if kind == "k" else q_hbm
            for fc in range(8):
                pA = fc % 2
                P.wait("tensor", "sSc", psP_fr[pA])
                for dc in range(8):
                    mm(PP[pA][:, :],
                       big1[:, woff + dc * 1024 + fc * 128:woff + dc * 1024 + (fc + 1) * 128],
                       xlnsl[b][:, dc * 512:(dc + 1) * 512], dc == 0, dc == 7)
                t_p = P.mark("tensor", "sPe")
                P.wait("scalar", "sPe", t_p)
                P.wait("scalar", "sWr", kst_fr[pA])
                sv = sop(lambda E, pA=pA, bias=bias, fc=fc, scale=scale: E.activation(
                    kstage[pA][:, :], PP[pA][:, :], AF.Identity,
                    bias=bias[:, fc:fc + 1], scale=scale), "sSc")
                psP_fr[pA] = sv
                P.wait("sync", "sSc", sv)
                kst_fr[pA] = dma("sync", hbm.ap()[fc, :, s_i * 512:(s_i + 1) * 512],
                                 kstage[pA][:, :], "sWr")
        for tc in range(4):
            kc_g = s_i * 4 + tc
            P.wait("tensor", "sVe", psV_fr)
            for jp in range(2):
                for dc in range(8):
                    mm(PA[:, jp * 512:(jp + 1) * 512],
                       xlnsl[b][:, dc * 512 + tc * 128:dc * 512 + (tc + 1) * 128],
                       big2[:, dc * 1024 + jp * 512:dc * 1024 + (jp + 1) * 512],
                       dc == 0, dc == 7)
            t_v = P.mark("tensor", "sPe")
            P.wait("vector", "sPe", t_v)
            P.wait("vector", "sWr", vst_fr)
            vop(lambda E: E.tensor_add(vstage[:, :], PA[:, 0:1024], bVbc[:, :]))
            vdrain()
            psV_fr = P.mark("vector", "sVe")
            P.wait("sync", "sVe", psV_fr)
            vst_fr = dma("sync", v_hbm.ap()[kc_g], vstage[:, :], "sWr")
        slab_xln_free[b] = P.val("sPe")

    p1_pe = P.val("sPe")
    p1_ve = P.val("sVe")
    p1_sc = P.val("sSc")
    p1_wr = P.val("sWr")

    ppfree = [("sSc", p1_sc), ("sSc", p1_sc)]
    pp_i = [0]

    def pp_next():
        b = pp_i[0] % 2
        pp_i[0] += 1
        return b

    kv_rdy = [0] * 8
    pair_free = [p1_pe, p1_pe]
    epil_sems = {}
    band_region_rdy = {}
    G = {"sqb_rd": 0}

    def emit_pair_loads(ch):
        pr = ch % 2
        P.wait("gpsimd", "sPe", pair_free[pr])
        if ch < 2:
            P.wait("gpsimd", "sWr", p1_wr)
        dma("gpsimd", kpair[pr][:, :], k_hbm.ap()[ch], "sKV")
        dma("gpsimd", qpair[pr][:, :], q_hbm.ap()[ch], "sKV")
        vsrc = bass.AP(v_hbm, ch * 128, [[D, 128], [128 * D, NKC], [1, 128]])
        vdst = bass.AP(big3, pr * 4096, [[8192, 128], [128, NKC], [1, 128]])
        dma("gpsimd", vdst, vsrc, "sKV")
        kv_rdy[ch] = P.val("sKV")

    def prep_steps(ch):
        pr = ch % 2
        st = {"aux_rdy": 0, "mcol_rdy": 0}
        loc = {"rp_wr": [0] * NQT, "rp_st_fr": [0, 0], "m_wr": 0, "m_ga": 0,
               "pS_fr": [("sPe", 0), ("sPe", 0)], "band_exp": [0] * NQT}

        def sqq_step():
            P.wait("vector", "sKV", kv_rdy[ch])
            P.wait("vector", "sPe", G["sqb_rd"])
            vop(lambda E: E.tensor_mul(SQB[:, 0:2048], qpair[pr][:, :], qpair[pr][:, :]))
            vdrain()
            loc["s_sqq"] = P.mark("vector", "sVe")

        def normq_step(g):
            P.wait("tensor", "sVe", loc["s_sqq"])
            if g == 0:
                b = pp_next()
                P.wait("tensor", *ppfree[b])
                b2 = pp_next()
                P.wait("tensor", *ppfree[b2])
                loc["nq_banks"] = (b, b2)
            b, b2 = loc["nq_banks"]
            for qt in range(g * 4, g * 4 + 4):
                mm(PP[b][:, qt:qt + 1], SQB[0:64, qt * 128:(qt + 1) * 128],
                   ones_bf[0:64, 0:1], True, True)
                mm(PP[b2][:, qt:qt + 1], SQB[64:128, qt * 128:(qt + 1) * 128],
                   ones_bf[64:128, 0:1], True, True)
            t_nq = P.mark("tensor", "sPe")
            G["sqb_rd"] = t_nq
            if g == 3:
                P.wait("vector", "sPe", t_nq)
                vop(lambda E, b=b: E.tensor_copy(nsq[:, 0:16], PP[b][:, 0:16]))
                vop(lambda E, b2=b2: E.tensor_copy(nsq[:, 16:32], PP[b2][:, 0:16]))
                vdrain()
                v_nq = P.mark("vector", "sVe")
                ppfree[b] = ("sVe", v_nq)
                ppfree[b2] = ("sVe", v_nq)
                loc["v_nsq"] = v_nq

        def knorm_step(g):
            cb, sub = g // 2, g % 2
            if sub == 0:
                P.wait("vector", "sPe", G["sqb_rd"])
                vop(lambda E, cb=cb: E.tensor_mul(
                    SQB[:, 0:2048], kpair[pr][:, cb * 2048:(cb + 1) * 2048],
                    kpair[pr][:, cb * 2048:(cb + 1) * 2048]))
                vdrain()
                loc["s_sqk"] = P.mark("vector", "sVe")
            P.wait("tensor", "sVe", loc["s_sqk"])
            for ss in range(2):
                b = pp_next()
                P.wait("tensor", *ppfree[b])
                b2 = pp_next()
                P.wait("tensor", *ppfree[b2])
                off = (sub * 2 + ss) * 512
                mm(PP[b][0:1, :], ones_bf[0:64, 0:1], SQB[0:64, off:off + 512],
                   True, True)
                mm(PP[b2][0:1, :], ones_bf[64:128, 0:1], SQB[64:128, off:off + 512],
                   True, True)
                t_kn = P.mark("tensor", "sPe")
                G["sqb_rd"] = t_kn
                P.wait("vector", "sPe", t_kn)
                first = (g == 0 and ss == 0)
                vop(lambda E, b=b: E.reduce_max(scal[0:1, 16:17], PP[b][0:1, :],
                                                axis=AX.X))
                vop(lambda E, b2=b2: E.reduce_max(scal[0:1, 18:19], PP[b2][0:1, :],
                                                  axis=AX.X))
                vdrain()
                if first:
                    vop(lambda E: E.tensor_copy(scal[0:1, 17:18], scal[0:1, 16:17]))
                    vop(lambda E: E.tensor_copy(scal[0:1, 19:20], scal[0:1, 18:19]))
                else:
                    vop(lambda E: E.tensor_max(scal[0:1, 17:18], scal[0:1, 17:18],
                                               scal[0:1, 16:17]))
                    vop(lambda E: E.tensor_max(scal[0:1, 19:20], scal[0:1, 19:20],
                                               scal[0:1, 18:19]))
                vdrain()
                v_kn = P.mark("vector", "sVe")
                ppfree[b] = ("sVe", v_kn)
                ppfree[b2] = ("sVe", v_kn)
            loc["v_knorm"] = P.val("sVe")

        def c1_step():
            P.wait("scalar", "sVe", loc["v_knorm"])
            sop(lambda E: E.sqrt(scal[0:1, 20:21], scal[0:1, 17:18]))
            sop(lambda E: E.sqrt(scal[0:1, 21:22], scal[0:1, 19:20]))
            sdrain()
            s_c1 = P.mark("scalar", "sSc")
            P.wait("vector", "sSc", s_c1)
            vop(lambda E: E.tensor_add(scal[0:1, 22:23], scal[0:1, 20:21],
                                       scal[0:1, 2:3]))
            vop(lambda E: E.tensor_add(scal[0:1, 23:24], scal[0:1, 21:22],
                                       scal[0:1, 2:3]))
            vdrain()
            vop(lambda E: E.tensor_copy(scb[0:1, 2:4], scal[0:1, 22:24])); vdrain()
            v_c1 = P.mark("vector", "sVe")
            P.wait("tensor", "sVe", v_c1)
            b = pp_next()
            P.wait("tensor", *ppfree[b])
            mm(PP[b][:, 0:2], ones_bf[0:1, :], scb[0:1, 2:4], True, True)
            t_c1 = P.mark("tensor", "sPe")
            P.wait("vector", "sPe", t_c1)
            vop(lambda E, b=b: E.tensor_copy(c1bc[:, :], PP[b][:, 0:2]))
            vdrain()
            v_c1b = P.mark("vector", "sVe")
            ppfree[b] = ("sVe", v_c1b)
            loc["v_c1b"] = v_c1b

        def mchain_step():
            P.wait("scalar", "sVe", loc["v_c1b"])
            sop(lambda E: E.activation(mwork[:, 0:32], nsq[:, 0:32], AF.Sqrt))
            sdrain()
            s_m = P.mark("scalar", "sSc")
            P.wait("vector", "sSc", s_m)
            P.wait("vector", "sVe", v_c2b)
            vop(lambda E: E.tensor_scalar(mwork[:, 0:16], mwork[:, 0:16],
                                          c1bc[:, 0:1], c2bc[:, 0:1],
                                          ALU.mult, ALU.add))
            vop(lambda E: E.tensor_scalar(mwork[:, 16:32], mwork[:, 16:32],
                                          c1bc[:, 1:2], c2bc[:, 0:1],
                                          ALU.mult, ALU.add))
            vdrain()
            vop(lambda E: E.tensor_copy(mhbf[:, :], mwork[:, :])); vdrain()
            vop(lambda E: E.tensor_scalar(mcolneg[:, :], mhbf[:, :], -1.0, None,
                                          ALU.mult))
            vdrain()
            st["mcol_rdy"] = P.mark("vector", "sVe")

        def mio_step():
            P.wait("tensor", "sVe", st["mcol_rdy"])
            b = pp_next()
            P.wait("tensor", *ppfree[b])
            P.op("tensor", lambda E, b=b: E.transpose(
                PP[b][0:32, 0:64].bitcast(BF16), mhbf[:, :], iden_bf[:, :]))
            t_tr = P.mark("tensor", "sPe")
            P.wait("vector", "sPe", t_tr)
            vop(lambda E, b=b: E.tensor_copy(msb[0:32, :],
                                             PP[b][0:32, 0:64].bitcast(BF16)))
            vdrain()
            v_mt = P.mark("vector", "sVe")
            ppfree[b] = ("sVe", v_mt)
            P.wait("gpsimd", "sVe", v_mt)
            for hh in range(2):
                dma("gpsimd", bass.AP(m_hbm, (pr * 2 + hh) * NQ, [[128, 16], [1, 128]]),
                    msb[hh * 16:(hh + 1) * 16, :], "sRp")
            loc["m_wr"] = P.val("sRp")
            P.wait("gpsimd", "sRp", loc["m_wr"])
            for hh in range(2):
                for r in range(2):
                    dma("gpsimd", M2V[hh * 32 + r:hh * 32 + r + 1, 0:2048],
                        bass.AP(m_hbm, (pr * 2 + hh) * NQ, [[NQ, 1], [1, NQ]]), "sGa")
            loc["m_ga"] = P.val("sGa")

        def edge_step(jp):
            P.wait("tensor", "sKV", kv_rdy[ch])
            b = pp_next()
            P.wait("tensor", *ppfree[b])
            b2 = pp_next()
            P.wait("tensor", *ppfree[b2])
            mm(PP[b][0:2, :], wredge_sb[0:64, 0:2],
               qpair[pr][0:64, jp * 512:(jp + 1) * 512], True, True)
            mm(PP[b2][0:34, :], wredge_sb[64:128, 0:34],
               qpair[pr][64:128, jp * 512:(jp + 1) * 512], True, True)
            t_e = P.mark("tensor", "sPe")
            P.wait("vector", "sPe", t_e)
            P.wait("vector", "sGa", loc["m_ga"])
            if jp == 0:
                P.wait("vector", "sPe", pair_free[pr])
            vop(lambda E, b=b, jp=jp: E.scalar_tensor_tensor(
                auxt[pr][0:2, jp * 512:(jp + 1) * 512], PP[b][0:2, :],
                bredge_sb[0:2, 0:1], M2V[0:2, jp * 512:(jp + 1) * 512],
                ALU.add, ALU.subtract))
            vop(lambda E, b2=b2, jp=jp: E.scalar_tensor_tensor(
                auxt[pr][32:34, jp * 512:(jp + 1) * 512], PP[b2][32:34, :],
                bredge_sb[32:34, 0:1], M2V[32:34, jp * 512:(jp + 1) * 512],
                ALU.add, ALU.subtract))
            vdrain()
            v_ed = P.mark("vector", "sVe")
            ppfree[b] = ("sVe", v_ed)
            ppfree[b2] = ("sVe", v_ed)
            if jp == 3:
                st["aux_rdy"] = v_ed

        def rp_step(qt):
            P.wait("tensor", "sKV", kv_rdy[ch])
            b = pp_next()
            P.wait("tensor", *ppfree[b])
            b2 = pp_next()
            P.wait("tensor", *ppfree[b2])
            mm(PP[b][:, :], qpair[pr][0:64, qt * 128:(qt + 1) * 128],
               wrpad2[0:64, :], True, True)
            mm(PP[b2][:, :], qpair[pr][64:128, qt * 128:(qt + 1) * 128],
               wrpad2[64:128, :], True, True)
            t_rp = P.mark("tensor", "sPe")
            P.wait("vector", "sPe", t_rp)
            P.wait("vector", "sRp", loc["rp_st_fr"][0])
            bb = (b, b2)
            if qt in (0, 15):
                mi = 0 if qt == 0 else 1
                ecol = 384 if qt == 0 else 128
                for hh in range(2):
                    vop(lambda E, hh=hh, bb=bb: E.tensor_add(
                        rpstg2[hh][:, :], PP[bb[hh]][:, :], brbc[:, :]))
                vdrain()
                for hh in range(2):
                    vop(lambda E, hh=hh, ecol=ecol: E.tensor_copy(
                        scal[:, 24 + hh:25 + hh], rpstg2[hh][:, ecol:ecol + 1]))
                vdrain()
                for hh in range(2):
                    vop(lambda E, hh=hh: E.tensor_scalar(
                        T1[:, 0:512], rpstg2[hh][:, :], scal[:, 24 + hh:25 + hh],
                        None, ALU.subtract))
                    vdrain()
                    vop(lambda E, mi=mi: E.tensor_mul(
                        T1[:, 0:512], T1[:, 0:512], ovrb[:, mi * 512:(mi + 1) * 512]))
                    vdrain()
                    vop(lambda E, hh=hh: E.tensor_sub(rpstg2[hh][:, :],
                                                      rpstg2[hh][:, :], T1[:, 0:512]))
                    vdrain()
                for hh in range(2):
                    vop(lambda E, hh=hh: E.tensor_copy(rpbf[hh][:, :], rpstg2[hh][:, :]))
                vdrain()
            else:
                for hh in range(2):
                    vop(lambda E, hh=hh, bb=bb: E.tensor_add(
                        rpbf[hh][:, :], PP[bb[hh]][:, :], brbc[:, :]))
                vdrain()
            v_rp = P.mark("vector", "sVe")
            ppfree[b] = ("sVe", v_rp)
            ppfree[b2] = ("sVe", v_rp)
            P.wait("gpsimd", "sVe", v_rp)
            for hh in range(2):
                dma("gpsimd",
                    bass.AP(rp_hbm, ((pr * 2 + hh) * NQT + qt) * 128 * 512,
                            [[512, 128], [1, 512]]),
                    rpbf[hh][:, :], "sRp")
            loc["rp_st_fr"][0] = P.val("sRp")
            loc["rp_wr"][qt] = P.val("sRp")

        def band_step(qt):
            pS = qt % 2
            qb = qt // 4
            P.wait("gpsimd", "sRp", loc["rp_wr"][qt])
            P.wait("gpsimd", "sVe", loc.get(("pos_fr", pS), 0))
            for hh in range(2):
                dma("gpsimd", possb[pS][hh][:, :],
                    bass.AP(rp_hbm, ((pr * 2 + hh) * NQT + qt) * 128 * 512 + 128,
                            [[511, 128], [1, 384]]), "sGa")
            g_pos = P.val("sGa")
            b = pp_next()
            P.wait("tensor", *ppfree[b])
            b2 = pp_next()
            P.wait("tensor", *ppfree[b2])
            if qt == 0:
                mm(PP[b][:, 0:128], qpair[pr][0:64, 0:128],
                   kpair[pr][0:64, 3968:4096], True, True)
                mm(PP[b2][:, 0:128], qpair[pr][64:128, 0:128],
                   kpair[pr][64:128, 3968:4096], True, True)
                mm(PP[b][:, 128:384], qpair[pr][0:64, 0:128],
                   kpair[pr][0:64, 0:256], True, True)
                mm(PP[b2][:, 128:384], qpair[pr][64:128, 0:128],
                   kpair[pr][64:128, 0:256], True, True)
            else:
                j0 = qt * 128 - 128
                mm(PP[b][:, 0:384], qpair[pr][0:64, qt * 128:(qt + 1) * 128],
                   kpair[pr][0:64, j0:j0 + 384], True, True)
                mm(PP[b2][:, 0:384], qpair[pr][64:128, qt * 128:(qt + 1) * 128],
                   kpair[pr][64:128, j0:j0 + 384], True, True)
            t_b = P.mark("tensor", "sPe")
            P.wait("vector", "sPe", t_b)
            P.wait("vector", "sGa", g_pos)
            bb = (b, b2)
            for hh in range(2):
                vop(lambda E, hh=hh, bb=bb, pS=pS: E.tensor_add(
                    PP[bb[hh]][:, 0:384], PP[bb[hh]][:, 0:384], possb[pS][hh][:, :]))
            vdrain()
            v_b = P.mark("vector", "sVe")
            loc[("pos_fr", pS)] = v_b
            P.wait("scalar", "sVe", v_b)
            P.wait("scalar", "sVe", st["mcol_rdy"])
            P.wait("scalar", *loc["pS_fr"][pS])
            for hh in range(2):
                sop(lambda E, hh=hh, bb=bb, pS=pS, qt=qt: E.activation(
                    bandP[pS][hh][:, :], PP[bb[hh]][:, 0:384], AF.Exp,
                    bias=mcolneg[:, hh * 16 + qt:hh * 16 + qt + 1]))
            sdrain()
            s_b = P.mark("scalar", "sSc")
            ppfree[b] = ("sSc", s_b)
            ppfree[b2] = ("sSc", s_b)
            loc["band_exp"][qt] = s_b

        def band_tp_step(qt):
            pS = qt % 2
            qb = qt // 4
            bt = pp_next()
            P.wait("tensor", *ppfree[bt])
            P.wait("tensor", "sSc", loc["band_exp"][qt])
            view = PP[bt][:, 0:384].bitcast(BF16)
            for p in range(3):
                for hh in range(2):
                    t = p * 2 + hh
                    P.op("tensor", lambda E, pS=pS, hh=hh, p=p, t=t:
                         E.transpose(view[:, t * 128:(t + 1) * 128],
                                     bandP[pS][hh][:, p * 128:(p + 1) * 128],
                                     iden_bf[:, :]))
            t_tp = P.mark("tensor", "sPe")
            loc["pS_fr"][pS] = ("sPe", t_tp)
            P.wait("vector", "sPe", t_tp)
            col = qb * 3072 + (qt % 4) * 768
            vop(lambda E, col=col, view=view: E.tensor_copy(
                bandT[:, col:col + 768], view[:, 0:768]))
            vdrain()
            v_bt = P.mark("vector", "sVe")
            ppfree[bt] = ("sVe", v_bt)
            if qt % 4 == 3:
                band_region_rdy[(ch, qb)] = v_bt

        steps = [(0, sqq_step)]
        steps += [(0, lambda g=g: normq_step(g)) for g in range(4)]
        steps += [(0, lambda g=g: knorm_step(g)) for g in range(4)]
        steps += [(0, c1_step), (0, mchain_step), (0, mio_step)]
        steps += [(0, lambda jp=jp: edge_step(jp)) for jp in range(4)]
        steps += [(0, lambda qt=qt: rp_step(qt)) for qt in range(NQT)]
        ab = [(0, lambda qt=0: band_step(0)), (0, lambda qt=1: band_step(1))]
        for k in range(NQT):
            ab.append(((k // 4) * 32 + 32, lambda qt=k: band_tp_step(qt)))
            if k + 2 < NQT:
                ab.append((0, lambda qt=k + 2: band_step(qt)))
        steps += ab
        return steps, st

    def emit_main(ch, st, steps, tail):
        pr = ch % 2
        exp_done = {-2: tail["exp"][0], -1: tail["exp"][1]}
        pvden_done = {-3: tail["pvden"][0], -2: tail["pvden"][1],
                      -1: tail["pvden"][2]}
        lag = None
        nsteps = len(steps)
        next_step_at = [0]
        pieces_total = 128

        def consume_steps(i):
            want = (i + 1) * nsteps // pieces_total
            while next_step_at[0] < want and steps and steps[0][0] <= i:
                steps.pop(0)[1]()
                next_step_at[0] += 1

        P.wait("tensor", "sKV", kv_rdy[ch])
        P.wait("tensor", "sVe", st["aux_rdy"])
        pv_start_guard = tail["pv_free"]
        for qblk in range(4):
            q0 = qblk * 512
            for kc in range(NKC):
                i = qblk * NKC + kc
                pa = PA if i % 2 == 0 else PB
                pc = i % 3
                P.wait("tensor", "sSc", exp_done.get(i - 2, 0))
                mm(pa[:, 0:512], kpair[pr][0:64, kc * 128:(kc + 1) * 128],
                   qpair[pr][0:64, q0:q0 + 512], True, False)
                mm(pa[:, 512:1024], kpair[pr][64:128, kc * 128:(kc + 1) * 128],
                   qpair[pr][64:128, q0:q0 + 512], True, False)
                subr = [(max(lo, q0), min(hi, q0 + 512), v)
                        for (lo, hi, v) in region_table(kc)
                        if max(lo, q0) < min(hi, q0 + 512)]
                for si, (lo, hi, v) in enumerate(subr):
                    last = si == len(subr) - 1
                    mm(pa[:, lo - q0:hi - q0], auxsel[0:2, v * 128:v * 128 + 128],
                       auxt[pr][0:2, lo:hi], False, last)
                    mm(pa[:, 512 + lo - q0:512 + hi - q0],
                       auxsel[32:34, v * 128:v * 128 + 128],
                       auxt[pr][32:34, lo:hi], False, last)
                t_qk = P.mark("tensor", "sPe")
                if lag is not None:
                    lag()
                    lag = None
                P.wait("scalar", "sPe", t_qk)
                P.wait("scalar", "sPe", pvden_done.get(i - 3, 0))
                sop(lambda E, pa=pa, pc=pc: E.activation(
                    pchunk[pc][:, :], pa[:, 0:1024], AF.Exp))
                s_e = P.mark("scalar", "sSc")
                exp_done[i] = s_e
                ncopy = 0
                P.wait("vector", "sSc", s_e)
                P.wait("vector", "sPe", pvden_done.get(i - 3, 0))
                for (qt, p, bq0) in band_pieces_of_kc(kc):
                    if q0 <= bq0 < q0 + 512:
                        col = qblk * 3072 + (qt % 4) * 768 + p * 256
                        off = bq0 - q0
                        vop(lambda E, pc=pc, col=col, off=off: E.tensor_copy(
                            pchunk[pc][:, off:off + 128], bandT[:, col:col + 128]))
                        vop(lambda E, pc=pc, col=col, off=off: E.tensor_copy(
                            pchunk[pc][:, 512 + off:512 + off + 128],
                            bandT[:, col + 128:col + 256]))
                        ncopy += 1
                v_cp = P.mark("vector", "sVe") if ncopy else P.val("sVe")

                def den_pv(i=i, kc=kc, pc=pc, s_e=s_e, v_cp=v_cp, q0=q0,
                           guard=pv_start_guard):
                    P.wait("tensor", "sSc", s_e)
                    P.wait("tensor", "sVe", v_cp)
                    if kc == 0:
                        P.wait("tensor", "sVe", guard)
                    mm(PD[0:1, :], ones_bf[:, 0:1], pchunk[pc][:, 0:512],
                       kc == 0, kc == NKC - 1)
                    mm(PD[64:65, :], ones_bf[:, 0:1], pchunk[pc][:, 512:1024],
                       kc == 0, kc == NKC - 1, tp=(0, 64))
                    mm(PV1[0:64, :], vpair[pr][:, kc * 128:kc * 128 + 64],
                       pchunk[pc][:, 0:512], kc == 0, kc == NKC - 1)
                    mm(PV1[64:128, :], vpair[pr][:, kc * 128 + 64:kc * 128 + 128],
                       pchunk[pc][:, 512:1024], kc == 0, kc == NKC - 1, tp=(0, 64))
                    pvden_done[i] = P.mark("tensor", "sPe")
                lag = den_pv
                consume_steps(i)
            if lag is not None:
                lag()
                lag = None
            t_last = pvden_done[qblk * NKC + NKC - 1]
            P.wait("vector", "sPe", t_last)
            vop(lambda E: E.tensor_copy(T2[0:65, 0:512], PD[0:65, :]))
            vdrain()
            vop(lambda E: E.tensor_copy(T2[:, 512:1024], PV1[:, :]))
            vdrain()
            v_pvc = P.mark("vector", "sVe")
            recip_nr(T1[0:65, 0:512], T2[0:65, 0:512], T1[0:65, 512:1024])
            vop(lambda E: E.tensor_copy(rcbf[0:65, :], T1[0:65, 0:512]))
            vdrain()
            vop(lambda E: E.tensor_sub(rclo[0:65, :], T1[0:65, 0:512],
                                       rcbf[0:65, :]))
            vdrain()
            v_rc = P.mark("vector", "sVe")
            b = pp_next()
            P.wait("tensor", *ppfree[b])
            P.wait("tensor", "sVe", v_rc)
            mm(PP[b][0:64, :], ones_bf[0:1, 0:64], rcbf[0:1, :], True, False)
            mm(PP[b][64:128, :], ones_bf[64:65, 0:64], rcbf[64:65, :], True, False,
               tp=(64, 64))
            mm(PP[b][0:64, :], ones_bf[0:1, 0:64], rclo[0:1, :], False, True)
            mm(PP[b][64:128, :], ones_bf[64:65, 0:64], rclo[64:65, :], False, True,
               tp=(64, 64))
            t_bc = P.mark("tensor", "sPe")
            P.wait("vector", "sPe", t_bc)
            vop(lambda E, b=b: E.tensor_mul(T1[:, 512:1024], T2[:, 512:1024],
                                            PP[b][:, :]))
            vdrain()
            v_pvf = P.mark("vector", "sVe")
            ppfree[b] = ("sVe", v_pvf)
            if ch == 0 and qblk == 0:
                P.wait("vector", "sXl", x_own_ld)
            vop(lambda E, ch=ch, q0=q0: E.tensor_add(
                x_own[:, ch * 2048 + q0:ch * 2048 + q0 + 512],
                x_own[:, ch * 2048 + q0:ch * 2048 + q0 + 512], T1[:, 512:1024]))
            vdrain()
            v_ep = P.mark("vector", "sVe")
            epil_sems[(ch, qblk)] = v_ep
            pv_start_guard = v_pvc
        while steps:
            steps.pop(0)[1]()
        n = 4 * NKC
        tail_out = {"exp": [exp_done[n - 2], exp_done[n - 1]],
                    "pvden": [pvden_done[n - 3], pvden_done[n - 2],
                              pvden_done[n - 1]],
                    "pv_free": pv_start_guard}
        return P.val("sPe"), P.val("sVe"), tail_out

    if n_pairs > 0:
        emit_pair_loads(0)
        if n_pairs > 1:
            emit_pair_loads(1)
        steps0, st0 = prep_steps(0)
        for _, f in steps0:
            f()
        states = {0: st0}
        tail = {"exp": [0, 0], "pvden": [0, 0, 0], "pv_free": 0}
        for ch in range(n_pairs):
            st = states[ch]
            if ch + 1 < n_pairs:
                steps_n, st_n = prep_steps(ch + 1)
                states[ch + 1] = st_n
            else:
                steps_n = []
            main_pe, main_ve, tail = emit_main(ch, st, steps_n, tail)
            pair_free[ch % 2] = main_pe
            if ch + 2 < n_pairs:
                emit_pair_loads(ch + 2)

    p2_ve = P.val("sVe")
    p2_sc = P.val("sSc")
    p2_pe = P.val("sPe")

    PPX = PP

    P.wait("gpsimd", "sPe", p2_pe)
    for c in range(8):
        dma("gpsimd", wrt_sb[:, c * 16:(c + 1) * 16],
            wrtT.ap()[c * 128:(c + 1) * 128, :], "sFw")
    wrt_rdy = P.val("sFw")

    wexp_free = [p2_pe, p2_pe]
    up_fr = [0, 0]
    dn_fr = [0, 0]
    gbc_fr = 0
    glt_fr = [0, 0]
    out_st = 0
    gstore = 0

    MUB = f32c[:, 2048:3072]
    IVB = f32c[:, 3072:4096]

    for th in range(2):
        toff = th * 1024
        P.wait("scalar", "sPe", P.val("sPe"))
        P.wait("scalar", "sVe", p2_ve if th == 0 else P.val("sVe"))
        for c in range(8):
            sop(lambda E, c=c, toff=toff: E.copy(
                xln2[:, c * 1024:(c + 1) * 1024],
                x_own[:, c * 2048 + toff:c * 2048 + toff + 1024]))
        s_cast2 = P.mark("scalar", "sSc")
        sdrain()
        P.wait("tensor", "sSc", s_cast2)
        v_prev = 0
        for h2 in range(2):
            P.wait("scalar", "sPe", P.val("sPe"))
            P.wait("tensor", "sVe", v_prev)
            for c in range(8):
                mm(PV1[0:1, :], ones_bf[:, 0:1],
                   xln2[:, c * 1024 + h2 * 512:c * 1024 + h2 * 512 + 512], c == 0, c == 7)
            t_a = P.mark("tensor", "sPe")
            P.wait("vector", "sPe", t_a)
            v_prev = vop(lambda E, h2=h2: E.tensor_scalar(
                f32c[0:1, 2048 + h2 * 512:2048 + h2 * 512 + 512], PV1[0:1, :],
                1.0 / 1024, None, ALU.mult), "sVe")
            for c in range(8):
                s_last = sop(lambda E, c=c, h2=h2: E.square(
                    sqP3[:, :],
                    xln2[:, c * 1024 + h2 * 512:c * 1024 + h2 * 512 + 512]), "sSc")
                P.wait("tensor", "sSc", s_last)
                mm(PD[0:1, :], ones_bf[:, 0:1], sqP3[:, :], c == 0, c == 7)
                t_s3 = P.mark("tensor", "sPe")
                P.wait("scalar", "sPe", t_s3)
            t_b2 = P.val("sPe")
            P.wait("vector", "sPe", t_b2)
            v_prev = vop(lambda E, h2=h2: E.tensor_scalar(
                f32c[0:1, 3072 + h2 * 512:3072 + h2 * 512 + 512], PD[0:1, :],
                1.0 / 1024, None, ALU.mult), "sVe")
        vdrain()
        MU2 = f32c[0:1, 2048:3072]
        EX2 = f32c[0:1, 3072:4096]
        vop(lambda E: E.tensor_mul(f32c[0:1, 1024:2048], MU2, MU2)); vdrain()
        vop(lambda E: E.tensor_sub(EX2, EX2, f32c[0:1, 1024:2048])); vdrain()
        vop(lambda E: E.tensor_scalar(EX2, EX2, 1.0, EPS, ALU.mult, ALU.add)); vdrain()
        recip_nr(f32c[0:1, 1024:2048], EX2, f32c[0:1, 0:1024])
        v_v2 = P.mark("vector", "sVe")
        P.wait("scalar", "sVe", v_v2)
        s_r2 = sop(lambda E: E.sqrt(f32c[0:1, 0:1024], f32c[0:1, 1024:2048]), "sSc")
        P.wait("vector", "sSc", s_r2)
        vop(lambda E: E.tensor_copy(bfc[0:1, 2048:3072], MU2))
        vop(lambda E: E.tensor_copy(bfc[0:1, 3072:4096], f32c[0:1, 0:1024]))
        vdrain()
        v_r2 = P.mark("vector", "sVe")
        P.wait("tensor", "sVe", v_r2)
        v_prev = 0
        for h2 in range(2):
            P.wait("tensor", "sVe", v_prev)
            mm(PV1[:, :], ones_bf[0:1, :], bfc[0:1, 2048 + h2 * 512:2048 + h2 * 512 + 512],
               True, True)
            mm(PD[:, :], ones_bf[0:1, :], bfc[0:1, 3072 + h2 * 512:3072 + h2 * 512 + 512],
               True, True)
            t_bc2 = P.mark("tensor", "sPe")
            P.wait("vector", "sPe", t_bc2)
            vop(lambda E, h2=h2: E.tensor_copy(MUB[:, h2 * 512:h2 * 512 + 512],
                                               PV1[:, :]))
            v_prev = vop(lambda E, h2=h2: E.tensor_copy(
                IVB[:, h2 * 512:h2 * 512 + 512], PD[:, :]), "sVe")
        vdrain()
        for c in range(8):
            vop(lambda E, c=c, toff=toff: E.tensor_sub(
                T1, x_own[:, c * 2048 + toff:c * 2048 + toff + 1024], MUB))
            vdrain()
            vop(lambda E, c=c: E.tensor_mul(
                xln2[:, c * 1024:(c + 1) * 1024], T1, IVB))
            vdrain()
        v_x2 = P.mark("vector", "sVe")

        P.wait("tensor", "sFw", wrt_rdy)
        P.wait("tensor", "sVe", v_x2)
        v_lgfree = 0
        v_gtfree = 0
        for tc in range(8):
            P.wait("tensor", "sVe", v_lgfree)
            for dc in range(8):
                mm(PA[0:128, 0:16],
                   xln2[:, dc * 1024 + tc * 128:dc * 1024 + (tc + 1) * 128],
                   wrt_sb[:, dc * 16:(dc + 1) * 16], dc == 0, dc == 7)
            t_lg = P.mark("tensor", "sPe")
            P.wait("vector", "sPe", t_lg)
            W0 = rw[:, 0:16]
            W1 = rw[:, 16:32]
            W2 = rw[:, 32:48]
            KEEP = rw[:, 48:64]
            MX = rw[:, 64:65]
            SM = rw[:, 65:66]
            vop(lambda E: E.reduce_max(MX, PA[0:128, 0:16], axis=AX.X)); vdrain()
            v_lgfree = vop(lambda E: E.tensor_scalar(
                W0, PA[0:128, 0:16], MX, None, ALU.subtract), "sVe")
            vdrain()
            v_sf = v_lgfree
            P.wait("scalar", "sVe", v_sf)
            s_ex = sop(lambda E: E.activation(W1, W0, AF.Exp), "sSc")
            P.wait("vector", "sSc", s_ex)
            vop(lambda E: E.reduce_sum(SM, W1, axis=AX.X)); vdrain()
            vop(lambda E: E.reciprocal(MX, SM)); vdrain()
            vop(lambda E: E.tensor_scalar(W1, W1, MX, None, ALU.mult)); vdrain()
            vop(lambda E: E.tensor_copy(W0, W1))
            vop(lambda E: E.memset(KEEP, 0.0)); vdrain()
            for kk in range(4):
                vop(lambda E: E.reduce_max(MX, W0, axis=AX.X)); vdrain()
                vop(lambda E: E.tensor_scalar(W2, W0, MX, None, ALU.is_ge)); vdrain()
                vop(lambda E: E.tensor_scalar(W2, W2, 1e30, -1e30, ALU.mult, ALU.add))
                vdrain()
                vop(lambda E: E.tensor_sub(W2, W2, iota_sb[:, :])); vdrain()
                vop(lambda E: E.reduce_max(MX, W2, axis=AX.X)); vdrain()
                vop(lambda E: E.tensor_scalar(W2, W2, MX, None, ALU.is_ge)); vdrain()
                vop(lambda E: E.tensor_add(KEEP, KEEP, W2)); vdrain()
                vop(lambda E: E.scalar_tensor_tensor(W0, W2, -1e30, W0, ALU.mult, ALU.add))
                vdrain()
            vop(lambda E: E.tensor_mul(W2, W1, KEEP)); vdrain()
            vop(lambda E: E.tensor_copy(scb[:, 0:16], W2)); vdrain()
            v_g = P.mark("vector", "sVe")
            P.wait("tensor", "sVe", v_g)
            P.wait("tensor", "sVe", v_gtfree)
            mm(PV1[0:16, 0:128], scb[:, 0:16], iden_bf[:, :], True, True)
            t_g = P.mark("tensor", "sPe")
            P.wait("vector", "sPe", t_g)
            v_gtfree = vop(lambda E, tc=tc: E.tensor_copy(
                gateT[:, tc * 128:(tc + 1) * 128], PV1[0:16, 0:128]), "sVe")
        v_gT = v_gtfree
        P.wait("sync", "sVe", v_gT)
        P.wait("sync", "sGa", gstore)
        w_g = dma("sync", gate_hbm.ap(), gateT[:, :], "sWr")

        g_row = 0
        t_gbc_prev = 0
        for e in range(NE):
            we = e % 2
            P.wait("gpsimd", "sPe", wexp_free[we])
            for c in range(8):
                dma("gpsimd", wup[we][:, c * 512:(c + 1) * 512],
                    wupT.ap()[e, c * 128:(c + 1) * 128, :], "sFw")
            for c in range(4):
                dma("gpsimd", wdn[we][:, c * 1024:(c + 1) * 1024],
                    wdnT.ap()[e, c * 128:(c + 1) * 128, :], "sFw")
            w_rdy = P.val("sFw")
            routed = e >= 2
            if routed:
                P.wait("sync", "sWr", w_g)
                P.wait("sync", "sPe", t_gbc_prev)
                g_row = dma("sync", grow[:, :], gate_hbm.ap()[e - 2:e - 1, :], "sGa2")
                P.wait("tensor", "sGa2", g_row)
                P.wait("tensor", "sVe", max(gbc_fr, v_gT))
                for jp in range(2):
                    mm([PV1, PD][jp][:, :], ones_bf[0:1, :],
                       grow[0:1, jp * 512:(jp + 1) * 512], True, True)
            t_gbc = P.mark("tensor", "sPe") if routed else P.val("sPe")
            t_gbc_prev = t_gbc
            gstore = g_row
            P.wait("tensor", "sFw", w_rdy)
            for ec in range(4):
                for jp in range(2):
                    pA = (ec * 2 + jp) % 2
                    P.wait("tensor", "sSc", up_fr[pA])
                    P.wait("tensor", "sVe", dn_fr[pA])
                    for dc in range(8):
                        mm(PPX[pA][:, :],
                           wup[we][:, dc * 512 + ec * 128:dc * 512 + (ec + 1) * 128],
                           xln2[:, dc * 1024 + jp * 512:dc * 1024 + (jp + 1) * 512],
                           dc == 0, dc == 7)
                    t_u = P.mark("tensor", "sPe")
                    P.wait("scalar", "sPe", t_u)
                    if routed:
                        P.wait("scalar", "sVe", glt_fr[pA])
                        s_gl = sop(lambda E, pA=pA: E.activation(
                            f32c[:, pA * 512:(pA + 1) * 512], PPX[pA][:, :], AF.Gelu),
                            "sSc")
                        up_fr[pA] = s_gl
                        P.wait("vector", "sSc", s_gl)
                        P.wait("vector", "sPe", t_gbc)
                        v_ac = vop(lambda E, ec=ec, jp=jp, pA=pA: E.tensor_mul(
                            act[:, ec * 1024 + jp * 512:ec * 1024 + (jp + 1) * 512],
                            f32c[:, pA * 512:(pA + 1) * 512], [PV1, PD][jp][:, :]), "sVe")
                        gbc_fr = v_ac
                        glt_fr[pA] = v_ac
                    else:
                        s_gl = sop(lambda E, pA=pA, ec=ec, jp=jp: E.activation(
                            act[:, ec * 1024 + jp * 512:ec * 1024 + (jp + 1) * 512],
                            PPX[pA][:, :], AF.Gelu), "sSc")
                        up_fr[pA] = s_gl
            vdrain()
            v_actdone = vop(lambda E: E.nop(), "sVe")
            sdrain()
            s_actdone = sop(lambda E: E.nop(), "sSc")
            P.wait("tensor", "sVe", v_actdone)
            P.wait("tensor", "sSc", s_actdone)
            for hc in range(8):
                for jp in range(2):
                    pA = (hc * 2 + jp) % 2
                    P.wait("tensor", "sSc", up_fr[pA])
                    P.wait("tensor", "sVe", dn_fr[pA])
                    for ec in range(4):
                        mm(PPX[pA][:, :],
                           wdn[we][:, ec * 1024 + hc * 128:ec * 1024 + (hc + 1) * 128],
                           act[:, ec * 1024 + jp * 512:ec * 1024 + (jp + 1) * 512],
                           ec == 0, ec == 3)
                    t_d = P.mark("tensor", "sPe")
                    P.wait("vector", "sPe", t_d)
                    if e == 0:
                        v_aa = vop(lambda E, pA=pA, hc=hc, jp=jp: E.tensor_copy(
                            acc[:, hc * 1024 + jp * 512:hc * 1024 + (jp + 1) * 512],
                            PPX[pA][:, :]), "sVe")
                    else:
                        v_aa = vop(lambda E, pA=pA, hc=hc, jp=jp: E.tensor_add(
                            acc[:, hc * 1024 + jp * 512:hc * 1024 + (jp + 1) * 512],
                            acc[:, hc * 1024 + jp * 512:hc * 1024 + (jp + 1) * 512],
                            PPX[pA][:, :]), "sVe")
                    dn_fr[pA] = v_aa
            wexp_free[we] = P.val("sPe")

        P.emit("vector", lambda E: E.drain())
        P.wait("vector", "sSt", out_st)
        for c in range(8):
            vop(lambda E, c=c, toff=toff: E.tensor_add(
                T2, acc[:, c * 1024:(c + 1) * 1024],
                x_own[:, c * 2048 + toff:c * 2048 + toff + 1024]))
            vdrain()
            v_o = P.mark("vector", "sVe")
            P.wait("sync", "sVe", v_o)
            out_st = dma("sync", out_d.ap()[c * 128:(c + 1) * 128, toff:toff + 1024],
                         T2, "sSt")
            P.wait("vector", "sSt", out_st)

    P.wait("sync", "sSt", P.val("sSt"))
    P.wait("sync", "sWr", P.val("sWr"))
    P.wait("gpsimd", "sRp", P.val("sRp"))
    P.wait("gpsimd", "sGa", P.val("sGa"))

    with nc.Block() as block:
        P.run(block)
    es.close()
    return nc


_CACHE = {}

def _prep_core_inputs(inputs, c):
    x = inputs["L0_attn_x"]
    b, qh = c // 2, c % 2
    key = "shared"
    if key not in _CACHE:
        Wqkv = np.asarray(inputs["L0_attn_Wqkv"]).reshape(16, 3, 64, 1024)
        WQ = Wqkv[:, 0].reshape(1024, 1024)
        WK = Wqkv[:, 1].reshape(1024, 1024)
        WV = Wqkv[:, 2].reshape(1024, 1024)
        bqkv = np.asarray(inputs["L0_attn_bqkv"]).reshape(16, 3, 64)
        bQ, bK, bV = [np.ascontiguousarray(bqkv[:, i].reshape(1024)) for i in range(3)]
        Wr = np.asarray(inputs["L0_attn_Wr"])
        br = np.asarray(inputs["L0_attn_br"])
        WrT = np.ascontiguousarray(Wr.T)
        wrpadT = np.concatenate([np.repeat(WrT[:, 0:1], 128, 1), WrT,
                                 np.repeat(WrT[:, 256:257], 127, 1)], axis=1)
        brpad = np.concatenate([np.repeat(br[0:1], 128), br,
                                np.repeat(br[256:257], 127)])[None, :]
        wredge4 = np.ascontiguousarray(WrT[:, [256, 0, 256, 0]])
        bredge = np.ascontiguousarray(br[[256, 0, 256, 0]][:, None])
        su = np.asarray(inputs["L0_ffn_shared_up"])
        sd = np.asarray(inputs["L0_ffn_shared_down"])
        ru = np.asarray(inputs["L0_ffn_routed_up"])
        rd = np.asarray(inputs["L0_ffn_routed_down"])
        wupT = np.ascontiguousarray(
            np.concatenate([su, ru], axis=0).transpose(0, 2, 1))
        wdnT = np.ascontiguousarray(
            np.concatenate([sd, rd], axis=0).transpose(0, 2, 1))
        wrtT = np.ascontiguousarray(np.asarray(inputs["L0_ffn_router_w"]).T)
        iden = np.eye(128, dtype=BF)
        iota = np.broadcast_to(np.arange(16, dtype=np.float32), (128, 16)).copy()
        _CACHE[key] = dict(
            wqT=np.ascontiguousarray(WQ.T), wkT=np.ascontiguousarray(WK.T),
            wvT=np.ascontiguousarray(WV.T),
            bq=np.ascontiguousarray(bQ.reshape(8, 128).T),
            bk=np.ascontiguousarray(bK.reshape(8, 128).T),
            bv_row=np.ascontiguousarray(bV[None, :]),
            wrpadT=np.ascontiguousarray(wrpadT), brpad=np.ascontiguousarray(brpad),
            wredge4=wredge4.astype(np.float32), bredge=bredge.astype(np.float32),
            wupT=wupT, wdnT=wdnT, wrtT=wrtT, iden=iden, iota=iota)
    shared = _CACHE[key]

    ovr = np.zeros((2, 128, 512), np.float32)
    if qh == 0:
        for di in range(128):
            ovr[0, di, 128 - di:256 - di] = 1.0
    else:
        for di in range(128):
            ovr[1, di, 384 - di:512] = 1.0
    ovr = ovr.astype(BF)
    Rsel = np.array([[1.0] * 128, [0.0] * 128], dtype=BF)
    Lsel = np.array([[0.0] * 128, [1.0] * 128], dtype=BF)
    Fsel = Rsel if qh == 0 else Lsel
    auxsel = np.concatenate([Rsel, Lsel, Fsel], axis=1)
    auxsel = np.concatenate([auxsel, auxsel], axis=0)

    xr = np.roll(np.asarray(x[b]), -2048 * qh, axis=0)
    xTc = np.ascontiguousarray(xr.T)

    d = dict(shared)
    d.update(xT=xTc, auxsel_in=auxsel, ovr=ovr)
    return d


_NC = None

def kernel(**inputs):
    global _NC
    in_maps = [_prep_core_inputs(inputs, c) for c in range(8)]
    if _NC is None:
        _NC = build_program()
    res = run_bass_kernel_spmd(_NC, in_maps, core_ids=list(range(8)))
    x = np.asarray(inputs["L0_attn_x"])
    out = np.empty_like(x)
    for c in range(8):
        b, qh = c // 2, c % 2
        out[b, 2048 * qh:2048 * (qh + 1), :] = res.results[c]["out"].T
    _CACHE.clear()
    return out


# version 09
# speedup vs baseline: 2.2406x; 1.0322x over previous
import contextlib
import numpy as np
import ml_dtypes
import concourse.bass as bass
import concourse.mybir as mybir
from concourse.bass_utils import run_bass_kernel_spmd

F32 = mybir.dt.float32
BF16 = mybir.dt.bfloat16
I32 = mybir.dt.int32
AF = mybir.ActivationFunctionType
ALU = mybir.AluOpType
AX = mybir.AxisListType
BF = ml_dtypes.bfloat16

H, HD, D, S, NQ = 16, 64, 1024, 4096, 2048
NKC, NQT, NE, ED = 32, 16, 18, 512
EPS = 1e-5
MSLACK = 24.0

def region_table(kc):
    if kc == 0:
        return [(0, 2048, 1)]
    if 1 <= kc <= 15:
        qa = 128 * (kc - 1)
        cut = min(qa + 384, 2048)
        regs = [(0, cut, 0)]
        if cut < 2048:
            regs.append((cut, 2048, 1))
        return regs
    return [(0, 2048, 2)]

def band_pieces_of_kc(kc):
    out = []
    for p in (0, 1, 2):
        qt = (kc if kc != 31 else -1) + 1 - p
        if 0 <= qt < NQT:
            out.append((qt, p, qt * 128))
    return out

ENGS = ("sync", "gpsimd", "tensor", "vector", "scalar")

DMA_ISSUER = {"sIn": "sync", "sWr": "sync", "sTp": "sync", "sSt": "sync",
              "sGa2": "sync", "sXl": "gpsimd",
              "sKV": "gpsimd", "sFw": "gpsimd", "sRp": "gpsimd", "sGa": "gpsimd"}
RELAY_SEM = {"sync": "sSy", "gpsimd": "sGy"}

class Plan:
    def __init__(self, nc):
        self.nc = nc
        self.stream = {e: [] for e in ENGS}
        self.sems = {}
        self.relay_cache = {}

    def sem(self, es, name):
        h = es.enter_context(self.nc.semaphore(name))
        self.sems[name] = [h, 0]

    def inc(self, name, k):
        s = self.sems[name]
        s[1] += k
        return s[1]

    def val(self, name):
        return self.sems[name][1]

    def emit(self, eng, fn):
        self.stream[eng].append(fn)

    def _rawwait(self, eng, name, v):
        h = self.sems[name][0]
        fn = lambda E, h=h, v=v: E.wait_ge(h, v)
        fn._is_wait = True
        self.emit(eng, fn)

    def wait(self, eng, name, v):
        if v <= 0:
            return
        assert v <= self.sems[name][1], f"wait on future {name}: {v} > {self.sems[name][1]}"
        if name in DMA_ISSUER:
            issuer = DMA_ISSUER[name]
            total = self.sems[name][1]
            if eng == issuer:
                self._rawwait(eng, name, total)
                return
            rs = RELAY_SEM[issuer]
            cached = self.relay_cache.get(name)
            if cached is None or cached[0] != total:
                self._rawwait(issuer, name, total)
                h = self.sems[rs][0]
                rv = self.inc(rs, 1)
                self.emit(issuer, lambda E, h=h: E.nop().then_inc(h, 1))
                self.relay_cache[name] = (total, rv)
            rv = self.relay_cache[name][1]
            self._rawwait(eng, rs, rv)
            return
        self._rawwait(eng, name, v)

    def mark(self, eng, sem):
        assert self.stream[eng], "mark on empty stream"
        last = self.stream[eng][-1]
        assert not getattr(last, "_has_upd", False), f"double mark on {eng}"
        assert not getattr(last, "_is_wait", False), f"mark on wait on {eng}"
        v = self.inc(sem, 1)
        h = self.sems[sem][0]
        self.stream[eng].pop()
        wrapped = lambda E, last=last, h=h: last(E).then_inc(h, 1)
        wrapped._has_upd = True
        self.stream[eng].append(wrapped)
        return v

    def op(self, eng, fn, sem=None, k=1):
        if sem is None:
            self.emit(eng, lambda E, fn=fn: fn(E))
            return None
        v = self.inc(sem, k)
        h = self.sems[sem][0]
        wrapped = lambda E, fn=fn, h=h, k=k: fn(E).then_inc(h, k)
        wrapped._has_upd = True
        self.emit(eng, wrapped)
        return v

    def run(self, block):
        for eng in ENGS:
            ops = self.stream[eng]
            getattr(block, eng)(lambda E, ops=ops: [f(E) for f in ops])


def build_program(n_pairs=8, dbg=False):
    nc = bass.Bass("TRN2", target_bir_lowering=False)
    es = contextlib.ExitStack()

    din = lambda n, s, dt=F32: nc.dram_tensor(n, s, dt, kind="ExternalInput")
    xT = din("xT", [D, S])
    wqT = din("wqT", [D, D]); wkT = din("wkT", [D, D]); wvT = din("wvT", [D, D])
    bq_i = din("bq", [128, 8]); bk_i = din("bk", [128, 8])
    bv_row = din("bv_row", [1, D])
    wrpadT = din("wrpadT", [64, 512]); brpad_i = din("brpad", [1, 512])
    wredge_i = din("wredge4", [64, 4]); bredge_i = din("bredge", [4, 1])
    wupT = din("wupT", [NE, D, ED]); wdnT = din("wdnT", [NE, ED, D])
    wrtT = din("wrtT", [D, 16])
    iden_i = din("iden", [128, 128], BF16)
    iota_i = din("iota", [128, 16])
    auxsel_i = din("auxsel_in", [4, 384], BF16)
    ovr_i = din("ovr", [2, 128, 512], BF16)

    out_d = nc.dram_tensor("out", [D, NQ], F32, kind="ExternalOutput")

    k_hbm = nc.dram_tensor("k_hbm", [8, 128, S], BF16)
    q_hbm = nc.dram_tensor("q_hbm", [8, 128, NQ], BF16)
    v_hbm = nc.dram_tensor("v_hbm", [NKC, 128, D], BF16)
    rp_hbm = nc.dram_tensor("rp_hbm", [2, 2, NQT, 128, 512], BF16)
    m_hbm = nc.dram_tensor("m_hbm", [2, 2, NQ], BF16)
    gate_hbm = nc.dram_tensor("gate_hbm", [16, 1024], BF16)

    sbt = lambda n, s, dt: es.enter_context(nc.sbuf_tensor(n, s, dt))
    x_own = sbt("x_own", [128, 16384], F32)
    big1 = sbt("big1", [128, 16384], BF16)
    big2 = sbt("big2", [128, 8192], BF16)
    big3 = sbt("big3", [128, 8192], BF16)
    f32s = sbt("f32s", [128, 8192], F32)
    f32c = sbt("f32c", [128, 4096], F32)
    bfc = sbt("bfc", [128, 4096], BF16)
    rw = sbt("rw", [128, 80], F32)
    ones_bf = sbt("ones_bf", [128, 128], BF16)
    iden_bf = sbt("iden_bf", [128, 128], BF16)
    iota_sb = sbt("iota_sb", [128, 16], F32)
    auxsel = sbt("auxsel", [128, 384], BF16)
    ovrb = sbt("ovrb", [128, 1024], BF16)
    bq_sb = sbt("bq_sb", [128, 8], F32)
    bk8_sb = sbt("bk8_sb", [128, 8], F32)
    bVbc = sbt("bVbc", [128, 1024], BF16)
    brbc = sbt("brbc", [128, 512], BF16)
    wrpad2 = sbt("wrpad2", [128, 512], BF16)
    wredge_sb = sbt("wredge_sb", [128, 36], BF16)
    bredge_sb = sbt("bredge_sb", [128, 1], F32)
    scal = sbt("scal", [128, 32], F32)
    scb = sbt("scb", [128, 32], BF16)
    nsq = sbt("nsq", [128, 32], F32)
    mwork = sbt("mwork", [128, 32], F32)
    mhbf = sbt("mhbf", [128, 32], BF16)
    mcolneg = sbt("mcolneg", [128, 32], F32)
    c1bc = sbt("c1bc", [128, 2], F32)
    c2bc = sbt("c2bc", [128, 1], F32)

    T1 = f32c[:, 0:1024]
    T2 = f32c[:, 1024:2048]
    BC1 = f32c[:, 2048:3072]
    BC2 = f32c[:, 3072:4096]

    xslab = [f32s[:, 0:4096], f32s[:, 4096:8192]]
    xlnsl = [big3[:, 0:4096], big3[:, 4096:8192]]
    kstage = [bfc[:, 0:512], bfc[:, 512:1024]]
    vstage = bfc[:, 1536:2560]
    sqP1 = bfc[:, 3072:3584]
    QP = bass.AP(f32s, 1024, [[8192, 128], [1, 2048]]).bitcast(BF16)
    AXT = bass.AP(f32s, 3072, [[8192, 128], [1, 2048]]).bitcast(BF16)
    M2V = bass.AP(f32s, 5120, [[8192, 128], [1, 1024]]).bitcast(BF16)
    SQB = bass.AP(f32s, 6144, [[8192, 128], [1, 1024]]).bitcast(BF16)
    rpstg2 = [f32s[:, 0:512], f32s[:, 512:1024]]
    qpair = [QP[:, 0:2048], QP[:, 2048:4096]]
    auxt = [AXT[:, 0:2048], AXT[:, 2048:4096]]
    kpair = [big2[:, 0:4096], big2[:, 4096:8192]]
    vpair = [big3[:, 0:4096], big3[:, 4096:8192]]
    bandT = big1[:, 0:12288]
    bandP = [[big1[:, 12288 + (2 * pS + hh) * 384:12288 + (2 * pS + hh + 1) * 384]
              for hh in range(2)] for pS in range(2)]
    msb = big1[:, 13824:13952]
    rpbf = [big1[:, 13952 + hh * 512:13952 + (hh + 1) * 512] for hh in range(2)]
    rcbf = big1[:, 14976:15488]
    rclo = big1[:, 15488:16000]
    possb = [[bfc[:, (2 * pS + hh) * 384:(2 * pS + hh + 1) * 384] for hh in range(2)]
             for pS in range(2)]
    RCV = bass.AP(f32s, 7168, [[8192, 128], [1, 1024]]).bitcast(BF16)
    rcbf = RCV[:, 0:512]
    rclo = RCV[:, 512:1024]
    pchunk = [bfc[:, 1536 + i * 1024:1536 + (i + 1) * 1024] for i in range(2)] + \
             [big1[:, 14976:16000]]
    xln2 = big1[:, 0:8192]
    act = big1[:, 8192:12288]
    wrt_sb = big1[:, 12288:12416]
    wup = [big2[:, 0:4096], big2[:, 4096:8192]]
    wdn = [big3[:, 0:4096], big3[:, 4096:8192]]
    acc = f32s
    grow = bfc[0:1, 0:1024]
    gateT = bfc[0:16, 1024:2048]
    sqP3 = bfc[:, 2048:2560]

    PA = es.enter_context(nc.psum_tensor("PA", [128, 1024], F32))
    PB = es.enter_context(nc.psum_tensor("PB", [128, 1024], F32))
    PV1 = es.enter_context(nc.psum_tensor("PV1", [128, 512], F32))
    PD = es.enter_context(nc.psum_tensor("PD", [128, 512], F32))
    PP = [es.enter_context(nc.psum_tensor("PP0", [128, 512], F32)),
          es.enter_context(nc.psum_tensor("PP1", [128, 512], F32))]

    P = Plan(nc)
    for s in ("sIn", "sWr", "sTp", "sSt", "sGa2", "sXl", "sKV", "sFw", "sRp", "sGa",
              "sPe", "sVe", "sSc", "sSy", "sGy"):
        P.sem(es, s)

    def dma(eng, dst, src, sem="sIn", slow=False):
        return P.op(eng, lambda E, d=dst, s=src, sl=slow:
                    E.dma_start(out=d, in_=s, allow_slow_non_contiguous=sl), sem, 16)

    def mm(dst, lhsT, rhs, start, stop, sem=None, tp=None):
        return P.op("tensor", lambda E, d=dst, l=lhsT, r=rhs, st=start, sp=stop, tp=tp:
                    E.matmul(d, l, r, start=st, stop=sp, tile_position=tp,
                             skip_group_check=True), sem, 1)

    def vop(fn, sem=None):
        return P.op("vector", fn, sem, 1)

    def sop(fn, sem=None):
        return P.op("scalar", fn, sem, 1)

    def vdrain():
        P.emit("vector", lambda E: E.drain())

    def sdrain():
        P.emit("scalar", lambda E: E.drain())

    def recip_nr(dst, src, ws):
        vop(lambda E: E.reciprocal(dst, src)); vdrain()

    dma("sync", iden_bf[:, :], iden_i.ap())
    dma("sync", iota_sb[:, :], iota_i.ap())
    dma("sync", auxsel[0:2, :], auxsel_i.ap()[0:2, :])
    dma("sync", auxsel[32:34, :], auxsel_i.ap()[2:4, :])
    dma("sync", ovrb[:, 0:512], ovr_i.ap()[0])
    dma("sync", ovrb[:, 512:1024], ovr_i.ap()[1])
    dma("sync", bq_sb[:, :], bq_i.ap())
    dma("sync", bk8_sb[:, :], bk_i.ap())
    dma("sync", f32c[0:1, 2048:3072], bv_row.ap())
    dma("sync", bredge_sb[0:2, 0:1], bredge_i.ap()[0:2])
    dma("sync", bredge_sb[32:34, 0:1], bredge_i.ap()[2:4])
    dma("sync", f32c[0:1, 3072:3584], brpad_i.ap())
    dma("gpsimd", wredge_sb[0:64, 0:2], wredge_i.ap()[:, 0:2], "sFw")
    dma("gpsimd", wredge_sb[64:128, 32:34], wredge_i.ap()[:, 2:4], "sFw")
    for r0 in (0, 64):
        dma("gpsimd", wrpad2[r0:r0 + 64, :], wrpadT.ap(), "sFw")
    for c in range(8):
        dma("gpsimd", big1[:, c * 1024:(c + 1) * 1024], wqT.ap()[c * 128:(c + 1) * 128, :],
            "sFw")
        dma("gpsimd", big1[:, 8192 + c * 1024:8192 + (c + 1) * 1024],
            wkT.ap()[c * 128:(c + 1) * 128, :], "sFw")
        dma("gpsimd", big2[:, c * 1024:(c + 1) * 1024], wvT.ap()[c * 128:(c + 1) * 128, :],
            "sFw")
    v_in = P.val("sIn")
    v_gw = P.val("sFw")

    P.wait("vector", "sIn", v_in)
    vop(lambda E: E.memset(ones_bf[:, :], 1.0))
    vop(lambda E: E.memset(wredge_sb[64:128, 0:32], 0.0))
    vop(lambda E: E.tensor_scalar(bk8_sb[:, :], bk8_sb[:, :], 0.125, None, ALU.mult))
    vop(lambda E: E.tensor_copy(bfc[0:1, 3072:4096], f32c[0:1, 2048:3072]))
    vop(lambda E: E.tensor_copy(bfc[0:1, 1024:1536], f32c[0:1, 3072:3584]))
    vdrain()
    v_rows0 = P.mark("vector", "sVe")
    P.wait("tensor", "sVe", v_rows0)
    for jp in range(2):
        mm(PA[:, jp * 512:(jp + 1) * 512], ones_bf[0:1, :],
           bfc[0:1, 3072 + jp * 512:3072 + (jp + 1) * 512], True, True)
    mm(PP[0][:, :], ones_bf[0:1, :], bfc[0:1, 1024:1536], True, True)
    t_bc0 = P.mark("tensor", "sPe")
    P.wait("vector", "sPe", t_bc0)
    vop(lambda E: E.tensor_copy(bVbc[:, :], PA[:, 0:1024]))
    vop(lambda E: E.tensor_copy(brbc[:, :], PP[0][:, :]))
    vdrain()
    v_bc0 = P.mark("vector", "sVe")

    P.wait("scalar", "sIn", v_in)
    P.wait("scalar", "sFw", v_gw)
    P.wait("scalar", "sPe", t_bc0)
    s_wr2 = sop(lambda E: E.square(sqP1[0:64, :], wrpad2[0:64, :]), "sSc")
    P.wait("tensor", "sSc", s_wr2)
    P.wait("tensor", "sVe", v_rows0)
    mm(PP[1][0:1, :], ones_bf[0:64, 0:1], sqP1[0:64, :], True, True)
    t_wr = P.mark("tensor", "sPe")
    P.wait("vector", "sPe", t_wr)
    vop(lambda E: E.reduce_max(scal[0:1, 0:1], PP[1][0:1, 128:385], axis=AX.X)); vdrain()
    v_mx = P.mark("vector", "sVe")
    P.wait("scalar", "sVe", v_mx)
    s_sq = sop(lambda E: E.sqrt(scal[0:1, 2:3], scal[0:1, 0:1]), "sSc")
    P.wait("vector", "sSc", s_sq)
    vop(lambda E: E.tensor_scalar(f32c[0:1, 1024:1536], f32c[0:1, 3072:3584],
                                  -1.0, None, ALU.mult)); vdrain()
    vop(lambda E: E.tensor_max(f32c[0:1, 1024:1536], f32c[0:1, 1024:1536],
                               f32c[0:1, 3072:3584])); vdrain()
    vop(lambda E: E.reduce_max(scal[0:1, 3:4], f32c[0:1, 1152:1409],
                               axis=AX.X)); vdrain()
    vop(lambda E: E.tensor_scalar(scal[0:1, 5:6], scal[0:1, 3:4], 1.0, -MSLACK,
                                  ALU.mult, ALU.add)); vdrain()
    vop(lambda E: E.tensor_copy(scb[0:1, 0:1], scal[0:1, 5:6])); vdrain()
    v_c2 = P.mark("vector", "sVe")
    P.wait("tensor", "sVe", v_c2)
    mm(PP[1][:, 0:1], ones_bf[0:1, :], scb[0:1, 0:1], True, True)
    t_c2 = P.mark("tensor", "sPe")
    P.wait("vector", "sPe", t_c2)
    vop(lambda E: E.tensor_copy(c2bc[:, :], PP[1][:, 0:1]))
    vdrain()
    v_c2b = P.mark("vector", "sVe")

    P.wait("tensor", "sFw", v_gw)

    slab_free_v = [0, 0]
    slab_xln_free = [0, 0]
    psP_fr = [0, 0]
    psV_fr = v_bc0
    kst_fr = [0, 0]
    vst_fr = 0
    x_own_ld = 0
    prev_bcc = [v_c2b]

    for s_i in range(8):
        b = s_i % 2
        P.wait("gpsimd", "sVe", slab_free_v[b])
        for c in range(8):
            dma("gpsimd", xslab[b][:, c * 512:(c + 1) * 512],
                xT.ap()[c * 128:(c + 1) * 128, s_i * 512:(s_i + 1) * 512], "sXl")
        if s_i < 4:
            for c in range(8):
                dma("gpsimd", x_own[:, c * 2048 + s_i * 512:c * 2048 + (s_i + 1) * 512],
                    xT.ap()[c * 128:(c + 1) * 128, s_i * 512:(s_i + 1) * 512], "sXl")
            x_own_ld = P.val("sXl")
        ld_done = P.val("sXl")

        P.wait("scalar", "sXl", ld_done)
        P.wait("scalar", "sPe", slab_xln_free[b])
        for c in range(8):
            sop(lambda E, b=b, c=c: E.copy(
                xlnsl[b][:, c * 512:(c + 1) * 512], xslab[b][:, c * 512:(c + 1) * 512]))
        s_cast = P.mark("scalar", "sSc")
        sdrain()
        P.wait("tensor", "sSc", s_cast)
        P.wait("tensor", "sVe", prev_bcc[0])
        for c in range(8):
            mm(PV1[0:1, :], ones_bf[:, 0:1], xlnsl[b][:, c * 512:(c + 1) * 512],
               c == 0, c == 7)
        for c in range(8):
            s_sq = sop(lambda E, b=b, c=c: E.square(
                sqP1[:, :], xlnsl[b][:, c * 512:(c + 1) * 512]), "sSc")
            P.wait("tensor", "sSc", s_sq)
            mm(PD[0:1, :], ones_bf[:, 0:1], sqP1[:, :], c == 0, c == 7)
            t_sqc = P.mark("tensor", "sPe")
            P.wait("scalar", "sPe", t_sqc)
        t_st = P.val("sPe")
        P.wait("vector", "sPe", t_st)
        MUr = f32c[0:1, 2048:2560]
        VARr = f32c[0:1, 2560:3072]
        MSQ = f32c[0:1, 3072:3584]
        INVr = f32c[0:1, 3584:4096]
        vop(lambda E: E.tensor_scalar(MUr, PV1[0:1, :], 1.0 / 1024, None, ALU.mult))
        vop(lambda E: E.tensor_scalar(VARr, PD[0:1, :], 1.0 / 1024, None, ALU.mult))
        vdrain()
        v_rowrd = P.mark("vector", "sVe")
        vop(lambda E: E.tensor_mul(MSQ, MUr, MUr)); vdrain()
        vop(lambda E: E.tensor_sub(VARr, VARr, MSQ)); vdrain()
        vop(lambda E: E.tensor_scalar(VARr, VARr, 1.0, EPS, ALU.mult, ALU.add)); vdrain()
        recip_nr(f32c[0:1, 3072:3584], VARr, f32c[0:1, 1024:1536])
        v_var = P.mark("vector", "sVe")
        P.wait("scalar", "sVe", v_var)
        s_rs = sop(lambda E: E.sqrt(INVr, f32c[0:1, 3072:3584]), "sSc")
        P.wait("vector", "sSc", s_rs)
        vop(lambda E: E.tensor_copy(bfc[0:1, 3072:3584], MUr))
        vop(lambda E: E.tensor_copy(bfc[0:1, 1024:1536], INVr))
        vdrain()
        v_r = P.mark("vector", "sVe")
        P.wait("tensor", "sVe", v_r)
        mm(PV1[:, :], ones_bf[0:1, :], bfc[0:1, 3072:3584], True, True)
        mm(PD[:, :], ones_bf[0:1, :], bfc[0:1, 1024:1536], True, True)
        t_bc = P.mark("tensor", "sPe")
        P.wait("vector", "sPe", t_bc)
        vop(lambda E: E.tensor_copy(BC1[:, 0:512], PV1[:, :]))
        vop(lambda E: E.tensor_copy(BC2[:, 0:512], PD[:, :]))
        vdrain()
        v_bcc = P.mark("vector", "sVe")
        prev_bcc[0] = v_bcc
        for c in range(8):
            vop(lambda E, b=b, c=c: E.tensor_sub(
                T1[:, 0:512], xslab[b][:, c * 512:(c + 1) * 512], f32c[:, 2048:2560]))
            vdrain()
            vop(lambda E, b=b, c=c: E.tensor_mul(
                xlnsl[b][:, c * 512:(c + 1) * 512], T1[:, 0:512], f32c[:, 3072:3584]))
            vdrain()
        v_xln = P.mark("vector", "sVe")
        slab_free_v[b] = v_xln

        P.wait("tensor", "sVe", v_xln)
        kinds = ("k", "q") if s_i < 4 else ("k",)
        for kind in kinds:
            woff = 8192 if kind == "k" else 0
            bias = bk8_sb if kind == "k" else bq_sb
            scale = 0.125 if kind == "k" else 1.0
            hbm = k_hbm if kind == "k" else q_hbm
            for fc in range(8):
                pA = fc % 2
                P.wait("tensor", "sSc", psP_fr[pA])
                for dc in range(8):
                    mm(PP[pA][:, :],
                       big1[:, woff + dc * 1024 + fc * 128:woff + dc * 1024 + (fc + 1) * 128],
                       xlnsl[b][:, dc * 512:(dc + 1) * 512], dc == 0, dc == 7)
                t_p = P.mark("tensor", "sPe")
                P.wait("scalar", "sPe", t_p)
                P.wait("scalar", "sWr", kst_fr[pA])
                sv = sop(lambda E, pA=pA, bias=bias, fc=fc, scale=scale: E.activation(
                    kstage[pA][:, :], PP[pA][:, :], AF.Identity,
                    bias=bias[:, fc:fc + 1], scale=scale), "sSc")
                psP_fr[pA] = sv
                P.wait("sync", "sSc", sv)
                kst_fr[pA] = dma("sync", hbm.ap()[fc, :, s_i * 512:(s_i + 1) * 512],
                                 kstage[pA][:, :], "sWr")
        for tc in range(4):
            kc_g = s_i * 4 + tc
            P.wait("tensor", "sVe", psV_fr)
            for jp in range(2):
                for dc in range(8):
                    mm(PA[:, jp * 512:(jp + 1) * 512],
                       xlnsl[b][:, dc * 512 + tc * 128:dc * 512 + (tc + 1) * 128],
                       big2[:, dc * 1024 + jp * 512:dc * 1024 + (jp + 1) * 512],
                       dc == 0, dc == 7)
            t_v = P.mark("tensor", "sPe")
            P.wait("vector", "sPe", t_v)
            P.wait("vector", "sWr", vst_fr)
            vop(lambda E: E.tensor_add(vstage[:, :], PA[:, 0:1024], bVbc[:, :]))
            vdrain()
            psV_fr = P.mark("vector", "sVe")
            P.wait("sync", "sVe", psV_fr)
            vst_fr = dma("sync", v_hbm.ap()[kc_g], vstage[:, :], "sWr")
        slab_xln_free[b] = P.val("sPe")

    p1_pe = P.val("sPe")
    p1_ve = P.val("sVe")
    p1_sc = P.val("sSc")
    p1_wr = P.val("sWr")

    ppfree = [("sSc", p1_sc), ("sSc", p1_sc)]
    pp_i = [0]

    def pp_next():
        b = pp_i[0] % 2
        pp_i[0] += 1
        return b

    kv_rdy = [0] * 8
    pair_free = [p1_pe, p1_pe]
    epil_sems = {}
    band_region_rdy = {}
    G = {"sqb_rd": 0}

    def emit_pair_loads(ch):
        pr = ch % 2
        P.wait("gpsimd", "sPe", pair_free[pr])
        if ch < 2:
            P.wait("gpsimd", "sWr", p1_wr)
        dma("gpsimd", kpair[pr][:, :], k_hbm.ap()[ch], "sKV")
        dma("gpsimd", qpair[pr][:, :], q_hbm.ap()[ch], "sKV")
        vsrc = bass.AP(v_hbm, ch * 128, [[D, 128], [128 * D, NKC], [1, 128]])
        vdst = bass.AP(big3, pr * 4096, [[8192, 128], [128, NKC], [1, 128]])
        dma("gpsimd", vdst, vsrc, "sKV")
        kv_rdy[ch] = P.val("sKV")

    def prep_steps(ch):
        pr = ch % 2
        st = {"aux_rdy": 0, "mcol_rdy": 0}
        loc = {"rp_wr": [0] * NQT, "rp_st_fr": [0, 0], "m_wr": 0, "m_ga": 0,
               "pS_fr": [("sPe", 0), ("sPe", 0)], "band_exp": [0] * NQT}

        def sqq_step():
            P.wait("vector", "sKV", kv_rdy[ch])
            P.wait("vector", "sPe", G["sqb_rd"])
            vop(lambda E: E.tensor_mul(SQB[:, 0:2048], qpair[pr][:, :], qpair[pr][:, :]))
            vdrain()
            loc["s_sqq"] = P.mark("vector", "sVe")

        def normq_step(g):
            P.wait("tensor", "sVe", loc["s_sqq"])
            if g == 0:
                b = pp_next()
                P.wait("tensor", *ppfree[b])
                b2 = pp_next()
                P.wait("tensor", *ppfree[b2])
                loc["nq_banks"] = (b, b2)
            b, b2 = loc["nq_banks"]
            for qt in range(g * 4, g * 4 + 4):
                mm(PP[b][:, qt:qt + 1], SQB[0:64, qt * 128:(qt + 1) * 128],
                   ones_bf[0:64, 0:1], True, True)
                mm(PP[b2][:, qt:qt + 1], SQB[64:128, qt * 128:(qt + 1) * 128],
                   ones_bf[64:128, 0:1], True, True)
            t_nq = P.mark("tensor", "sPe")
            G["sqb_rd"] = t_nq
            if g == 3:
                P.wait("vector", "sPe", t_nq)
                vop(lambda E, b=b: E.tensor_copy(nsq[:, 0:16], PP[b][:, 0:16]))
                vop(lambda E, b2=b2: E.tensor_copy(nsq[:, 16:32], PP[b2][:, 0:16]))
                vdrain()
                v_nq = P.mark("vector", "sVe")
                ppfree[b] = ("sVe", v_nq)
                ppfree[b2] = ("sVe", v_nq)
                loc["v_nsq"] = v_nq

        def knorm_step(g):
            cb, sub = g // 2, g % 2
            if sub == 0:
                P.wait("vector", "sPe", G["sqb_rd"])
                vop(lambda E, cb=cb: E.tensor_mul(
                    SQB[:, 0:2048], kpair[pr][:, cb * 2048:(cb + 1) * 2048],
                    kpair[pr][:, cb * 2048:(cb + 1) * 2048]))
                vdrain()
                loc["s_sqk"] = P.mark("vector", "sVe")
            P.wait("tensor", "sVe", loc["s_sqk"])
            for ss in range(2):
                b = pp_next()
                P.wait("tensor", *ppfree[b])
                b2 = pp_next()
                P.wait("tensor", *ppfree[b2])
                off = (sub * 2 + ss) * 512
                mm(PP[b][0:1, :], ones_bf[0:64, 0:1], SQB[0:64, off:off + 512],
                   True, True)
                mm(PP[b2][0:1, :], ones_bf[64:128, 0:1], SQB[64:128, off:off + 512],
                   True, True)
                t_kn = P.mark("tensor", "sPe")
                G["sqb_rd"] = t_kn
                P.wait("vector", "sPe", t_kn)
                first = (g == 0 and ss == 0)
                vop(lambda E, b=b: E.reduce_max(scal[0:1, 16:17], PP[b][0:1, :],
                                                axis=AX.X))
                vop(lambda E, b2=b2: E.reduce_max(scal[0:1, 18:19], PP[b2][0:1, :],
                                                  axis=AX.X))
                vdrain()
                if first:
                    vop(lambda E: E.tensor_copy(scal[0:1, 17:18], scal[0:1, 16:17]))
                    vop(lambda E: E.tensor_copy(scal[0:1, 19:20], scal[0:1, 18:19]))
                else:
                    vop(lambda E: E.tensor_max(scal[0:1, 17:18], scal[0:1, 17:18],
                                               scal[0:1, 16:17]))
                    vop(lambda E: E.tensor_max(scal[0:1, 19:20], scal[0:1, 19:20],
                                               scal[0:1, 18:19]))
                vdrain()
                v_kn = P.mark("vector", "sVe")
                ppfree[b] = ("sVe", v_kn)
                ppfree[b2] = ("sVe", v_kn)
            loc["v_knorm"] = P.val("sVe")

        def c1_step():
            P.wait("scalar", "sVe", loc["v_knorm"])
            sop(lambda E: E.sqrt(scal[0:1, 20:21], scal[0:1, 17:18]))
            sop(lambda E: E.sqrt(scal[0:1, 21:22], scal[0:1, 19:20]))
            sdrain()
            s_c1 = P.mark("scalar", "sSc")
            P.wait("vector", "sSc", s_c1)
            vop(lambda E: E.tensor_add(scal[0:1, 22:23], scal[0:1, 20:21],
                                       scal[0:1, 2:3]))
            vop(lambda E: E.tensor_add(scal[0:1, 23:24], scal[0:1, 21:22],
                                       scal[0:1, 2:3]))
            vdrain()
            vop(lambda E: E.tensor_copy(scb[0:1, 2:4], scal[0:1, 22:24])); vdrain()
            v_c1 = P.mark("vector", "sVe")
            P.wait("tensor", "sVe", v_c1)
            b = pp_next()
            P.wait("tensor", *ppfree[b])
            mm(PP[b][:, 0:2], ones_bf[0:1, :], scb[0:1, 2:4], True, True)
            t_c1 = P.mark("tensor", "sPe")
            P.wait("vector", "sPe", t_c1)
            vop(lambda E, b=b: E.tensor_copy(c1bc[:, :], PP[b][:, 0:2]))
            vdrain()
            v_c1b = P.mark("vector", "sVe")
            ppfree[b] = ("sVe", v_c1b)
            loc["v_c1b"] = v_c1b

        def mchain_step():
            P.wait("scalar", "sVe", loc["v_c1b"])
            sop(lambda E: E.activation(mwork[:, 0:32], nsq[:, 0:32], AF.Sqrt))
            sdrain()
            s_m = P.mark("scalar", "sSc")
            P.wait("vector", "sSc", s_m)
            P.wait("vector", "sVe", v_c2b)
            vop(lambda E: E.tensor_scalar(mwork[:, 0:16], mwork[:, 0:16],
                                          c1bc[:, 0:1], c2bc[:, 0:1],
                                          ALU.mult, ALU.add))
            vop(lambda E: E.tensor_scalar(mwork[:, 16:32], mwork[:, 16:32],
                                          c1bc[:, 1:2], c2bc[:, 0:1],
                                          ALU.mult, ALU.add))
            vdrain()
            vop(lambda E: E.tensor_copy(mhbf[:, :], mwork[:, :])); vdrain()
            vop(lambda E: E.tensor_scalar(mcolneg[:, :], mhbf[:, :], -1.0, None,
                                          ALU.mult))
            vdrain()
            st["mcol_rdy"] = P.mark("vector", "sVe")

        def mio_step():
            P.wait("tensor", "sVe", st["mcol_rdy"])
            b = pp_next()
            P.wait("tensor", *ppfree[b])
            P.op("tensor", lambda E, b=b: E.transpose(
                PP[b][0:32, 0:64].bitcast(BF16), mhbf[:, :], iden_bf[:, :]))
            t_tr = P.mark("tensor", "sPe")
            P.wait("vector", "sPe", t_tr)
            vop(lambda E, b=b: E.tensor_copy(msb[0:32, :],
                                             PP[b][0:32, 0:64].bitcast(BF16)))
            vdrain()
            v_mt = P.mark("vector", "sVe")
            ppfree[b] = ("sVe", v_mt)
            P.wait("gpsimd", "sVe", v_mt)
            for hh in range(2):
                dma("gpsimd", bass.AP(m_hbm, (pr * 2 + hh) * NQ, [[128, 16], [1, 128]]),
                    msb[hh * 16:(hh + 1) * 16, :], "sRp")
            loc["m_wr"] = P.val("sRp")
            P.wait("gpsimd", "sRp", loc["m_wr"])
            for hh in range(2):
                for r in range(2):
                    dma("gpsimd", M2V[hh * 32 + r:hh * 32 + r + 1, 0:2048],
                        bass.AP(m_hbm, (pr * 2 + hh) * NQ, [[NQ, 1], [1, NQ]]), "sGa")
            loc["m_ga"] = P.val("sGa")

        def edge_step(jp):
            P.wait("tensor", "sKV", kv_rdy[ch])
            b = pp_next()
            P.wait("tensor", *ppfree[b])
            b2 = pp_next()
            P.wait("tensor", *ppfree[b2])
            mm(PP[b][0:2, :], wredge_sb[0:64, 0:2],
               qpair[pr][0:64, jp * 512:(jp + 1) * 512], True, True)
            mm(PP[b2][0:34, :], wredge_sb[64:128, 0:34],
               qpair[pr][64:128, jp * 512:(jp + 1) * 512], True, True)
            t_e = P.mark("tensor", "sPe")
            P.wait("vector", "sPe", t_e)
            P.wait("vector", "sGa", loc["m_ga"])
            if jp == 0:
                P.wait("vector", "sPe", pair_free[pr])
            vop(lambda E, b=b, jp=jp: E.scalar_tensor_tensor(
                auxt[pr][0:2, jp * 512:(jp + 1) * 512], PP[b][0:2, :],
                bredge_sb[0:2, 0:1], M2V[0:2, jp * 512:(jp + 1) * 512],
                ALU.add, ALU.subtract))
            vop(lambda E, b2=b2, jp=jp: E.scalar_tensor_tensor(
                auxt[pr][32:34, jp * 512:(jp + 1) * 512], PP[b2][32:34, :],
                bredge_sb[32:34, 0:1], M2V[32:34, jp * 512:(jp + 1) * 512],
                ALU.add, ALU.subtract))
            vdrain()
            v_ed = P.mark("vector", "sVe")
            ppfree[b] = ("sVe", v_ed)
            ppfree[b2] = ("sVe", v_ed)
            if jp == 3:
                st["aux_rdy"] = v_ed

        def rp_step(qt):
            P.wait("tensor", "sKV", kv_rdy[ch])
            b = pp_next()
            P.wait("tensor", *ppfree[b])
            b2 = pp_next()
            P.wait("tensor", *ppfree[b2])
            mm(PP[b][:, :], qpair[pr][0:64, qt * 128:(qt + 1) * 128],
               wrpad2[0:64, :], True, True)
            mm(PP[b2][:, :], qpair[pr][64:128, qt * 128:(qt + 1) * 128],
               wrpad2[64:128, :], True, True)
            t_rp = P.mark("tensor", "sPe")
            P.wait("vector", "sPe", t_rp)
            P.wait("vector", "sRp", loc["rp_st_fr"][0])
            bb = (b, b2)
            if qt in (0, 15):
                mi = 0 if qt == 0 else 1
                ecol = 384 if qt == 0 else 128
                for hh in range(2):
                    vop(lambda E, hh=hh, bb=bb: E.tensor_add(
                        rpstg2[hh][:, :], PP[bb[hh]][:, :], brbc[:, :]))
                vdrain()
                for hh in range(2):
                    vop(lambda E, hh=hh, ecol=ecol: E.tensor_copy(
                        scal[:, 24 + hh:25 + hh], rpstg2[hh][:, ecol:ecol + 1]))
                vdrain()
                for hh in range(2):
                    vop(lambda E, hh=hh: E.tensor_scalar(
                        T1[:, 0:512], rpstg2[hh][:, :], scal[:, 24 + hh:25 + hh],
                        None, ALU.subtract))
                    vdrain()
                    vop(lambda E, mi=mi: E.tensor_mul(
                        T1[:, 0:512], T1[:, 0:512], ovrb[:, mi * 512:(mi + 1) * 512]))
                    vdrain()
                    vop(lambda E, hh=hh: E.tensor_sub(rpstg2[hh][:, :],
                                                      rpstg2[hh][:, :], T1[:, 0:512]))
                    vdrain()
                for hh in range(2):
                    vop(lambda E, hh=hh: E.tensor_copy(rpbf[hh][:, :], rpstg2[hh][:, :]))
                vdrain()
            else:
                for hh in range(2):
                    vop(lambda E, hh=hh, bb=bb: E.tensor_add(
                        rpbf[hh][:, :], PP[bb[hh]][:, :], brbc[:, :]))
                vdrain()
            v_rp = P.mark("vector", "sVe")
            ppfree[b] = ("sVe", v_rp)
            ppfree[b2] = ("sVe", v_rp)
            P.wait("gpsimd", "sVe", v_rp)
            for hh in range(2):
                dma("gpsimd",
                    bass.AP(rp_hbm, ((pr * 2 + hh) * NQT + qt) * 128 * 512,
                            [[512, 128], [1, 512]]),
                    rpbf[hh][:, :], "sRp")
            loc["rp_st_fr"][0] = P.val("sRp")
            loc["rp_wr"][qt] = P.val("sRp")

        def band_step(qt):
            pS = qt % 2
            qb = qt // 4
            P.wait("gpsimd", "sRp", loc["rp_wr"][qt])
            P.wait("gpsimd", "sVe", loc.get(("pos_fr", pS), 0))
            for hh in range(2):
                dma("gpsimd", possb[pS][hh][:, :],
                    bass.AP(rp_hbm, ((pr * 2 + hh) * NQT + qt) * 128 * 512 + 128,
                            [[511, 128], [1, 384]]), "sGa")
            g_pos = P.val("sGa")
            b = pp_next()
            P.wait("tensor", *ppfree[b])
            b2 = pp_next()
            P.wait("tensor", *ppfree[b2])
            if qt == 0:
                mm(PP[b][:, 0:128], qpair[pr][0:64, 0:128],
                   kpair[pr][0:64, 3968:4096], True, True)
                mm(PP[b2][:, 0:128], qpair[pr][64:128, 0:128],
                   kpair[pr][64:128, 3968:4096], True, True)
                mm(PP[b][:, 128:384], qpair[pr][0:64, 0:128],
                   kpair[pr][0:64, 0:256], True, True)
                mm(PP[b2][:, 128:384], qpair[pr][64:128, 0:128],
                   kpair[pr][64:128, 0:256], True, True)
            else:
                j0 = qt * 128 - 128
                mm(PP[b][:, 0:384], qpair[pr][0:64, qt * 128:(qt + 1) * 128],
                   kpair[pr][0:64, j0:j0 + 384], True, True)
                mm(PP[b2][:, 0:384], qpair[pr][64:128, qt * 128:(qt + 1) * 128],
                   kpair[pr][64:128, j0:j0 + 384], True, True)
            t_b = P.mark("tensor", "sPe")
            P.wait("vector", "sPe", t_b)
            P.wait("vector", "sGa", g_pos)
            bb = (b, b2)
            for hh in range(2):
                vop(lambda E, hh=hh, bb=bb, pS=pS: E.tensor_add(
                    PP[bb[hh]][:, 0:384], PP[bb[hh]][:, 0:384], possb[pS][hh][:, :]))
            vdrain()
            v_b = P.mark("vector", "sVe")
            loc[("pos_fr", pS)] = v_b
            P.wait("scalar", "sVe", v_b)
            P.wait("scalar", "sVe", st["mcol_rdy"])
            P.wait("scalar", *loc["pS_fr"][pS])
            for hh in range(2):
                sop(lambda E, hh=hh, bb=bb, pS=pS, qt=qt: E.activation(
                    bandP[pS][hh][:, :], PP[bb[hh]][:, 0:384], AF.Exp,
                    bias=mcolneg[:, hh * 16 + qt:hh * 16 + qt + 1]))
            sdrain()
            s_b = P.mark("scalar", "sSc")
            ppfree[b] = ("sSc", s_b)
            ppfree[b2] = ("sSc", s_b)
            loc["band_exp"][qt] = s_b

        def band_tp_step(qt):
            pS = qt % 2
            qb = qt // 4
            bt = pp_next()
            P.wait("tensor", *ppfree[bt])
            P.wait("tensor", "sSc", loc["band_exp"][qt])
            view = PP[bt][:, 0:384].bitcast(BF16)
            for p in range(3):
                for hh in range(2):
                    t = p * 2 + hh
                    P.op("tensor", lambda E, pS=pS, hh=hh, p=p, t=t:
                         E.transpose(view[:, t * 128:(t + 1) * 128],
                                     bandP[pS][hh][:, p * 128:(p + 1) * 128],
                                     iden_bf[:, :]))
            t_tp = P.mark("tensor", "sPe")
            loc["pS_fr"][pS] = ("sPe", t_tp)
            P.wait("vector", "sPe", t_tp)
            col = qb * 3072 + (qt % 4) * 768
            vop(lambda E, col=col, view=view: E.tensor_copy(
                bandT[:, col:col + 768], view[:, 0:768]))
            vdrain()
            v_bt = P.mark("vector", "sVe")
            ppfree[bt] = ("sVe", v_bt)
            if qt % 4 == 3:
                band_region_rdy[(ch, qb)] = v_bt

        steps = [(0, sqq_step)]
        steps += [(0, lambda g=g: normq_step(g)) for g in range(4)]
        steps += [(0, lambda g=g: knorm_step(g)) for g in range(4)]
        steps += [(0, c1_step), (0, mchain_step), (0, mio_step)]
        steps += [(0, lambda jp=jp: edge_step(jp)) for jp in range(4)]
        steps += [(0, lambda qt=qt: rp_step(qt)) for qt in range(NQT)]
        ab = [(0, lambda qt=0: band_step(0)), (0, lambda qt=1: band_step(1))]
        for k in range(NQT):
            ab.append(((k // 4) * 32 + 32, lambda qt=k: band_tp_step(qt)))
            if k + 2 < NQT:
                ab.append((0, lambda qt=k + 2: band_step(qt)))
        steps += ab
        return steps, st

    def emit_main(ch, st, steps, tail):
        pr = ch % 2
        exp_done = {-2: tail["exp"][0], -1: tail["exp"][1]}
        pvden_done = {-3: tail["pvden"][0], -2: tail["pvden"][1],
                      -1: tail["pvden"][2]}
        pending = []
        nsteps = len(steps)
        next_step_at = [0]
        pieces_total = 128

        def consume_steps(i):
            want = (i + 1) * nsteps // pieces_total
            while next_step_at[0] < want and steps and steps[0][0] <= i:
                steps.pop(0)[1]()
                next_step_at[0] += 1

        P.wait("tensor", "sKV", kv_rdy[ch])
        P.wait("tensor", "sVe", st["aux_rdy"])
        pv_start_guard = tail["pv_free"]
        for qblk in range(4):
            q0 = qblk * 512
            for kc in range(NKC):
                i = qblk * NKC + kc
                pa = PA if i % 2 == 0 else PB
                pc = i % 3
                P.wait("tensor", "sSc", exp_done.get(i - 2, 0))
                mm(pa[:, 0:512], kpair[pr][0:64, kc * 128:(kc + 1) * 128],
                   qpair[pr][0:64, q0:q0 + 512], True, False)
                mm(pa[:, 512:1024], kpair[pr][64:128, kc * 128:(kc + 1) * 128],
                   qpair[pr][64:128, q0:q0 + 512], True, False)
                subr = [(max(lo, q0), min(hi, q0 + 512), v)
                        for (lo, hi, v) in region_table(kc)
                        if max(lo, q0) < min(hi, q0 + 512)]
                for si, (lo, hi, v) in enumerate(subr):
                    last = si == len(subr) - 1
                    mm(pa[:, lo - q0:hi - q0], auxsel[0:2, v * 128:v * 128 + 128],
                       auxt[pr][0:2, lo:hi], False, last)
                    mm(pa[:, 512 + lo - q0:512 + hi - q0],
                       auxsel[32:34, v * 128:v * 128 + 128],
                       auxt[pr][32:34, lo:hi], False, last)
                t_qk = P.mark("tensor", "sPe")
                if len(pending) >= 2:
                    pending.pop(0)()
                P.wait("scalar", "sPe", t_qk)
                P.wait("scalar", "sPe", pvden_done.get(i - 3, 0))
                sop(lambda E, pa=pa, pc=pc: E.activation(
                    pchunk[pc][:, :], pa[:, 0:1024], AF.Exp))
                s_e = P.mark("scalar", "sSc")
                exp_done[i] = s_e
                ncopy = 0
                P.wait("vector", "sSc", s_e)
                P.wait("vector", "sPe", pvden_done.get(i - 3, 0))
                for (qt, p, bq0) in band_pieces_of_kc(kc):
                    if q0 <= bq0 < q0 + 512:
                        col = qblk * 3072 + (qt % 4) * 768 + p * 256
                        off = bq0 - q0
                        vop(lambda E, pc=pc, col=col, off=off: E.tensor_copy(
                            pchunk[pc][:, off:off + 128], bandT[:, col:col + 128]))
                        vop(lambda E, pc=pc, col=col, off=off: E.tensor_copy(
                            pchunk[pc][:, 512 + off:512 + off + 128],
                            bandT[:, col + 128:col + 256]))
                        ncopy += 1
                v_cp = P.mark("vector", "sVe") if ncopy else P.val("sVe")

                def den_pv(i=i, kc=kc, pc=pc, s_e=s_e, v_cp=v_cp, q0=q0,
                           guard=pv_start_guard):
                    P.wait("tensor", "sSc", s_e)
                    P.wait("tensor", "sVe", v_cp)
                    if kc == 0:
                        P.wait("tensor", "sVe", guard)
                    mm(PD[0:1, :], ones_bf[:, 0:1], pchunk[pc][:, 0:512],
                       kc == 0, kc == NKC - 1)
                    mm(PD[64:65, :], ones_bf[:, 0:1], pchunk[pc][:, 512:1024],
                       kc == 0, kc == NKC - 1, tp=(0, 64))
                    mm(PV1[0:64, :], vpair[pr][:, kc * 128:kc * 128 + 64],
                       pchunk[pc][:, 0:512], kc == 0, kc == NKC - 1)
                    mm(PV1[64:128, :], vpair[pr][:, kc * 128 + 64:kc * 128 + 128],
                       pchunk[pc][:, 512:1024], kc == 0, kc == NKC - 1, tp=(0, 64))
                    pvden_done[i] = P.mark("tensor", "sPe")
                pending.append(den_pv)
                consume_steps(i)
            while pending:
                pending.pop(0)()
            t_last = pvden_done[qblk * NKC + NKC - 1]
            P.wait("vector", "sPe", t_last)
            vop(lambda E: E.tensor_copy(T2[0:65, 0:512], PD[0:65, :]))
            vdrain()
            vop(lambda E: E.tensor_copy(T2[:, 512:1024], PV1[:, :]))
            vdrain()
            v_pvc = P.mark("vector", "sVe")
            recip_nr(T1[0:65, 0:512], T2[0:65, 0:512], T1[0:65, 512:1024])
            vop(lambda E: E.tensor_copy(rcbf[0:65, :], T1[0:65, 0:512]))
            vdrain()
            vop(lambda E: E.tensor_sub(rclo[0:65, :], T1[0:65, 0:512],
                                       rcbf[0:65, :]))
            vdrain()
            v_rc = P.mark("vector", "sVe")
            b = pp_next()
            P.wait("tensor", *ppfree[b])
            P.wait("tensor", "sVe", v_rc)
            mm(PP[b][0:64, :], ones_bf[0:1, 0:64], rcbf[0:1, :], True, False)
            mm(PP[b][64:128, :], ones_bf[64:65, 0:64], rcbf[64:65, :], True, False,
               tp=(64, 64))
            mm(PP[b][0:64, :], ones_bf[0:1, 0:64], rclo[0:1, :], False, True)
            mm(PP[b][64:128, :], ones_bf[64:65, 0:64], rclo[64:65, :], False, True,
               tp=(64, 64))
            t_bc = P.mark("tensor", "sPe")
            P.wait("vector", "sPe", t_bc)
            vop(lambda E, b=b: E.tensor_mul(T1[:, 512:1024], T2[:, 512:1024],
                                            PP[b][:, :]))
            vdrain()
            v_pvf = P.mark("vector", "sVe")
            ppfree[b] = ("sVe", v_pvf)
            if ch == 0 and qblk == 0:
                P.wait("vector", "sXl", x_own_ld)
            vop(lambda E, ch=ch, q0=q0: E.tensor_add(
                x_own[:, ch * 2048 + q0:ch * 2048 + q0 + 512],
                x_own[:, ch * 2048 + q0:ch * 2048 + q0 + 512], T1[:, 512:1024]))
            vdrain()
            v_ep = P.mark("vector", "sVe")
            epil_sems[(ch, qblk)] = v_ep
            pv_start_guard = v_pvc
        while steps:
            steps.pop(0)[1]()
        n = 4 * NKC
        tail_out = {"exp": [exp_done[n - 2], exp_done[n - 1]],
                    "pvden": [pvden_done[n - 3], pvden_done[n - 2],
                              pvden_done[n - 1]],
                    "pv_free": pv_start_guard}
        return P.val("sPe"), P.val("sVe"), tail_out

    if n_pairs > 0:
        emit_pair_loads(0)
        if n_pairs > 1:
            emit_pair_loads(1)
        steps0, st0 = prep_steps(0)
        for _, f in steps0:
            f()
        states = {0: st0}
        tail = {"exp": [0, 0], "pvden": [0, 0, 0], "pv_free": 0}
        for ch in range(n_pairs):
            st = states[ch]
            if ch + 1 < n_pairs:
                steps_n, st_n = prep_steps(ch + 1)
                states[ch + 1] = st_n
            else:
                steps_n = []
            main_pe, main_ve, tail = emit_main(ch, st, steps_n, tail)
            pair_free[ch % 2] = main_pe
            if ch + 2 < n_pairs:
                emit_pair_loads(ch + 2)

    p2_ve = P.val("sVe")
    p2_sc = P.val("sSc")
    p2_pe = P.val("sPe")

    PPX = PP

    P.wait("gpsimd", "sPe", p2_pe)
    for c in range(8):
        dma("gpsimd", wrt_sb[:, c * 16:(c + 1) * 16],
            wrtT.ap()[c * 128:(c + 1) * 128, :], "sFw")
    wrt_rdy = P.val("sFw")

    wexp_free = [p2_pe, p2_pe]
    up_fr = [0, 0]
    dn_fr = [0, 0]
    gbc_fr = 0
    glt_fr = [0, 0]
    out_st = 0
    gstore = 0

    MUB = f32c[:, 2048:3072]
    IVB = f32c[:, 3072:4096]

    for th in range(2):
        toff = th * 1024
        P.wait("scalar", "sPe", P.val("sPe"))
        P.wait("scalar", "sVe", p2_ve if th == 0 else P.val("sVe"))
        for c in range(8):
            sop(lambda E, c=c, toff=toff: E.copy(
                xln2[:, c * 1024:(c + 1) * 1024],
                x_own[:, c * 2048 + toff:c * 2048 + toff + 1024]))
        s_cast2 = P.mark("scalar", "sSc")
        sdrain()
        P.wait("tensor", "sSc", s_cast2)
        v_prev = 0
        for h2 in range(2):
            P.wait("scalar", "sPe", P.val("sPe"))
            P.wait("tensor", "sVe", v_prev)
            for c in range(8):
                mm(PV1[0:1, :], ones_bf[:, 0:1],
                   xln2[:, c * 1024 + h2 * 512:c * 1024 + h2 * 512 + 512], c == 0, c == 7)
            t_a = P.mark("tensor", "sPe")
            P.wait("vector", "sPe", t_a)
            v_prev = vop(lambda E, h2=h2: E.tensor_scalar(
                f32c[0:1, 2048 + h2 * 512:2048 + h2 * 512 + 512], PV1[0:1, :],
                1.0 / 1024, None, ALU.mult), "sVe")
            for c in range(8):
                s_last = sop(lambda E, c=c, h2=h2: E.square(
                    sqP3[:, :],
                    xln2[:, c * 1024 + h2 * 512:c * 1024 + h2 * 512 + 512]), "sSc")
                P.wait("tensor", "sSc", s_last)
                mm(PD[0:1, :], ones_bf[:, 0:1], sqP3[:, :], c == 0, c == 7)
                t_s3 = P.mark("tensor", "sPe")
                P.wait("scalar", "sPe", t_s3)
            t_b2 = P.val("sPe")
            P.wait("vector", "sPe", t_b2)
            v_prev = vop(lambda E, h2=h2: E.tensor_scalar(
                f32c[0:1, 3072 + h2 * 512:3072 + h2 * 512 + 512], PD[0:1, :],
                1.0 / 1024, None, ALU.mult), "sVe")
        vdrain()
        MU2 = f32c[0:1, 2048:3072]
        EX2 = f32c[0:1, 3072:4096]
        vop(lambda E: E.tensor_mul(f32c[0:1, 1024:2048], MU2, MU2)); vdrain()
        vop(lambda E: E.tensor_sub(EX2, EX2, f32c[0:1, 1024:2048])); vdrain()
        vop(lambda E: E.tensor_scalar(EX2, EX2, 1.0, EPS, ALU.mult, ALU.add)); vdrain()
        recip_nr(f32c[0:1, 1024:2048], EX2, f32c[0:1, 0:1024])
        v_v2 = P.mark("vector", "sVe")
        P.wait("scalar", "sVe", v_v2)
        s_r2 = sop(lambda E: E.sqrt(f32c[0:1, 0:1024], f32c[0:1, 1024:2048]), "sSc")
        P.wait("vector", "sSc", s_r2)
        vop(lambda E: E.tensor_copy(bfc[0:1, 2048:3072], MU2))
        vop(lambda E: E.tensor_copy(bfc[0:1, 3072:4096], f32c[0:1, 0:1024]))
        vdrain()
        v_r2 = P.mark("vector", "sVe")
        P.wait("tensor", "sVe", v_r2)
        v_prev = 0
        for h2 in range(2):
            P.wait("tensor", "sVe", v_prev)
            mm(PV1[:, :], ones_bf[0:1, :], bfc[0:1, 2048 + h2 * 512:2048 + h2 * 512 + 512],
               True, True)
            mm(PD[:, :], ones_bf[0:1, :], bfc[0:1, 3072 + h2 * 512:3072 + h2 * 512 + 512],
               True, True)
            t_bc2 = P.mark("tensor", "sPe")
            P.wait("vector", "sPe", t_bc2)
            vop(lambda E, h2=h2: E.tensor_copy(MUB[:, h2 * 512:h2 * 512 + 512],
                                               PV1[:, :]))
            v_prev = vop(lambda E, h2=h2: E.tensor_copy(
                IVB[:, h2 * 512:h2 * 512 + 512], PD[:, :]), "sVe")
        vdrain()
        for c in range(8):
            vop(lambda E, c=c, toff=toff: E.tensor_sub(
                T1, x_own[:, c * 2048 + toff:c * 2048 + toff + 1024], MUB))
            vdrain()
            vop(lambda E, c=c: E.tensor_mul(
                xln2[:, c * 1024:(c + 1) * 1024], T1, IVB))
            vdrain()
        v_x2 = P.mark("vector", "sVe")

        P.wait("tensor", "sFw", wrt_rdy)
        P.wait("tensor", "sVe", v_x2)
        v_lgfree = 0
        v_gtfree = 0
        for tc in range(8):
            P.wait("tensor", "sVe", v_lgfree)
            for dc in range(8):
                mm(PA[0:128, 0:16],
                   xln2[:, dc * 1024 + tc * 128:dc * 1024 + (tc + 1) * 128],
                   wrt_sb[:, dc * 16:(dc + 1) * 16], dc == 0, dc == 7)
            t_lg = P.mark("tensor", "sPe")
            P.wait("vector", "sPe", t_lg)
            W0 = rw[:, 0:16]
            W1 = rw[:, 16:32]
            W2 = rw[:, 32:48]
            KEEP = rw[:, 48:64]
            MX = rw[:, 64:65]
            SM = rw[:, 65:66]
            vop(lambda E: E.reduce_max(MX, PA[0:128, 0:16], axis=AX.X)); vdrain()
            v_lgfree = vop(lambda E: E.tensor_scalar(
                W0, PA[0:128, 0:16], MX, None, ALU.subtract), "sVe")
            vdrain()
            v_sf = v_lgfree
            P.wait("scalar", "sVe", v_sf)
            s_ex = sop(lambda E: E.activation(W1, W0, AF.Exp), "sSc")
            P.wait("vector", "sSc", s_ex)
            vop(lambda E: E.reduce_sum(SM, W1, axis=AX.X)); vdrain()
            vop(lambda E: E.reciprocal(MX, SM)); vdrain()
            vop(lambda E: E.tensor_scalar(W1, W1, MX, None, ALU.mult)); vdrain()
            vop(lambda E: E.tensor_copy(W0, W1))
            vop(lambda E: E.memset(KEEP, 0.0)); vdrain()
            for kk in range(4):
                vop(lambda E: E.reduce_max(MX, W0, axis=AX.X)); vdrain()
                vop(lambda E: E.tensor_scalar(W2, W0, MX, None, ALU.is_ge)); vdrain()
                vop(lambda E: E.tensor_scalar(W2, W2, 1e30, -1e30, ALU.mult, ALU.add))
                vdrain()
                vop(lambda E: E.tensor_sub(W2, W2, iota_sb[:, :])); vdrain()
                vop(lambda E: E.reduce_max(MX, W2, axis=AX.X)); vdrain()
                vop(lambda E: E.tensor_scalar(W2, W2, MX, None, ALU.is_ge)); vdrain()
                vop(lambda E: E.tensor_add(KEEP, KEEP, W2)); vdrain()
                vop(lambda E: E.scalar_tensor_tensor(W0, W2, -1e30, W0, ALU.mult, ALU.add))
                vdrain()
            vop(lambda E: E.tensor_mul(W2, W1, KEEP)); vdrain()
            vop(lambda E: E.tensor_copy(scb[:, 0:16], W2)); vdrain()
            v_g = P.mark("vector", "sVe")
            P.wait("tensor", "sVe", v_g)
            P.wait("tensor", "sVe", v_gtfree)
            mm(PV1[0:16, 0:128], scb[:, 0:16], iden_bf[:, :], True, True)
            t_g = P.mark("tensor", "sPe")
            P.wait("vector", "sPe", t_g)
            v_gtfree = vop(lambda E, tc=tc: E.tensor_copy(
                gateT[:, tc * 128:(tc + 1) * 128], PV1[0:16, 0:128]), "sVe")
        v_gT = v_gtfree
        P.wait("sync", "sVe", v_gT)
        P.wait("sync", "sGa", gstore)
        w_g = dma("sync", gate_hbm.ap(), gateT[:, :], "sWr")

        g_row = 0
        t_gbc_prev = 0
        for e in range(NE):
            we = e % 2
            P.wait("gpsimd", "sPe", wexp_free[we])
            for c in range(8):
                dma("gpsimd", wup[we][:, c * 512:(c + 1) * 512],
                    wupT.ap()[e, c * 128:(c + 1) * 128, :], "sFw")
            for c in range(4):
                dma("gpsimd", wdn[we][:, c * 1024:(c + 1) * 1024],
                    wdnT.ap()[e, c * 128:(c + 1) * 128, :], "sFw")
            w_rdy = P.val("sFw")
            routed = e >= 2
            if routed:
                P.wait("sync", "sWr", w_g)
                P.wait("sync", "sPe", t_gbc_prev)
                g_row = dma("sync", grow[:, :], gate_hbm.ap()[e - 2:e - 1, :], "sGa2")
                P.wait("tensor", "sGa2", g_row)
                P.wait("tensor", "sVe", max(gbc_fr, v_gT))
                for jp in range(2):
                    mm([PV1, PD][jp][:, :], ones_bf[0:1, :],
                       grow[0:1, jp * 512:(jp + 1) * 512], True, True)
            t_gbc = P.mark("tensor", "sPe") if routed else P.val("sPe")
            t_gbc_prev = t_gbc
            gstore = g_row
            P.wait("tensor", "sFw", w_rdy)
            for ec in range(4):
                for jp in range(2):
                    pA = (ec * 2 + jp) % 2
                    P.wait("tensor", "sSc", up_fr[pA])
                    P.wait("tensor", "sVe", dn_fr[pA])
                    for dc in range(8):
                        mm(PPX[pA][:, :],
                           wup[we][:, dc * 512 + ec * 128:dc * 512 + (ec + 1) * 128],
                           xln2[:, dc * 1024 + jp * 512:dc * 1024 + (jp + 1) * 512],
                           dc == 0, dc == 7)
                    t_u = P.mark("tensor", "sPe")
                    P.wait("scalar", "sPe", t_u)
                    if routed:
                        P.wait("scalar", "sVe", glt_fr[pA])
                        s_gl = sop(lambda E, pA=pA: E.activation(
                            f32c[:, pA * 512:(pA + 1) * 512], PPX[pA][:, :], AF.Gelu),
                            "sSc")
                        up_fr[pA] = s_gl
                        P.wait("vector", "sSc", s_gl)
                        P.wait("vector", "sPe", t_gbc)
                        v_ac = vop(lambda E, ec=ec, jp=jp, pA=pA: E.tensor_mul(
                            act[:, ec * 1024 + jp * 512:ec * 1024 + (jp + 1) * 512],
                            f32c[:, pA * 512:(pA + 1) * 512], [PV1, PD][jp][:, :]), "sVe")
                        gbc_fr = v_ac
                        glt_fr[pA] = v_ac
                    else:
                        s_gl = sop(lambda E, pA=pA, ec=ec, jp=jp: E.activation(
                            act[:, ec * 1024 + jp * 512:ec * 1024 + (jp + 1) * 512],
                            PPX[pA][:, :], AF.Gelu), "sSc")
                        up_fr[pA] = s_gl
            vdrain()
            v_actdone = vop(lambda E: E.nop(), "sVe")
            sdrain()
            s_actdone = sop(lambda E: E.nop(), "sSc")
            P.wait("tensor", "sVe", v_actdone)
            P.wait("tensor", "sSc", s_actdone)
            for hc in range(8):
                for jp in range(2):
                    pA = (hc * 2 + jp) % 2
                    P.wait("tensor", "sSc", up_fr[pA])
                    P.wait("tensor", "sVe", dn_fr[pA])
                    for ec in range(4):
                        mm(PPX[pA][:, :],
                           wdn[we][:, ec * 1024 + hc * 128:ec * 1024 + (hc + 1) * 128],
                           act[:, ec * 1024 + jp * 512:ec * 1024 + (jp + 1) * 512],
                           ec == 0, ec == 3)
                    t_d = P.mark("tensor", "sPe")
                    P.wait("vector", "sPe", t_d)
                    if e == 0:
                        v_aa = vop(lambda E, pA=pA, hc=hc, jp=jp: E.tensor_copy(
                            acc[:, hc * 1024 + jp * 512:hc * 1024 + (jp + 1) * 512],
                            PPX[pA][:, :]), "sVe")
                    else:
                        v_aa = vop(lambda E, pA=pA, hc=hc, jp=jp: E.tensor_add(
                            acc[:, hc * 1024 + jp * 512:hc * 1024 + (jp + 1) * 512],
                            acc[:, hc * 1024 + jp * 512:hc * 1024 + (jp + 1) * 512],
                            PPX[pA][:, :]), "sVe")
                    dn_fr[pA] = v_aa
            wexp_free[we] = P.val("sPe")

        P.emit("vector", lambda E: E.drain())
        P.wait("vector", "sSt", out_st)
        for c in range(8):
            vop(lambda E, c=c, toff=toff: E.tensor_add(
                T2, acc[:, c * 1024:(c + 1) * 1024],
                x_own[:, c * 2048 + toff:c * 2048 + toff + 1024]))
            vdrain()
            v_o = P.mark("vector", "sVe")
            P.wait("sync", "sVe", v_o)
            out_st = dma("sync", out_d.ap()[c * 128:(c + 1) * 128, toff:toff + 1024],
                         T2, "sSt")
            P.wait("vector", "sSt", out_st)

    P.wait("sync", "sSt", P.val("sSt"))
    P.wait("sync", "sWr", P.val("sWr"))
    P.wait("gpsimd", "sRp", P.val("sRp"))
    P.wait("gpsimd", "sGa", P.val("sGa"))

    with nc.Block() as block:
        P.run(block)
    es.close()
    return nc


_CACHE = {}

def _prep_core_inputs(inputs, c):
    x = inputs["L0_attn_x"]
    b, qh = c // 2, c % 2
    key = "shared"
    if key not in _CACHE:
        Wqkv = np.asarray(inputs["L0_attn_Wqkv"]).reshape(16, 3, 64, 1024)
        WQ = Wqkv[:, 0].reshape(1024, 1024)
        WK = Wqkv[:, 1].reshape(1024, 1024)
        WV = Wqkv[:, 2].reshape(1024, 1024)
        bqkv = np.asarray(inputs["L0_attn_bqkv"]).reshape(16, 3, 64)
        bQ, bK, bV = [np.ascontiguousarray(bqkv[:, i].reshape(1024)) for i in range(3)]
        Wr = np.asarray(inputs["L0_attn_Wr"])
        br = np.asarray(inputs["L0_attn_br"])
        WrT = np.ascontiguousarray(Wr.T)
        wrpadT = np.concatenate([np.repeat(WrT[:, 0:1], 128, 1), WrT,
                                 np.repeat(WrT[:, 256:257], 127, 1)], axis=1)
        brpad = np.concatenate([np.repeat(br[0:1], 128), br,
                                np.repeat(br[256:257], 127)])[None, :]
        wredge4 = np.ascontiguousarray(WrT[:, [256, 0, 256, 0]])
        bredge = np.ascontiguousarray(br[[256, 0, 256, 0]][:, None])
        su = np.asarray(inputs["L0_ffn_shared_up"])
        sd = np.asarray(inputs["L0_ffn_shared_down"])
        ru = np.asarray(inputs["L0_ffn_routed_up"])
        rd = np.asarray(inputs["L0_ffn_routed_down"])
        wupT = np.ascontiguousarray(
            np.concatenate([su, ru], axis=0).transpose(0, 2, 1))
        wdnT = np.ascontiguousarray(
            np.concatenate([sd, rd], axis=0).transpose(0, 2, 1))
        wrtT = np.ascontiguousarray(np.asarray(inputs["L0_ffn_router_w"]).T)
        iden = np.eye(128, dtype=BF)
        iota = np.broadcast_to(np.arange(16, dtype=np.float32), (128, 16)).copy()
        _CACHE[key] = dict(
            wqT=np.ascontiguousarray(WQ.T), wkT=np.ascontiguousarray(WK.T),
            wvT=np.ascontiguousarray(WV.T),
            bq=np.ascontiguousarray(bQ.reshape(8, 128).T),
            bk=np.ascontiguousarray(bK.reshape(8, 128).T),
            bv_row=np.ascontiguousarray(bV[None, :]),
            wrpadT=np.ascontiguousarray(wrpadT), brpad=np.ascontiguousarray(brpad),
            wredge4=wredge4.astype(np.float32), bredge=bredge.astype(np.float32),
            wupT=wupT, wdnT=wdnT, wrtT=wrtT, iden=iden, iota=iota)
    shared = _CACHE[key]

    ovr = np.zeros((2, 128, 512), np.float32)
    if qh == 0:
        for di in range(128):
            ovr[0, di, 128 - di:256 - di] = 1.0
    else:
        for di in range(128):
            ovr[1, di, 384 - di:512] = 1.0
    ovr = ovr.astype(BF)
    Rsel = np.array([[1.0] * 128, [0.0] * 128], dtype=BF)
    Lsel = np.array([[0.0] * 128, [1.0] * 128], dtype=BF)
    Fsel = Rsel if qh == 0 else Lsel
    auxsel = np.concatenate([Rsel, Lsel, Fsel], axis=1)
    auxsel = np.concatenate([auxsel, auxsel], axis=0)

    xr = np.roll(np.asarray(x[b]), -2048 * qh, axis=0)
    xTc = np.ascontiguousarray(xr.T)

    d = dict(shared)
    d.update(xT=xTc, auxsel_in=auxsel, ovr=ovr)
    return d


_NC = None

def kernel(**inputs):
    global _NC
    in_maps = [_prep_core_inputs(inputs, c) for c in range(8)]
    if _NC is None:
        _NC = build_program()
    res = run_bass_kernel_spmd(_NC, in_maps, core_ids=list(range(8)))
    x = np.asarray(inputs["L0_attn_x"])
    out = np.empty_like(x)
    for c in range(8):
        b, qh = c // 2, c % 2
        out[b, 2048 * qh:2048 * (qh + 1), :] = res.results[c]["out"].T
    _CACHE.clear()
    return out


# version 10
# speedup vs baseline: 2.4028x; 1.0724x over previous
import contextlib
import numpy as np
import ml_dtypes
import concourse.bass as bass
import concourse.mybir as mybir
from concourse.bass_utils import run_bass_kernel_spmd

F32 = mybir.dt.float32
BF16 = mybir.dt.bfloat16
I32 = mybir.dt.int32
AF = mybir.ActivationFunctionType
ALU = mybir.AluOpType
AX = mybir.AxisListType
BF = ml_dtypes.bfloat16

H, HD, D, S, NQ = 16, 64, 1024, 4096, 2048
NKC, NQT, NE, ED = 32, 16, 18, 512
EPS = 1e-5
MSLACK = 24.0

def region_table(kc):
    if kc == 0:
        return [(0, 2048, 1)]
    if 1 <= kc <= 15:
        qa = 128 * (kc - 1)
        cut = min(qa + 384, 2048)
        regs = [(0, cut, 0)]
        if cut < 2048:
            regs.append((cut, 2048, 1))
        return regs
    return [(0, 2048, 2)]

def band_pieces_of_kc(kc):
    out = []
    for p in (0, 1, 2):
        qt = (kc if kc != 31 else -1) + 1 - p
        if 0 <= qt < NQT:
            out.append((qt, p, qt * 128))
    return out

ENGS = ("sync", "gpsimd", "tensor", "vector", "scalar")

DMA_ISSUER = {"sIn": "sync", "sWr": "sync", "sTp": "sync", "sSt": "sync",
              "sGa2": "sync", "sXl": "gpsimd",
              "sKV": "gpsimd", "sFw": "gpsimd", "sRp": "gpsimd", "sGa": "gpsimd"}
RELAY_SEM = {"sync": "sSy", "gpsimd": "sGy"}

class Plan:
    def __init__(self, nc):
        self.nc = nc
        self.stream = {e: [] for e in ENGS}
        self.sems = {}
        self.relay_cache = {}

    def sem(self, es, name):
        h = es.enter_context(self.nc.semaphore(name))
        self.sems[name] = [h, 0]

    def inc(self, name, k):
        s = self.sems[name]
        s[1] += k
        return s[1]

    def val(self, name):
        return self.sems[name][1]

    def emit(self, eng, fn):
        self.stream[eng].append(fn)

    def _rawwait(self, eng, name, v):
        h = self.sems[name][0]
        fn = lambda E, h=h, v=v: E.wait_ge(h, v)
        fn._is_wait = True
        self.emit(eng, fn)

    def wait(self, eng, name, v):
        if v <= 0:
            return
        assert v <= self.sems[name][1], f"wait on future {name}: {v} > {self.sems[name][1]}"
        if name in DMA_ISSUER:
            issuer = DMA_ISSUER[name]
            total = self.sems[name][1]
            if eng == issuer:
                self._rawwait(eng, name, total)
                return
            rs = RELAY_SEM[issuer]
            cached = self.relay_cache.get(name)
            if cached is None or cached[0] != total:
                self._rawwait(issuer, name, total)
                h = self.sems[rs][0]
                rv = self.inc(rs, 1)
                self.emit(issuer, lambda E, h=h: E.nop().then_inc(h, 1))
                self.relay_cache[name] = (total, rv)
            rv = self.relay_cache[name][1]
            self._rawwait(eng, rs, rv)
            return
        self._rawwait(eng, name, v)

    def mark(self, eng, sem):
        assert self.stream[eng], "mark on empty stream"
        last = self.stream[eng][-1]
        assert not getattr(last, "_has_upd", False), f"double mark on {eng}"
        assert not getattr(last, "_is_wait", False), f"mark on wait on {eng}"
        v = self.inc(sem, 1)
        h = self.sems[sem][0]
        self.stream[eng].pop()
        wrapped = lambda E, last=last, h=h: last(E).then_inc(h, 1)
        wrapped._has_upd = True
        self.stream[eng].append(wrapped)
        return v

    def op(self, eng, fn, sem=None, k=1):
        if sem is None:
            self.emit(eng, lambda E, fn=fn: fn(E))
            return None
        v = self.inc(sem, k)
        h = self.sems[sem][0]
        wrapped = lambda E, fn=fn, h=h, k=k: fn(E).then_inc(h, k)
        wrapped._has_upd = True
        self.emit(eng, wrapped)
        return v

    def run(self, block):
        for eng in ENGS:
            ops = self.stream[eng]
            getattr(block, eng)(lambda E, ops=ops: [f(E) for f in ops])


def build_program(n_pairs=8, dbg=False):
    nc = bass.Bass("TRN2", target_bir_lowering=False)
    es = contextlib.ExitStack()

    din = lambda n, s, dt=F32: nc.dram_tensor(n, s, dt, kind="ExternalInput")
    xT = din("xT", [D, S])
    wqT = din("wqT", [D, D]); wkT = din("wkT", [D, D]); wvT = din("wvT", [D, D])
    bq_i = din("bq", [128, 8]); bk_i = din("bk", [128, 8])
    bv_row = din("bv_row", [1, D])
    wrpadT = din("wrpadT", [64, 512]); brpad_i = din("brpad", [1, 512])
    wredge_i = din("wredge4", [64, 4]); bredge_i = din("bredge", [4, 1])
    wupT = din("wupT", [NE, D, ED]); wdnT = din("wdnT", [NE, ED, D])
    wrtT = din("wrtT", [D, 16])
    iden_i = din("iden", [128, 128], BF16)
    iota_i = din("iota", [128, 16])
    auxsel_i = din("auxsel_in", [128, 384], BF16)
    ovr_i = din("ovr", [2, 128, 512], BF16)

    out_d = nc.dram_tensor("out", [D, NQ], F32, kind="ExternalOutput")

    k_hbm = nc.dram_tensor("k_hbm", [8, 128, S], BF16)
    q_hbm = nc.dram_tensor("q_hbm", [8, 128, NQ], BF16)
    v_hbm = nc.dram_tensor("v_hbm", [NKC, 128, D], BF16)
    rp_hbm = nc.dram_tensor("rp_hbm", [2, 2, NQT, 128, 512], BF16)
    m_hbm = nc.dram_tensor("m_hbm", [2, 2, NQ], BF16)
    gate_hbm = nc.dram_tensor("gate_hbm", [16, 1024], BF16)

    sbt = lambda n, s, dt: es.enter_context(nc.sbuf_tensor(n, s, dt))
    x_own = sbt("x_own", [128, 16384], F32)
    big1 = sbt("big1", [128, 16384], BF16)
    big2 = sbt("big2", [128, 8192], BF16)
    big3 = sbt("big3", [128, 8192], BF16)
    f32s = sbt("f32s", [128, 8192], F32)
    f32c = sbt("f32c", [128, 4096], F32)
    bfc = sbt("bfc", [128, 4096], BF16)
    rw = sbt("rw", [128, 80], F32)
    ones_bf = sbt("ones_bf", [128, 128], BF16)
    iden_bf = sbt("iden_bf", [128, 128], BF16)
    iota_sb = sbt("iota_sb", [128, 16], F32)
    auxsel = sbt("auxsel", [128, 384], BF16)
    ovrb = sbt("ovrb", [128, 1024], BF16)
    bq_sb = sbt("bq_sb", [128, 8], F32)
    bk8_sb = sbt("bk8_sb", [128, 8], F32)
    bVbc = sbt("bVbc", [128, 1024], BF16)
    brbc = sbt("brbc", [128, 512], BF16)
    wrpad2 = sbt("wrpad2", [128, 512], BF16)
    wredge_sb = sbt("wredge_sb", [128, 66], BF16)
    bredge_sb = sbt("bredge_sb", [128, 1], F32)
    scal = sbt("scal", [128, 32], F32)
    scb = sbt("scb", [128, 32], BF16)
    nsq = sbt("nsq", [128, 32], F32)
    mwork = sbt("mwork", [128, 32], F32)
    mhbf = sbt("mhbf", [128, 32], BF16)
    mcolneg = sbt("mcolneg", [128, 32], F32)
    c1bc = sbt("c1bc", [128, 2], F32)
    c2bc = sbt("c2bc", [128, 1], F32)

    T1 = f32c[:, 0:1024]
    T2 = f32c[:, 1024:2048]
    BC1 = f32c[:, 2048:3072]
    BC2 = f32c[:, 3072:4096]

    xslab = [f32s[:, 0:4096], f32s[:, 4096:8192]]
    xlnsl = [big3[:, 0:4096], big3[:, 4096:8192]]
    kstage = [bfc[:, 0:512], bfc[:, 512:1024]]
    vstage = bfc[:, 1536:2560]
    sqP1 = bfc[:, 3072:3584]
    QP = bass.AP(f32s, 1024, [[8192, 128], [1, 2048]]).bitcast(BF16)
    AXT = bass.AP(f32s, 3072, [[8192, 128], [1, 2048]]).bitcast(BF16)
    M2V = bass.AP(f32s, 5120, [[8192, 128], [1, 1024]]).bitcast(BF16)
    SQB = bass.AP(f32s, 6144, [[8192, 128], [1, 1024]]).bitcast(BF16)
    rpstg2 = [f32s[:, 0:512], f32s[:, 512:1024]]
    qpair = [QP[:, 0:2048], QP[:, 2048:4096]]
    auxt = [AXT[:, 0:2048], AXT[:, 2048:4096]]
    kpair = [big2[:, 0:4096], big2[:, 4096:8192]]
    vpair = [big3[:, 0:4096], big3[:, 4096:8192]]
    bandT = big1[:, 0:12288]
    bandP = [[big1[:, 12288 + (2 * pS + hh) * 384:12288 + (2 * pS + hh + 1) * 384]
              for hh in range(2)] for pS in range(2)]
    msb = big1[:, 13824:13952]
    rpbf = [big1[:, 13952 + hh * 512:13952 + (hh + 1) * 512] for hh in range(2)]
    rcbf = big1[:, 14976:15488]
    rclo = big1[:, 15488:16000]
    possb = [[bfc[:, (2 * pS + hh) * 384:(2 * pS + hh + 1) * 384] for hh in range(2)]
             for pS in range(2)]
    RCV = bass.AP(f32s, 7168, [[8192, 128], [1, 1024]]).bitcast(BF16)
    rcbf = RCV[:, 0:512]
    rclo = RCV[:, 512:1024]
    pchunk = [bfc[:, 1536 + i * 1024:1536 + (i + 1) * 1024] for i in range(2)] + \
             [big1[:, 14976:16000]]
    xln2 = big1[:, 0:8192]
    act = big1[:, 8192:12288]
    wrt_sb = big1[:, 12288:12416]
    wup = [big2[:, 0:4096], big2[:, 4096:8192]]
    wdn = [big3[:, 0:4096], big3[:, 4096:8192]]
    acc = f32s
    grow = bfc[0:1, 0:1024]
    gateT = bfc[0:16, 1024:2048]
    sqP3 = bfc[:, 2048:2560]

    PA = es.enter_context(nc.psum_tensor("PA", [128, 1024], F32))
    PB = es.enter_context(nc.psum_tensor("PB", [128, 1024], F32))
    PV1 = es.enter_context(nc.psum_tensor("PV1", [128, 512], F32))
    PD = es.enter_context(nc.psum_tensor("PD", [128, 512], F32))
    PP = [es.enter_context(nc.psum_tensor("PP0", [128, 512], F32)),
          es.enter_context(nc.psum_tensor("PP1", [128, 512], F32))]

    P = Plan(nc)
    for s in ("sIn", "sWr", "sTp", "sSt", "sGa2", "sXl", "sKV", "sFw", "sRp", "sGa",
              "sPe", "sVe", "sSc", "sSy", "sGy"):
        P.sem(es, s)

    def dma(eng, dst, src, sem="sIn", slow=False):
        return P.op(eng, lambda E, d=dst, s=src, sl=slow:
                    E.dma_start(out=d, in_=s, allow_slow_non_contiguous=sl), sem, 16)

    def mm(dst, lhsT, rhs, start, stop, sem=None, tp=None):
        return P.op("tensor", lambda E, d=dst, l=lhsT, r=rhs, st=start, sp=stop, tp=tp:
                    E.matmul(d, l, r, start=st, stop=sp, tile_position=tp,
                             skip_group_check=True), sem, 1)

    def vop(fn, sem=None):
        return P.op("vector", fn, sem, 1)

    def sop(fn, sem=None):
        return P.op("scalar", fn, sem, 1)

    def vdrain():
        P.emit("vector", lambda E: E.drain())

    def sdrain():
        P.emit("scalar", lambda E: E.drain())

    def recip_nr(dst, src, ws):
        vop(lambda E: E.reciprocal(dst, src)); vdrain()

    dma("sync", iden_bf[:, :], iden_i.ap())
    dma("sync", iota_sb[:, :], iota_i.ap())
    dma("sync", auxsel[:, :], auxsel_i.ap())
    dma("sync", ovrb[:, 0:512], ovr_i.ap()[0])
    dma("sync", ovrb[:, 512:1024], ovr_i.ap()[1])
    dma("sync", bq_sb[:, :], bq_i.ap())
    dma("sync", bk8_sb[:, :], bk_i.ap())
    dma("sync", f32c[0:1, 2048:3072], bv_row.ap())
    dma("sync", bredge_sb[0:2, 0:1], bredge_i.ap()[0:2])
    dma("sync", bredge_sb[64:66, 0:1], bredge_i.ap()[2:4])
    dma("sync", f32c[0:1, 3072:3584], brpad_i.ap())
    dma("gpsimd", wredge_sb[0:64, 0:2], wredge_i.ap()[:, 0:2], "sFw")
    dma("gpsimd", wredge_sb[64:128, 64:66], wredge_i.ap()[:, 2:4], "sFw")
    for r0 in (0, 64):
        dma("gpsimd", wrpad2[r0:r0 + 64, :], wrpadT.ap(), "sFw")
    for c in range(8):
        dma("gpsimd", big1[:, c * 1024:(c + 1) * 1024], wqT.ap()[c * 128:(c + 1) * 128, :],
            "sFw")
        dma("gpsimd", big1[:, 8192 + c * 1024:8192 + (c + 1) * 1024],
            wkT.ap()[c * 128:(c + 1) * 128, :], "sFw")
        dma("gpsimd", big2[:, c * 1024:(c + 1) * 1024], wvT.ap()[c * 128:(c + 1) * 128, :],
            "sFw")
    v_in = P.val("sIn")
    v_gw = P.val("sFw")

    P.wait("vector", "sIn", v_in)
    vop(lambda E: E.memset(ones_bf[:, :], 1.0))
    vop(lambda E: E.memset(wredge_sb[64:128, 0:64], 0.0))
    vop(lambda E: E.tensor_scalar(bk8_sb[:, :], bk8_sb[:, :], 0.125, None, ALU.mult))
    vop(lambda E: E.tensor_copy(bfc[0:1, 3072:4096], f32c[0:1, 2048:3072]))
    vop(lambda E: E.tensor_copy(bfc[0:1, 1024:1536], f32c[0:1, 3072:3584]))
    vdrain()
    v_rows0 = P.mark("vector", "sVe")
    P.wait("tensor", "sVe", v_rows0)
    for jp in range(2):
        mm(PA[:, jp * 512:(jp + 1) * 512], ones_bf[0:1, :],
           bfc[0:1, 3072 + jp * 512:3072 + (jp + 1) * 512], True, True)
    mm(PP[0][:, :], ones_bf[0:1, :], bfc[0:1, 1024:1536], True, True)
    t_bc0 = P.mark("tensor", "sPe")
    P.wait("vector", "sPe", t_bc0)
    vop(lambda E: E.tensor_copy(bVbc[:, :], PA[:, 0:1024]))
    vop(lambda E: E.tensor_copy(brbc[:, :], PP[0][:, :]))
    vdrain()
    v_bc0 = P.mark("vector", "sVe")

    P.wait("scalar", "sIn", v_in)
    P.wait("scalar", "sFw", v_gw)
    P.wait("scalar", "sPe", t_bc0)
    s_wr2 = sop(lambda E: E.square(sqP1[0:64, :], wrpad2[0:64, :]), "sSc")
    P.wait("tensor", "sSc", s_wr2)
    P.wait("tensor", "sVe", v_rows0)
    mm(PP[1][0:1, :], ones_bf[0:64, 0:1], sqP1[0:64, :], True, True)
    t_wr = P.mark("tensor", "sPe")
    P.wait("vector", "sPe", t_wr)
    vop(lambda E: E.reduce_max(scal[0:1, 0:1], PP[1][0:1, 128:385], axis=AX.X)); vdrain()
    v_mx = P.mark("vector", "sVe")
    P.wait("scalar", "sVe", v_mx)
    s_sq = sop(lambda E: E.sqrt(scal[0:1, 2:3], scal[0:1, 0:1]), "sSc")
    P.wait("vector", "sSc", s_sq)
    vop(lambda E: E.tensor_scalar(f32c[0:1, 1024:1536], f32c[0:1, 3072:3584],
                                  -1.0, None, ALU.mult)); vdrain()
    vop(lambda E: E.tensor_max(f32c[0:1, 1024:1536], f32c[0:1, 1024:1536],
                               f32c[0:1, 3072:3584])); vdrain()
    vop(lambda E: E.reduce_max(scal[0:1, 3:4], f32c[0:1, 1152:1409],
                               axis=AX.X)); vdrain()
    vop(lambda E: E.tensor_scalar(scal[0:1, 5:6], scal[0:1, 3:4], 1.0, -MSLACK,
                                  ALU.mult, ALU.add)); vdrain()
    vop(lambda E: E.tensor_copy(scb[0:1, 0:1], scal[0:1, 5:6])); vdrain()
    v_c2 = P.mark("vector", "sVe")
    P.wait("tensor", "sVe", v_c2)
    mm(PP[1][:, 0:1], ones_bf[0:1, :], scb[0:1, 0:1], True, True)
    t_c2 = P.mark("tensor", "sPe")
    P.wait("vector", "sPe", t_c2)
    vop(lambda E: E.tensor_copy(c2bc[:, :], PP[1][:, 0:1]))
    vdrain()
    v_c2b = P.mark("vector", "sVe")

    P.wait("tensor", "sFw", v_gw)

    slab_free_v = [0, 0]
    slab_xln_free = [0, 0]
    psP_fr = [0, 0]
    psV_fr = v_bc0
    kst_fr = [0, 0]
    vst_fr = 0
    x_own_ld = 0
    prev_bcc = [v_c2b]

    for s_i in range(8):
        b = s_i % 2
        P.wait("gpsimd", "sVe", slab_free_v[b])
        for c in range(8):
            dma("gpsimd", xslab[b][:, c * 512:(c + 1) * 512],
                xT.ap()[c * 128:(c + 1) * 128, s_i * 512:(s_i + 1) * 512], "sXl")
        if s_i < 4:
            for c in range(8):
                dma("gpsimd", x_own[:, c * 2048 + s_i * 512:c * 2048 + (s_i + 1) * 512],
                    xT.ap()[c * 128:(c + 1) * 128, s_i * 512:(s_i + 1) * 512], "sXl")
            x_own_ld = P.val("sXl")
        ld_done = P.val("sXl")

        P.wait("scalar", "sXl", ld_done)
        P.wait("scalar", "sPe", slab_xln_free[b])
        for c in range(8):
            sop(lambda E, b=b, c=c: E.copy(
                xlnsl[b][:, c * 512:(c + 1) * 512], xslab[b][:, c * 512:(c + 1) * 512]))
        s_cast = P.mark("scalar", "sSc")
        sdrain()
        P.wait("tensor", "sSc", s_cast)
        P.wait("tensor", "sVe", prev_bcc[0])
        for c in range(8):
            mm(PV1[0:1, :], ones_bf[:, 0:1], xlnsl[b][:, c * 512:(c + 1) * 512],
               c == 0, c == 7)
        for c in range(8):
            s_sq = sop(lambda E, b=b, c=c: E.square(
                sqP1[:, :], xlnsl[b][:, c * 512:(c + 1) * 512]), "sSc")
            P.wait("tensor", "sSc", s_sq)
            mm(PD[0:1, :], ones_bf[:, 0:1], sqP1[:, :], c == 0, c == 7)
            t_sqc = P.mark("tensor", "sPe")
            P.wait("scalar", "sPe", t_sqc)
        t_st = P.val("sPe")
        P.wait("vector", "sPe", t_st)
        MUr = f32c[0:1, 2048:2560]
        VARr = f32c[0:1, 2560:3072]
        MSQ = f32c[0:1, 3072:3584]
        INVr = f32c[0:1, 3584:4096]
        vop(lambda E: E.tensor_scalar(MUr, PV1[0:1, :], 1.0 / 1024, None, ALU.mult))
        vop(lambda E: E.tensor_scalar(VARr, PD[0:1, :], 1.0 / 1024, None, ALU.mult))
        vdrain()
        v_rowrd = P.mark("vector", "sVe")
        vop(lambda E: E.tensor_mul(MSQ, MUr, MUr)); vdrain()
        vop(lambda E: E.tensor_sub(VARr, VARr, MSQ)); vdrain()
        vop(lambda E: E.tensor_scalar(VARr, VARr, 1.0, EPS, ALU.mult, ALU.add)); vdrain()
        recip_nr(f32c[0:1, 3072:3584], VARr, f32c[0:1, 1024:1536])
        v_var = P.mark("vector", "sVe")
        P.wait("scalar", "sVe", v_var)
        s_rs = sop(lambda E: E.sqrt(INVr, f32c[0:1, 3072:3584]), "sSc")
        P.wait("vector", "sSc", s_rs)
        vop(lambda E: E.tensor_copy(bfc[0:1, 3072:3584], MUr))
        vop(lambda E: E.tensor_copy(bfc[0:1, 1024:1536], INVr))
        vdrain()
        v_r = P.mark("vector", "sVe")
        P.wait("tensor", "sVe", v_r)
        mm(PV1[:, :], ones_bf[0:1, :], bfc[0:1, 3072:3584], True, True)
        mm(PD[:, :], ones_bf[0:1, :], bfc[0:1, 1024:1536], True, True)
        t_bc = P.mark("tensor", "sPe")
        P.wait("vector", "sPe", t_bc)
        vop(lambda E: E.tensor_copy(BC1[:, 0:512], PV1[:, :]))
        vop(lambda E: E.tensor_copy(BC2[:, 0:512], PD[:, :]))
        vdrain()
        v_bcc = P.mark("vector", "sVe")
        prev_bcc[0] = v_bcc
        for c in range(8):
            vop(lambda E, b=b, c=c: E.tensor_sub(
                T1[:, 0:512], xslab[b][:, c * 512:(c + 1) * 512], f32c[:, 2048:2560]))
            vdrain()
            vop(lambda E, b=b, c=c: E.tensor_mul(
                xlnsl[b][:, c * 512:(c + 1) * 512], T1[:, 0:512], f32c[:, 3072:3584]))
            vdrain()
        v_xln = P.mark("vector", "sVe")
        slab_free_v[b] = v_xln

        P.wait("tensor", "sVe", v_xln)
        kinds = ("k", "q") if s_i < 4 else ("k",)
        for kind in kinds:
            woff = 8192 if kind == "k" else 0
            bias = bk8_sb if kind == "k" else bq_sb
            scale = 0.125 if kind == "k" else 1.0
            hbm = k_hbm if kind == "k" else q_hbm
            for fc in range(8):
                pA = fc % 2
                P.wait("tensor", "sSc", psP_fr[pA])
                for dc in range(8):
                    mm(PP[pA][:, :],
                       big1[:, woff + dc * 1024 + fc * 128:woff + dc * 1024 + (fc + 1) * 128],
                       xlnsl[b][:, dc * 512:(dc + 1) * 512], dc == 0, dc == 7)
                t_p = P.mark("tensor", "sPe")
                P.wait("scalar", "sPe", t_p)
                P.wait("scalar", "sWr", kst_fr[pA])
                sv = sop(lambda E, pA=pA, bias=bias, fc=fc, scale=scale: E.activation(
                    kstage[pA][:, :], PP[pA][:, :], AF.Identity,
                    bias=bias[:, fc:fc + 1], scale=scale), "sSc")
                psP_fr[pA] = sv
                P.wait("sync", "sSc", sv)
                kst_fr[pA] = dma("sync", hbm.ap()[fc, :, s_i * 512:(s_i + 1) * 512],
                                 kstage[pA][:, :], "sWr")
        for tc in range(4):
            kc_g = s_i * 4 + tc
            P.wait("tensor", "sVe", psV_fr)
            for jp in range(2):
                for dc in range(8):
                    mm(PA[:, jp * 512:(jp + 1) * 512],
                       xlnsl[b][:, dc * 512 + tc * 128:dc * 512 + (tc + 1) * 128],
                       big2[:, dc * 1024 + jp * 512:dc * 1024 + (jp + 1) * 512],
                       dc == 0, dc == 7)
            t_v = P.mark("tensor", "sPe")
            P.wait("vector", "sPe", t_v)
            P.wait("vector", "sWr", vst_fr)
            vop(lambda E: E.tensor_add(vstage[:, :], PA[:, 0:1024], bVbc[:, :]))
            vdrain()
            psV_fr = P.mark("vector", "sVe")
            P.wait("sync", "sVe", psV_fr)
            vst_fr = dma("sync", v_hbm.ap()[kc_g], vstage[:, :], "sWr")
        slab_xln_free[b] = P.val("sPe")

    p1_pe = P.val("sPe")
    p1_ve = P.val("sVe")
    p1_sc = P.val("sSc")
    p1_wr = P.val("sWr")

    ppfree = [("sSc", p1_sc), ("sSc", p1_sc)]
    pp_i = [0]

    def pp_next():
        b = pp_i[0] % 2
        pp_i[0] += 1
        return b

    kv_rdy = [0] * 8
    pair_free = [p1_pe, p1_pe]
    epil_sems = {}
    band_region_rdy = {}
    G = {"sqb_rd": 0}

    def emit_pair_loads(ch):
        pr = ch % 2
        P.wait("gpsimd", "sPe", pair_free[pr])
        if ch < 2:
            P.wait("gpsimd", "sWr", p1_wr)
        dma("gpsimd", kpair[pr][:, :], k_hbm.ap()[ch], "sKV")
        dma("gpsimd", qpair[pr][:, :], q_hbm.ap()[ch], "sKV")
        vsrc = bass.AP(v_hbm, ch * 128, [[D, 128], [128 * D, NKC], [1, 128]])
        vdst = bass.AP(big3, pr * 4096, [[8192, 128], [128, NKC], [1, 128]])
        dma("gpsimd", vdst, vsrc, "sKV")
        kv_rdy[ch] = P.val("sKV")

    def prep_steps(ch):
        pr = ch % 2
        st = {"aux_rdy": 0, "mcol_rdy": 0}
        loc = {"rp_wr": [0] * NQT, "rp_st_fr": [0, 0], "m_wr": 0, "m_ga": 0,
               "pS_fr": [("sPe", 0), ("sPe", 0)], "band_exp": [0] * NQT}

        def sqq_step():
            P.wait("vector", "sKV", kv_rdy[ch])
            P.wait("vector", "sPe", G["sqb_rd"])
            vop(lambda E: E.tensor_mul(SQB[:, 0:2048], qpair[pr][:, :], qpair[pr][:, :]))
            vdrain()
            loc["s_sqq"] = P.mark("vector", "sVe")

        def normq_step(g):
            P.wait("tensor", "sVe", loc["s_sqq"])
            if g == 0:
                b = pp_next()
                P.wait("tensor", *ppfree[b])
                b2 = pp_next()
                P.wait("tensor", *ppfree[b2])
                loc["nq_banks"] = (b, b2)
            b, b2 = loc["nq_banks"]
            for qt in range(g * 4, g * 4 + 4):
                mm(PP[b][:, qt:qt + 1], SQB[0:64, qt * 128:(qt + 1) * 128],
                   ones_bf[0:64, 0:1], True, True)
                mm(PP[b2][:, qt:qt + 1], SQB[64:128, qt * 128:(qt + 1) * 128],
                   ones_bf[64:128, 0:1], True, True)
            t_nq = P.mark("tensor", "sPe")
            G["sqb_rd"] = t_nq
            if g == 3:
                P.wait("vector", "sPe", t_nq)
                vop(lambda E, b=b: E.tensor_copy(nsq[:, 0:16], PP[b][:, 0:16]))
                vop(lambda E, b2=b2: E.tensor_copy(nsq[:, 16:32], PP[b2][:, 0:16]))
                vdrain()
                v_nq = P.mark("vector", "sVe")
                ppfree[b] = ("sVe", v_nq)
                ppfree[b2] = ("sVe", v_nq)
                loc["v_nsq"] = v_nq

        def knorm_step(g):
            cb, sub = g // 2, g % 2
            if sub == 0:
                P.wait("vector", "sPe", G["sqb_rd"])
                vop(lambda E, cb=cb: E.tensor_mul(
                    SQB[:, 0:2048], kpair[pr][:, cb * 2048:(cb + 1) * 2048],
                    kpair[pr][:, cb * 2048:(cb + 1) * 2048]))
                vdrain()
                loc["s_sqk"] = P.mark("vector", "sVe")
            P.wait("tensor", "sVe", loc["s_sqk"])
            for ss in range(2):
                b = pp_next()
                P.wait("tensor", *ppfree[b])
                b2 = pp_next()
                P.wait("tensor", *ppfree[b2])
                off = (sub * 2 + ss) * 512
                mm(PP[b][0:1, :], ones_bf[0:64, 0:1], SQB[0:64, off:off + 512],
                   True, True)
                mm(PP[b2][0:1, :], ones_bf[64:128, 0:1], SQB[64:128, off:off + 512],
                   True, True)
                t_kn = P.mark("tensor", "sPe")
                G["sqb_rd"] = t_kn
                P.wait("vector", "sPe", t_kn)
                first = (g == 0 and ss == 0)
                vop(lambda E, b=b: E.reduce_max(scal[0:1, 16:17], PP[b][0:1, :],
                                                axis=AX.X))
                vop(lambda E, b2=b2: E.reduce_max(scal[0:1, 18:19], PP[b2][0:1, :],
                                                  axis=AX.X))
                vdrain()
                if first:
                    vop(lambda E: E.tensor_copy(scal[0:1, 17:18], scal[0:1, 16:17]))
                    vop(lambda E: E.tensor_copy(scal[0:1, 19:20], scal[0:1, 18:19]))
                else:
                    vop(lambda E: E.tensor_max(scal[0:1, 17:18], scal[0:1, 17:18],
                                               scal[0:1, 16:17]))
                    vop(lambda E: E.tensor_max(scal[0:1, 19:20], scal[0:1, 19:20],
                                               scal[0:1, 18:19]))
                vdrain()
                v_kn = P.mark("vector", "sVe")
                ppfree[b] = ("sVe", v_kn)
                ppfree[b2] = ("sVe", v_kn)
            loc["v_knorm"] = P.val("sVe")

        def c1_step():
            P.wait("scalar", "sVe", loc["v_knorm"])
            sop(lambda E: E.sqrt(scal[0:1, 20:21], scal[0:1, 17:18]))
            sop(lambda E: E.sqrt(scal[0:1, 21:22], scal[0:1, 19:20]))
            sdrain()
            s_c1 = P.mark("scalar", "sSc")
            P.wait("vector", "sSc", s_c1)
            vop(lambda E: E.tensor_add(scal[0:1, 22:23], scal[0:1, 20:21],
                                       scal[0:1, 2:3]))
            vop(lambda E: E.tensor_add(scal[0:1, 23:24], scal[0:1, 21:22],
                                       scal[0:1, 2:3]))
            vdrain()
            vop(lambda E: E.tensor_copy(scb[0:1, 2:4], scal[0:1, 22:24])); vdrain()
            v_c1 = P.mark("vector", "sVe")
            P.wait("tensor", "sVe", v_c1)
            b = pp_next()
            P.wait("tensor", *ppfree[b])
            mm(PP[b][:, 0:2], ones_bf[0:1, :], scb[0:1, 2:4], True, True)
            t_c1 = P.mark("tensor", "sPe")
            P.wait("vector", "sPe", t_c1)
            vop(lambda E, b=b: E.tensor_copy(c1bc[:, :], PP[b][:, 0:2]))
            vdrain()
            v_c1b = P.mark("vector", "sVe")
            ppfree[b] = ("sVe", v_c1b)
            loc["v_c1b"] = v_c1b

        def mchain_step():
            P.wait("scalar", "sVe", loc["v_c1b"])
            sop(lambda E: E.activation(mwork[:, 0:32], nsq[:, 0:32], AF.Sqrt))
            sdrain()
            s_m = P.mark("scalar", "sSc")
            P.wait("vector", "sSc", s_m)
            P.wait("vector", "sVe", v_c2b)
            vop(lambda E: E.tensor_scalar(mwork[:, 0:16], mwork[:, 0:16],
                                          c1bc[:, 0:1], c2bc[:, 0:1],
                                          ALU.mult, ALU.add))
            vop(lambda E: E.tensor_scalar(mwork[:, 16:32], mwork[:, 16:32],
                                          c1bc[:, 1:2], c2bc[:, 0:1],
                                          ALU.mult, ALU.add))
            vdrain()
            vop(lambda E: E.tensor_copy(mhbf[:, :], mwork[:, :])); vdrain()
            vop(lambda E: E.tensor_scalar(mcolneg[:, :], mhbf[:, :], -1.0, None,
                                          ALU.mult))
            vdrain()
            st["mcol_rdy"] = P.mark("vector", "sVe")

        def mio_step():
            P.wait("tensor", "sVe", st["mcol_rdy"])
            b = pp_next()
            P.wait("tensor", *ppfree[b])
            P.op("tensor", lambda E, b=b: E.transpose(
                PP[b][0:32, 0:64].bitcast(BF16), mhbf[:, :], iden_bf[:, :]))
            t_tr = P.mark("tensor", "sPe")
            P.wait("vector", "sPe", t_tr)
            vop(lambda E, b=b: E.tensor_copy(msb[0:32, :],
                                             PP[b][0:32, 0:64].bitcast(BF16)))
            vdrain()
            v_mt = P.mark("vector", "sVe")
            ppfree[b] = ("sVe", v_mt)
            P.wait("gpsimd", "sVe", v_mt)
            for hh in range(2):
                dma("gpsimd", bass.AP(m_hbm, (pr * 2 + hh) * NQ, [[128, 16], [1, 128]]),
                    msb[hh * 16:(hh + 1) * 16, :], "sRp")
            loc["m_wr"] = P.val("sRp")
            P.wait("gpsimd", "sRp", loc["m_wr"])
            for hh in range(2):
                for r in range(2):
                    dma("gpsimd", M2V[hh * 64 + r:hh * 64 + r + 1, 0:2048],
                        bass.AP(m_hbm, (pr * 2 + hh) * NQ, [[NQ, 1], [1, NQ]]), "sGa")
            loc["m_ga"] = P.val("sGa")

        def edge_step(jp):
            P.wait("tensor", "sKV", kv_rdy[ch])
            b = pp_next()
            P.wait("tensor", *ppfree[b])
            b2 = pp_next()
            P.wait("tensor", *ppfree[b2])
            mm(PP[b][0:2, :], wredge_sb[0:64, 0:2],
               qpair[pr][0:64, jp * 512:(jp + 1) * 512], True, True)
            mm(PP[b2][0:66, :], wredge_sb[64:128, 0:66],
               qpair[pr][64:128, jp * 512:(jp + 1) * 512], True, True)
            t_e = P.mark("tensor", "sPe")
            P.wait("vector", "sPe", t_e)
            P.wait("vector", "sGa", loc["m_ga"])
            if jp == 0:
                P.wait("vector", "sPe", pair_free[pr])
            vop(lambda E, b=b, jp=jp: E.scalar_tensor_tensor(
                auxt[pr][0:2, jp * 512:(jp + 1) * 512], PP[b][0:2, :],
                bredge_sb[0:2, 0:1], M2V[0:2, jp * 512:(jp + 1) * 512],
                ALU.add, ALU.subtract))
            vop(lambda E, b2=b2, jp=jp: E.scalar_tensor_tensor(
                auxt[pr][64:66, jp * 512:(jp + 1) * 512], PP[b2][64:66, :],
                bredge_sb[64:66, 0:1], M2V[64:66, jp * 512:(jp + 1) * 512],
                ALU.add, ALU.subtract))
            vdrain()
            v_ed = P.mark("vector", "sVe")
            ppfree[b] = ("sVe", v_ed)
            ppfree[b2] = ("sVe", v_ed)
            if jp == 3:
                st["aux_rdy"] = v_ed

        def rp_step(qt):
            P.wait("tensor", "sKV", kv_rdy[ch])
            b = pp_next()
            P.wait("tensor", *ppfree[b])
            b2 = pp_next()
            P.wait("tensor", *ppfree[b2])
            mm(PP[b][:, :], qpair[pr][0:64, qt * 128:(qt + 1) * 128],
               wrpad2[0:64, :], True, True)
            mm(PP[b2][:, :], qpair[pr][64:128, qt * 128:(qt + 1) * 128],
               wrpad2[64:128, :], True, True)
            t_rp = P.mark("tensor", "sPe")
            P.wait("vector", "sPe", t_rp)
            P.wait("vector", "sRp", loc["rp_st_fr"][0])
            bb = (b, b2)
            if qt in (0, 15):
                mi = 0 if qt == 0 else 1
                ecol = 384 if qt == 0 else 128
                for hh in range(2):
                    vop(lambda E, hh=hh, bb=bb: E.tensor_add(
                        rpstg2[hh][:, :], PP[bb[hh]][:, :], brbc[:, :]))
                vdrain()
                for hh in range(2):
                    vop(lambda E, hh=hh, ecol=ecol: E.tensor_copy(
                        scal[:, 24 + hh:25 + hh], rpstg2[hh][:, ecol:ecol + 1]))
                vdrain()
                for hh in range(2):
                    vop(lambda E, hh=hh: E.tensor_scalar(
                        T1[:, 0:512], rpstg2[hh][:, :], scal[:, 24 + hh:25 + hh],
                        None, ALU.subtract))
                    vdrain()
                    vop(lambda E, mi=mi: E.tensor_mul(
                        T1[:, 0:512], T1[:, 0:512], ovrb[:, mi * 512:(mi + 1) * 512]))
                    vdrain()
                    vop(lambda E, hh=hh: E.tensor_sub(rpstg2[hh][:, :],
                                                      rpstg2[hh][:, :], T1[:, 0:512]))
                    vdrain()
                for hh in range(2):
                    vop(lambda E, hh=hh: E.tensor_copy(rpbf[hh][:, :], rpstg2[hh][:, :]))
                vdrain()
            else:
                for hh in range(2):
                    vop(lambda E, hh=hh, bb=bb: E.tensor_add(
                        rpbf[hh][:, :], PP[bb[hh]][:, :], brbc[:, :]))
                vdrain()
            v_rp = P.mark("vector", "sVe")
            ppfree[b] = ("sVe", v_rp)
            ppfree[b2] = ("sVe", v_rp)
            P.wait("gpsimd", "sVe", v_rp)
            for hh in range(2):
                dma("gpsimd",
                    bass.AP(rp_hbm, ((pr * 2 + hh) * NQT + qt) * 128 * 512,
                            [[512, 128], [1, 512]]),
                    rpbf[hh][:, :], "sRp")
            loc["rp_st_fr"][0] = P.val("sRp")
            loc["rp_wr"][qt] = P.val("sRp")

        def band_step(qt):
            pS = qt % 2
            qb = qt // 4
            P.wait("gpsimd", "sRp", loc["rp_wr"][qt])
            P.wait("gpsimd", "sVe", loc.get(("pos_fr", pS), 0))
            for hh in range(2):
                dma("gpsimd", possb[pS][hh][:, :],
                    bass.AP(rp_hbm, ((pr * 2 + hh) * NQT + qt) * 128 * 512 + 128,
                            [[511, 128], [1, 384]]), "sGa")
            g_pos = P.val("sGa")
            b = pp_next()
            P.wait("tensor", *ppfree[b])
            b2 = pp_next()
            P.wait("tensor", *ppfree[b2])
            if qt == 0:
                mm(PP[b][:, 0:128], qpair[pr][0:64, 0:128],
                   kpair[pr][0:64, 3968:4096], True, True)
                mm(PP[b2][:, 0:128], qpair[pr][64:128, 0:128],
                   kpair[pr][64:128, 3968:4096], True, True)
                mm(PP[b][:, 128:384], qpair[pr][0:64, 0:128],
                   kpair[pr][0:64, 0:256], True, True)
                mm(PP[b2][:, 128:384], qpair[pr][64:128, 0:128],
                   kpair[pr][64:128, 0:256], True, True)
            else:
                j0 = qt * 128 - 128
                mm(PP[b][:, 0:384], qpair[pr][0:64, qt * 128:(qt + 1) * 128],
                   kpair[pr][0:64, j0:j0 + 384], True, True)
                mm(PP[b2][:, 0:384], qpair[pr][64:128, qt * 128:(qt + 1) * 128],
                   kpair[pr][64:128, j0:j0 + 384], True, True)
            t_b = P.mark("tensor", "sPe")
            P.wait("vector", "sPe", t_b)
            P.wait("vector", "sGa", g_pos)
            bb = (b, b2)
            for hh in range(2):
                vop(lambda E, hh=hh, bb=bb, pS=pS: E.tensor_add(
                    PP[bb[hh]][:, 0:384], PP[bb[hh]][:, 0:384], possb[pS][hh][:, :]))
            vdrain()
            v_b = P.mark("vector", "sVe")
            loc[("pos_fr", pS)] = v_b
            P.wait("scalar", "sVe", v_b)
            P.wait("scalar", "sVe", st["mcol_rdy"])
            P.wait("scalar", *loc["pS_fr"][pS])
            for hh in range(2):
                sop(lambda E, hh=hh, bb=bb, pS=pS, qt=qt: E.activation(
                    bandP[pS][hh][:, :], PP[bb[hh]][:, 0:384], AF.Exp,
                    bias=mcolneg[:, hh * 16 + qt:hh * 16 + qt + 1]))
            sdrain()
            s_b = P.mark("scalar", "sSc")
            ppfree[b] = ("sSc", s_b)
            ppfree[b2] = ("sSc", s_b)
            loc["band_exp"][qt] = s_b

        def band_tp_step(qt):
            pS = qt % 2
            qb = qt // 4
            bt = pp_next()
            P.wait("tensor", *ppfree[bt])
            P.wait("tensor", "sSc", loc["band_exp"][qt])
            view = PP[bt][:, 0:384].bitcast(BF16)
            for p in range(3):
                for hh in range(2):
                    t = p * 2 + hh
                    P.op("tensor", lambda E, pS=pS, hh=hh, p=p, t=t:
                         E.transpose(view[:, t * 128:(t + 1) * 128],
                                     bandP[pS][hh][:, p * 128:(p + 1) * 128],
                                     iden_bf[:, :]))
            t_tp = P.mark("tensor", "sPe")
            loc["pS_fr"][pS] = ("sPe", t_tp)
            P.wait("vector", "sPe", t_tp)
            col = qb * 3072 + (qt % 4) * 768
            vop(lambda E, col=col, view=view: E.tensor_copy(
                bandT[:, col:col + 768], view[:, 0:768]))
            vdrain()
            v_bt = P.mark("vector", "sVe")
            ppfree[bt] = ("sVe", v_bt)
            if qt % 4 == 3:
                band_region_rdy[(ch, qb)] = v_bt

        steps = [(0, sqq_step)]
        steps += [(0, lambda g=g: normq_step(g)) for g in range(4)]
        steps += [(0, lambda g=g: knorm_step(g)) for g in range(4)]
        steps += [(0, c1_step), (0, mchain_step), (0, mio_step)]
        steps += [(0, lambda jp=jp: edge_step(jp)) for jp in range(4)]
        steps += [(0, lambda qt=qt: rp_step(qt)) for qt in range(NQT)]
        ab = [(0, lambda qt=0: band_step(0)), (0, lambda qt=1: band_step(1))]
        for k in range(NQT):
            ab.append(((k // 4) * 32 + 32, lambda qt=k: band_tp_step(qt)))
            if k + 2 < NQT:
                ab.append((0, lambda qt=k + 2: band_step(qt)))
        steps += ab
        return steps, st

    def emit_main(ch, st, steps, tail):
        pr = ch % 2
        exp_done = {-2: tail["exp"][0], -1: tail["exp"][1]}
        pvden_done = {-3: tail["pvden"][0], -2: tail["pvden"][1],
                      -1: tail["pvden"][2]}
        pending = []
        nsteps = len(steps)
        next_step_at = [0]
        pieces_total = 128

        def consume_steps(i):
            want = (i + 1) * nsteps // pieces_total
            while next_step_at[0] < want and steps and steps[0][0] <= i:
                steps.pop(0)[1]()
                next_step_at[0] += 1

        P.wait("tensor", "sKV", kv_rdy[ch])
        P.wait("tensor", "sVe", st["aux_rdy"])
        pv_start_guard = tail["pv_free"]
        for qblk in range(4):
            q0 = qblk * 512
            for kc in range(NKC):
                i = qblk * NKC + kc
                pa = PA if i % 2 == 0 else PB
                pc = i % 3
                P.wait("tensor", "sSc", exp_done.get(i - 2, 0))
                mm(pa[:, 0:512], kpair[pr][0:64, kc * 128:(kc + 1) * 128],
                   qpair[pr][0:64, q0:q0 + 512], True, False)
                mm(pa[:, 512:1024], kpair[pr][64:128, kc * 128:(kc + 1) * 128],
                   qpair[pr][64:128, q0:q0 + 512], True, False)
                subr = [(max(lo, q0), min(hi, q0 + 512), v)
                        for (lo, hi, v) in region_table(kc)
                        if max(lo, q0) < min(hi, q0 + 512)]
                for si, (lo, hi, v) in enumerate(subr):
                    last = si == len(subr) - 1
                    mm(pa[:, lo - q0:hi - q0], auxsel[0:64, v * 128:v * 128 + 128],
                       auxt[pr][0:64, lo:hi], False, last)
                    mm(pa[:, 512 + lo - q0:512 + hi - q0],
                       auxsel[64:128, v * 128:v * 128 + 128],
                       auxt[pr][64:128, lo:hi], False, last)
                t_qk = P.mark("tensor", "sPe")
                if len(pending) >= 2:
                    pending.pop(0)()
                P.wait("scalar", "sPe", t_qk)
                P.wait("scalar", "sPe", pvden_done.get(i - 3, 0))
                sop(lambda E, pa=pa, pc=pc: E.activation(
                    pchunk[pc][:, :], pa[:, 0:1024], AF.Exp))
                s_e = P.mark("scalar", "sSc")
                exp_done[i] = s_e
                ncopy = 0
                P.wait("vector", "sSc", s_e)
                P.wait("vector", "sPe", pvden_done.get(i - 3, 0))
                for (qt, p, bq0) in band_pieces_of_kc(kc):
                    if q0 <= bq0 < q0 + 512:
                        col = qblk * 3072 + (qt % 4) * 768 + p * 256
                        off = bq0 - q0
                        vop(lambda E, pc=pc, col=col, off=off: E.tensor_copy(
                            pchunk[pc][:, off:off + 128], bandT[:, col:col + 128]))
                        vop(lambda E, pc=pc, col=col, off=off: E.tensor_copy(
                            pchunk[pc][:, 512 + off:512 + off + 128],
                            bandT[:, col + 128:col + 256]))
                        ncopy += 1
                v_cp = P.mark("vector", "sVe") if ncopy else P.val("sVe")

                def den_pv(i=i, kc=kc, pc=pc, s_e=s_e, v_cp=v_cp, q0=q0,
                           guard=pv_start_guard):
                    P.wait("tensor", "sSc", s_e)
                    P.wait("tensor", "sVe", v_cp)
                    if kc == 0:
                        P.wait("tensor", "sVe", guard)
                    mm(PD[0:1, :], ones_bf[:, 0:1], pchunk[pc][:, 0:512],
                       kc == 0, kc == NKC - 1)
                    mm(PD[64:65, :], ones_bf[:, 0:1], pchunk[pc][:, 512:1024],
                       kc == 0, kc == NKC - 1, tp=(0, 64))
                    mm(PV1[0:64, :], vpair[pr][:, kc * 128:kc * 128 + 64],
                       pchunk[pc][:, 0:512], kc == 0, kc == NKC - 1)
                    mm(PV1[64:128, :], vpair[pr][:, kc * 128 + 64:kc * 128 + 128],
                       pchunk[pc][:, 512:1024], kc == 0, kc == NKC - 1, tp=(0, 64))
                    pvden_done[i] = P.mark("tensor", "sPe")
                pending.append(den_pv)
                consume_steps(i)
            while pending:
                pending.pop(0)()
            t_last = pvden_done[qblk * NKC + NKC - 1]
            P.wait("vector", "sPe", t_last)
            vop(lambda E: E.tensor_copy(T2[0:65, 0:512], PD[0:65, :]))
            vdrain()
            vop(lambda E: E.tensor_copy(T2[:, 512:1024], PV1[:, :]))
            vdrain()
            v_pvc = P.mark("vector", "sVe")
            recip_nr(T1[0:65, 0:512], T2[0:65, 0:512], T1[0:65, 512:1024])
            vop(lambda E: E.tensor_copy(rcbf[0:65, :], T1[0:65, 0:512]))
            vdrain()
            vop(lambda E: E.tensor_sub(rclo[0:65, :], T1[0:65, 0:512],
                                       rcbf[0:65, :]))
            vdrain()
            v_rc = P.mark("vector", "sVe")
            b = pp_next()
            P.wait("tensor", *ppfree[b])
            P.wait("tensor", "sVe", v_rc)
            mm(PP[b][0:64, :], ones_bf[0:1, 0:64], rcbf[0:1, :], True, False)
            mm(PP[b][64:128, :], ones_bf[64:65, 0:64], rcbf[64:65, :], True, False,
               tp=(64, 64))
            mm(PP[b][0:64, :], ones_bf[0:1, 0:64], rclo[0:1, :], False, True)
            mm(PP[b][64:128, :], ones_bf[64:65, 0:64], rclo[64:65, :], False, True,
               tp=(64, 64))
            t_bc = P.mark("tensor", "sPe")
            P.wait("vector", "sPe", t_bc)
            vop(lambda E, b=b: E.tensor_mul(T1[:, 512:1024], T2[:, 512:1024],
                                            PP[b][:, :]))
            vdrain()
            v_pvf = P.mark("vector", "sVe")
            ppfree[b] = ("sVe", v_pvf)
            if ch == 0 and qblk == 0:
                P.wait("vector", "sXl", x_own_ld)
            vop(lambda E, ch=ch, q0=q0: E.tensor_add(
                x_own[:, ch * 2048 + q0:ch * 2048 + q0 + 512],
                x_own[:, ch * 2048 + q0:ch * 2048 + q0 + 512], T1[:, 512:1024]))
            vdrain()
            v_ep = P.mark("vector", "sVe")
            epil_sems[(ch, qblk)] = v_ep
            pv_start_guard = v_pvc
        while steps:
            steps.pop(0)[1]()
        n = 4 * NKC
        tail_out = {"exp": [exp_done[n - 2], exp_done[n - 1]],
                    "pvden": [pvden_done[n - 3], pvden_done[n - 2],
                              pvden_done[n - 1]],
                    "pv_free": pv_start_guard}
        return P.val("sPe"), P.val("sVe"), tail_out

    if n_pairs > 0:
        emit_pair_loads(0)
        if n_pairs > 1:
            emit_pair_loads(1)
        vop(lambda E: E.memset(AXT[:, 0:4096], 0.0))
        vdrain()
        P.mark("vector", "sVe")
        steps0, st0 = prep_steps(0)
        for _, f in steps0:
            f()
        states = {0: st0}
        tail = {"exp": [0, 0], "pvden": [0, 0, 0], "pv_free": 0}
        for ch in range(n_pairs):
            st = states[ch]
            if ch + 1 < n_pairs:
                steps_n, st_n = prep_steps(ch + 1)
                states[ch + 1] = st_n
            else:
                steps_n = []
            main_pe, main_ve, tail = emit_main(ch, st, steps_n, tail)
            pair_free[ch % 2] = main_pe
            if ch + 2 < n_pairs:
                emit_pair_loads(ch + 2)

    p2_ve = P.val("sVe")
    p2_sc = P.val("sSc")
    p2_pe = P.val("sPe")

    PPX = PP

    P.wait("gpsimd", "sPe", p2_pe)
    for c in range(8):
        dma("gpsimd", wrt_sb[:, c * 16:(c + 1) * 16],
            wrtT.ap()[c * 128:(c + 1) * 128, :], "sFw")
    wrt_rdy = P.val("sFw")

    wexp_free = [p2_pe, p2_pe]
    up_fr = [0, 0]
    dn_fr = [0, 0]
    gbc_fr = 0
    glt_fr = [0, 0]
    out_st = 0
    gstore = 0

    MUB = f32c[:, 2048:3072]
    IVB = f32c[:, 3072:4096]

    for th in range(2):
        toff = th * 1024
        P.wait("scalar", "sPe", P.val("sPe"))
        P.wait("scalar", "sVe", p2_ve if th == 0 else P.val("sVe"))
        for c in range(8):
            sop(lambda E, c=c, toff=toff: E.copy(
                xln2[:, c * 1024:(c + 1) * 1024],
                x_own[:, c * 2048 + toff:c * 2048 + toff + 1024]))
        s_cast2 = P.mark("scalar", "sSc")
        sdrain()
        P.wait("tensor", "sSc", s_cast2)
        v_prev = 0
        for h2 in range(2):
            P.wait("scalar", "sPe", P.val("sPe"))
            P.wait("tensor", "sVe", v_prev)
            for c in range(8):
                mm(PV1[0:1, :], ones_bf[:, 0:1],
                   xln2[:, c * 1024 + h2 * 512:c * 1024 + h2 * 512 + 512], c == 0, c == 7)
            t_a = P.mark("tensor", "sPe")
            P.wait("vector", "sPe", t_a)
            v_prev = vop(lambda E, h2=h2: E.tensor_scalar(
                f32c[0:1, 2048 + h2 * 512:2048 + h2 * 512 + 512], PV1[0:1, :],
                1.0 / 1024, None, ALU.mult), "sVe")
            for c in range(8):
                s_last = sop(lambda E, c=c, h2=h2: E.square(
                    sqP3[:, :],
                    xln2[:, c * 1024 + h2 * 512:c * 1024 + h2 * 512 + 512]), "sSc")
                P.wait("tensor", "sSc", s_last)
                mm(PD[0:1, :], ones_bf[:, 0:1], sqP3[:, :], c == 0, c == 7)
                t_s3 = P.mark("tensor", "sPe")
                P.wait("scalar", "sPe", t_s3)
            t_b2 = P.val("sPe")
            P.wait("vector", "sPe", t_b2)
            v_prev = vop(lambda E, h2=h2: E.tensor_scalar(
                f32c[0:1, 3072 + h2 * 512:3072 + h2 * 512 + 512], PD[0:1, :],
                1.0 / 1024, None, ALU.mult), "sVe")
        vdrain()
        MU2 = f32c[0:1, 2048:3072]
        EX2 = f32c[0:1, 3072:4096]
        vop(lambda E: E.tensor_mul(f32c[0:1, 1024:2048], MU2, MU2)); vdrain()
        vop(lambda E: E.tensor_sub(EX2, EX2, f32c[0:1, 1024:2048])); vdrain()
        vop(lambda E: E.tensor_scalar(EX2, EX2, 1.0, EPS, ALU.mult, ALU.add)); vdrain()
        recip_nr(f32c[0:1, 1024:2048], EX2, f32c[0:1, 0:1024])
        v_v2 = P.mark("vector", "sVe")
        P.wait("scalar", "sVe", v_v2)
        s_r2 = sop(lambda E: E.sqrt(f32c[0:1, 0:1024], f32c[0:1, 1024:2048]), "sSc")
        P.wait("vector", "sSc", s_r2)
        vop(lambda E: E.tensor_copy(bfc[0:1, 2048:3072], MU2))
        vop(lambda E: E.tensor_copy(bfc[0:1, 3072:4096], f32c[0:1, 0:1024]))
        vdrain()
        v_r2 = P.mark("vector", "sVe")
        P.wait("tensor", "sVe", v_r2)
        v_prev = 0
        for h2 in range(2):
            P.wait("tensor", "sVe", v_prev)
            mm(PV1[:, :], ones_bf[0:1, :], bfc[0:1, 2048 + h2 * 512:2048 + h2 * 512 + 512],
               True, True)
            mm(PD[:, :], ones_bf[0:1, :], bfc[0:1, 3072 + h2 * 512:3072 + h2 * 512 + 512],
               True, True)
            t_bc2 = P.mark("tensor", "sPe")
            P.wait("vector", "sPe", t_bc2)
            vop(lambda E, h2=h2: E.tensor_copy(MUB[:, h2 * 512:h2 * 512 + 512],
                                               PV1[:, :]))
            v_prev = vop(lambda E, h2=h2: E.tensor_copy(
                IVB[:, h2 * 512:h2 * 512 + 512], PD[:, :]), "sVe")
        vdrain()
        for c in range(8):
            vop(lambda E, c=c, toff=toff: E.tensor_sub(
                T1, x_own[:, c * 2048 + toff:c * 2048 + toff + 1024], MUB))
            vdrain()
            vop(lambda E, c=c: E.tensor_mul(
                xln2[:, c * 1024:(c + 1) * 1024], T1, IVB))
            vdrain()
        v_x2 = P.mark("vector", "sVe")

        P.wait("tensor", "sFw", wrt_rdy)
        P.wait("tensor", "sVe", v_x2)
        v_lgfree = 0
        v_gtfree = 0
        for tc in range(8):
            P.wait("tensor", "sVe", v_lgfree)
            for dc in range(8):
                mm(PA[0:128, 0:16],
                   xln2[:, dc * 1024 + tc * 128:dc * 1024 + (tc + 1) * 128],
                   wrt_sb[:, dc * 16:(dc + 1) * 16], dc == 0, dc == 7)
            t_lg = P.mark("tensor", "sPe")
            P.wait("vector", "sPe", t_lg)
            W0 = rw[:, 0:16]
            W1 = rw[:, 16:32]
            W2 = rw[:, 32:48]
            KEEP = rw[:, 48:64]
            MX = rw[:, 64:65]
            SM = rw[:, 65:66]
            vop(lambda E: E.reduce_max(MX, PA[0:128, 0:16], axis=AX.X)); vdrain()
            v_lgfree = vop(lambda E: E.tensor_scalar(
                W0, PA[0:128, 0:16], MX, None, ALU.subtract), "sVe")
            vdrain()
            v_sf = v_lgfree
            P.wait("scalar", "sVe", v_sf)
            s_ex = sop(lambda E: E.activation(W1, W0, AF.Exp), "sSc")
            P.wait("vector", "sSc", s_ex)
            vop(lambda E: E.reduce_sum(SM, W1, axis=AX.X)); vdrain()
            vop(lambda E: E.reciprocal(MX, SM)); vdrain()
            vop(lambda E: E.tensor_scalar(W1, W1, MX, None, ALU.mult)); vdrain()
            vop(lambda E: E.tensor_copy(W0, W1))
            vop(lambda E: E.memset(KEEP, 0.0)); vdrain()
            for kk in range(4):
                vop(lambda E: E.reduce_max(MX, W0, axis=AX.X)); vdrain()
                vop(lambda E: E.tensor_scalar(W2, W0, MX, None, ALU.is_ge)); vdrain()
                vop(lambda E: E.tensor_scalar(W2, W2, 1e30, -1e30, ALU.mult, ALU.add))
                vdrain()
                vop(lambda E: E.tensor_sub(W2, W2, iota_sb[:, :])); vdrain()
                vop(lambda E: E.reduce_max(MX, W2, axis=AX.X)); vdrain()
                vop(lambda E: E.tensor_scalar(W2, W2, MX, None, ALU.is_ge)); vdrain()
                vop(lambda E: E.tensor_add(KEEP, KEEP, W2)); vdrain()
                vop(lambda E: E.scalar_tensor_tensor(W0, W2, -1e30, W0, ALU.mult, ALU.add))
                vdrain()
            vop(lambda E: E.tensor_mul(W2, W1, KEEP)); vdrain()
            vop(lambda E: E.tensor_copy(scb[:, 0:16], W2)); vdrain()
            v_g = P.mark("vector", "sVe")
            P.wait("tensor", "sVe", v_g)
            P.wait("tensor", "sVe", v_gtfree)
            mm(PV1[0:16, 0:128], scb[:, 0:16], iden_bf[:, :], True, True)
            t_g = P.mark("tensor", "sPe")
            P.wait("vector", "sPe", t_g)
            v_gtfree = vop(lambda E, tc=tc: E.tensor_copy(
                gateT[:, tc * 128:(tc + 1) * 128], PV1[0:16, 0:128]), "sVe")
        v_gT = v_gtfree
        P.wait("sync", "sVe", v_gT)
        P.wait("sync", "sGa", gstore)
        w_g = dma("sync", gate_hbm.ap(), gateT[:, :], "sWr")

        g_row = 0
        t_gbc_prev = 0
        for e in range(NE):
            we = e % 2
            P.wait("gpsimd", "sPe", wexp_free[we])
            for c in range(8):
                dma("gpsimd", wup[we][:, c * 512:(c + 1) * 512],
                    wupT.ap()[e, c * 128:(c + 1) * 128, :], "sFw")
            for c in range(4):
                dma("gpsimd", wdn[we][:, c * 1024:(c + 1) * 1024],
                    wdnT.ap()[e, c * 128:(c + 1) * 128, :], "sFw")
            w_rdy = P.val("sFw")
            routed = e >= 2
            if routed:
                P.wait("sync", "sWr", w_g)
                P.wait("sync", "sPe", t_gbc_prev)
                g_row = dma("sync", grow[:, :], gate_hbm.ap()[e - 2:e - 1, :], "sGa2")
                P.wait("tensor", "sGa2", g_row)
                P.wait("tensor", "sVe", max(gbc_fr, v_gT))
                for jp in range(2):
                    mm([PV1, PD][jp][:, :], ones_bf[0:1, :],
                       grow[0:1, jp * 512:(jp + 1) * 512], True, True)
            t_gbc = P.mark("tensor", "sPe") if routed else P.val("sPe")
            t_gbc_prev = t_gbc
            gstore = g_row
            P.wait("tensor", "sFw", w_rdy)
            for ec in range(4):
                for jp in range(2):
                    pA = (ec * 2 + jp) % 2
                    P.wait("tensor", "sSc", up_fr[pA])
                    P.wait("tensor", "sVe", dn_fr[pA])
                    for dc in range(8):
                        mm(PPX[pA][:, :],
                           wup[we][:, dc * 512 + ec * 128:dc * 512 + (ec + 1) * 128],
                           xln2[:, dc * 1024 + jp * 512:dc * 1024 + (jp + 1) * 512],
                           dc == 0, dc == 7)
                    t_u = P.mark("tensor", "sPe")
                    P.wait("scalar", "sPe", t_u)
                    if routed:
                        P.wait("scalar", "sVe", glt_fr[pA])
                        s_gl = sop(lambda E, pA=pA: E.activation(
                            f32c[:, pA * 512:(pA + 1) * 512], PPX[pA][:, :], AF.Gelu),
                            "sSc")
                        up_fr[pA] = s_gl
                        P.wait("vector", "sSc", s_gl)
                        P.wait("vector", "sPe", t_gbc)
                        v_ac = vop(lambda E, ec=ec, jp=jp, pA=pA: E.tensor_mul(
                            act[:, ec * 1024 + jp * 512:ec * 1024 + (jp + 1) * 512],
                            f32c[:, pA * 512:(pA + 1) * 512], [PV1, PD][jp][:, :]), "sVe")
                        gbc_fr = v_ac
                        glt_fr[pA] = v_ac
                    else:
                        s_gl = sop(lambda E, pA=pA, ec=ec, jp=jp: E.activation(
                            act[:, ec * 1024 + jp * 512:ec * 1024 + (jp + 1) * 512],
                            PPX[pA][:, :], AF.Gelu), "sSc")
                        up_fr[pA] = s_gl
            vdrain()
            v_actdone = vop(lambda E: E.nop(), "sVe")
            sdrain()
            s_actdone = sop(lambda E: E.nop(), "sSc")
            P.wait("tensor", "sVe", v_actdone)
            P.wait("tensor", "sSc", s_actdone)
            for hc in range(8):
                for jp in range(2):
                    pA = (hc * 2 + jp) % 2
                    P.wait("tensor", "sSc", up_fr[pA])
                    P.wait("tensor", "sVe", dn_fr[pA])
                    for ec in range(4):
                        mm(PPX[pA][:, :],
                           wdn[we][:, ec * 1024 + hc * 128:ec * 1024 + (hc + 1) * 128],
                           act[:, ec * 1024 + jp * 512:ec * 1024 + (jp + 1) * 512],
                           ec == 0, ec == 3)
                    t_d = P.mark("tensor", "sPe")
                    P.wait("vector", "sPe", t_d)
                    if e == 0:
                        v_aa = vop(lambda E, pA=pA, hc=hc, jp=jp: E.tensor_copy(
                            acc[:, hc * 1024 + jp * 512:hc * 1024 + (jp + 1) * 512],
                            PPX[pA][:, :]), "sVe")
                    else:
                        v_aa = vop(lambda E, pA=pA, hc=hc, jp=jp: E.tensor_add(
                            acc[:, hc * 1024 + jp * 512:hc * 1024 + (jp + 1) * 512],
                            acc[:, hc * 1024 + jp * 512:hc * 1024 + (jp + 1) * 512],
                            PPX[pA][:, :]), "sVe")
                    dn_fr[pA] = v_aa
            wexp_free[we] = P.val("sPe")

        P.emit("vector", lambda E: E.drain())
        P.wait("vector", "sSt", out_st)
        for c in range(8):
            vop(lambda E, c=c, toff=toff: E.tensor_add(
                T2, acc[:, c * 1024:(c + 1) * 1024],
                x_own[:, c * 2048 + toff:c * 2048 + toff + 1024]))
            vdrain()
            v_o = P.mark("vector", "sVe")
            P.wait("sync", "sVe", v_o)
            out_st = dma("sync", out_d.ap()[c * 128:(c + 1) * 128, toff:toff + 1024],
                         T2, "sSt")
            P.wait("vector", "sSt", out_st)

    P.wait("sync", "sSt", P.val("sSt"))
    P.wait("sync", "sWr", P.val("sWr"))
    P.wait("gpsimd", "sRp", P.val("sRp"))
    P.wait("gpsimd", "sGa", P.val("sGa"))

    with nc.Block() as block:
        P.run(block)
    es.close()
    return nc


_CACHE = {}

def _prep_core_inputs(inputs, c):
    x = inputs["L0_attn_x"]
    b, qh = c // 2, c % 2
    key = "shared"
    if key not in _CACHE:
        Wqkv = np.asarray(inputs["L0_attn_Wqkv"]).reshape(16, 3, 64, 1024)
        WQ = Wqkv[:, 0].reshape(1024, 1024)
        WK = Wqkv[:, 1].reshape(1024, 1024)
        WV = Wqkv[:, 2].reshape(1024, 1024)
        bqkv = np.asarray(inputs["L0_attn_bqkv"]).reshape(16, 3, 64)
        bQ, bK, bV = [np.ascontiguousarray(bqkv[:, i].reshape(1024)) for i in range(3)]
        Wr = np.asarray(inputs["L0_attn_Wr"])
        br = np.asarray(inputs["L0_attn_br"])
        WrT = np.ascontiguousarray(Wr.T)
        wrpadT = np.concatenate([np.repeat(WrT[:, 0:1], 128, 1), WrT,
                                 np.repeat(WrT[:, 256:257], 127, 1)], axis=1)
        brpad = np.concatenate([np.repeat(br[0:1], 128), br,
                                np.repeat(br[256:257], 127)])[None, :]
        wredge4 = np.ascontiguousarray(WrT[:, [256, 0, 256, 0]])
        bredge = np.ascontiguousarray(br[[256, 0, 256, 0]][:, None])
        su = np.asarray(inputs["L0_ffn_shared_up"])
        sd = np.asarray(inputs["L0_ffn_shared_down"])
        ru = np.asarray(inputs["L0_ffn_routed_up"])
        rd = np.asarray(inputs["L0_ffn_routed_down"])
        wupT = np.ascontiguousarray(
            np.concatenate([su, ru], axis=0).transpose(0, 2, 1))
        wdnT = np.ascontiguousarray(
            np.concatenate([sd, rd], axis=0).transpose(0, 2, 1))
        wrtT = np.ascontiguousarray(np.asarray(inputs["L0_ffn_router_w"]).T)
        iden = np.eye(128, dtype=BF)
        iota = np.broadcast_to(np.arange(16, dtype=np.float32), (128, 16)).copy()
        _CACHE[key] = dict(
            wqT=np.ascontiguousarray(WQ.T), wkT=np.ascontiguousarray(WK.T),
            wvT=np.ascontiguousarray(WV.T),
            bq=np.ascontiguousarray(bQ.reshape(8, 128).T),
            bk=np.ascontiguousarray(bK.reshape(8, 128).T),
            bv_row=np.ascontiguousarray(bV[None, :]),
            wrpadT=np.ascontiguousarray(wrpadT), brpad=np.ascontiguousarray(brpad),
            wredge4=wredge4.astype(np.float32), bredge=bredge.astype(np.float32),
            wupT=wupT, wdnT=wdnT, wrtT=wrtT, iden=iden, iota=iota)
    shared = _CACHE[key]

    ovr = np.zeros((2, 128, 512), np.float32)
    if qh == 0:
        for di in range(128):
            ovr[0, di, 128 - di:256 - di] = 1.0
    else:
        for di in range(128):
            ovr[1, di, 384 - di:512] = 1.0
    ovr = ovr.astype(BF)
    auxsel = np.zeros((128, 384), dtype=BF)
    for r0 in (0, 64):
        auxsel[r0 + 0, 0:128] = 1.0
        auxsel[r0 + 1, 128:256] = 1.0
        auxsel[r0 + (0 if qh == 0 else 1), 256:384] = 1.0
    

    xr = np.roll(np.asarray(x[b]), -2048 * qh, axis=0)
    xTc = np.ascontiguousarray(xr.T)

    d = dict(shared)
    d.update(xT=xTc, auxsel_in=auxsel, ovr=ovr)
    return d


_NC = None

def kernel(**inputs):
    global _NC
    in_maps = [_prep_core_inputs(inputs, c) for c in range(8)]
    if _NC is None:
        _NC = build_program()
    res = run_bass_kernel_spmd(_NC, in_maps, core_ids=list(range(8)))
    x = np.asarray(inputs["L0_attn_x"])
    out = np.empty_like(x)
    for c in range(8):
        b, qh = c // 2, c % 2
        out[b, 2048 * qh:2048 * (qh + 1), :] = res.results[c]["out"].T
    _CACHE.clear()
    return out


# version 11
# speedup vs baseline: 2.4338x; 1.0129x over previous
import contextlib
import numpy as np
import ml_dtypes
import concourse.bass as bass
import concourse.mybir as mybir
from concourse.bass_utils import run_bass_kernel_spmd

F32 = mybir.dt.float32
BF16 = mybir.dt.bfloat16
I32 = mybir.dt.int32
AF = mybir.ActivationFunctionType
ALU = mybir.AluOpType
AX = mybir.AxisListType
BF = ml_dtypes.bfloat16

H, HD, D, S, NQ = 16, 64, 1024, 4096, 2048
NKC, NQT, NE, ED = 32, 16, 18, 512
EPS = 1e-5
MSLACK = 24.0

def region_table(kc):
    if kc == 0:
        return [(0, 2048, 1)]
    if 1 <= kc <= 15:
        qa = 128 * (kc - 1)
        cut = min(qa + 384, 2048)
        regs = [(0, cut, 0)]
        if cut < 2048:
            regs.append((cut, 2048, 1))
        return regs
    return [(0, 2048, 2)]

def band_pieces_of_kc(kc):
    out = []
    for p in (0, 1, 2):
        qt = (kc if kc != 31 else -1) + 1 - p
        if 0 <= qt < NQT:
            out.append((qt, p, qt * 128))
    return out

ENGS = ("sync", "gpsimd", "tensor", "vector", "scalar")

DMA_ISSUER = {"sIn": "sync", "sWr": "sync", "sTp": "sync", "sSt": "sync",
              "sGa2": "sync", "sXl": "gpsimd",
              "sKV": "gpsimd", "sFw": "gpsimd", "sRp": "gpsimd", "sGa": "gpsimd"}
RELAY_SEM = {"sync": "sSy", "gpsimd": "sGy"}

class Plan:
    def __init__(self, nc):
        self.nc = nc
        self.stream = {e: [] for e in ENGS}
        self.sems = {}
        self.relay_cache = {}

    def sem(self, es, name):
        h = es.enter_context(self.nc.semaphore(name))
        self.sems[name] = [h, 0]

    def inc(self, name, k):
        s = self.sems[name]
        s[1] += k
        return s[1]

    def val(self, name):
        return self.sems[name][1]

    def emit(self, eng, fn):
        self.stream[eng].append(fn)

    def _rawwait(self, eng, name, v):
        h = self.sems[name][0]
        fn = lambda E, h=h, v=v: E.wait_ge(h, v)
        fn._is_wait = True
        self.emit(eng, fn)

    def wait(self, eng, name, v):
        if v <= 0:
            return
        assert v <= self.sems[name][1], f"wait on future {name}: {v} > {self.sems[name][1]}"
        if name in DMA_ISSUER:
            issuer = DMA_ISSUER[name]
            total = self.sems[name][1]
            if eng == issuer:
                self._rawwait(eng, name, total)
                return
            rs = RELAY_SEM[issuer]
            cached = self.relay_cache.get(name)
            if cached is None or cached[0] != total:
                self._rawwait(issuer, name, total)
                h = self.sems[rs][0]
                rv = self.inc(rs, 1)
                self.emit(issuer, lambda E, h=h: E.nop().then_inc(h, 1))
                self.relay_cache[name] = (total, rv)
            rv = self.relay_cache[name][1]
            self._rawwait(eng, rs, rv)
            return
        self._rawwait(eng, name, v)

    def mark(self, eng, sem):
        assert self.stream[eng], "mark on empty stream"
        last = self.stream[eng][-1]
        assert not getattr(last, "_has_upd", False), f"double mark on {eng}"
        assert not getattr(last, "_is_wait", False), f"mark on wait on {eng}"
        v = self.inc(sem, 1)
        h = self.sems[sem][0]
        self.stream[eng].pop()
        wrapped = lambda E, last=last, h=h: last(E).then_inc(h, 1)
        wrapped._has_upd = True
        self.stream[eng].append(wrapped)
        return v

    def op(self, eng, fn, sem=None, k=1):
        if sem is None:
            self.emit(eng, lambda E, fn=fn: fn(E))
            return None
        v = self.inc(sem, k)
        h = self.sems[sem][0]
        wrapped = lambda E, fn=fn, h=h, k=k: fn(E).then_inc(h, k)
        wrapped._has_upd = True
        self.emit(eng, wrapped)
        return v

    def run(self, block):
        for eng in ENGS:
            ops = self.stream[eng]
            getattr(block, eng)(lambda E, ops=ops: [f(E) for f in ops])


def build_program(n_pairs=8, dbg=False):
    nc = bass.Bass("TRN2", target_bir_lowering=False)
    es = contextlib.ExitStack()

    din = lambda n, s, dt=F32: nc.dram_tensor(n, s, dt, kind="ExternalInput")
    xT = din("xT", [D, S])
    wqT = din("wqT", [D, D]); wkT = din("wkT", [D, D]); wvT = din("wvT", [D, D])
    bq_i = din("bq", [128, 8]); bk_i = din("bk", [128, 8])
    bv_row = din("bv_row", [1, D])
    wrpadT = din("wrpadT", [64, 512]); brpad_i = din("brpad", [1, 512])
    wredge_i = din("wredge4", [64, 4]); bredge_i = din("bredge", [4, 1])
    wupT = din("wupT", [NE, D, ED]); wdnT = din("wdnT", [NE, ED, D])
    wrtT = din("wrtT", [D, 16])
    iden_i = din("iden", [128, 128], BF16)
    iota_i = din("iota", [128, 16])
    auxsel_i = din("auxsel_in", [128, 384], BF16)
    ovr_i = din("ovr", [2, 128, 512], BF16)

    out_d = nc.dram_tensor("out", [D, NQ], F32, kind="ExternalOutput")

    k_hbm = nc.dram_tensor("k_hbm", [8, 128, S], BF16)
    q_hbm = nc.dram_tensor("q_hbm", [8, 128, NQ], BF16)
    v_hbm = nc.dram_tensor("v_hbm", [NKC, 128, D], BF16)
    rp_hbm = nc.dram_tensor("rp_hbm", [2, 2, NQT, 128, 512], BF16)
    m_hbm = nc.dram_tensor("m_hbm", [2, 2, NQ], BF16)
    gate_hbm = nc.dram_tensor("gate_hbm", [16, 1024], BF16)

    sbt = lambda n, s, dt: es.enter_context(nc.sbuf_tensor(n, s, dt))
    x_own = sbt("x_own", [128, 16384], F32)
    big1 = sbt("big1", [128, 16384], BF16)
    big2 = sbt("big2", [128, 8192], BF16)
    big3 = sbt("big3", [128, 8192], BF16)
    f32s = sbt("f32s", [128, 8192], F32)
    f32c = sbt("f32c", [128, 4096], F32)
    bfc = sbt("bfc", [128, 4096], BF16)
    rw = sbt("rw", [128, 80], F32)
    ones_bf = sbt("ones_bf", [128, 128], BF16)
    iden_bf = sbt("iden_bf", [128, 128], BF16)
    iota_sb = sbt("iota_sb", [128, 16], F32)
    auxsel = sbt("auxsel", [128, 384], BF16)
    ovrb = sbt("ovrb", [128, 1024], BF16)
    bq_sb = sbt("bq_sb", [128, 8], F32)
    bk8_sb = sbt("bk8_sb", [128, 8], F32)
    bVbc = sbt("bVbc", [128, 1024], BF16)
    brbc = sbt("brbc", [128, 512], BF16)
    wrpad2 = sbt("wrpad2", [128, 512], BF16)
    wredge_sb = sbt("wredge_sb", [128, 66], BF16)
    bredge_sb = sbt("bredge_sb", [128, 1], F32)
    scal = sbt("scal", [128, 32], F32)
    scb = sbt("scb", [128, 32], BF16)
    nsq = sbt("nsq", [128, 32], F32)
    mwork = sbt("mwork", [128, 32], F32)
    mhbf = sbt("mhbf", [128, 32], BF16)
    mcolneg = sbt("mcolneg", [128, 32], F32)
    c1bc = sbt("c1bc", [128, 2], F32)
    c2bc = sbt("c2bc", [128, 1], F32)

    T1 = f32c[:, 0:1024]
    T2 = f32c[:, 1024:2048]
    BC1 = f32c[:, 2048:3072]
    BC2 = f32c[:, 3072:4096]

    xslab = [f32s[:, 0:4096], f32s[:, 4096:8192]]
    xlnsl = [big3[:, 0:4096], big3[:, 4096:8192]]
    kstage = [bfc[:, 0:512], bfc[:, 512:1024]]
    vstage = bfc[:, 1536:2560]
    sqP1 = bfc[:, 3072:3584]
    QP = bass.AP(f32s, 1024, [[8192, 128], [1, 2048]]).bitcast(BF16)
    AXT = bass.AP(f32s, 3072, [[8192, 128], [1, 2048]]).bitcast(BF16)
    M2V = bass.AP(f32s, 5120, [[8192, 128], [1, 1024]]).bitcast(BF16)
    SQB = bass.AP(f32s, 6144, [[8192, 128], [1, 1024]]).bitcast(BF16)
    rpstg2 = [f32s[:, 0:512], f32s[:, 512:1024]]
    qpair = [QP[:, 0:2048], QP[:, 2048:4096]]
    auxt = [AXT[:, 0:2048], AXT[:, 2048:4096]]
    kpair = [big2[:, 0:4096], big2[:, 4096:8192]]
    vpair = [big3[:, 0:4096], big3[:, 4096:8192]]
    bandT = big1[:, 0:12288]
    bandP = [[big1[:, 12288 + (2 * pS + hh) * 384:12288 + (2 * pS + hh + 1) * 384]
              for hh in range(2)] for pS in range(2)]
    msb = big1[:, 13824:13952]
    rpbf = [big1[:, 13952 + hh * 512:13952 + (hh + 1) * 512] for hh in range(2)]
    rcbf = big1[:, 14976:15488]
    rclo = big1[:, 15488:16000]
    possb = [[bfc[:, (2 * pS + hh) * 384:(2 * pS + hh + 1) * 384] for hh in range(2)]
             for pS in range(2)]
    RCV = bass.AP(f32s, 7168, [[8192, 128], [1, 1024]]).bitcast(BF16)
    rcbf = RCV[:, 0:512]
    rclo = RCV[:, 512:1024]
    pchunk = [bfc[:, 1536 + i * 1024:1536 + (i + 1) * 1024] for i in range(2)] + \
             [big1[:, 14976:16000]]
    xln2 = big1[:, 0:8192]
    act = big1[:, 8192:12288]
    wrt_sb = big1[:, 12288:12416]
    wup = [big2[:, 0:4096], big2[:, 4096:8192]]
    wdn = [big3[:, 0:4096], big3[:, 4096:8192]]
    acc = f32s
    grow = bfc[0:1, 0:1024]
    gateT = bfc[0:16, 1024:2048]
    sqP3 = bfc[:, 2048:2560]

    PA = es.enter_context(nc.psum_tensor("PA", [128, 1024], F32))
    PB = es.enter_context(nc.psum_tensor("PB", [128, 1024], F32))
    PV1 = es.enter_context(nc.psum_tensor("PV1", [128, 512], F32))
    PD = es.enter_context(nc.psum_tensor("PD", [128, 512], F32))
    PP = [es.enter_context(nc.psum_tensor("PP0", [128, 512], F32)),
          es.enter_context(nc.psum_tensor("PP1", [128, 512], F32))]

    P = Plan(nc)
    for s in ("sIn", "sWr", "sTp", "sSt", "sGa2", "sXl", "sKV", "sFw", "sRp", "sGa",
              "sPe", "sVe", "sSc", "sSy", "sGy"):
        P.sem(es, s)

    def dma(eng, dst, src, sem="sIn", slow=False):
        return P.op(eng, lambda E, d=dst, s=src, sl=slow:
                    E.dma_start(out=d, in_=s, allow_slow_non_contiguous=sl), sem, 16)

    def mm(dst, lhsT, rhs, start, stop, sem=None, tp=None):
        return P.op("tensor", lambda E, d=dst, l=lhsT, r=rhs, st=start, sp=stop, tp=tp:
                    E.matmul(d, l, r, start=st, stop=sp, tile_position=tp,
                             skip_group_check=True), sem, 1)

    def vop(fn, sem=None):
        return P.op("vector", fn, sem, 1)

    def sop(fn, sem=None):
        return P.op("scalar", fn, sem, 1)

    def vdrain():
        P.emit("vector", lambda E: E.drain())

    def sdrain():
        P.emit("scalar", lambda E: E.drain())

    def recip_nr(dst, src, ws):
        vop(lambda E: E.reciprocal(dst, src)); vdrain()

    dma("sync", iden_bf[:, :], iden_i.ap())
    dma("sync", iota_sb[:, :], iota_i.ap())
    dma("sync", auxsel[:, :], auxsel_i.ap())
    dma("sync", ovrb[:, 0:512], ovr_i.ap()[0])
    dma("sync", ovrb[:, 512:1024], ovr_i.ap()[1])
    dma("sync", bq_sb[:, :], bq_i.ap())
    dma("sync", bk8_sb[:, :], bk_i.ap())
    dma("sync", f32c[0:1, 2048:3072], bv_row.ap())
    dma("sync", bredge_sb[0:2, 0:1], bredge_i.ap()[0:2])
    dma("sync", bredge_sb[64:66, 0:1], bredge_i.ap()[2:4])
    dma("sync", f32c[0:1, 3072:3584], brpad_i.ap())
    dma("gpsimd", wredge_sb[0:64, 0:2], wredge_i.ap()[:, 0:2], "sFw")
    dma("gpsimd", wredge_sb[64:128, 64:66], wredge_i.ap()[:, 2:4], "sFw")
    for r0 in (0, 64):
        dma("gpsimd", wrpad2[r0:r0 + 64, :], wrpadT.ap(), "sFw")
    for c in range(8):
        dma("gpsimd", big1[:, c * 1024:(c + 1) * 1024], wqT.ap()[c * 128:(c + 1) * 128, :],
            "sFw")
        dma("gpsimd", big1[:, 8192 + c * 1024:8192 + (c + 1) * 1024],
            wkT.ap()[c * 128:(c + 1) * 128, :], "sFw")
        dma("gpsimd", big2[:, c * 1024:(c + 1) * 1024], wvT.ap()[c * 128:(c + 1) * 128, :],
            "sFw")
    v_in = P.val("sIn")
    v_gw = P.val("sFw")

    P.wait("vector", "sIn", v_in)
    vop(lambda E: E.memset(ones_bf[:, :], 1.0))
    vop(lambda E: E.memset(wredge_sb[64:128, 0:64], 0.0))
    vop(lambda E: E.tensor_scalar(bk8_sb[:, :], bk8_sb[:, :], 0.125, None, ALU.mult))
    vop(lambda E: E.tensor_copy(bfc[0:1, 3072:4096], f32c[0:1, 2048:3072]))
    vop(lambda E: E.tensor_copy(bfc[0:1, 1024:1536], f32c[0:1, 3072:3584]))
    vdrain()
    v_rows0 = P.mark("vector", "sVe")
    P.wait("tensor", "sVe", v_rows0)
    for jp in range(2):
        mm(PA[:, jp * 512:(jp + 1) * 512], ones_bf[0:1, :],
           bfc[0:1, 3072 + jp * 512:3072 + (jp + 1) * 512], True, True)
    mm(PP[0][:, :], ones_bf[0:1, :], bfc[0:1, 1024:1536], True, True)
    t_bc0 = P.mark("tensor", "sPe")
    P.wait("vector", "sPe", t_bc0)
    vop(lambda E: E.tensor_copy(bVbc[:, :], PA[:, 0:1024]))
    vop(lambda E: E.tensor_copy(brbc[:, :], PP[0][:, :]))
    vdrain()
    v_bc0 = P.mark("vector", "sVe")

    P.wait("scalar", "sIn", v_in)
    P.wait("scalar", "sFw", v_gw)
    P.wait("scalar", "sPe", t_bc0)
    s_wr2 = sop(lambda E: E.square(sqP1[0:64, :], wrpad2[0:64, :]), "sSc")
    P.wait("tensor", "sSc", s_wr2)
    P.wait("tensor", "sVe", v_rows0)
    mm(PP[1][0:1, :], ones_bf[0:64, 0:1], sqP1[0:64, :], True, True)
    t_wr = P.mark("tensor", "sPe")
    P.wait("vector", "sPe", t_wr)
    vop(lambda E: E.reduce_max(scal[0:1, 0:1], PP[1][0:1, 128:385], axis=AX.X)); vdrain()
    v_mx = P.mark("vector", "sVe")
    P.wait("scalar", "sVe", v_mx)
    s_sq = sop(lambda E: E.sqrt(scal[0:1, 2:3], scal[0:1, 0:1]), "sSc")
    P.wait("vector", "sSc", s_sq)
    vop(lambda E: E.tensor_scalar(f32c[0:1, 1024:1536], f32c[0:1, 3072:3584],
                                  -1.0, None, ALU.mult)); vdrain()
    vop(lambda E: E.tensor_max(f32c[0:1, 1024:1536], f32c[0:1, 1024:1536],
                               f32c[0:1, 3072:3584])); vdrain()
    vop(lambda E: E.reduce_max(scal[0:1, 3:4], f32c[0:1, 1152:1409],
                               axis=AX.X)); vdrain()
    vop(lambda E: E.tensor_scalar(scal[0:1, 5:6], scal[0:1, 3:4], 1.0, -MSLACK,
                                  ALU.mult, ALU.add)); vdrain()
    vop(lambda E: E.tensor_copy(scb[0:1, 0:1], scal[0:1, 5:6])); vdrain()
    v_c2 = P.mark("vector", "sVe")
    P.wait("tensor", "sVe", v_c2)
    mm(PP[1][:, 0:1], ones_bf[0:1, :], scb[0:1, 0:1], True, True)
    t_c2 = P.mark("tensor", "sPe")
    P.wait("vector", "sPe", t_c2)
    vop(lambda E: E.tensor_copy(c2bc[:, :], PP[1][:, 0:1]))
    vdrain()
    v_c2b = P.mark("vector", "sVe")

    P.wait("tensor", "sFw", v_gw)

    slab_free_v = [0, 0]
    slab_xln_free = [0, 0]
    psP_fr = [0, 0]
    psV_fr = v_bc0
    kst_fr = [0, 0]
    vst_fr = 0
    x_own_ld = 0
    prev_bcc = [v_c2b]

    for s_i in range(8):
        b = s_i % 2
        P.wait("gpsimd", "sVe", slab_free_v[b])
        for c in range(8):
            dma("gpsimd", xslab[b][:, c * 512:(c + 1) * 512],
                xT.ap()[c * 128:(c + 1) * 128, s_i * 512:(s_i + 1) * 512], "sXl")
        if s_i < 4:
            for c in range(8):
                dma("gpsimd", x_own[:, c * 2048 + s_i * 512:c * 2048 + (s_i + 1) * 512],
                    xT.ap()[c * 128:(c + 1) * 128, s_i * 512:(s_i + 1) * 512], "sXl")
            x_own_ld = P.val("sXl")
        ld_done = P.val("sXl")

        P.wait("scalar", "sXl", ld_done)
        P.wait("scalar", "sPe", slab_xln_free[b])
        for c in range(8):
            sop(lambda E, b=b, c=c: E.copy(
                xlnsl[b][:, c * 512:(c + 1) * 512], xslab[b][:, c * 512:(c + 1) * 512]))
        s_cast = P.mark("scalar", "sSc")
        sdrain()
        P.wait("tensor", "sSc", s_cast)
        P.wait("tensor", "sVe", prev_bcc[0])
        for c in range(8):
            mm(PV1[0:1, :], ones_bf[:, 0:1], xlnsl[b][:, c * 512:(c + 1) * 512],
               c == 0, c == 7)
        for c in range(8):
            s_sq = sop(lambda E, b=b, c=c: E.square(
                sqP1[:, :], xlnsl[b][:, c * 512:(c + 1) * 512]), "sSc")
            P.wait("tensor", "sSc", s_sq)
            mm(PD[0:1, :], ones_bf[:, 0:1], sqP1[:, :], c == 0, c == 7)
            t_sqc = P.mark("tensor", "sPe")
            P.wait("scalar", "sPe", t_sqc)
        t_st = P.val("sPe")
        P.wait("vector", "sPe", t_st)
        MUr = f32c[0:1, 2048:2560]
        VARr = f32c[0:1, 2560:3072]
        MSQ = f32c[0:1, 3072:3584]
        INVr = f32c[0:1, 3584:4096]
        vop(lambda E: E.tensor_scalar(MUr, PV1[0:1, :], 1.0 / 1024, None, ALU.mult))
        vop(lambda E: E.tensor_scalar(VARr, PD[0:1, :], 1.0 / 1024, None, ALU.mult))
        vdrain()
        v_rowrd = P.mark("vector", "sVe")
        vop(lambda E: E.tensor_mul(MSQ, MUr, MUr)); vdrain()
        vop(lambda E: E.tensor_sub(VARr, VARr, MSQ)); vdrain()
        vop(lambda E: E.tensor_scalar(VARr, VARr, 1.0, EPS, ALU.mult, ALU.add)); vdrain()
        recip_nr(f32c[0:1, 3072:3584], VARr, f32c[0:1, 1024:1536])
        v_var = P.mark("vector", "sVe")
        P.wait("scalar", "sVe", v_var)
        s_rs = sop(lambda E: E.sqrt(INVr, f32c[0:1, 3072:3584]), "sSc")
        P.wait("vector", "sSc", s_rs)
        vop(lambda E: E.tensor_copy(bfc[0:1, 3072:3584], MUr))
        vop(lambda E: E.tensor_copy(bfc[0:1, 1024:1536], INVr))
        vdrain()
        v_r = P.mark("vector", "sVe")
        P.wait("tensor", "sVe", v_r)
        mm(PV1[:, :], ones_bf[0:1, :], bfc[0:1, 3072:3584], True, True)
        mm(PD[:, :], ones_bf[0:1, :], bfc[0:1, 1024:1536], True, True)
        t_bc = P.mark("tensor", "sPe")
        P.wait("vector", "sPe", t_bc)
        vop(lambda E: E.tensor_copy(BC1[:, 0:512], PV1[:, :]))
        vop(lambda E: E.tensor_copy(BC2[:, 0:512], PD[:, :]))
        vdrain()
        v_bcc = P.mark("vector", "sVe")
        prev_bcc[0] = v_bcc
        for c in range(8):
            vop(lambda E, b=b, c=c: E.tensor_sub(
                T1[:, 0:512], xslab[b][:, c * 512:(c + 1) * 512], f32c[:, 2048:2560]))
            vdrain()
            vop(lambda E, b=b, c=c: E.tensor_mul(
                xlnsl[b][:, c * 512:(c + 1) * 512], T1[:, 0:512], f32c[:, 3072:3584]))
            vdrain()
        v_xln = P.mark("vector", "sVe")
        slab_free_v[b] = v_xln

        P.wait("tensor", "sVe", v_xln)
        kinds = ("k", "q") if s_i < 4 else ("k",)
        for kind in kinds:
            woff = 8192 if kind == "k" else 0
            bias = bk8_sb if kind == "k" else bq_sb
            scale = 0.125 if kind == "k" else 1.0
            hbm = k_hbm if kind == "k" else q_hbm
            for fc in range(8):
                pA = fc % 2
                P.wait("tensor", "sSc", psP_fr[pA])
                for dc in range(8):
                    mm(PP[pA][:, :],
                       big1[:, woff + dc * 1024 + fc * 128:woff + dc * 1024 + (fc + 1) * 128],
                       xlnsl[b][:, dc * 512:(dc + 1) * 512], dc == 0, dc == 7)
                t_p = P.mark("tensor", "sPe")
                P.wait("scalar", "sPe", t_p)
                P.wait("scalar", "sWr", kst_fr[pA])
                sv = sop(lambda E, pA=pA, bias=bias, fc=fc, scale=scale: E.activation(
                    kstage[pA][:, :], PP[pA][:, :], AF.Identity,
                    bias=bias[:, fc:fc + 1], scale=scale), "sSc")
                psP_fr[pA] = sv
                P.wait("sync", "sSc", sv)
                kst_fr[pA] = dma("sync", hbm.ap()[fc, :, s_i * 512:(s_i + 1) * 512],
                                 kstage[pA][:, :], "sWr")
        for tc in range(4):
            kc_g = s_i * 4 + tc
            P.wait("tensor", "sVe", psV_fr)
            for jp in range(2):
                for dc in range(8):
                    mm(PA[:, jp * 512:(jp + 1) * 512],
                       xlnsl[b][:, dc * 512 + tc * 128:dc * 512 + (tc + 1) * 128],
                       big2[:, dc * 1024 + jp * 512:dc * 1024 + (jp + 1) * 512],
                       dc == 0, dc == 7)
            t_v = P.mark("tensor", "sPe")
            P.wait("vector", "sPe", t_v)
            P.wait("vector", "sWr", vst_fr)
            vop(lambda E: E.tensor_add(vstage[:, :], PA[:, 0:1024], bVbc[:, :]))
            vdrain()
            psV_fr = P.mark("vector", "sVe")
            P.wait("sync", "sVe", psV_fr)
            vst_fr = dma("sync", v_hbm.ap()[kc_g], vstage[:, :], "sWr")
        slab_xln_free[b] = P.val("sPe")

    p1_pe = P.val("sPe")
    p1_ve = P.val("sVe")
    p1_sc = P.val("sSc")
    p1_wr = P.val("sWr")

    ppfree = [("sSc", p1_sc), ("sSc", p1_sc)]
    pp_i = [0]

    def pp_next():
        b = pp_i[0] % 2
        pp_i[0] += 1
        return b

    kv_rdy = [0] * 8
    pair_free = [p1_pe, p1_pe]
    epil_sems = {}
    band_region_rdy = {}
    G = {"sqb_rd": 0}

    def emit_pair_loads(ch):
        pr = ch % 2
        P.wait("gpsimd", "sPe", pair_free[pr])
        if ch < 2:
            P.wait("gpsimd", "sWr", p1_wr)
        dma("gpsimd", kpair[pr][:, :], k_hbm.ap()[ch], "sKV")
        dma("gpsimd", qpair[pr][:, :], q_hbm.ap()[ch], "sKV")
        vsrc = bass.AP(v_hbm, ch * 128, [[D, 128], [128 * D, NKC], [1, 128]])
        vdst = bass.AP(big3, pr * 4096, [[8192, 128], [128, NKC], [1, 128]])
        dma("gpsimd", vdst, vsrc, "sKV")
        kv_rdy[ch] = P.val("sKV")

    def prep_steps(ch):
        pr = ch % 2
        st = {"aux_rdy": 0, "mcol_rdy": 0}
        loc = {"rp_wr": [0] * NQT, "rp_st_fr": [0, 0], "m_wr": 0, "m_ga": 0,
               "pS_fr": [("sPe", 0), ("sPe", 0)], "band_exp": [0] * NQT}

        def sqq_step():
            P.wait("vector", "sKV", kv_rdy[ch])
            P.wait("vector", "sPe", G["sqb_rd"])
            vop(lambda E: E.tensor_mul(SQB[:, 0:2048], qpair[pr][:, :], qpair[pr][:, :]))
            vdrain()
            loc["s_sqq"] = P.mark("vector", "sVe")

        def normq_step(g):
            P.wait("tensor", "sVe", loc["s_sqq"])
            if g == 0:
                b = pp_next()
                P.wait("tensor", *ppfree[b])
                b2 = pp_next()
                P.wait("tensor", *ppfree[b2])
                loc["nq_banks"] = (b, b2)
            b, b2 = loc["nq_banks"]
            for qt in range(g * 4, g * 4 + 4):
                mm(PP[b][:, qt:qt + 1], SQB[0:64, qt * 128:(qt + 1) * 128],
                   ones_bf[0:64, 0:1], True, True)
                mm(PP[b2][:, qt:qt + 1], SQB[64:128, qt * 128:(qt + 1) * 128],
                   ones_bf[64:128, 0:1], True, True)
            t_nq = P.mark("tensor", "sPe")
            G["sqb_rd"] = t_nq
            if g == 3:
                P.wait("vector", "sPe", t_nq)
                vop(lambda E, b=b: E.tensor_copy(nsq[:, 0:16], PP[b][:, 0:16]))
                vop(lambda E, b2=b2: E.tensor_copy(nsq[:, 16:32], PP[b2][:, 0:16]))
                vdrain()
                v_nq = P.mark("vector", "sVe")
                ppfree[b] = ("sVe", v_nq)
                ppfree[b2] = ("sVe", v_nq)
                loc["v_nsq"] = v_nq

        def knorm_step(g):
            cb, sub = g // 2, g % 2
            if sub == 0:
                P.wait("vector", "sPe", G["sqb_rd"])
                vop(lambda E, cb=cb: E.tensor_mul(
                    SQB[:, 0:2048], kpair[pr][:, cb * 2048:(cb + 1) * 2048],
                    kpair[pr][:, cb * 2048:(cb + 1) * 2048]))
                vdrain()
                loc["s_sqk"] = P.mark("vector", "sVe")
            P.wait("tensor", "sVe", loc["s_sqk"])
            for ss in range(2):
                b = pp_next()
                P.wait("tensor", *ppfree[b])
                b2 = pp_next()
                P.wait("tensor", *ppfree[b2])
                off = (sub * 2 + ss) * 512
                mm(PP[b][0:1, :], ones_bf[0:64, 0:1], SQB[0:64, off:off + 512],
                   True, True)
                mm(PP[b2][0:1, :], ones_bf[64:128, 0:1], SQB[64:128, off:off + 512],
                   True, True)
                t_kn = P.mark("tensor", "sPe")
                G["sqb_rd"] = t_kn
                P.wait("vector", "sPe", t_kn)
                first = (g == 0 and ss == 0)
                vop(lambda E, b=b: E.reduce_max(scal[0:1, 16:17], PP[b][0:1, :],
                                                axis=AX.X))
                vop(lambda E, b2=b2: E.reduce_max(scal[0:1, 18:19], PP[b2][0:1, :],
                                                  axis=AX.X))
                vdrain()
                if first:
                    vop(lambda E: E.tensor_copy(scal[0:1, 17:18], scal[0:1, 16:17]))
                    vop(lambda E: E.tensor_copy(scal[0:1, 19:20], scal[0:1, 18:19]))
                else:
                    vop(lambda E: E.tensor_max(scal[0:1, 17:18], scal[0:1, 17:18],
                                               scal[0:1, 16:17]))
                    vop(lambda E: E.tensor_max(scal[0:1, 19:20], scal[0:1, 19:20],
                                               scal[0:1, 18:19]))
                vdrain()
                v_kn = P.mark("vector", "sVe")
                ppfree[b] = ("sVe", v_kn)
                ppfree[b2] = ("sVe", v_kn)
            loc["v_knorm"] = P.val("sVe")

        def c1_step():
            P.wait("scalar", "sVe", loc["v_knorm"])
            sop(lambda E: E.sqrt(scal[0:1, 20:21], scal[0:1, 17:18]))
            sop(lambda E: E.sqrt(scal[0:1, 21:22], scal[0:1, 19:20]))
            sdrain()
            s_c1 = P.mark("scalar", "sSc")
            P.wait("vector", "sSc", s_c1)
            vop(lambda E: E.tensor_add(scal[0:1, 22:23], scal[0:1, 20:21],
                                       scal[0:1, 2:3]))
            vop(lambda E: E.tensor_add(scal[0:1, 23:24], scal[0:1, 21:22],
                                       scal[0:1, 2:3]))
            vdrain()
            vop(lambda E: E.tensor_copy(scb[0:1, 2:4], scal[0:1, 22:24])); vdrain()
            v_c1 = P.mark("vector", "sVe")
            P.wait("tensor", "sVe", v_c1)
            b = pp_next()
            P.wait("tensor", *ppfree[b])
            mm(PP[b][:, 0:2], ones_bf[0:1, :], scb[0:1, 2:4], True, True)
            t_c1 = P.mark("tensor", "sPe")
            P.wait("vector", "sPe", t_c1)
            vop(lambda E, b=b: E.tensor_copy(c1bc[:, :], PP[b][:, 0:2]))
            vdrain()
            v_c1b = P.mark("vector", "sVe")
            ppfree[b] = ("sVe", v_c1b)
            loc["v_c1b"] = v_c1b

        def mchain_step():
            P.wait("scalar", "sVe", loc["v_c1b"])
            sop(lambda E: E.activation(mwork[:, 0:32], nsq[:, 0:32], AF.Sqrt))
            sdrain()
            s_m = P.mark("scalar", "sSc")
            P.wait("vector", "sSc", s_m)
            P.wait("vector", "sVe", v_c2b)
            vop(lambda E: E.tensor_scalar(mwork[:, 0:16], mwork[:, 0:16],
                                          c1bc[:, 0:1], c2bc[:, 0:1],
                                          ALU.mult, ALU.add))
            vop(lambda E: E.tensor_scalar(mwork[:, 16:32], mwork[:, 16:32],
                                          c1bc[:, 1:2], c2bc[:, 0:1],
                                          ALU.mult, ALU.add))
            vdrain()
            vop(lambda E: E.tensor_copy(mhbf[:, :], mwork[:, :])); vdrain()
            vop(lambda E: E.tensor_scalar(mcolneg[:, :], mhbf[:, :], -1.0, None,
                                          ALU.mult))
            vdrain()
            st["mcol_rdy"] = P.mark("vector", "sVe")

        def mio_step():
            P.wait("tensor", "sVe", st["mcol_rdy"])
            b = pp_next()
            P.wait("tensor", *ppfree[b])
            P.op("tensor", lambda E, b=b: E.transpose(
                PP[b][0:32, 0:64].bitcast(BF16), mhbf[:, :], iden_bf[:, :]))
            t_tr = P.mark("tensor", "sPe")
            P.wait("vector", "sPe", t_tr)
            vop(lambda E, b=b: E.tensor_copy(msb[0:32, :],
                                             PP[b][0:32, 0:64].bitcast(BF16)))
            vdrain()
            v_mt = P.mark("vector", "sVe")
            ppfree[b] = ("sVe", v_mt)
            P.wait("gpsimd", "sVe", v_mt)
            for hh in range(2):
                dma("gpsimd", bass.AP(m_hbm, (pr * 2 + hh) * NQ, [[128, 16], [1, 128]]),
                    msb[hh * 16:(hh + 1) * 16, :], "sRp")
            loc["m_wr"] = P.val("sRp")
            P.wait("gpsimd", "sRp", loc["m_wr"])
            for hh in range(2):
                for r in range(2):
                    dma("gpsimd", M2V[hh * 64 + r:hh * 64 + r + 1, 0:2048],
                        bass.AP(m_hbm, (pr * 2 + hh) * NQ, [[NQ, 1], [1, NQ]]), "sGa")
            loc["m_ga"] = P.val("sGa")

        def edge_step(jp):
            P.wait("tensor", "sKV", kv_rdy[ch])
            b = pp_next()
            P.wait("tensor", *ppfree[b])
            b2 = pp_next()
            P.wait("tensor", *ppfree[b2])
            mm(PP[b][0:2, :], wredge_sb[0:64, 0:2],
               qpair[pr][0:64, jp * 512:(jp + 1) * 512], True, True)
            mm(PP[b2][0:66, :], wredge_sb[64:128, 0:66],
               qpair[pr][64:128, jp * 512:(jp + 1) * 512], True, True)
            t_e = P.mark("tensor", "sPe")
            P.wait("vector", "sPe", t_e)
            P.wait("vector", "sGa", loc["m_ga"])
            if jp == 0:
                P.wait("vector", "sPe", pair_free[pr])
            vop(lambda E, b=b, jp=jp: E.scalar_tensor_tensor(
                auxt[pr][0:2, jp * 512:(jp + 1) * 512], PP[b][0:2, :],
                bredge_sb[0:2, 0:1], M2V[0:2, jp * 512:(jp + 1) * 512],
                ALU.add, ALU.subtract))
            vop(lambda E, b2=b2, jp=jp: E.scalar_tensor_tensor(
                auxt[pr][64:66, jp * 512:(jp + 1) * 512], PP[b2][64:66, :],
                bredge_sb[64:66, 0:1], M2V[64:66, jp * 512:(jp + 1) * 512],
                ALU.add, ALU.subtract))
            vdrain()
            v_ed = P.mark("vector", "sVe")
            ppfree[b] = ("sVe", v_ed)
            ppfree[b2] = ("sVe", v_ed)
            if jp == 3:
                st["aux_rdy"] = v_ed

        def rp_step(qt):
            P.wait("tensor", "sKV", kv_rdy[ch])
            b = pp_next()
            P.wait("tensor", *ppfree[b])
            b2 = pp_next()
            P.wait("tensor", *ppfree[b2])
            mm(PP[b][:, :], qpair[pr][0:64, qt * 128:(qt + 1) * 128],
               wrpad2[0:64, :], True, True)
            mm(PP[b2][:, :], qpair[pr][64:128, qt * 128:(qt + 1) * 128],
               wrpad2[64:128, :], True, True)
            t_rp = P.mark("tensor", "sPe")
            P.wait("vector", "sPe", t_rp)
            P.wait("vector", "sRp", loc["rp_st_fr"][0])
            bb = (b, b2)
            if qt in (0, 15):
                mi = 0 if qt == 0 else 1
                ecol = 384 if qt == 0 else 128
                for hh in range(2):
                    vop(lambda E, hh=hh, bb=bb: E.tensor_add(
                        rpstg2[hh][:, :], PP[bb[hh]][:, :], brbc[:, :]))
                vdrain()
                for hh in range(2):
                    vop(lambda E, hh=hh, ecol=ecol: E.tensor_copy(
                        scal[:, 24 + hh:25 + hh], rpstg2[hh][:, ecol:ecol + 1]))
                vdrain()
                for hh in range(2):
                    rmt = f32s[:, 6144:6656]
                    vop(lambda E, hh=hh, rmt=rmt: E.tensor_scalar(
                        rmt, rpstg2[hh][:, :], scal[:, 24 + hh:25 + hh],
                        None, ALU.subtract))
                    vdrain()
                    vop(lambda E, mi=mi, rmt=rmt: E.tensor_mul(
                        rmt, rmt, ovrb[:, mi * 512:(mi + 1) * 512]))
                    vdrain()
                    vop(lambda E, hh=hh, rmt=rmt: E.tensor_sub(rpstg2[hh][:, :],
                                                               rpstg2[hh][:, :], rmt))
                    vdrain()
                for hh in range(2):
                    vop(lambda E, hh=hh: E.tensor_copy(rpbf[hh][:, :], rpstg2[hh][:, :]))
                vdrain()
            else:
                for hh in range(2):
                    vop(lambda E, hh=hh, bb=bb: E.tensor_add(
                        rpbf[hh][:, :], PP[bb[hh]][:, :], brbc[:, :]))
                vdrain()
            v_rp = P.mark("vector", "sVe")
            ppfree[b] = ("sVe", v_rp)
            ppfree[b2] = ("sVe", v_rp)
            P.wait("gpsimd", "sVe", v_rp)
            for hh in range(2):
                dma("gpsimd",
                    bass.AP(rp_hbm, ((pr * 2 + hh) * NQT + qt) * 128 * 512,
                            [[512, 128], [1, 512]]),
                    rpbf[hh][:, :], "sRp")
            loc["rp_st_fr"][0] = P.val("sRp")
            loc["rp_wr"][qt] = P.val("sRp")

        def band_step(qt):
            pS = qt % 2
            qb = qt // 4
            P.wait("gpsimd", "sRp", loc["rp_wr"][qt])
            P.wait("gpsimd", "sVe", loc.get(("pos_fr", pS), 0))
            for hh in range(2):
                dma("gpsimd", possb[pS][hh][:, :],
                    bass.AP(rp_hbm, ((pr * 2 + hh) * NQT + qt) * 128 * 512 + 128,
                            [[511, 128], [1, 384]]), "sGa")
            g_pos = P.val("sGa")
            b = pp_next()
            P.wait("tensor", *ppfree[b])
            b2 = pp_next()
            P.wait("tensor", *ppfree[b2])
            if qt == 0:
                mm(PP[b][:, 0:128], qpair[pr][0:64, 0:128],
                   kpair[pr][0:64, 3968:4096], True, True)
                mm(PP[b2][:, 0:128], qpair[pr][64:128, 0:128],
                   kpair[pr][64:128, 3968:4096], True, True)
                mm(PP[b][:, 128:384], qpair[pr][0:64, 0:128],
                   kpair[pr][0:64, 0:256], True, True)
                mm(PP[b2][:, 128:384], qpair[pr][64:128, 0:128],
                   kpair[pr][64:128, 0:256], True, True)
            else:
                j0 = qt * 128 - 128
                mm(PP[b][:, 0:384], qpair[pr][0:64, qt * 128:(qt + 1) * 128],
                   kpair[pr][0:64, j0:j0 + 384], True, True)
                mm(PP[b2][:, 0:384], qpair[pr][64:128, qt * 128:(qt + 1) * 128],
                   kpair[pr][64:128, j0:j0 + 384], True, True)
            t_b = P.mark("tensor", "sPe")
            P.wait("vector", "sPe", t_b)
            P.wait("vector", "sGa", g_pos)
            bb = (b, b2)
            for hh in range(2):
                vop(lambda E, hh=hh, bb=bb, pS=pS: E.tensor_add(
                    PP[bb[hh]][:, 0:384], PP[bb[hh]][:, 0:384], possb[pS][hh][:, :]))
            vdrain()
            v_b = P.mark("vector", "sVe")
            loc[("pos_fr", pS)] = v_b
            P.wait("scalar", "sVe", v_b)
            P.wait("scalar", "sVe", st["mcol_rdy"])
            P.wait("scalar", *loc["pS_fr"][pS])
            for hh in range(2):
                sop(lambda E, hh=hh, bb=bb, pS=pS, qt=qt: E.activation(
                    bandP[pS][hh][:, :], PP[bb[hh]][:, 0:384], AF.Exp,
                    bias=mcolneg[:, hh * 16 + qt:hh * 16 + qt + 1]))
            sdrain()
            s_b = P.mark("scalar", "sSc")
            ppfree[b] = ("sSc", s_b)
            ppfree[b2] = ("sSc", s_b)
            loc["band_exp"][qt] = s_b

        def band_tp_step(qt):
            pS = qt % 2
            qb = qt // 4
            bt = pp_next()
            P.wait("tensor", *ppfree[bt])
            P.wait("tensor", "sSc", loc["band_exp"][qt])
            view = PP[bt][:, 0:384].bitcast(BF16)
            for p in range(3):
                for hh in range(2):
                    t = p * 2 + hh
                    P.op("tensor", lambda E, pS=pS, hh=hh, p=p, t=t:
                         E.transpose(view[:, t * 128:(t + 1) * 128],
                                     bandP[pS][hh][:, p * 128:(p + 1) * 128],
                                     iden_bf[:, :]))
            t_tp = P.mark("tensor", "sPe")
            loc["pS_fr"][pS] = ("sPe", t_tp)
            P.wait("vector", "sPe", t_tp)
            col = qb * 3072 + (qt % 4) * 768
            vop(lambda E, col=col, view=view: E.tensor_copy(
                bandT[:, col:col + 768], view[:, 0:768]))
            vdrain()
            v_bt = P.mark("vector", "sVe")
            ppfree[bt] = ("sVe", v_bt)
            if qt % 4 == 3:
                band_region_rdy[(ch, qb)] = v_bt

        steps = [(0, sqq_step)]
        steps += [(0, lambda g=g: normq_step(g)) for g in range(4)]
        steps += [(0, lambda g=g: knorm_step(g)) for g in range(4)]
        steps += [(0, c1_step), (0, mchain_step), (0, mio_step)]
        steps += [(0, lambda jp=jp: edge_step(jp)) for jp in range(4)]
        steps += [(0, lambda qt=qt: rp_step(qt)) for qt in range(NQT)]
        ab = [(0, lambda qt=0: band_step(0)), (0, lambda qt=1: band_step(1))]
        for k in range(NQT):
            ab.append(((k // 4) * 32 + 32, lambda qt=k: band_tp_step(qt)))
            if k + 2 < NQT:
                ab.append((0, lambda qt=k + 2: band_step(qt)))
        steps += ab
        return steps, st

    def emit_main(ch, st, steps, tail):
        pr = ch % 2
        exp_done = {-2: tail["exp"][0], -1: tail["exp"][1]}
        pvden_done = {-3: tail["pvden"][0], -2: tail["pvden"][1],
                      -1: tail["pvden"][2]}
        pending = []
        nsteps = len(steps)
        next_step_at = [0]
        pieces_total = 128

        def consume_steps(i):
            want = (i + 1) * nsteps // pieces_total
            while next_step_at[0] < want and steps and steps[0][0] <= i:
                steps.pop(0)[1]()
                next_step_at[0] += 1

        P.wait("tensor", "sKV", kv_rdy[ch])
        P.wait("tensor", "sVe", st["aux_rdy"])
        pending_epi = []
        pv_start_guard_ref = [(None, {"v_pvc": tail["pv_free"]})]
        for qblk in range(4):
            q0 = qblk * 512
            for kc in range(NKC):
                i = qblk * NKC + kc
                pa = PA if i % 2 == 0 else PB
                pc = i % 3
                P.wait("tensor", "sSc", exp_done.get(i - 2, 0))
                mm(pa[:, 0:512], kpair[pr][0:64, kc * 128:(kc + 1) * 128],
                   qpair[pr][0:64, q0:q0 + 512], True, False)
                mm(pa[:, 512:1024], kpair[pr][64:128, kc * 128:(kc + 1) * 128],
                   qpair[pr][64:128, q0:q0 + 512], True, False)
                subr = [(max(lo, q0), min(hi, q0 + 512), v)
                        for (lo, hi, v) in region_table(kc)
                        if max(lo, q0) < min(hi, q0 + 512)]
                for si, (lo, hi, v) in enumerate(subr):
                    last = si == len(subr) - 1
                    mm(pa[:, lo - q0:hi - q0], auxsel[0:64, v * 128:v * 128 + 128],
                       auxt[pr][0:64, lo:hi], False, last)
                    mm(pa[:, 512 + lo - q0:512 + hi - q0],
                       auxsel[64:128, v * 128:v * 128 + 128],
                       auxt[pr][64:128, lo:hi], False, last)
                t_qk = P.mark("tensor", "sPe")
                for _ in range(2):
                    if pending_epi:
                        pending_epi.pop(0)()
                if len(pending) >= 2:
                    pending.pop(0)()
                P.wait("scalar", "sPe", t_qk)
                P.wait("scalar", "sPe", pvden_done.get(i - 3, 0))
                sop(lambda E, pa=pa, pc=pc: E.activation(
                    pchunk[pc][:, :], pa[:, 0:1024], AF.Exp))
                s_e = P.mark("scalar", "sSc")
                exp_done[i] = s_e
                ncopy = 0
                P.wait("vector", "sSc", s_e)
                P.wait("vector", "sPe", pvden_done.get(i - 3, 0))
                for (qt, p, bq0) in band_pieces_of_kc(kc):
                    if q0 <= bq0 < q0 + 512:
                        col = qblk * 3072 + (qt % 4) * 768 + p * 256
                        off = bq0 - q0
                        vop(lambda E, pc=pc, col=col, off=off: E.tensor_copy(
                            pchunk[pc][:, off:off + 128], bandT[:, col:col + 128]))
                        vop(lambda E, pc=pc, col=col, off=off: E.tensor_copy(
                            pchunk[pc][:, 512 + off:512 + off + 128],
                            bandT[:, col + 128:col + 256]))
                        ncopy += 1
                v_cp = P.mark("vector", "sVe") if ncopy else P.val("sVe")

                def den_pv(i=i, kc=kc, pc=pc, s_e=s_e, v_cp=v_cp, q0=q0,
                           guard=pv_start_guard_ref[0][1]):
                    P.wait("tensor", "sSc", s_e)
                    P.wait("tensor", "sVe", v_cp)
                    if kc == 0:
                        P.wait("tensor", "sVe", guard["v_pvc"])
                    mm(PD[0:1, :], ones_bf[:, 0:1], pchunk[pc][:, 0:512],
                       kc == 0, kc == NKC - 1)
                    mm(PD[64:65, :], ones_bf[:, 0:1], pchunk[pc][:, 512:1024],
                       kc == 0, kc == NKC - 1, tp=(0, 64))
                    mm(PV1[0:64, :], vpair[pr][:, kc * 128:kc * 128 + 64],
                       pchunk[pc][:, 0:512], kc == 0, kc == NKC - 1)
                    mm(PV1[64:128, :], vpair[pr][:, kc * 128 + 64:kc * 128 + 128],
                       pchunk[pc][:, 512:1024], kc == 0, kc == NKC - 1, tp=(0, 64))
                    pvden_done[i] = P.mark("tensor", "sPe")
                pending.append(den_pv)
                consume_steps(i)
            while pending:
                pending.pop(0)()
            t_last = pvden_done[qblk * NKC + NKC - 1]
            est = {}

            def ep_copies(qblk=qblk, t_last=t_last):
                P.wait("vector", "sPe", t_last)
                vop(lambda E: E.tensor_copy(T2[0:65, 0:512], PD[0:65, :]))
                vop(lambda E: E.tensor_copy(T2[:, 512:1024], PV1[:, :]))
                vdrain()
                est["v_pvc"] = P.mark("vector", "sVe")

            def ep_recip(part):
                c0 = part * 128
                vop(lambda E, c0=c0: E.reciprocal(T1[0:65, c0:c0 + 128],
                                                  T2[0:65, c0:c0 + 128]))
                vdrain()
                if part == 3:
                    P.mark("vector", "sVe")

            def ep_cast():
                vop(lambda E: E.tensor_copy(rcbf[0:65, :], T1[0:65, 0:512]))
                vdrain()
                vop(lambda E: E.tensor_sub(rclo[0:65, :], T1[0:65, 0:512],
                                           rcbf[0:65, :]))
                vdrain()
                est["v_rc"] = P.mark("vector", "sVe")

            def ep_fin(ch=ch, qblk=qblk, q0=q0):
                b = pp_next()
                P.wait("tensor", *ppfree[b])
                P.wait("tensor", "sVe", est["v_rc"])
                mm(PP[b][0:64, :], ones_bf[0:1, 0:64], rcbf[0:1, :], True, False)
                mm(PP[b][64:128, :], ones_bf[64:65, 0:64], rcbf[64:65, :], True, False,
                   tp=(64, 64))
                mm(PP[b][0:64, :], ones_bf[0:1, 0:64], rclo[0:1, :], False, True)
                mm(PP[b][64:128, :], ones_bf[64:65, 0:64], rclo[64:65, :], False, True,
                   tp=(64, 64))
                t_bc = P.mark("tensor", "sPe")
                P.wait("vector", "sPe", t_bc)
                vop(lambda E, b=b: E.tensor_mul(T1[:, 512:1024], T2[:, 512:1024],
                                                PP[b][:, :]))
                vdrain()
                v_pvf = P.mark("vector", "sVe")
                ppfree[b] = ("sVe", v_pvf)
                if ch == 0 and qblk == 0:
                    P.wait("vector", "sXl", x_own_ld)
                vop(lambda E, ch=ch, q0=q0: E.tensor_add(
                    x_own[:, ch * 2048 + q0:ch * 2048 + q0 + 512],
                    x_own[:, ch * 2048 + q0:ch * 2048 + q0 + 512], T1[:, 512:1024]))
                vdrain()
                epil_sems[(ch, qblk)] = P.mark("vector", "sVe")

            ep_steps = [ep_copies] + [lambda p=p: ep_recip(p) for p in range(4)] + \
                       [ep_cast, ep_fin]
            if qblk == 3:
                for f in ep_steps:
                    f()
            else:
                pending_epi.extend(ep_steps)
            pv_start_guard_ref[0] = (qblk, est)
        while pending_epi:
            pending_epi.pop(0)()
        while steps:
            steps.pop(0)[1]()
        n = 4 * NKC
        tail_out = {"exp": [exp_done[n - 2], exp_done[n - 1]],
                    "pvden": [pvden_done[n - 3], pvden_done[n - 2],
                              pvden_done[n - 1]],
                    "pv_free": epil_sems[(ch, 3)]}
        return P.val("sPe"), P.val("sVe"), tail_out

    if n_pairs > 0:
        emit_pair_loads(0)
        if n_pairs > 1:
            emit_pair_loads(1)
        vop(lambda E: E.memset(AXT[:, 0:4096], 0.0))
        vdrain()
        P.mark("vector", "sVe")
        steps0, st0 = prep_steps(0)
        for _, f in steps0:
            f()
        states = {0: st0}
        tail = {"exp": [0, 0], "pvden": [0, 0, 0], "pv_free": 0}
        for ch in range(n_pairs):
            st = states[ch]
            if ch + 1 < n_pairs:
                steps_n, st_n = prep_steps(ch + 1)
                states[ch + 1] = st_n
            else:
                steps_n = []
            main_pe, main_ve, tail = emit_main(ch, st, steps_n, tail)
            pair_free[ch % 2] = main_pe
            if ch + 2 < n_pairs:
                emit_pair_loads(ch + 2)

    p2_ve = P.val("sVe")
    p2_sc = P.val("sSc")
    p2_pe = P.val("sPe")

    PPX = PP

    P.wait("gpsimd", "sPe", p2_pe)
    for c in range(8):
        dma("gpsimd", wrt_sb[:, c * 16:(c + 1) * 16],
            wrtT.ap()[c * 128:(c + 1) * 128, :], "sFw")
    wrt_rdy = P.val("sFw")

    wexp_free = [p2_pe, p2_pe]
    up_fr = [0, 0]
    dn_fr = [0, 0]
    gbc_fr = 0
    glt_fr = [0, 0]
    out_st = 0
    gstore = 0

    MUB = f32c[:, 2048:3072]
    IVB = f32c[:, 3072:4096]

    for th in range(2):
        toff = th * 1024
        P.wait("scalar", "sPe", P.val("sPe"))
        P.wait("scalar", "sVe", p2_ve if th == 0 else P.val("sVe"))
        for c in range(8):
            sop(lambda E, c=c, toff=toff: E.copy(
                xln2[:, c * 1024:(c + 1) * 1024],
                x_own[:, c * 2048 + toff:c * 2048 + toff + 1024]))
        s_cast2 = P.mark("scalar", "sSc")
        sdrain()
        P.wait("tensor", "sSc", s_cast2)
        v_prev = 0
        for h2 in range(2):
            P.wait("scalar", "sPe", P.val("sPe"))
            P.wait("tensor", "sVe", v_prev)
            for c in range(8):
                mm(PV1[0:1, :], ones_bf[:, 0:1],
                   xln2[:, c * 1024 + h2 * 512:c * 1024 + h2 * 512 + 512], c == 0, c == 7)
            t_a = P.mark("tensor", "sPe")
            P.wait("vector", "sPe", t_a)
            v_prev = vop(lambda E, h2=h2: E.tensor_scalar(
                f32c[0:1, 2048 + h2 * 512:2048 + h2 * 512 + 512], PV1[0:1, :],
                1.0 / 1024, None, ALU.mult), "sVe")
            for c in range(8):
                s_last = sop(lambda E, c=c, h2=h2: E.square(
                    sqP3[:, :],
                    xln2[:, c * 1024 + h2 * 512:c * 1024 + h2 * 512 + 512]), "sSc")
                P.wait("tensor", "sSc", s_last)
                mm(PD[0:1, :], ones_bf[:, 0:1], sqP3[:, :], c == 0, c == 7)
                t_s3 = P.mark("tensor", "sPe")
                P.wait("scalar", "sPe", t_s3)
            t_b2 = P.val("sPe")
            P.wait("vector", "sPe", t_b2)
            v_prev = vop(lambda E, h2=h2: E.tensor_scalar(
                f32c[0:1, 3072 + h2 * 512:3072 + h2 * 512 + 512], PD[0:1, :],
                1.0 / 1024, None, ALU.mult), "sVe")
        vdrain()
        MU2 = f32c[0:1, 2048:3072]
        EX2 = f32c[0:1, 3072:4096]
        vop(lambda E: E.tensor_mul(f32c[0:1, 1024:2048], MU2, MU2)); vdrain()
        vop(lambda E: E.tensor_sub(EX2, EX2, f32c[0:1, 1024:2048])); vdrain()
        vop(lambda E: E.tensor_scalar(EX2, EX2, 1.0, EPS, ALU.mult, ALU.add)); vdrain()
        recip_nr(f32c[0:1, 1024:2048], EX2, f32c[0:1, 0:1024])
        v_v2 = P.mark("vector", "sVe")
        P.wait("scalar", "sVe", v_v2)
        s_r2 = sop(lambda E: E.sqrt(f32c[0:1, 0:1024], f32c[0:1, 1024:2048]), "sSc")
        P.wait("vector", "sSc", s_r2)
        vop(lambda E: E.tensor_copy(bfc[0:1, 2048:3072], MU2))
        vop(lambda E: E.tensor_copy(bfc[0:1, 3072:4096], f32c[0:1, 0:1024]))
        vdrain()
        v_r2 = P.mark("vector", "sVe")
        P.wait("tensor", "sVe", v_r2)
        v_prev = 0
        for h2 in range(2):
            P.wait("tensor", "sVe", v_prev)
            mm(PV1[:, :], ones_bf[0:1, :], bfc[0:1, 2048 + h2 * 512:2048 + h2 * 512 + 512],
               True, True)
            mm(PD[:, :], ones_bf[0:1, :], bfc[0:1, 3072 + h2 * 512:3072 + h2 * 512 + 512],
               True, True)
            t_bc2 = P.mark("tensor", "sPe")
            P.wait("vector", "sPe", t_bc2)
            vop(lambda E, h2=h2: E.tensor_copy(MUB[:, h2 * 512:h2 * 512 + 512],
                                               PV1[:, :]))
            v_prev = vop(lambda E, h2=h2: E.tensor_copy(
                IVB[:, h2 * 512:h2 * 512 + 512], PD[:, :]), "sVe")
        vdrain()
        for c in range(8):
            vop(lambda E, c=c, toff=toff: E.tensor_sub(
                T1, x_own[:, c * 2048 + toff:c * 2048 + toff + 1024], MUB))
            vdrain()
            vop(lambda E, c=c: E.tensor_mul(
                xln2[:, c * 1024:(c + 1) * 1024], T1, IVB))
            vdrain()
        v_x2 = P.mark("vector", "sVe")

        P.wait("tensor", "sFw", wrt_rdy)
        P.wait("tensor", "sVe", v_x2)
        v_lgfree = 0
        v_gtfree = 0
        for tc in range(8):
            P.wait("tensor", "sVe", v_lgfree)
            for dc in range(8):
                mm(PA[0:128, 0:16],
                   xln2[:, dc * 1024 + tc * 128:dc * 1024 + (tc + 1) * 128],
                   wrt_sb[:, dc * 16:(dc + 1) * 16], dc == 0, dc == 7)
            t_lg = P.mark("tensor", "sPe")
            P.wait("vector", "sPe", t_lg)
            W0 = rw[:, 0:16]
            W1 = rw[:, 16:32]
            W2 = rw[:, 32:48]
            KEEP = rw[:, 48:64]
            MX = rw[:, 64:65]
            SM = rw[:, 65:66]
            vop(lambda E: E.reduce_max(MX, PA[0:128, 0:16], axis=AX.X)); vdrain()
            v_lgfree = vop(lambda E: E.tensor_scalar(
                W0, PA[0:128, 0:16], MX, None, ALU.subtract), "sVe")
            vdrain()
            v_sf = v_lgfree
            P.wait("scalar", "sVe", v_sf)
            s_ex = sop(lambda E: E.activation(W1, W0, AF.Exp), "sSc")
            P.wait("vector", "sSc", s_ex)
            vop(lambda E: E.reduce_sum(SM, W1, axis=AX.X)); vdrain()
            vop(lambda E: E.reciprocal(MX, SM)); vdrain()
            vop(lambda E: E.tensor_scalar(W1, W1, MX, None, ALU.mult)); vdrain()
            vop(lambda E: E.tensor_copy(W0, W1))
            vop(lambda E: E.memset(KEEP, 0.0)); vdrain()
            for kk in range(4):
                vop(lambda E: E.reduce_max(MX, W0, axis=AX.X)); vdrain()
                vop(lambda E: E.tensor_scalar(W2, W0, MX, None, ALU.is_ge)); vdrain()
                vop(lambda E: E.tensor_scalar(W2, W2, 1e30, -1e30, ALU.mult, ALU.add))
                vdrain()
                vop(lambda E: E.tensor_sub(W2, W2, iota_sb[:, :])); vdrain()
                vop(lambda E: E.reduce_max(MX, W2, axis=AX.X)); vdrain()
                vop(lambda E: E.tensor_scalar(W2, W2, MX, None, ALU.is_ge)); vdrain()
                vop(lambda E: E.tensor_add(KEEP, KEEP, W2)); vdrain()
                vop(lambda E: E.scalar_tensor_tensor(W0, W2, -1e30, W0, ALU.mult, ALU.add))
                vdrain()
            vop(lambda E: E.tensor_mul(W2, W1, KEEP)); vdrain()
            vop(lambda E: E.tensor_copy(scb[:, 0:16], W2)); vdrain()
            v_g = P.mark("vector", "sVe")
            P.wait("tensor", "sVe", v_g)
            P.wait("tensor", "sVe", v_gtfree)
            mm(PV1[0:16, 0:128], scb[:, 0:16], iden_bf[:, :], True, True)
            t_g = P.mark("tensor", "sPe")
            P.wait("vector", "sPe", t_g)
            v_gtfree = vop(lambda E, tc=tc: E.tensor_copy(
                gateT[:, tc * 128:(tc + 1) * 128], PV1[0:16, 0:128]), "sVe")
        v_gT = v_gtfree
        P.wait("sync", "sVe", v_gT)
        P.wait("sync", "sGa", gstore)
        w_g = dma("sync", gate_hbm.ap(), gateT[:, :], "sWr")

        g_row = 0
        t_gbc_prev = 0
        for e in range(NE):
            we = e % 2
            P.wait("gpsimd", "sPe", wexp_free[we])
            for c in range(8):
                dma("gpsimd", wup[we][:, c * 512:(c + 1) * 512],
                    wupT.ap()[e, c * 128:(c + 1) * 128, :], "sFw")
            for c in range(4):
                dma("gpsimd", wdn[we][:, c * 1024:(c + 1) * 1024],
                    wdnT.ap()[e, c * 128:(c + 1) * 128, :], "sFw")
            w_rdy = P.val("sFw")
            routed = e >= 2
            if routed:
                P.wait("sync", "sWr", w_g)
                P.wait("sync", "sPe", t_gbc_prev)
                g_row = dma("sync", grow[:, :], gate_hbm.ap()[e - 2:e - 1, :], "sGa2")
                P.wait("tensor", "sGa2", g_row)
                P.wait("tensor", "sVe", max(gbc_fr, v_gT))
                for jp in range(2):
                    mm([PV1, PD][jp][:, :], ones_bf[0:1, :],
                       grow[0:1, jp * 512:(jp + 1) * 512], True, True)
            t_gbc = P.mark("tensor", "sPe") if routed else P.val("sPe")
            t_gbc_prev = t_gbc
            gstore = g_row
            P.wait("tensor", "sFw", w_rdy)
            for ec in range(4):
                for jp in range(2):
                    pA = (ec * 2 + jp) % 2
                    P.wait("tensor", "sSc", up_fr[pA])
                    P.wait("tensor", "sVe", dn_fr[pA])
                    for dc in range(8):
                        mm(PPX[pA][:, :],
                           wup[we][:, dc * 512 + ec * 128:dc * 512 + (ec + 1) * 128],
                           xln2[:, dc * 1024 + jp * 512:dc * 1024 + (jp + 1) * 512],
                           dc == 0, dc == 7)
                    t_u = P.mark("tensor", "sPe")
                    P.wait("scalar", "sPe", t_u)
                    if routed:
                        P.wait("scalar", "sVe", glt_fr[pA])
                        s_gl = sop(lambda E, pA=pA: E.activation(
                            f32c[:, pA * 512:(pA + 1) * 512], PPX[pA][:, :], AF.Gelu),
                            "sSc")
                        up_fr[pA] = s_gl
                        P.wait("vector", "sSc", s_gl)
                        P.wait("vector", "sPe", t_gbc)
                        v_ac = vop(lambda E, ec=ec, jp=jp, pA=pA: E.tensor_mul(
                            act[:, ec * 1024 + jp * 512:ec * 1024 + (jp + 1) * 512],
                            f32c[:, pA * 512:(pA + 1) * 512], [PV1, PD][jp][:, :]), "sVe")
                        gbc_fr = v_ac
                        glt_fr[pA] = v_ac
                    else:
                        s_gl = sop(lambda E, pA=pA, ec=ec, jp=jp: E.activation(
                            act[:, ec * 1024 + jp * 512:ec * 1024 + (jp + 1) * 512],
                            PPX[pA][:, :], AF.Gelu), "sSc")
                        up_fr[pA] = s_gl
            vdrain()
            v_actdone = vop(lambda E: E.nop(), "sVe")
            sdrain()
            s_actdone = sop(lambda E: E.nop(), "sSc")
            P.wait("tensor", "sVe", v_actdone)
            P.wait("tensor", "sSc", s_actdone)
            for hc in range(8):
                for jp in range(2):
                    pA = (hc * 2 + jp) % 2
                    P.wait("tensor", "sSc", up_fr[pA])
                    P.wait("tensor", "sVe", dn_fr[pA])
                    for ec in range(4):
                        mm(PPX[pA][:, :],
                           wdn[we][:, ec * 1024 + hc * 128:ec * 1024 + (hc + 1) * 128],
                           act[:, ec * 1024 + jp * 512:ec * 1024 + (jp + 1) * 512],
                           ec == 0, ec == 3)
                    t_d = P.mark("tensor", "sPe")
                    P.wait("vector", "sPe", t_d)
                    if e == 0:
                        v_aa = vop(lambda E, pA=pA, hc=hc, jp=jp: E.tensor_copy(
                            acc[:, hc * 1024 + jp * 512:hc * 1024 + (jp + 1) * 512],
                            PPX[pA][:, :]), "sVe")
                    else:
                        v_aa = vop(lambda E, pA=pA, hc=hc, jp=jp: E.tensor_add(
                            acc[:, hc * 1024 + jp * 512:hc * 1024 + (jp + 1) * 512],
                            acc[:, hc * 1024 + jp * 512:hc * 1024 + (jp + 1) * 512],
                            PPX[pA][:, :]), "sVe")
                    dn_fr[pA] = v_aa
            wexp_free[we] = P.val("sPe")

        P.emit("vector", lambda E: E.drain())
        P.wait("vector", "sSt", out_st)
        for c in range(8):
            vop(lambda E, c=c, toff=toff: E.tensor_add(
                T2, acc[:, c * 1024:(c + 1) * 1024],
                x_own[:, c * 2048 + toff:c * 2048 + toff + 1024]))
            vdrain()
            v_o = P.mark("vector", "sVe")
            P.wait("sync", "sVe", v_o)
            out_st = dma("sync", out_d.ap()[c * 128:(c + 1) * 128, toff:toff + 1024],
                         T2, "sSt")
            P.wait("vector", "sSt", out_st)

    P.wait("sync", "sSt", P.val("sSt"))
    P.wait("sync", "sWr", P.val("sWr"))
    P.wait("gpsimd", "sRp", P.val("sRp"))
    P.wait("gpsimd", "sGa", P.val("sGa"))

    with nc.Block() as block:
        P.run(block)
    es.close()
    return nc


_CACHE = {}

def _prep_core_inputs(inputs, c):
    x = inputs["L0_attn_x"]
    b, qh = c // 2, c % 2
    key = "shared"
    if key not in _CACHE:
        Wqkv = np.asarray(inputs["L0_attn_Wqkv"]).reshape(16, 3, 64, 1024)
        WQ = Wqkv[:, 0].reshape(1024, 1024)
        WK = Wqkv[:, 1].reshape(1024, 1024)
        WV = Wqkv[:, 2].reshape(1024, 1024)
        bqkv = np.asarray(inputs["L0_attn_bqkv"]).reshape(16, 3, 64)
        bQ, bK, bV = [np.ascontiguousarray(bqkv[:, i].reshape(1024)) for i in range(3)]
        Wr = np.asarray(inputs["L0_attn_Wr"])
        br = np.asarray(inputs["L0_attn_br"])
        WrT = np.ascontiguousarray(Wr.T)
        wrpadT = np.concatenate([np.repeat(WrT[:, 0:1], 128, 1), WrT,
                                 np.repeat(WrT[:, 256:257], 127, 1)], axis=1)
        brpad = np.concatenate([np.repeat(br[0:1], 128), br,
                                np.repeat(br[256:257], 127)])[None, :]
        wredge4 = np.ascontiguousarray(WrT[:, [256, 0, 256, 0]])
        bredge = np.ascontiguousarray(br[[256, 0, 256, 0]][:, None])
        su = np.asarray(inputs["L0_ffn_shared_up"])
        sd = np.asarray(inputs["L0_ffn_shared_down"])
        ru = np.asarray(inputs["L0_ffn_routed_up"])
        rd = np.asarray(inputs["L0_ffn_routed_down"])
        wupT = np.ascontiguousarray(
            np.concatenate([su, ru], axis=0).transpose(0, 2, 1))
        wdnT = np.ascontiguousarray(
            np.concatenate([sd, rd], axis=0).transpose(0, 2, 1))
        wrtT = np.ascontiguousarray(np.asarray(inputs["L0_ffn_router_w"]).T)
        iden = np.eye(128, dtype=BF)
        iota = np.broadcast_to(np.arange(16, dtype=np.float32), (128, 16)).copy()
        _CACHE[key] = dict(
            wqT=np.ascontiguousarray(WQ.T), wkT=np.ascontiguousarray(WK.T),
            wvT=np.ascontiguousarray(WV.T),
            bq=np.ascontiguousarray(bQ.reshape(8, 128).T),
            bk=np.ascontiguousarray(bK.reshape(8, 128).T),
            bv_row=np.ascontiguousarray(bV[None, :]),
            wrpadT=np.ascontiguousarray(wrpadT), brpad=np.ascontiguousarray(brpad),
            wredge4=wredge4.astype(np.float32), bredge=bredge.astype(np.float32),
            wupT=wupT, wdnT=wdnT, wrtT=wrtT, iden=iden, iota=iota)
    shared = _CACHE[key]

    ovr = np.zeros((2, 128, 512), np.float32)
    if qh == 0:
        for di in range(128):
            ovr[0, di, 128 - di:256 - di] = 1.0
    else:
        for di in range(128):
            ovr[1, di, 384 - di:512] = 1.0
    ovr = ovr.astype(BF)
    auxsel = np.zeros((128, 384), dtype=BF)
    for r0 in (0, 64):
        auxsel[r0 + 0, 0:128] = 1.0
        auxsel[r0 + 1, 128:256] = 1.0
        auxsel[r0 + (0 if qh == 0 else 1), 256:384] = 1.0
    

    xr = np.roll(np.asarray(x[b]), -2048 * qh, axis=0)
    xTc = np.ascontiguousarray(xr.T)

    d = dict(shared)
    d.update(xT=xTc, auxsel_in=auxsel, ovr=ovr)
    return d


_NC = None

def kernel(**inputs):
    global _NC
    in_maps = [_prep_core_inputs(inputs, c) for c in range(8)]
    if _NC is None:
        _NC = build_program()
    res = run_bass_kernel_spmd(_NC, in_maps, core_ids=list(range(8)))
    x = np.asarray(inputs["L0_attn_x"])
    out = np.empty_like(x)
    for c in range(8):
        b, qh = c // 2, c % 2
        out[b, 2048 * qh:2048 * (qh + 1), :] = res.results[c]["out"].T
    _CACHE.clear()
    return out


# version 12
# speedup vs baseline: 2.4482x; 1.0059x over previous
import contextlib
import numpy as np
import ml_dtypes
import concourse.bass as bass
import concourse.mybir as mybir
from concourse.bass_utils import run_bass_kernel_spmd

F32 = mybir.dt.float32
BF16 = mybir.dt.bfloat16
I32 = mybir.dt.int32
AF = mybir.ActivationFunctionType
ALU = mybir.AluOpType
AX = mybir.AxisListType
BF = ml_dtypes.bfloat16

H, HD, D, S, NQ = 16, 64, 1024, 4096, 2048
NKC, NQT, NE, ED = 32, 16, 18, 512
EPS = 1e-5
MSLACK = 24.0

def region_table(kc):
    if kc == 0:
        return [(0, 2048, 1)]
    if 1 <= kc <= 15:
        qa = 128 * (kc - 1)
        cut = min(qa + 384, 2048)
        regs = [(0, cut, 0)]
        if cut < 2048:
            regs.append((cut, 2048, 1))
        return regs
    return [(0, 2048, 2)]

def band_pieces_of_kc(kc):
    out = []
    for p in (0, 1, 2):
        qt = (kc if kc != 31 else -1) + 1 - p
        if 0 <= qt < NQT:
            out.append((qt, p, qt * 128))
    return out

ENGS = ("sync", "gpsimd", "tensor", "vector", "scalar")

DMA_ISSUER = {"sIn": "sync", "sWr": "sync", "sTp": "sync", "sSt": "sync",
              "sGa2": "sync", "sXl": "gpsimd",
              "sKV": "gpsimd", "sFw": "gpsimd", "sRp": "gpsimd", "sGa": "gpsimd"}
RELAY_SEM = {"sync": "sSy", "gpsimd": "sGy"}

class Plan:
    def __init__(self, nc):
        self.nc = nc
        self.stream = {e: [] for e in ENGS}
        self.sems = {}
        self.relay_cache = {}

    def sem(self, es, name):
        h = es.enter_context(self.nc.semaphore(name))
        self.sems[name] = [h, 0]

    def inc(self, name, k):
        s = self.sems[name]
        s[1] += k
        return s[1]

    def val(self, name):
        return self.sems[name][1]

    def emit(self, eng, fn):
        self.stream[eng].append(fn)

    def _rawwait(self, eng, name, v):
        h = self.sems[name][0]
        fn = lambda E, h=h, v=v: E.wait_ge(h, v)
        fn._is_wait = True
        self.emit(eng, fn)

    def wait(self, eng, name, v):
        if v <= 0:
            return
        assert v <= self.sems[name][1], f"wait on future {name}: {v} > {self.sems[name][1]}"
        if name in DMA_ISSUER:
            issuer = DMA_ISSUER[name]
            total = self.sems[name][1]
            if eng == issuer:
                self._rawwait(eng, name, total)
                return
            rs = RELAY_SEM[issuer]
            cached = self.relay_cache.get(name)
            if cached is None or cached[0] != total:
                self._rawwait(issuer, name, total)
                h = self.sems[rs][0]
                rv = self.inc(rs, 1)
                self.emit(issuer, lambda E, h=h: E.nop().then_inc(h, 1))
                self.relay_cache[name] = (total, rv)
            rv = self.relay_cache[name][1]
            self._rawwait(eng, rs, rv)
            return
        self._rawwait(eng, name, v)

    def mark(self, eng, sem):
        assert self.stream[eng], "mark on empty stream"
        last = self.stream[eng][-1]
        assert not getattr(last, "_has_upd", False), f"double mark on {eng}"
        assert not getattr(last, "_is_wait", False), f"mark on wait on {eng}"
        v = self.inc(sem, 1)
        h = self.sems[sem][0]
        self.stream[eng].pop()
        wrapped = lambda E, last=last, h=h: last(E).then_inc(h, 1)
        wrapped._has_upd = True
        self.stream[eng].append(wrapped)
        return v

    def op(self, eng, fn, sem=None, k=1):
        if sem is None:
            self.emit(eng, lambda E, fn=fn: fn(E))
            return None
        v = self.inc(sem, k)
        h = self.sems[sem][0]
        wrapped = lambda E, fn=fn, h=h, k=k: fn(E).then_inc(h, k)
        wrapped._has_upd = True
        self.emit(eng, wrapped)
        return v

    def run(self, block):
        for eng in ENGS:
            ops = self.stream[eng]
            getattr(block, eng)(lambda E, ops=ops: [f(E) for f in ops])


def build_program(n_pairs=8, dbg=False):
    nc = bass.Bass("TRN2", target_bir_lowering=False)
    es = contextlib.ExitStack()

    din = lambda n, s, dt=F32: nc.dram_tensor(n, s, dt, kind="ExternalInput")
    xT = din("xT", [D, S])
    wqT = din("wqT", [D, D]); wkT = din("wkT", [D, D]); wvT = din("wvT", [D, D])
    bq_i = din("bq", [128, 8]); bk_i = din("bk", [128, 8])
    bv_row = din("bv_row", [1, D])
    wrpadT = din("wrpadT", [64, 512]); brpad_i = din("brpad", [1, 512])
    wredge_i = din("wredge4", [64, 4]); bredge_i = din("bredge", [4, 1])
    wupT = din("wupT", [NE, D, ED]); wdnT = din("wdnT", [NE, ED, D])
    wrtT = din("wrtT", [D, 16])
    iden_i = din("iden", [128, 128], BF16)
    iota_i = din("iota", [128, 16])
    auxsel_i = din("auxsel_in", [128, 384], BF16)
    ovr_i = din("ovr", [2, 128, 512], BF16)

    out_d = nc.dram_tensor("out", [D, NQ], F32, kind="ExternalOutput")

    k_hbm = nc.dram_tensor("k_hbm", [8, 128, S], BF16)
    q_hbm = nc.dram_tensor("q_hbm", [8, 128, NQ], BF16)
    v_hbm = nc.dram_tensor("v_hbm", [NKC, 128, D], BF16)
    rp_hbm = nc.dram_tensor("rp_hbm", [2, 2, NQT, 128, 512], BF16)
    m_hbm = nc.dram_tensor("m_hbm", [2, 2, NQ], BF16)
    gate_hbm = nc.dram_tensor("gate_hbm", [16, 1024], BF16)

    sbt = lambda n, s, dt: es.enter_context(nc.sbuf_tensor(n, s, dt))
    x_own = sbt("x_own", [128, 16384], F32)
    big1 = sbt("big1", [128, 16384], BF16)
    big2 = sbt("big2", [128, 8192], BF16)
    big3 = sbt("big3", [128, 8192], BF16)
    f32s = sbt("f32s", [128, 8192], F32)
    f32c = sbt("f32c", [128, 4096], F32)
    bfc = sbt("bfc", [128, 4096], BF16)
    rw = sbt("rw", [128, 80], F32)
    ones_bf = sbt("ones_bf", [128, 128], BF16)
    iden_bf = sbt("iden_bf", [128, 128], BF16)
    iota_sb = sbt("iota_sb", [128, 16], F32)
    auxsel = sbt("auxsel", [128, 384], BF16)
    ovrb = sbt("ovrb", [128, 1024], BF16)
    bq_sb = sbt("bq_sb", [128, 8], F32)
    bk8_sb = sbt("bk8_sb", [128, 8], F32)
    bVbc = sbt("bVbc", [128, 1024], BF16)
    brbc = sbt("brbc", [128, 512], BF16)
    wrpad2 = sbt("wrpad2", [128, 512], BF16)
    wredge_sb = sbt("wredge_sb", [128, 66], BF16)
    bredge_sb = sbt("bredge_sb", [128, 1], F32)
    scal = sbt("scal", [128, 32], F32)
    scb = sbt("scb", [128, 32], BF16)
    nsq = sbt("nsq", [128, 32], F32)
    mwork = sbt("mwork", [128, 32], F32)
    mhbf = sbt("mhbf", [128, 32], BF16)
    mcolneg = sbt("mcolneg", [128, 32], F32)
    c1bc = sbt("c1bc", [128, 2], F32)
    c2bc = sbt("c2bc", [128, 1], F32)

    T1 = f32c[:, 0:1024]
    T2 = f32c[:, 1024:2048]
    BC1 = f32c[:, 2048:3072]
    BC2 = f32c[:, 3072:4096]

    xslab = [f32s[:, 0:4096], f32s[:, 4096:8192]]
    xlnsl = [big3[:, 0:4096], big3[:, 4096:8192]]
    kstage = [bfc[:, 0:512], bfc[:, 512:1024]]
    vstage = bfc[:, 1536:2560]
    sqP1 = bfc[:, 3072:3584]
    QP = bass.AP(f32s, 1024, [[8192, 128], [1, 2048]]).bitcast(BF16)
    AXT = bass.AP(f32s, 3072, [[8192, 128], [1, 2048]]).bitcast(BF16)
    M2V = bass.AP(f32s, 5120, [[8192, 128], [1, 1024]]).bitcast(BF16)
    SQB = bass.AP(f32s, 6144, [[8192, 128], [1, 1024]]).bitcast(BF16)
    rpstg2 = [f32s[:, 0:512], f32s[:, 512:1024]]
    qpair = [QP[:, 0:2048], QP[:, 2048:4096]]
    auxt = [AXT[:, 0:2048], AXT[:, 2048:4096]]
    kpair = [big2[:, 0:4096], big2[:, 4096:8192]]
    vpair = [big3[:, 0:4096], big3[:, 4096:8192]]
    bandT = big1[:, 0:12288]
    bandP = [[big1[:, 12288 + (2 * pS + hh) * 384:12288 + (2 * pS + hh + 1) * 384]
              for hh in range(2)] for pS in range(2)]
    msb = big1[:, 13824:13952]
    rpbf = [big1[:, 13952 + hh * 512:13952 + (hh + 1) * 512] for hh in range(2)]
    rcbf = big1[:, 14976:15488]
    rclo = big1[:, 15488:16000]
    possb = [[bfc[:, (2 * pS + hh) * 384:(2 * pS + hh + 1) * 384] for hh in range(2)]
             for pS in range(2)]
    RCV = bass.AP(f32s, 7168, [[8192, 128], [1, 1024]]).bitcast(BF16)
    rcbf = RCV[:, 0:512]
    rclo = RCV[:, 512:1024]
    pchunk = [bfc[:, 1536 + i * 1024:1536 + (i + 1) * 1024] for i in range(2)] + \
             [big1[:, 14976:16000]]
    xln2 = big1[:, 0:8192]
    act = big1[:, 8192:12288]
    wrt_sb = big1[:, 12288:12416]
    wup = [big2[:, 0:4096], big2[:, 4096:8192]]
    wdn = [big3[:, 0:4096], big3[:, 4096:8192]]
    acc = f32s
    grow = bfc[0:1, 0:1024]
    gateT = bfc[0:16, 1024:2048]
    sqP3 = bfc[:, 2048:2560]

    PA = es.enter_context(nc.psum_tensor("PA", [128, 1024], F32))
    PB = es.enter_context(nc.psum_tensor("PB", [128, 1024], F32))
    PV1 = es.enter_context(nc.psum_tensor("PV1", [128, 512], F32))
    PD = es.enter_context(nc.psum_tensor("PD", [128, 512], F32))
    PP = [es.enter_context(nc.psum_tensor("PP0", [128, 512], F32)),
          es.enter_context(nc.psum_tensor("PP1", [128, 512], F32))]

    P = Plan(nc)
    for s in ("sIn", "sWr", "sTp", "sSt", "sGa2", "sXl", "sKV", "sFw", "sRp", "sGa",
              "sPe", "sVe", "sSc", "sSy", "sGy"):
        P.sem(es, s)

    def dma(eng, dst, src, sem="sIn", slow=False):
        return P.op(eng, lambda E, d=dst, s=src, sl=slow:
                    E.dma_start(out=d, in_=s, allow_slow_non_contiguous=sl), sem, 16)

    def mm(dst, lhsT, rhs, start, stop, sem=None, tp=None):
        return P.op("tensor", lambda E, d=dst, l=lhsT, r=rhs, st=start, sp=stop, tp=tp:
                    E.matmul(d, l, r, start=st, stop=sp, tile_position=tp,
                             skip_group_check=True), sem, 1)

    def vop(fn, sem=None):
        return P.op("vector", fn, sem, 1)

    def sop(fn, sem=None):
        return P.op("scalar", fn, sem, 1)

    def vdrain():
        P.emit("vector", lambda E: E.drain())

    def sdrain():
        P.emit("scalar", lambda E: E.drain())

    def recip_nr(dst, src, ws):
        vop(lambda E: E.reciprocal(dst, src)); vdrain()

    dma("sync", iden_bf[:, :], iden_i.ap())
    dma("sync", iota_sb[:, :], iota_i.ap())
    dma("sync", auxsel[:, :], auxsel_i.ap())
    dma("sync", ovrb[:, 0:512], ovr_i.ap()[0])
    dma("sync", ovrb[:, 512:1024], ovr_i.ap()[1])
    dma("sync", bq_sb[:, :], bq_i.ap())
    dma("sync", bk8_sb[:, :], bk_i.ap())
    dma("sync", f32c[0:1, 2048:3072], bv_row.ap())
    dma("sync", bredge_sb[0:2, 0:1], bredge_i.ap()[0:2])
    dma("sync", bredge_sb[64:66, 0:1], bredge_i.ap()[2:4])
    dma("sync", f32c[0:1, 3072:3584], brpad_i.ap())
    dma("gpsimd", wredge_sb[0:64, 0:2], wredge_i.ap()[:, 0:2], "sFw")
    dma("gpsimd", wredge_sb[64:128, 64:66], wredge_i.ap()[:, 2:4], "sFw")
    for r0 in (0, 64):
        dma("gpsimd", wrpad2[r0:r0 + 64, :], wrpadT.ap(), "sFw")
    for c in range(8):
        dma("gpsimd", big1[:, c * 1024:(c + 1) * 1024], wqT.ap()[c * 128:(c + 1) * 128, :],
            "sFw")
        dma("gpsimd", big1[:, 8192 + c * 1024:8192 + (c + 1) * 1024],
            wkT.ap()[c * 128:(c + 1) * 128, :], "sFw")
        dma("gpsimd", big2[:, c * 1024:(c + 1) * 1024], wvT.ap()[c * 128:(c + 1) * 128, :],
            "sFw")
    v_in = P.val("sIn")
    v_gw = P.val("sFw")

    P.wait("vector", "sIn", v_in)
    vop(lambda E: E.memset(ones_bf[:, :], 1.0))
    vop(lambda E: E.memset(wredge_sb[64:128, 0:64], 0.0))
    vop(lambda E: E.tensor_scalar(bk8_sb[:, :], bk8_sb[:, :], 0.125, None, ALU.mult))
    vop(lambda E: E.tensor_copy(bfc[0:1, 3072:4096], f32c[0:1, 2048:3072]))
    vop(lambda E: E.tensor_copy(bfc[0:1, 1024:1536], f32c[0:1, 3072:3584]))
    vdrain()
    v_rows0 = P.mark("vector", "sVe")
    P.wait("tensor", "sVe", v_rows0)
    for jp in range(2):
        mm(PA[:, jp * 512:(jp + 1) * 512], ones_bf[0:1, :],
           bfc[0:1, 3072 + jp * 512:3072 + (jp + 1) * 512], True, True)
    mm(PP[0][:, :], ones_bf[0:1, :], bfc[0:1, 1024:1536], True, True)
    t_bc0 = P.mark("tensor", "sPe")
    P.wait("vector", "sPe", t_bc0)
    vop(lambda E: E.tensor_copy(bVbc[:, :], PA[:, 0:1024]))
    vop(lambda E: E.tensor_copy(brbc[:, :], PP[0][:, :]))
    vdrain()
    v_bc0 = P.mark("vector", "sVe")

    P.wait("scalar", "sIn", v_in)
    P.wait("scalar", "sFw", v_gw)
    P.wait("scalar", "sPe", t_bc0)
    s_wr2 = sop(lambda E: E.square(sqP1[0:64, :], wrpad2[0:64, :]), "sSc")
    P.wait("tensor", "sSc", s_wr2)
    P.wait("tensor", "sVe", v_rows0)
    mm(PP[1][0:1, :], ones_bf[0:64, 0:1], sqP1[0:64, :], True, True)
    t_wr = P.mark("tensor", "sPe")
    P.wait("vector", "sPe", t_wr)
    vop(lambda E: E.reduce_max(scal[0:1, 0:1], PP[1][0:1, 128:385], axis=AX.X)); vdrain()
    v_mx = P.mark("vector", "sVe")
    P.wait("scalar", "sVe", v_mx)
    s_sq = sop(lambda E: E.sqrt(scal[0:1, 2:3], scal[0:1, 0:1]), "sSc")
    P.wait("vector", "sSc", s_sq)
    vop(lambda E: E.tensor_scalar(f32c[0:1, 1024:1536], f32c[0:1, 3072:3584],
                                  -1.0, None, ALU.mult)); vdrain()
    vop(lambda E: E.tensor_max(f32c[0:1, 1024:1536], f32c[0:1, 1024:1536],
                               f32c[0:1, 3072:3584])); vdrain()
    vop(lambda E: E.reduce_max(scal[0:1, 3:4], f32c[0:1, 1152:1409],
                               axis=AX.X)); vdrain()
    vop(lambda E: E.tensor_scalar(scal[0:1, 5:6], scal[0:1, 3:4], 1.0, -MSLACK,
                                  ALU.mult, ALU.add)); vdrain()
    vop(lambda E: E.tensor_copy(scb[0:1, 0:1], scal[0:1, 5:6])); vdrain()
    v_c2 = P.mark("vector", "sVe")
    P.wait("tensor", "sVe", v_c2)
    mm(PP[1][:, 0:1], ones_bf[0:1, :], scb[0:1, 0:1], True, True)
    t_c2 = P.mark("tensor", "sPe")
    P.wait("vector", "sPe", t_c2)
    vop(lambda E: E.tensor_copy(c2bc[:, :], PP[1][:, 0:1]))
    vdrain()
    v_c2b = P.mark("vector", "sVe")

    P.wait("tensor", "sFw", v_gw)

    slab_free_v = [0, 0]
    slab_xln_free = [0, 0]
    psP_fr = [0, 0]
    psV_fr = v_bc0
    kst_fr = [0, 0]
    vst_fr = 0
    x_own_ld = 0
    prev_bcc = [v_c2b]

    for s_i in range(8):
        b = s_i % 2
        P.wait("gpsimd", "sVe", slab_free_v[b])
        for c in range(8):
            dma("gpsimd", xslab[b][:, c * 512:(c + 1) * 512],
                xT.ap()[c * 128:(c + 1) * 128, s_i * 512:(s_i + 1) * 512], "sXl")
        if s_i < 4:
            for c in range(8):
                dma("gpsimd", x_own[:, c * 2048 + s_i * 512:c * 2048 + (s_i + 1) * 512],
                    xT.ap()[c * 128:(c + 1) * 128, s_i * 512:(s_i + 1) * 512], "sXl")
            x_own_ld = P.val("sXl")
        ld_done = P.val("sXl")

        P.wait("scalar", "sXl", ld_done)
        P.wait("scalar", "sPe", slab_xln_free[b])
        for c in range(8):
            sop(lambda E, b=b, c=c: E.copy(
                xlnsl[b][:, c * 512:(c + 1) * 512], xslab[b][:, c * 512:(c + 1) * 512]))
        s_cast = P.mark("scalar", "sSc")
        sdrain()
        P.wait("tensor", "sSc", s_cast)
        P.wait("tensor", "sVe", prev_bcc[0])
        for c in range(8):
            mm(PV1[0:1, :], ones_bf[:, 0:1], xlnsl[b][:, c * 512:(c + 1) * 512],
               c == 0, c == 7)
        for c in range(8):
            s_sq = sop(lambda E, b=b, c=c: E.square(
                sqP1[:, :], xlnsl[b][:, c * 512:(c + 1) * 512]), "sSc")
            P.wait("tensor", "sSc", s_sq)
            mm(PD[0:1, :], ones_bf[:, 0:1], sqP1[:, :], c == 0, c == 7)
            t_sqc = P.mark("tensor", "sPe")
            P.wait("scalar", "sPe", t_sqc)
        t_st = P.val("sPe")
        P.wait("vector", "sPe", t_st)
        MUr = f32c[0:1, 2048:2560]
        VARr = f32c[0:1, 2560:3072]
        MSQ = f32c[0:1, 3072:3584]
        INVr = f32c[0:1, 3584:4096]
        vop(lambda E: E.tensor_scalar(MUr, PV1[0:1, :], 1.0 / 1024, None, ALU.mult))
        vop(lambda E: E.tensor_scalar(VARr, PD[0:1, :], 1.0 / 1024, None, ALU.mult))
        vdrain()
        v_rowrd = P.mark("vector", "sVe")
        vop(lambda E: E.tensor_mul(MSQ, MUr, MUr)); vdrain()
        vop(lambda E: E.tensor_sub(VARr, VARr, MSQ)); vdrain()
        vop(lambda E: E.tensor_scalar(VARr, VARr, 1.0, EPS, ALU.mult, ALU.add)); vdrain()
        recip_nr(f32c[0:1, 3072:3584], VARr, f32c[0:1, 1024:1536])
        v_var = P.mark("vector", "sVe")
        P.wait("scalar", "sVe", v_var)
        s_rs = sop(lambda E: E.sqrt(INVr, f32c[0:1, 3072:3584]), "sSc")
        P.wait("vector", "sSc", s_rs)
        vop(lambda E: E.tensor_copy(bfc[0:1, 3072:3584], MUr))
        vop(lambda E: E.tensor_copy(bfc[0:1, 1024:1536], INVr))
        vdrain()
        v_r = P.mark("vector", "sVe")
        P.wait("tensor", "sVe", v_r)
        mm(PV1[:, :], ones_bf[0:1, :], bfc[0:1, 3072:3584], True, True)
        mm(PD[:, :], ones_bf[0:1, :], bfc[0:1, 1024:1536], True, True)
        t_bc = P.mark("tensor", "sPe")
        P.wait("vector", "sPe", t_bc)
        vop(lambda E: E.tensor_copy(BC1[:, 0:512], PV1[:, :]))
        vop(lambda E: E.tensor_copy(BC2[:, 0:512], PD[:, :]))
        vdrain()
        v_bcc = P.mark("vector", "sVe")
        prev_bcc[0] = v_bcc
        for c in range(8):
            vop(lambda E, b=b, c=c: E.tensor_sub(
                T1[:, 0:512], xslab[b][:, c * 512:(c + 1) * 512], f32c[:, 2048:2560]))
            vdrain()
            vop(lambda E, b=b, c=c: E.tensor_mul(
                xlnsl[b][:, c * 512:(c + 1) * 512], T1[:, 0:512], f32c[:, 3072:3584]))
            vdrain()
        v_xln = P.mark("vector", "sVe")
        slab_free_v[b] = v_xln

        P.wait("tensor", "sVe", v_xln)
        kinds = ("k", "q") if s_i < 4 else ("k",)
        for kind in kinds:
            woff = 8192 if kind == "k" else 0
            bias = bk8_sb if kind == "k" else bq_sb
            scale = 0.125 if kind == "k" else 1.0
            hbm = k_hbm if kind == "k" else q_hbm
            for fc in range(8):
                pA = fc % 2
                P.wait("tensor", "sSc", psP_fr[pA])
                for dc in range(8):
                    mm(PP[pA][:, :],
                       big1[:, woff + dc * 1024 + fc * 128:woff + dc * 1024 + (fc + 1) * 128],
                       xlnsl[b][:, dc * 512:(dc + 1) * 512], dc == 0, dc == 7)
                t_p = P.mark("tensor", "sPe")
                P.wait("scalar", "sPe", t_p)
                P.wait("scalar", "sWr", kst_fr[pA])
                sv = sop(lambda E, pA=pA, bias=bias, fc=fc, scale=scale: E.activation(
                    kstage[pA][:, :], PP[pA][:, :], AF.Identity,
                    bias=bias[:, fc:fc + 1], scale=scale), "sSc")
                psP_fr[pA] = sv
                P.wait("sync", "sSc", sv)
                kst_fr[pA] = dma("sync", hbm.ap()[fc, :, s_i * 512:(s_i + 1) * 512],
                                 kstage[pA][:, :], "sWr")
        for tc in range(4):
            kc_g = s_i * 4 + tc
            P.wait("tensor", "sVe", psV_fr)
            for jp in range(2):
                for dc in range(8):
                    mm(PA[:, jp * 512:(jp + 1) * 512],
                       xlnsl[b][:, dc * 512 + tc * 128:dc * 512 + (tc + 1) * 128],
                       big2[:, dc * 1024 + jp * 512:dc * 1024 + (jp + 1) * 512],
                       dc == 0, dc == 7)
            t_v = P.mark("tensor", "sPe")
            P.wait("vector", "sPe", t_v)
            P.wait("vector", "sWr", vst_fr)
            vop(lambda E: E.tensor_add(vstage[:, :], PA[:, 0:1024], bVbc[:, :]))
            vdrain()
            psV_fr = P.mark("vector", "sVe")
            P.wait("sync", "sVe", psV_fr)
            vst_fr = dma("sync", v_hbm.ap()[kc_g], vstage[:, :], "sWr")
        slab_xln_free[b] = P.val("sPe")

    p1_pe = P.val("sPe")
    p1_ve = P.val("sVe")
    p1_sc = P.val("sSc")
    p1_wr = P.val("sWr")

    ppfree = [("sSc", p1_sc), ("sSc", p1_sc)]
    pp_i = [0]

    def pp_next():
        b = pp_i[0] % 2
        pp_i[0] += 1
        return b

    kv_rdy = [0] * 8
    pair_free = [p1_pe, p1_pe]
    epil_sems = {}
    band_region_rdy = {}
    G = {"sqb_rd": 0}

    def emit_pair_loads(ch):
        pr = ch % 2
        P.wait("gpsimd", "sPe", pair_free[pr])
        if ch < 2:
            P.wait("gpsimd", "sWr", p1_wr)
        dma("gpsimd", kpair[pr][:, :], k_hbm.ap()[ch], "sKV")
        dma("gpsimd", qpair[pr][:, :], q_hbm.ap()[ch], "sKV")
        vsrc = bass.AP(v_hbm, ch * 128, [[D, 128], [128 * D, NKC], [1, 128]])
        vdst = bass.AP(big3, pr * 4096, [[8192, 128], [128, NKC], [1, 128]])
        dma("gpsimd", vdst, vsrc, "sKV")
        kv_rdy[ch] = P.val("sKV")

    def prep_steps(ch):
        pr = ch % 2
        st = {"aux_rdy": 0, "mcol_rdy": 0}
        loc = {"rp_wr": [0] * NQT, "rp_st_fr": [0, 0], "m_wr": 0, "m_ga": 0,
               "pS_fr": [("sPe", 0), ("sPe", 0)], "band_exp": [0] * NQT}

        def sqq_step():
            P.wait("vector", "sKV", kv_rdy[ch])
            P.wait("vector", "sPe", G["sqb_rd"])
            vop(lambda E: E.tensor_mul(SQB[:, 0:2048], qpair[pr][:, :], qpair[pr][:, :]))
            vdrain()
            loc["s_sqq"] = P.mark("vector", "sVe")

        def normq_step(g):
            P.wait("tensor", "sVe", loc["s_sqq"])
            if g == 0:
                b = pp_next()
                P.wait("tensor", *ppfree[b])
                b2 = pp_next()
                P.wait("tensor", *ppfree[b2])
                loc["nq_banks"] = (b, b2)
            b, b2 = loc["nq_banks"]
            for qt in range(g * 4, g * 4 + 4):
                mm(PP[b][:, qt:qt + 1], SQB[0:64, qt * 128:(qt + 1) * 128],
                   ones_bf[0:64, 0:1], True, True)
                mm(PP[b2][:, qt:qt + 1], SQB[64:128, qt * 128:(qt + 1) * 128],
                   ones_bf[64:128, 0:1], True, True)
            t_nq = P.mark("tensor", "sPe")
            G["sqb_rd"] = t_nq
            if g == 3:
                P.wait("vector", "sPe", t_nq)
                vop(lambda E, b=b: E.tensor_copy(nsq[:, 0:16], PP[b][:, 0:16]))
                vop(lambda E, b2=b2: E.tensor_copy(nsq[:, 16:32], PP[b2][:, 0:16]))
                vdrain()
                v_nq = P.mark("vector", "sVe")
                ppfree[b] = ("sVe", v_nq)
                ppfree[b2] = ("sVe", v_nq)
                loc["v_nsq"] = v_nq

        def knorm_step(g):
            cb, sub = g // 2, g % 2
            if sub == 0:
                P.wait("vector", "sPe", G["sqb_rd"])
                vop(lambda E, cb=cb: E.tensor_mul(
                    SQB[:, 0:2048], kpair[pr][:, cb * 2048:(cb + 1) * 2048],
                    kpair[pr][:, cb * 2048:(cb + 1) * 2048]))
                vdrain()
                loc["s_sqk"] = P.mark("vector", "sVe")
            P.wait("tensor", "sVe", loc["s_sqk"])
            for ss in range(2):
                b = pp_next()
                P.wait("tensor", *ppfree[b])
                b2 = pp_next()
                P.wait("tensor", *ppfree[b2])
                off = (sub * 2 + ss) * 512
                mm(PP[b][0:1, :], ones_bf[0:64, 0:1], SQB[0:64, off:off + 512],
                   True, True)
                mm(PP[b2][0:1, :], ones_bf[64:128, 0:1], SQB[64:128, off:off + 512],
                   True, True)
                t_kn = P.mark("tensor", "sPe")
                G["sqb_rd"] = t_kn
                P.wait("vector", "sPe", t_kn)
                first = (g == 0 and ss == 0)
                vop(lambda E, b=b: E.reduce_max(scal[0:1, 16:17], PP[b][0:1, :],
                                                axis=AX.X))
                vop(lambda E, b2=b2: E.reduce_max(scal[0:1, 18:19], PP[b2][0:1, :],
                                                  axis=AX.X))
                vdrain()
                if first:
                    vop(lambda E: E.tensor_copy(scal[0:1, 17:18], scal[0:1, 16:17]))
                    vop(lambda E: E.tensor_copy(scal[0:1, 19:20], scal[0:1, 18:19]))
                else:
                    vop(lambda E: E.tensor_max(scal[0:1, 17:18], scal[0:1, 17:18],
                                               scal[0:1, 16:17]))
                    vop(lambda E: E.tensor_max(scal[0:1, 19:20], scal[0:1, 19:20],
                                               scal[0:1, 18:19]))
                vdrain()
                v_kn = P.mark("vector", "sVe")
                ppfree[b] = ("sVe", v_kn)
                ppfree[b2] = ("sVe", v_kn)
            loc["v_knorm"] = P.val("sVe")

        def c1_step():
            P.wait("scalar", "sVe", loc["v_knorm"])
            sop(lambda E: E.sqrt(scal[0:1, 20:21], scal[0:1, 17:18]))
            sop(lambda E: E.sqrt(scal[0:1, 21:22], scal[0:1, 19:20]))
            sdrain()
            s_c1 = P.mark("scalar", "sSc")
            P.wait("vector", "sSc", s_c1)
            vop(lambda E: E.tensor_add(scal[0:1, 22:23], scal[0:1, 20:21],
                                       scal[0:1, 2:3]))
            vop(lambda E: E.tensor_add(scal[0:1, 23:24], scal[0:1, 21:22],
                                       scal[0:1, 2:3]))
            vdrain()
            vop(lambda E: E.tensor_copy(scb[0:1, 2:4], scal[0:1, 22:24])); vdrain()
            v_c1 = P.mark("vector", "sVe")
            P.wait("tensor", "sVe", v_c1)
            b = pp_next()
            P.wait("tensor", *ppfree[b])
            mm(PP[b][:, 0:2], ones_bf[0:1, :], scb[0:1, 2:4], True, True)
            t_c1 = P.mark("tensor", "sPe")
            P.wait("vector", "sPe", t_c1)
            vop(lambda E, b=b: E.tensor_copy(c1bc[:, :], PP[b][:, 0:2]))
            vdrain()
            v_c1b = P.mark("vector", "sVe")
            ppfree[b] = ("sVe", v_c1b)
            loc["v_c1b"] = v_c1b

        def mchain_step():
            P.wait("scalar", "sVe", loc["v_c1b"])
            sop(lambda E: E.activation(mwork[:, 0:32], nsq[:, 0:32], AF.Sqrt))
            sdrain()
            s_m = P.mark("scalar", "sSc")
            P.wait("vector", "sSc", s_m)
            P.wait("vector", "sVe", v_c2b)
            vop(lambda E: E.tensor_scalar(mwork[:, 0:16], mwork[:, 0:16],
                                          c1bc[:, 0:1], c2bc[:, 0:1],
                                          ALU.mult, ALU.add))
            vop(lambda E: E.tensor_scalar(mwork[:, 16:32], mwork[:, 16:32],
                                          c1bc[:, 1:2], c2bc[:, 0:1],
                                          ALU.mult, ALU.add))
            vdrain()
            vop(lambda E: E.tensor_copy(mhbf[:, :], mwork[:, :])); vdrain()
            vop(lambda E: E.tensor_scalar(mcolneg[:, :], mhbf[:, :], -1.0, None,
                                          ALU.mult))
            vdrain()
            st["mcol_rdy"] = P.mark("vector", "sVe")

        def mio_step():
            P.wait("tensor", "sVe", st["mcol_rdy"])
            b = pp_next()
            P.wait("tensor", *ppfree[b])
            P.op("tensor", lambda E, b=b: E.transpose(
                PP[b][0:32, 0:64].bitcast(BF16), mhbf[:, :], iden_bf[:, :]))
            t_tr = P.mark("tensor", "sPe")
            P.wait("vector", "sPe", t_tr)
            vop(lambda E, b=b: E.tensor_copy(msb[0:32, :],
                                             PP[b][0:32, 0:64].bitcast(BF16)))
            vdrain()
            v_mt = P.mark("vector", "sVe")
            ppfree[b] = ("sVe", v_mt)
            P.wait("gpsimd", "sVe", v_mt)
            for hh in range(2):
                dma("gpsimd", bass.AP(m_hbm, (pr * 2 + hh) * NQ, [[128, 16], [1, 128]]),
                    msb[hh * 16:(hh + 1) * 16, :], "sRp")
            loc["m_wr"] = P.val("sRp")
            P.wait("gpsimd", "sRp", loc["m_wr"])
            for hh in range(2):
                for r in range(2):
                    dma("gpsimd", M2V[hh * 64 + r:hh * 64 + r + 1, 0:2048],
                        bass.AP(m_hbm, (pr * 2 + hh) * NQ, [[NQ, 1], [1, NQ]]), "sGa")
            loc["m_ga"] = P.val("sGa")

        def edge_step(jp):
            P.wait("tensor", "sKV", kv_rdy[ch])
            b = pp_next()
            P.wait("tensor", *ppfree[b])
            b2 = pp_next()
            P.wait("tensor", *ppfree[b2])
            mm(PP[b][0:2, :], wredge_sb[0:64, 0:2],
               qpair[pr][0:64, jp * 512:(jp + 1) * 512], True, True)
            mm(PP[b2][0:66, :], wredge_sb[64:128, 0:66],
               qpair[pr][64:128, jp * 512:(jp + 1) * 512], True, True)
            t_e = P.mark("tensor", "sPe")
            P.wait("vector", "sPe", t_e)
            P.wait("vector", "sGa", loc["m_ga"])
            if jp == 0:
                P.wait("vector", "sPe", pair_free[pr])
            vop(lambda E, b=b, jp=jp: E.scalar_tensor_tensor(
                auxt[pr][0:2, jp * 512:(jp + 1) * 512], PP[b][0:2, :],
                bredge_sb[0:2, 0:1], M2V[0:2, jp * 512:(jp + 1) * 512],
                ALU.add, ALU.subtract))
            vop(lambda E, b2=b2, jp=jp: E.scalar_tensor_tensor(
                auxt[pr][64:66, jp * 512:(jp + 1) * 512], PP[b2][64:66, :],
                bredge_sb[64:66, 0:1], M2V[64:66, jp * 512:(jp + 1) * 512],
                ALU.add, ALU.subtract))
            vdrain()
            v_ed = P.mark("vector", "sVe")
            ppfree[b] = ("sVe", v_ed)
            ppfree[b2] = ("sVe", v_ed)
            if jp == 3:
                st["aux_rdy"] = v_ed

        def rp_step(qt):
            P.wait("tensor", "sKV", kv_rdy[ch])
            b = pp_next()
            P.wait("tensor", *ppfree[b])
            b2 = pp_next()
            P.wait("tensor", *ppfree[b2])
            mm(PP[b][:, :], qpair[pr][0:64, qt * 128:(qt + 1) * 128],
               wrpad2[0:64, :], True, True)
            mm(PP[b2][:, :], qpair[pr][64:128, qt * 128:(qt + 1) * 128],
               wrpad2[64:128, :], True, True)
            t_rp = P.mark("tensor", "sPe")
            P.wait("vector", "sPe", t_rp)
            P.wait("vector", "sRp", loc["rp_st_fr"][0])
            bb = (b, b2)
            if qt in (0, 15):
                mi = 0 if qt == 0 else 1
                ecol = 384 if qt == 0 else 128
                for hh in range(2):
                    vop(lambda E, hh=hh, bb=bb: E.tensor_add(
                        rpstg2[hh][:, :], PP[bb[hh]][:, :], brbc[:, :]))
                vdrain()
                for hh in range(2):
                    vop(lambda E, hh=hh, ecol=ecol: E.tensor_copy(
                        scal[:, 24 + hh:25 + hh], rpstg2[hh][:, ecol:ecol + 1]))
                vdrain()
                for hh in range(2):
                    rmt = f32s[:, 6144:6656]
                    vop(lambda E, hh=hh, rmt=rmt: E.tensor_scalar(
                        rmt, rpstg2[hh][:, :], scal[:, 24 + hh:25 + hh],
                        None, ALU.subtract))
                    vdrain()
                    vop(lambda E, mi=mi, rmt=rmt: E.tensor_mul(
                        rmt, rmt, ovrb[:, mi * 512:(mi + 1) * 512]))
                    vdrain()
                    vop(lambda E, hh=hh, rmt=rmt: E.tensor_sub(rpstg2[hh][:, :],
                                                               rpstg2[hh][:, :], rmt))
                    vdrain()
                for hh in range(2):
                    vop(lambda E, hh=hh: E.tensor_copy(rpbf[hh][:, :], rpstg2[hh][:, :]))
                vdrain()
            else:
                for hh in range(2):
                    vop(lambda E, hh=hh, bb=bb: E.tensor_add(
                        rpbf[hh][:, :], PP[bb[hh]][:, :], brbc[:, :]))
                vdrain()
            v_rp = P.mark("vector", "sVe")
            ppfree[b] = ("sVe", v_rp)
            ppfree[b2] = ("sVe", v_rp)
            P.wait("gpsimd", "sVe", v_rp)
            for hh in range(2):
                dma("gpsimd",
                    bass.AP(rp_hbm, ((pr * 2 + hh) * NQT + qt) * 128 * 512,
                            [[512, 128], [1, 512]]),
                    rpbf[hh][:, :], "sRp")
            loc["rp_st_fr"][0] = P.val("sRp")
            loc["rp_wr"][qt] = P.val("sRp")
            if qt == NQT - 1:
                emit_gather(0)
                emit_gather(1)

        def emit_gather(qt):
            pS = qt % 2
            P.wait("gpsimd", "sRp", loc["rp_wr"][qt])
            P.wait("gpsimd", "sVe", loc.get(("pos_fr", pS), 0))
            for hh in range(2):
                dma("gpsimd", possb[pS][hh][:, :],
                    bass.AP(rp_hbm, ((pr * 2 + hh) * NQT + qt) * 128 * 512 + 128,
                            [[511, 128], [1, 384]]), "sGa")
            loc[("g_pos", qt)] = P.val("sGa")

        def band_step(qt):
            pS = qt % 2
            qb = qt // 4
            g_pos = loc[("g_pos", qt)]
            b = pp_next()
            P.wait("tensor", *ppfree[b])
            b2 = pp_next()
            P.wait("tensor", *ppfree[b2])
            if qt == 0:
                mm(PP[b][:, 0:128], qpair[pr][0:64, 0:128],
                   kpair[pr][0:64, 3968:4096], True, True)
                mm(PP[b2][:, 0:128], qpair[pr][64:128, 0:128],
                   kpair[pr][64:128, 3968:4096], True, True)
                mm(PP[b][:, 128:384], qpair[pr][0:64, 0:128],
                   kpair[pr][0:64, 0:256], True, True)
                mm(PP[b2][:, 128:384], qpair[pr][64:128, 0:128],
                   kpair[pr][64:128, 0:256], True, True)
            else:
                j0 = qt * 128 - 128
                mm(PP[b][:, 0:384], qpair[pr][0:64, qt * 128:(qt + 1) * 128],
                   kpair[pr][0:64, j0:j0 + 384], True, True)
                mm(PP[b2][:, 0:384], qpair[pr][64:128, qt * 128:(qt + 1) * 128],
                   kpair[pr][64:128, j0:j0 + 384], True, True)
            t_b = P.mark("tensor", "sPe")
            P.wait("vector", "sPe", t_b)
            P.wait("vector", "sGa", g_pos)
            bb = (b, b2)
            for hh in range(2):
                vop(lambda E, hh=hh, bb=bb, pS=pS: E.tensor_add(
                    PP[bb[hh]][:, 0:384], PP[bb[hh]][:, 0:384], possb[pS][hh][:, :]))
            vdrain()
            v_b = P.mark("vector", "sVe")
            loc[("pos_fr", pS)] = v_b
            P.wait("scalar", "sVe", v_b)
            P.wait("scalar", "sVe", st["mcol_rdy"])
            P.wait("scalar", *loc["pS_fr"][pS])
            for hh in range(2):
                sop(lambda E, hh=hh, bb=bb, pS=pS, qt=qt: E.activation(
                    bandP[pS][hh][:, :], PP[bb[hh]][:, 0:384], AF.Exp,
                    bias=mcolneg[:, hh * 16 + qt:hh * 16 + qt + 1]))
            sdrain()
            s_b = P.mark("scalar", "sSc")
            ppfree[b] = ("sSc", s_b)
            ppfree[b2] = ("sSc", s_b)
            loc["band_exp"][qt] = s_b
            if qt + 2 < NQT:
                emit_gather(qt + 2)

        def band_tp_step(qt):
            pS = qt % 2
            qb = qt // 4
            bt = pp_next()
            P.wait("tensor", *ppfree[bt])
            P.wait("tensor", "sSc", loc["band_exp"][qt])
            view = PP[bt][:, 0:384].bitcast(BF16)
            for p in range(3):
                for hh in range(2):
                    t = p * 2 + hh
                    P.op("tensor", lambda E, pS=pS, hh=hh, p=p, t=t:
                         E.transpose(view[:, t * 128:(t + 1) * 128],
                                     bandP[pS][hh][:, p * 128:(p + 1) * 128],
                                     iden_bf[:, :]))
            t_tp = P.mark("tensor", "sPe")
            loc["pS_fr"][pS] = ("sPe", t_tp)
            P.wait("vector", "sPe", t_tp)
            col = qb * 3072 + (qt % 4) * 768
            vop(lambda E, col=col, view=view: E.tensor_copy(
                bandT[:, col:col + 768], view[:, 0:768]))
            vdrain()
            v_bt = P.mark("vector", "sVe")
            ppfree[bt] = ("sVe", v_bt)
            if qt % 4 == 3:
                band_region_rdy[(ch, qb)] = v_bt

        steps = [(0, sqq_step)]
        steps += [(0, lambda g=g: normq_step(g)) for g in range(4)]
        steps += [(0, lambda g=g: knorm_step(g)) for g in range(4)]
        steps += [(0, c1_step), (0, mchain_step), (0, mio_step)]
        steps += [(0, lambda jp=jp: edge_step(jp)) for jp in range(4)]
        steps += [(0, lambda qt=qt: rp_step(qt)) for qt in range(NQT)]
        ab = [(0, lambda qt=0: band_step(0)), (0, lambda qt=1: band_step(1))]
        for k in range(NQT):
            ab.append(((k // 4) * 32 + 32, lambda qt=k: band_tp_step(qt)))
            if k + 2 < NQT:
                ab.append((0, lambda qt=k + 2: band_step(qt)))
        steps += ab
        return steps, st

    def emit_main(ch, st, steps, tail):
        pr = ch % 2
        exp_done = {-2: tail["exp"][0], -1: tail["exp"][1]}
        pvden_done = {-3: tail["pvden"][0], -2: tail["pvden"][1],
                      -1: tail["pvden"][2]}
        pending = []
        nsteps = len(steps)
        next_step_at = [0]
        pieces_total = 128

        def consume_steps(i):
            want = (i + 1) * nsteps // pieces_total
            while next_step_at[0] < want and steps and steps[0][0] <= i:
                steps.pop(0)[1]()
                next_step_at[0] += 1

        P.wait("tensor", "sKV", kv_rdy[ch])
        P.wait("tensor", "sVe", st["aux_rdy"])
        pv_start_guard_ref = [(None, tail["pv_est"])]
        for qblk in range(4):
            q0 = qblk * 512
            for kc in range(NKC):
                i = qblk * NKC + kc
                pa = PA if i % 2 == 0 else PB
                pc = i % 3
                P.wait("tensor", "sSc", exp_done.get(i - 2, 0))
                mm(pa[:, 0:512], kpair[pr][0:64, kc * 128:(kc + 1) * 128],
                   qpair[pr][0:64, q0:q0 + 512], True, False)
                mm(pa[:, 512:1024], kpair[pr][64:128, kc * 128:(kc + 1) * 128],
                   qpair[pr][64:128, q0:q0 + 512], True, False)
                subr = [(max(lo, q0), min(hi, q0 + 512), v)
                        for (lo, hi, v) in region_table(kc)
                        if max(lo, q0) < min(hi, q0 + 512)]
                for si, (lo, hi, v) in enumerate(subr):
                    last = si == len(subr) - 1
                    mm(pa[:, lo - q0:hi - q0], auxsel[0:64, v * 128:v * 128 + 128],
                       auxt[pr][0:64, lo:hi], False, last)
                    mm(pa[:, 512 + lo - q0:512 + hi - q0],
                       auxsel[64:128, v * 128:v * 128 + 128],
                       auxt[pr][64:128, lo:hi], False, last)
                t_qk = P.mark("tensor", "sPe")
                for _ in range(2):
                    if pending_epi:
                        pending_epi.pop(0)()
                if len(pending) >= 2:
                    pending.pop(0)()
                P.wait("scalar", "sPe", t_qk)
                P.wait("scalar", "sPe", pvden_done.get(i - 3, 0))
                sop(lambda E, pa=pa, pc=pc: E.activation(
                    pchunk[pc][:, :], pa[:, 0:1024], AF.Exp))
                s_e = P.mark("scalar", "sSc")
                exp_done[i] = s_e
                ncopy = 0
                P.wait("vector", "sSc", s_e)
                P.wait("vector", "sPe", pvden_done.get(i - 3, 0))
                for (qt, p, bq0) in band_pieces_of_kc(kc):
                    if q0 <= bq0 < q0 + 512:
                        col = qblk * 3072 + (qt % 4) * 768 + p * 256
                        off = bq0 - q0
                        vop(lambda E, pc=pc, col=col, off=off: E.tensor_copy(
                            pchunk[pc][:, off:off + 128], bandT[:, col:col + 128]))
                        vop(lambda E, pc=pc, col=col, off=off: E.tensor_copy(
                            pchunk[pc][:, 512 + off:512 + off + 128],
                            bandT[:, col + 128:col + 256]))
                        ncopy += 1
                v_cp = P.mark("vector", "sVe") if ncopy else P.val("sVe")

                def den_pv(i=i, kc=kc, pc=pc, s_e=s_e, v_cp=v_cp, q0=q0,
                           guard=pv_start_guard_ref[0][1]):
                    P.wait("tensor", "sSc", s_e)
                    P.wait("tensor", "sVe", v_cp)
                    if kc == 0:
                        P.wait("tensor", "sVe", guard["v_pvc"])
                    mm(PD[0:1, :], ones_bf[:, 0:1], pchunk[pc][:, 0:512],
                       kc == 0, kc == NKC - 1)
                    mm(PD[64:65, :], ones_bf[:, 0:1], pchunk[pc][:, 512:1024],
                       kc == 0, kc == NKC - 1, tp=(0, 64))
                    mm(PV1[0:64, :], vpair[pr][:, kc * 128:kc * 128 + 64],
                       pchunk[pc][:, 0:512], kc == 0, kc == NKC - 1)
                    mm(PV1[64:128, :], vpair[pr][:, kc * 128 + 64:kc * 128 + 128],
                       pchunk[pc][:, 512:1024], kc == 0, kc == NKC - 1, tp=(0, 64))
                    pvden_done[i] = P.mark("tensor", "sPe")
                pending.append(den_pv)
                consume_steps(i)
            while pending:
                pending.pop(0)()
            t_last = pvden_done[qblk * NKC + NKC - 1]
            est = {}

            def ep_copies(qblk=qblk, t_last=t_last):
                P.wait("vector", "sPe", t_last)
                vop(lambda E: E.tensor_copy(T2[0:65, 0:512], PD[0:65, :]))
                vop(lambda E: E.tensor_copy(T2[:, 512:1024], PV1[:, :]))
                vdrain()
                est["v_pvc"] = P.mark("vector", "sVe")

            def ep_recip(part):
                c0 = part * 128
                vop(lambda E, c0=c0: E.reciprocal(T1[0:65, c0:c0 + 128],
                                                  T2[0:65, c0:c0 + 128]))
                vdrain()
                if part == 3:
                    P.mark("vector", "sVe")

            def ep_cast():
                vop(lambda E: E.tensor_copy(rcbf[0:65, :], T1[0:65, 0:512]))
                vdrain()
                vop(lambda E: E.tensor_sub(rclo[0:65, :], T1[0:65, 0:512],
                                           rcbf[0:65, :]))
                vdrain()
                est["v_rc"] = P.mark("vector", "sVe")

            def ep_fin(ch=ch, qblk=qblk, q0=q0):
                b = pp_next()
                P.wait("tensor", *ppfree[b])
                P.wait("tensor", "sVe", est["v_rc"])
                mm(PP[b][0:64, :], ones_bf[0:1, 0:64], rcbf[0:1, :], True, False)
                mm(PP[b][64:128, :], ones_bf[64:65, 0:64], rcbf[64:65, :], True, False,
                   tp=(64, 64))
                mm(PP[b][0:64, :], ones_bf[0:1, 0:64], rclo[0:1, :], False, True)
                mm(PP[b][64:128, :], ones_bf[64:65, 0:64], rclo[64:65, :], False, True,
                   tp=(64, 64))
                t_bc = P.mark("tensor", "sPe")
                P.wait("vector", "sPe", t_bc)
                vop(lambda E, b=b: E.tensor_mul(T1[:, 512:1024], T2[:, 512:1024],
                                                PP[b][:, :]))
                vdrain()
                v_pvf = P.mark("vector", "sVe")
                ppfree[b] = ("sVe", v_pvf)
                if ch == 0 and qblk == 0:
                    P.wait("vector", "sXl", x_own_ld)
                vop(lambda E, ch=ch, q0=q0: E.tensor_add(
                    x_own[:, ch * 2048 + q0:ch * 2048 + q0 + 512],
                    x_own[:, ch * 2048 + q0:ch * 2048 + q0 + 512], T1[:, 512:1024]))
                vdrain()
                epil_sems[(ch, qblk)] = P.mark("vector", "sVe")

            pending_epi.extend([ep_copies] + [lambda p=p: ep_recip(p)
                                              for p in range(4)] + [ep_cast, ep_fin])
            pv_start_guard_ref[0] = (qblk, est)
        while steps:
            steps.pop(0)[1]()
        n = 4 * NKC
        tail_out = {"exp": [exp_done[n - 2], exp_done[n - 1]],
                    "pvden": [pvden_done[n - 3], pvden_done[n - 2],
                              pvden_done[n - 1]],
                    "pv_est": pv_start_guard_ref[0][1]}
        return P.val("sPe"), P.val("sVe"), tail_out

    if n_pairs > 0:
        emit_pair_loads(0)
        if n_pairs > 1:
            emit_pair_loads(1)
        vop(lambda E: E.memset(AXT[:, 0:4096], 0.0))
        vdrain()
        P.mark("vector", "sVe")
        steps0, st0 = prep_steps(0)
        for _, f in steps0:
            f()
        states = {0: st0}
        pending_epi = []
        tail = {"exp": [0, 0], "pvden": [0, 0, 0], "pv_est": {"v_pvc": 0}}
        for ch in range(n_pairs):
            st = states[ch]
            if ch + 1 < n_pairs:
                steps_n, st_n = prep_steps(ch + 1)
                states[ch + 1] = st_n
            else:
                steps_n = []
            main_pe, main_ve, tail = emit_main(ch, st, steps_n, tail)
            pair_free[ch % 2] = main_pe
            if ch + 2 < n_pairs:
                emit_pair_loads(ch + 2)
        while pending_epi:
            pending_epi.pop(0)()

    p2_ve = P.val("sVe")
    p2_sc = P.val("sSc")
    p2_pe = P.val("sPe")

    PPX = PP

    P.wait("gpsimd", "sPe", p2_pe)
    for c in range(8):
        dma("gpsimd", wrt_sb[:, c * 16:(c + 1) * 16],
            wrtT.ap()[c * 128:(c + 1) * 128, :], "sFw")
    wrt_rdy = P.val("sFw")

    wexp_free = [p2_pe, p2_pe]
    up_fr = [0, 0]
    dn_fr = [0, 0]
    gbc_fr = 0
    glt_fr = [0, 0]
    out_st = 0
    gstore = 0

    MUB = f32c[:, 2048:3072]
    IVB = f32c[:, 3072:4096]

    for th in range(2):
        toff = th * 1024
        P.wait("scalar", "sPe", P.val("sPe"))
        P.wait("scalar", "sVe", p2_ve if th == 0 else P.val("sVe"))
        for c in range(8):
            sop(lambda E, c=c, toff=toff: E.copy(
                xln2[:, c * 1024:(c + 1) * 1024],
                x_own[:, c * 2048 + toff:c * 2048 + toff + 1024]))
        s_cast2 = P.mark("scalar", "sSc")
        sdrain()
        P.wait("tensor", "sSc", s_cast2)
        v_prev = 0
        for h2 in range(2):
            P.wait("scalar", "sPe", P.val("sPe"))
            P.wait("tensor", "sVe", v_prev)
            for c in range(8):
                mm(PV1[0:1, :], ones_bf[:, 0:1],
                   xln2[:, c * 1024 + h2 * 512:c * 1024 + h2 * 512 + 512], c == 0, c == 7)
            t_a = P.mark("tensor", "sPe")
            P.wait("vector", "sPe", t_a)
            v_prev = vop(lambda E, h2=h2: E.tensor_scalar(
                f32c[0:1, 2048 + h2 * 512:2048 + h2 * 512 + 512], PV1[0:1, :],
                1.0 / 1024, None, ALU.mult), "sVe")
            for c in range(8):
                s_last = sop(lambda E, c=c, h2=h2: E.square(
                    sqP3[:, :],
                    xln2[:, c * 1024 + h2 * 512:c * 1024 + h2 * 512 + 512]), "sSc")
                P.wait("tensor", "sSc", s_last)
                mm(PD[0:1, :], ones_bf[:, 0:1], sqP3[:, :], c == 0, c == 7)
                t_s3 = P.mark("tensor", "sPe")
                P.wait("scalar", "sPe", t_s3)
            t_b2 = P.val("sPe")
            P.wait("vector", "sPe", t_b2)
            v_prev = vop(lambda E, h2=h2: E.tensor_scalar(
                f32c[0:1, 3072 + h2 * 512:3072 + h2 * 512 + 512], PD[0:1, :],
                1.0 / 1024, None, ALU.mult), "sVe")
        vdrain()
        MU2 = f32c[0:1, 2048:3072]
        EX2 = f32c[0:1, 3072:4096]
        vop(lambda E: E.tensor_mul(f32c[0:1, 1024:2048], MU2, MU2)); vdrain()
        vop(lambda E: E.tensor_sub(EX2, EX2, f32c[0:1, 1024:2048])); vdrain()
        vop(lambda E: E.tensor_scalar(EX2, EX2, 1.0, EPS, ALU.mult, ALU.add)); vdrain()
        recip_nr(f32c[0:1, 1024:2048], EX2, f32c[0:1, 0:1024])
        v_v2 = P.mark("vector", "sVe")
        P.wait("scalar", "sVe", v_v2)
        s_r2 = sop(lambda E: E.sqrt(f32c[0:1, 0:1024], f32c[0:1, 1024:2048]), "sSc")
        P.wait("vector", "sSc", s_r2)
        vop(lambda E: E.tensor_copy(bfc[0:1, 2048:3072], MU2))
        vop(lambda E: E.tensor_copy(bfc[0:1, 3072:4096], f32c[0:1, 0:1024]))
        vdrain()
        v_r2 = P.mark("vector", "sVe")
        P.wait("tensor", "sVe", v_r2)
        v_prev = 0
        for h2 in range(2):
            P.wait("tensor", "sVe", v_prev)
            mm(PV1[:, :], ones_bf[0:1, :], bfc[0:1, 2048 + h2 * 512:2048 + h2 * 512 + 512],
               True, True)
            mm(PD[:, :], ones_bf[0:1, :], bfc[0:1, 3072 + h2 * 512:3072 + h2 * 512 + 512],
               True, True)
            t_bc2 = P.mark("tensor", "sPe")
            P.wait("vector", "sPe", t_bc2)
            vop(lambda E, h2=h2: E.tensor_copy(MUB[:, h2 * 512:h2 * 512 + 512],
                                               PV1[:, :]))
            v_prev = vop(lambda E, h2=h2: E.tensor_copy(
                IVB[:, h2 * 512:h2 * 512 + 512], PD[:, :]), "sVe")
        vdrain()
        for c in range(8):
            vop(lambda E, c=c, toff=toff: E.tensor_sub(
                T1, x_own[:, c * 2048 + toff:c * 2048 + toff + 1024], MUB))
            vdrain()
            vop(lambda E, c=c: E.tensor_mul(
                xln2[:, c * 1024:(c + 1) * 1024], T1, IVB))
            vdrain()
        v_x2 = P.mark("vector", "sVe")

        P.wait("tensor", "sFw", wrt_rdy)
        P.wait("tensor", "sVe", v_x2)
        v_lgfree = 0
        v_gtfree = 0
        for tc in range(8):
            P.wait("tensor", "sVe", v_lgfree)
            for dc in range(8):
                mm(PA[0:128, 0:16],
                   xln2[:, dc * 1024 + tc * 128:dc * 1024 + (tc + 1) * 128],
                   wrt_sb[:, dc * 16:(dc + 1) * 16], dc == 0, dc == 7)
            t_lg = P.mark("tensor", "sPe")
            P.wait("vector", "sPe", t_lg)
            W0 = rw[:, 0:16]
            W1 = rw[:, 16:32]
            W2 = rw[:, 32:48]
            KEEP = rw[:, 48:64]
            MX = rw[:, 64:65]
            SM = rw[:, 65:66]
            vop(lambda E: E.reduce_max(MX, PA[0:128, 0:16], axis=AX.X)); vdrain()
            v_lgfree = vop(lambda E: E.tensor_scalar(
                W0, PA[0:128, 0:16], MX, None, ALU.subtract), "sVe")
            vdrain()
            v_sf = v_lgfree
            P.wait("scalar", "sVe", v_sf)
            s_ex = sop(lambda E: E.activation(W1, W0, AF.Exp), "sSc")
            P.wait("vector", "sSc", s_ex)
            vop(lambda E: E.reduce_sum(SM, W1, axis=AX.X)); vdrain()
            vop(lambda E: E.reciprocal(MX, SM)); vdrain()
            vop(lambda E: E.tensor_scalar(W1, W1, MX, None, ALU.mult)); vdrain()
            vop(lambda E: E.tensor_copy(W0, W1))
            vop(lambda E: E.memset(KEEP, 0.0)); vdrain()
            for kk in range(4):
                vop(lambda E: E.reduce_max(MX, W0, axis=AX.X)); vdrain()
                vop(lambda E: E.tensor_scalar(W2, W0, MX, None, ALU.is_ge)); vdrain()
                vop(lambda E: E.tensor_scalar(W2, W2, 1e30, -1e30, ALU.mult, ALU.add))
                vdrain()
                vop(lambda E: E.tensor_sub(W2, W2, iota_sb[:, :])); vdrain()
                vop(lambda E: E.reduce_max(MX, W2, axis=AX.X)); vdrain()
                vop(lambda E: E.tensor_scalar(W2, W2, MX, None, ALU.is_ge)); vdrain()
                vop(lambda E: E.tensor_add(KEEP, KEEP, W2)); vdrain()
                vop(lambda E: E.scalar_tensor_tensor(W0, W2, -1e30, W0, ALU.mult, ALU.add))
                vdrain()
            vop(lambda E: E.tensor_mul(W2, W1, KEEP)); vdrain()
            vop(lambda E: E.tensor_copy(scb[:, 0:16], W2)); vdrain()
            v_g = P.mark("vector", "sVe")
            P.wait("tensor", "sVe", v_g)
            P.wait("tensor", "sVe", v_gtfree)
            mm(PV1[0:16, 0:128], scb[:, 0:16], iden_bf[:, :], True, True)
            t_g = P.mark("tensor", "sPe")
            P.wait("vector", "sPe", t_g)
            v_gtfree = vop(lambda E, tc=tc: E.tensor_copy(
                gateT[:, tc * 128:(tc + 1) * 128], PV1[0:16, 0:128]), "sVe")
        v_gT = v_gtfree
        P.wait("sync", "sVe", v_gT)
        P.wait("sync", "sGa", gstore)
        w_g = dma("sync", gate_hbm.ap(), gateT[:, :], "sWr")

        g_row = 0
        t_gbc_prev = 0
        for e in range(NE):
            we = e % 2
            P.wait("gpsimd", "sPe", wexp_free[we])
            for c in range(8):
                dma("gpsimd", wup[we][:, c * 512:(c + 1) * 512],
                    wupT.ap()[e, c * 128:(c + 1) * 128, :], "sFw")
            for c in range(4):
                dma("gpsimd", wdn[we][:, c * 1024:(c + 1) * 1024],
                    wdnT.ap()[e, c * 128:(c + 1) * 128, :], "sFw")
            w_rdy = P.val("sFw")
            routed = e >= 2
            if routed:
                P.wait("sync", "sWr", w_g)
                P.wait("sync", "sPe", t_gbc_prev)
                g_row = dma("sync", grow[:, :], gate_hbm.ap()[e - 2:e - 1, :], "sGa2")
                P.wait("tensor", "sGa2", g_row)
                P.wait("tensor", "sVe", max(gbc_fr, v_gT))
                for jp in range(2):
                    mm([PV1, PD][jp][:, :], ones_bf[0:1, :],
                       grow[0:1, jp * 512:(jp + 1) * 512], True, True)
            t_gbc = P.mark("tensor", "sPe") if routed else P.val("sPe")
            t_gbc_prev = t_gbc
            gstore = g_row
            P.wait("tensor", "sFw", w_rdy)
            for ec in range(4):
                for jp in range(2):
                    pA = (ec * 2 + jp) % 2
                    P.wait("tensor", "sSc", up_fr[pA])
                    P.wait("tensor", "sVe", dn_fr[pA])
                    for dc in range(8):
                        mm(PPX[pA][:, :],
                           wup[we][:, dc * 512 + ec * 128:dc * 512 + (ec + 1) * 128],
                           xln2[:, dc * 1024 + jp * 512:dc * 1024 + (jp + 1) * 512],
                           dc == 0, dc == 7)
                    t_u = P.mark("tensor", "sPe")
                    P.wait("scalar", "sPe", t_u)
                    if routed:
                        P.wait("scalar", "sVe", glt_fr[pA])
                        s_gl = sop(lambda E, pA=pA: E.activation(
                            f32c[:, pA * 512:(pA + 1) * 512], PPX[pA][:, :], AF.Gelu),
                            "sSc")
                        up_fr[pA] = s_gl
                        P.wait("vector", "sSc", s_gl)
                        P.wait("vector", "sPe", t_gbc)
                        v_ac = vop(lambda E, ec=ec, jp=jp, pA=pA: E.tensor_mul(
                            act[:, ec * 1024 + jp * 512:ec * 1024 + (jp + 1) * 512],
                            f32c[:, pA * 512:(pA + 1) * 512], [PV1, PD][jp][:, :]), "sVe")
                        gbc_fr = v_ac
                        glt_fr[pA] = v_ac
                    else:
                        s_gl = sop(lambda E, pA=pA, ec=ec, jp=jp: E.activation(
                            act[:, ec * 1024 + jp * 512:ec * 1024 + (jp + 1) * 512],
                            PPX[pA][:, :], AF.Gelu), "sSc")
                        up_fr[pA] = s_gl
            vdrain()
            v_actdone = vop(lambda E: E.nop(), "sVe")
            sdrain()
            s_actdone = sop(lambda E: E.nop(), "sSc")
            P.wait("tensor", "sVe", v_actdone)
            P.wait("tensor", "sSc", s_actdone)
            for hc in range(8):
                for jp in range(2):
                    pA = (hc * 2 + jp) % 2
                    P.wait("tensor", "sSc", up_fr[pA])
                    P.wait("tensor", "sVe", dn_fr[pA])
                    for ec in range(4):
                        mm(PPX[pA][:, :],
                           wdn[we][:, ec * 1024 + hc * 128:ec * 1024 + (hc + 1) * 128],
                           act[:, ec * 1024 + jp * 512:ec * 1024 + (jp + 1) * 512],
                           ec == 0, ec == 3)
                    t_d = P.mark("tensor", "sPe")
                    P.wait("vector", "sPe", t_d)
                    if e == 0:
                        v_aa = vop(lambda E, pA=pA, hc=hc, jp=jp: E.tensor_copy(
                            acc[:, hc * 1024 + jp * 512:hc * 1024 + (jp + 1) * 512],
                            PPX[pA][:, :]), "sVe")
                    else:
                        v_aa = vop(lambda E, pA=pA, hc=hc, jp=jp: E.tensor_add(
                            acc[:, hc * 1024 + jp * 512:hc * 1024 + (jp + 1) * 512],
                            acc[:, hc * 1024 + jp * 512:hc * 1024 + (jp + 1) * 512],
                            PPX[pA][:, :]), "sVe")
                    dn_fr[pA] = v_aa
            wexp_free[we] = P.val("sPe")

        P.emit("vector", lambda E: E.drain())
        P.wait("vector", "sSt", out_st)
        for c in range(8):
            vop(lambda E, c=c, toff=toff: E.tensor_add(
                T2, acc[:, c * 1024:(c + 1) * 1024],
                x_own[:, c * 2048 + toff:c * 2048 + toff + 1024]))
            vdrain()
            v_o = P.mark("vector", "sVe")
            P.wait("sync", "sVe", v_o)
            out_st = dma("sync", out_d.ap()[c * 128:(c + 1) * 128, toff:toff + 1024],
                         T2, "sSt")
            P.wait("vector", "sSt", out_st)

    P.wait("sync", "sSt", P.val("sSt"))
    P.wait("sync", "sWr", P.val("sWr"))
    P.wait("gpsimd", "sRp", P.val("sRp"))
    P.wait("gpsimd", "sGa", P.val("sGa"))

    with nc.Block() as block:
        P.run(block)
    es.close()
    return nc


_CACHE = {}

def _prep_core_inputs(inputs, c):
    x = inputs["L0_attn_x"]
    b, qh = c // 2, c % 2
    key = "shared"
    if key not in _CACHE:
        Wqkv = np.asarray(inputs["L0_attn_Wqkv"]).reshape(16, 3, 64, 1024)
        WQ = Wqkv[:, 0].reshape(1024, 1024)
        WK = Wqkv[:, 1].reshape(1024, 1024)
        WV = Wqkv[:, 2].reshape(1024, 1024)
        bqkv = np.asarray(inputs["L0_attn_bqkv"]).reshape(16, 3, 64)
        bQ, bK, bV = [np.ascontiguousarray(bqkv[:, i].reshape(1024)) for i in range(3)]
        Wr = np.asarray(inputs["L0_attn_Wr"])
        br = np.asarray(inputs["L0_attn_br"])
        WrT = np.ascontiguousarray(Wr.T)
        wrpadT = np.concatenate([np.repeat(WrT[:, 0:1], 128, 1), WrT,
                                 np.repeat(WrT[:, 256:257], 127, 1)], axis=1)
        brpad = np.concatenate([np.repeat(br[0:1], 128), br,
                                np.repeat(br[256:257], 127)])[None, :]
        wredge4 = np.ascontiguousarray(WrT[:, [256, 0, 256, 0]])
        bredge = np.ascontiguousarray(br[[256, 0, 256, 0]][:, None])
        su = np.asarray(inputs["L0_ffn_shared_up"])
        sd = np.asarray(inputs["L0_ffn_shared_down"])
        ru = np.asarray(inputs["L0_ffn_routed_up"])
        rd = np.asarray(inputs["L0_ffn_routed_down"])
        wupT = np.ascontiguousarray(
            np.concatenate([su, ru], axis=0).transpose(0, 2, 1))
        wdnT = np.ascontiguousarray(
            np.concatenate([sd, rd], axis=0).transpose(0, 2, 1))
        wrtT = np.ascontiguousarray(np.asarray(inputs["L0_ffn_router_w"]).T)
        iden = np.eye(128, dtype=BF)
        iota = np.broadcast_to(np.arange(16, dtype=np.float32), (128, 16)).copy()
        _CACHE[key] = dict(
            wqT=np.ascontiguousarray(WQ.T), wkT=np.ascontiguousarray(WK.T),
            wvT=np.ascontiguousarray(WV.T),
            bq=np.ascontiguousarray(bQ.reshape(8, 128).T),
            bk=np.ascontiguousarray(bK.reshape(8, 128).T),
            bv_row=np.ascontiguousarray(bV[None, :]),
            wrpadT=np.ascontiguousarray(wrpadT), brpad=np.ascontiguousarray(brpad),
            wredge4=wredge4.astype(np.float32), bredge=bredge.astype(np.float32),
            wupT=wupT, wdnT=wdnT, wrtT=wrtT, iden=iden, iota=iota)
    shared = _CACHE[key]

    ovr = np.zeros((2, 128, 512), np.float32)
    if qh == 0:
        for di in range(128):
            ovr[0, di, 128 - di:256 - di] = 1.0
    else:
        for di in range(128):
            ovr[1, di, 384 - di:512] = 1.0
    ovr = ovr.astype(BF)
    auxsel = np.zeros((128, 384), dtype=BF)
    for r0 in (0, 64):
        auxsel[r0 + 0, 0:128] = 1.0
        auxsel[r0 + 1, 128:256] = 1.0
        auxsel[r0 + (0 if qh == 0 else 1), 256:384] = 1.0
    

    xr = np.roll(np.asarray(x[b]), -2048 * qh, axis=0)
    xTc = np.ascontiguousarray(xr.T)

    d = dict(shared)
    d.update(xT=xTc, auxsel_in=auxsel, ovr=ovr)
    return d


_NC = None

def kernel(**inputs):
    global _NC
    in_maps = [_prep_core_inputs(inputs, c) for c in range(8)]
    if _NC is None:
        _NC = build_program()
    res = run_bass_kernel_spmd(_NC, in_maps, core_ids=list(range(8)))
    x = np.asarray(inputs["L0_attn_x"])
    out = np.empty_like(x)
    for c in range(8):
        b, qh = c // 2, c % 2
        out[b, 2048 * qh:2048 * (qh + 1), :] = res.results[c]["out"].T
    _CACHE.clear()
    return out
